# version 01
# baseline (speedup 1.0000x reference)
from contextlib import ExitStack

import numpy as np
import concourse.bass as bass
import concourse.mybir as mybir
from concourse.bass_utils import run_bass_kernel_spmd

F32 = mybir.dt.float32
F16 = mybir.dt.float16
AF = mybir.ActivationFunctionType
ALU = mybir.AluOpType

T = 2048
NT = T // 128
C = 1024
KCH = C // 128
QFULL = 4096
EPS = 1e-5
H_A, HD = 16, 64
H_F, E_F, DH = 8, 5, 64
SCALE_F = DH ** -0.5
KOFF = 0
VOFF = 512 * 2048
AGN = VOFF + 2048 * 520
PAIRS = [[0, 1], [2, 3], [4, 5], [6, 7]]
SEMS = ["wc", "w0", "w1", "wg0", "wg1", "x", "x0", "x1", "d0", "d1",
        "kh0", "kh1", "mm", "ve", "se", "gp", "dd", "cc", "out", "bar"]
NF32 = 8192
NF16 = 88064


class Alloc:
    def __init__(self, ncols, name):
        self.free = [(0, ncols)]
        self.ncols = ncols
        self.peak = 0
        self.name = name

    def alloc(self, n, tag=""):
        n = -(-n // 16) * 16
        for i, (s, ln) in enumerate(self.free):
            if ln >= n:
                if ln == n:
                    self.free.pop(i)
                else:
                    self.free[i] = (s + n, ln - n)
                self.peak = max(self.peak, self.ncols - sum(l for _, l in self.free))
                return s
        raise MemoryError(f"{self.name} OOM: {n} cols for {tag}; free={self.free}")

    def freeblk(self, s, n):
        n = -(-n // 16) * 16
        self.free.append((s, n))
        self.free.sort()
        out = []
        for blk in self.free:
            if out and out[-1][0] + out[-1][1] == blk[0]:
                out[-1] = (out[-1][0], out[-1][1] + blk[1])
            else:
                out.append(list(blk))
        self.free = [tuple(xx) for xx in out]


class B:

    def __init__(self, nc, role, sems, cnt):
        self.nc, self.role, self.sems, self.cnt = nc, role, sems, cnt
        self.wm = {}
        self.gp_chain = None

    def tok(self, sem):
        return (sem, self.cnt[sem])

    DMA_SEMS = ("wc", "w0", "w1", "wg0", "wg1", "x", "x0", "x1", "d0",
                "d1", "kh0", "kh1", "dd", "out")

    def _wait(self, eng, deps):
        for dd in deps or ():
            if dd is None:
                continue
            sem, val = dd
            if sem in self.DMA_SEMS:
                val = self.cnt[sem]
            if val <= 0 or self.wm.get(sem, 0) >= val:
                continue
            eng.wait_ge(self.sems[sem], val)
            self.wm[sem] = val

    def _run(self, owner, fn, deps, inc, n):
        ins = None
        if self.role == owner:
            self._wait(getattr(self.nc, owner), deps)
            ins = fn()
        if inc is not None:
            self.cnt[inc] += n
            if ins is not None:
                ins.then_inc(self.sems[inc], n)
            return (inc, self.cnt[inc])
        return None

    def dma(self, q, out, in_, deps=None, inc=None):
        return self._run(q, lambda: getattr(self.nc, q).dma_start(out=out, in_=in_),
                         deps, inc, 16)

    def mm(self, out, lhsT, rhs, start, stop, deps=None, inc=None, tp=None):
        return self._run("tensor",
                         lambda: self.nc.tensor.matmul(out, lhsT, rhs, start=start,
                                                       stop=stop, tile_position=tp),
                         deps, inc, 1)

    def trans(self, out, in_, ident, deps=None, inc=None):
        return self._run("tensor", lambda: self.nc.tensor.transpose(out, in_, ident),
                         deps, inc, 1)

    def act(self, out, in_, func, bias=None, scale=None, accum_out=None,
            deps=None, inc=None):
        kw = {}
        if bias is not None:
            kw["bias"] = bias
        if scale is not None:
            kw["scale"] = scale
        if accum_out is not None:
            kw["accum_out"] = accum_out
        return self._run("scalar",
                         lambda: self.nc.scalar.activation(out, in_, func, **kw),
                         deps, inc, 1)

    def tt(self, out, a, c, op, deps=None, inc=None, eng="vector"):
        return self._run(eng, lambda: getattr(self.nc, eng).tensor_tensor(out, a, c, op),
                         deps, inc, 1)

    def ts(self, out, in0, s1, s2, op0, op1=None, deps=None, inc=None, eng="vector"):
        def f():
            e = getattr(self.nc, eng)
            if op1 is None:
                return e.tensor_scalar(out, in0, s1, None, op0)
            return e.tensor_scalar(out, in0, s1, s2, op0, op1)
        return self._run(eng, f, deps, inc, 1)

    def stt(self, out, in0, scalar, in1, op0, op1, deps=None, inc=None, eng="vector"):
        return self._run(eng,
                         lambda: getattr(self.nc, eng).scalar_tensor_tensor(
                             out, in0, scalar, in1, op0, op1),
                         deps, inc, 1)

    def copy(self, out, in_, deps=None, inc=None, eng="vector"):
        return self._run(eng, lambda: getattr(self.nc, eng).tensor_copy(out, in_),
                         deps, inc, 1)

    def recip(self, out, in_, deps=None, inc=None):
        return self._run("vector", lambda: self.nc.vector.reciprocal(out, in_),
                         deps, inc, 1)

    def memset(self, ap, v, deps=None, inc=None):
        deps = list(deps or []) + [self.gp_chain]
        tok = self._run("gpsimd", lambda: self.nc.gpsimd.memset(ap, v), deps,
                        "gp", 1)
        self.gp_chain = tok
        return tok

    def affine(self, out, in_, pattern, cmp, fill, base=None, chan=None,
               deps=None, inc=None):
        deps = list(deps or []) + [self.gp_chain]
        tok = self._run("gpsimd",
                        lambda: self.nc.gpsimd.affine_select(
                            out=out, in_=in_, pattern=pattern, compare_op=cmp,
                            fill=fill, base=base, channel_multiplier=chan),
                        deps, "gp", 1)
        self.gp_chain = tok
        return tok

    def coll(self, kind, op, groups, ins, outs, deps=None, inc=None):
        return self._run("gpsimd",
                         lambda: self.nc.gpsimd.collective_compute(
                             kind, op, replica_groups=groups, ins=ins, outs=outs),
                         deps, inc, 1)

    def barrier(self):
        self.cnt["bar"] += 4
        tgt = self.cnt["bar"]
        if self.role in ("tensor", "vector", "scalar", "gpsimd"):
            eng = getattr(self.nc, self.role)
            self._wait(eng, [(s, self.cnt[s]) for s in self.DMA_SEMS])
            eng.drain().then_inc(self.sems["bar"], 1)
            eng.wait_ge(self.sems["bar"], tgt)
            self.wm["bar"] = tgt
        elif self.role == "sync":
            eng = self.nc.sync
            self._wait(eng, [(s, self.cnt[s]) for s in self.DMA_SEMS])
            eng.wait_ge(self.sems["bar"], tgt)
            self.wm["bar"] = tgt
        return ("bar", tgt)

    def final_wait(self, deps):
        if self.role == "gpsimd":
            self._wait(self.nc.gpsimd, deps)


def _program(b, E, taps, stop=None):
    d = E["d"]
    A32, A16 = E["A32"], E["A16"]
    psA, psB, psC, psD = E["psA"], E["psB"], E["psC"], E["psD"]
    tap_t = E["tap_t"]
    ar32 = Alloc(NF32, "af32")
    ar16 = Alloc(NF16, "af16")

    def v32(off, w, p0=0, pn=128):
        return A32[p0:p0 + pn, off:off + w]

    def v16(off, w, p0=0, pn=128):
        return A16[p0:p0 + pn, off:off + w]

    def tap(name, ap, deps):
        if name in taps:
            b.dma("sync", tap_t[name], ap, deps=deps, inc="out")

    class _Stop(Exception):
        pass

    def ckpt(name):
        if stop == name:
            b.barrier()
            b.final_wait([b.tok("out")])
            raise _Stop()

    o_id = ar16.alloc(128, "ident")
    o_ones_c = ar16.alloc(16, "ones col")
    o_ones_r = ar16.alloc(128, "ones row")
    o_Ekv = ar16.alloc(8 * 128, "Ekv")
    o_E6 = ar16.alloc(8 * 128, "E6")
    o_lngb = ar32.alloc(4, "lngb")
    o_geff = ar32.alloc(KCH, "geff")
    o_beff = ar32.alloc(KCH, "beff")
    o_tmpc = ar32.alloc(KCH, "tmpc")
    o_qkvb = ar32.alloc(24, "qkvb")
    o_p1b = ar32.alloc(KCH, "p1b")
    o_p2b = ar32.alloc(KCH, "p2b")
    o_abT = ar32.alloc(H_A * HD, "abT")
    ident = v16(o_id, 128)

    tw = [
        b.dma("sync", v32(o_lngb, 4), d["lngb"][:, :], inc="wc"),
        b.dma("sync", v32(o_tmpc, KCH), d["tmpc"][:, :], inc="wc"),
        b.dma("sync", v32(o_qkvb, 24), d["qkvb"][:, :], inc="wc"),
        b.dma("sync", v32(o_p1b, KCH), d["p1b"][:, :], inc="wc"),
        b.dma("sync", v32(o_p2b, KCH), d["p2b"][:, :], inc="wc"),
        b.dma("sync", v32(o_abT, H_A * HD, 0, 64), d["abT"][:, :], inc="wc"),
    ]

    b.memset(ident[:, :], 0.0)
    b.affine(ident[:, :], ident[:, :], [[-1, 128]], ALU.not_equal, 1.0, base=0, chan=1)
    b.memset(v16(o_ones_c, 1), 1.0)
    b.memset(v16(o_ones_r, 128, 0, 1), 1.0)
    b.memset(v16(o_Ekv, 8 * 128), 0.0)
    for j in range(8):
        base = o_Ekv + j * 128
        b.memset(v16(base + 2 * j, 1, 0, 64), 1.0)
        b.memset(v16(base + 2 * j + 1, 1, 64, 64), 1.0)
        b.memset(v16(base + 64 + 32 + 2 * j, 1, 0, 64), 1.0)
        b.memset(v16(base + 64 + 32 + 2 * j + 1, 1, 64, 64), 1.0)
    b.memset(v16(o_E6, 8 * 128), 1.0)
    for j in range(8):
        sub = v16(o_E6 + j * 128, 128, 0, 16)
        b.affine(sub, sub, [[1, 128]], ALU.is_ge, 0.0, base=128 * j, chan=-64)
        b.affine(sub, sub, [[-1, 128]], ALU.is_ge, 0.0, base=63 - 128 * j, chan=64)
    tsetup = b.tok("gp")

    tgeff = b.ts(v32(o_geff, KCH), v32(o_tmpc, KCH), v32(o_lngb, 1), None,
                 ALU.mult, deps=[tw[0], tw[1]], inc="ve")
    tbeff = b.ts(v32(o_beff, KCH), v32(o_tmpc, KCH), v32(o_lngb + 1, 1), None,
                 ALU.mult, deps=[tw[0], tw[1]], inc="ve")

    def emit_ln(src_dram, o_out16, tapon=False):
        o_stg = ar32.alloc(T, "ln stage")
        o_cst = ar16.alloc(2 * T, "ln cast")
        tmm = None
        tprev = []
        for k in range(KCH):
            tdx = b.dma("sync", v32(o_stg, T), src_dram[128 * k:128 * (k + 1), :],
                        deps=tprev, inc="x")
            xc = v16(o_cst, T)
            xq = v16(o_cst + T, T)
            tc1 = b.copy(xc, v32(o_stg, T), deps=[tdx, tmm], inc="ve")
            tc2 = b.act(xq, v32(o_stg, T), AF.Square, deps=[tdx, tmm], inc="se")
            tprev = [tc1, tc2]
            for nt in range(4):
                s = nt * 512
                b.mm(psA[0:1, s:s + 512], v16(o_ones_c, 1), xc[:, s:s + 512],
                     start=(k == 0), stop=(k == KCH - 1), deps=[tc1, tsetup])
                tmm = b.mm(psA[32:33, s:s + 512], v16(o_ones_c, 1),
                           xq[:, s:s + 512],
                           start=(k == 0), stop=(k == KCH - 1), deps=[tc2],
                           inc="mm" if nt == 3 else None)
        ar16.freeblk(o_cst, 2 * T)
        o_st = ar16.alloc(8 * 1024, "ln st")
        def q32(i, w, p0, pn):
            return A16[p0:p0 + pn, o_st + i * 2048:o_st + i * 2048 + 2 * w]\
                .bitcast(F32)
        o_ab = ar16.alloc(2 * T, "ln ab")
        Arow = v16(o_ab, T, 0, 1)
        Brow = v16(o_ab + T, T, 0, 1)
        tA = tB = None
        for hf in range(2):
            sl = hf * 1024
            mu = q32(0, 1024, 0, 1)
            var = q32(1, 1024, 0, 1)
            r = q32(2, 1024, 0, 1)
            tp_ = q32(3, 1024, 0, 1)
            dd = [tmm, tB]
            t1 = b.ts(mu, psA[0:1, sl:sl + 1024], 1.0 / C, None, ALU.mult,
                      deps=dd, inc="ve")
            t3 = b.act(tp_, mu, AF.Square, deps=[t1, tB], inc="se")
            t2 = b.ts(var, psA[32:33, sl:sl + 1024], 1.0 / C, EPS,
                      ALU.mult, ALU.add, deps=dd, inc="ve")
            t4 = b.tt(var, var, tp_, ALU.subtract, deps=[t2, t3], inc="ve")
            t6 = b.act(r, var, AF.Sqrt, deps=[t4], inc="se")
            t7 = b.recip(r, r, deps=[t6], inc="ve")
            t8 = b.act(tp_, r, AF.Square, deps=[t7], inc="se")
            t9 = b.tt(tp_, tp_, var, ALU.mult, deps=[t8], inc="ve")
            t10 = b.ts(tp_, tp_, -0.5, 1.5, ALU.mult, ALU.add, deps=[t9], inc="ve")
            t11 = b.tt(r, r, tp_, ALU.mult, deps=[t10], inc="ve")
            tA = b.copy(Arow[:, sl:sl + 1024], r, deps=[t11], inc="ve")
            t12 = b.tt(tp_, mu, r, ALU.mult, deps=[t11], inc="ve")
            tB = b.ts(Brow[:, sl:sl + 1024], tp_, -1.0, None, ALU.mult,
                      deps=[t12], inc="ve")
        if tapon and "lnab" in taps:
            tk_ = b.dma("sync", tap_t["lnab"], A16[0:1, o_ab:o_ab + 2 * T],
                        deps=[tA, tB], inc="out")
            b.final_wait([tk_])
        if tapon and "lnsums" in taps:
            o_sdbg = ar32.alloc(2 * T, "sums dbg")
            ts1 = b.copy(v32(o_sdbg, T, 0, 1), psA[0:1, 0:T], deps=[tmm], inc="ve")
            ts2 = b.copy(v32(o_sdbg + T, T, 0, 1), psA[32:33, 0:T], deps=[tmm],
                         inc="ve")
            b.dma("sync", tap_t["lnsums"][0:1, :], v32(o_sdbg, T, 0, 1),
                  deps=[ts1], inc="out")
            b.dma("sync", tap_t["lnsums"][1:2, :], v32(o_sdbg + T, T, 0, 1),
                  deps=[ts2], inc="out")
            ar32.freeblk(o_sdbg, 2 * T)
        for nt in range(4):
            s = nt * 512
            ta = b.mm(psB[:, 0:512], v16(o_ones_r, 128, 0, 1), Arow[:, s:s + 512],
                      True, True, deps=[tA] + tprev, inc="mm")
            tb = b.mm(psB[:, 512:1024], v16(o_ones_r, 128, 0, 1), Brow[:, s:s + 512],
                      True, True, deps=[tB] + tprev, inc="mm")
            for k in range(KCH):
                tdx = b.dma("sync", v32(o_stg, 512),
                            src_dram[128 * k:128 * (k + 1), s:s + 512],
                            deps=tprev, inc="x")
                o_t = ar16.alloc(512, "ln t")
                u1 = b.tt(v16(o_t, 512), v32(o_stg, 512), psB[:, 0:512], ALU.mult,
                          deps=[ta, tdx], inc="ve")
                u2 = b.tt(v16(o_out16 + k * T + s, 512), v16(o_t, 512),
                          psB[:, 512:1024], ALU.add, deps=[u1, tb], inc="ve")
                tprev = [u2]
                ar16.freeblk(o_t, 512)
        ar16.freeblk(o_st, 8 * 1024)
        ar32.freeblk(o_stg, T)
        ar16.freeblk(o_ab, 2 * T)
        b.barrier()

    o_xin = ar16.alloc(KCH * T, "xin")
    emit_ln(d["xT"], o_xin, tapon=True)
    tap("xin", A16[:, o_xin:o_xin + KCH * T], [b.tok("ve")])
    xin_ch = lambda k, c0, w: v16(o_xin + k * T + c0, w)
    ckpt("S1")

    o_wstg = ar32.alloc(2 * 128, "wstg")
    o_wblk = ar16.alloc(2 * 128, "wblk")
    wpar_tok = [None, None]

    def wblock(src_ap, par):
        s32 = v32(o_wstg + par * 128, 128)
        s16 = v16(o_wblk + par * 128, 128)
        td = b.dma("sync", s32, src_ap, deps=[wpar_tok[par]],
                   inc="w0" if par == 0 else "w1")
        tc = b.copy(s16, s32, deps=[td], inc="ve")
        return s16, tc

    tile_tok = [None] * 4

    def mm_weight(blkB, src_ch, out_evac):
        tiles = [psB[:, 0:512], psB[:, 512:1024], psC[:, :], psD[:, :]]
        tmm = None
        for kc in range(KCH):
            par = kc % 2
            w16, tcst = wblock(blkB[kc], par)
            for nt in range(4):
                tmm = b.mm(tiles[nt], w16, src_ch(kc, nt * 512, 512),
                           start=(kc == 0), stop=(kc == KCH - 1),
                           deps=[tcst, tile_tok[nt]],
                           inc="mm" if nt == 3 else None)
            wpar_tok[par] = tmm
        toks = out_evac(tiles, tmm)
        for nt in range(4):
            tile_tok[nt] = toks[nt]

    o_ktok = ar16.alloc(NT * C, "ktok")
    o_vtok = ar16.alloc(NT * C, "vtok")
    kt_ch = lambda tcn, c0, w: v16(o_ktok + tcn * C + c0, w)
    vt_ch = lambda tcn, c0, w: v16(o_vtok + tcn * C + c0, w)

    def kv_group(mc0, o_tok, gcol_fn, bcol_fn):
        o_kv = ar16.alloc(KCH * T, "kvT")
        o_sq = ar16.alloc(T, "sq")
        tstat = None
        tsq = None
        for j in range(KCH):
            mcx = mc0 + j

            def evac(tiles, tmm, j=j, mcx=mcx):
                toks = []
                for nt in range(4):
                    te = b.act(v16(o_kv + j * T + nt * 512, 512), tiles[nt],
                               AF.Identity, bias=v32(o_qkvb + mcx, 1),
                               deps=[tmm, tw[2]], inc="se")
                    toks.append(te)
                return toks

            mm_weight(d["qkvB"][mcx], xin_ch, evac)
            tevs = list(tile_tok)
            tsq = b.act(v16(o_sq, T), v16(o_kv + j * T, T), AF.Square,
                        deps=tevs + [tstat], inc="se")
            Ev = v16(o_Ekv + j * 128, 64)
            Eq = v16(o_Ekv + j * 128 + 64, 64)
            for nt in range(4):
                s = nt * 512
                b.mm(psA[0:64, s:s + 512], Ev, v16(o_kv + j * T + s, 512),
                     start=(j == 0), stop=False, deps=tevs + [tsetup])
                tstat = b.mm(psA[0:64, s:s + 512], Eq, v16(o_sq + s, 512),
                             start=False, stop=(j == KCH - 1), deps=[tsq],
                             inc="mm" if nt == 3 else None)
        ar16.freeblk(o_sq, T)
        if ("kvsums" in taps) and mc0 == 8:
            o_sdb = ar32.alloc(T, "kvs dbg")
            td1 = b.copy(v32(o_sdb, T, 0, 64), psA[0:64, 0:T], deps=[tstat], inc="ve")
            b.dma("sync", tap_t["kvsums"], v32(o_sdb, T, 0, 64), deps=[td1],
                  inc="out")
            ar32.freeblk(o_sdb, T)
        o_st = ar16.alloc(8 * 1024, "kv st")
        def q32(i, w):
            return A16[0:16, o_st + i * 2048:o_st + i * 2048 + 2 * w].bitcast(F32)
        o_ab = ar16.alloc(2 * T, "kv ab")
        Ar = v16(o_ab, T, 0, 16)
        Br = v16(o_ab + T, T, 0, 16)
        tA = tB = None
        for hf in range(2):
            sl = hf * 1024
            mu = q32(0, 1024)
            var = q32(1, 1024)
            r = q32(2, 1024)
            tp_ = q32(3, 1024)
            dd = [tstat, tB]
            t1 = b.ts(mu, psA[0:16, sl:sl + 1024], 1.0 / HD, None, ALU.mult,
                      deps=dd, inc="ve")
            t3 = b.act(tp_, mu, AF.Square, deps=[t1, tB], inc="se")
            t2 = b.ts(var, psA[32:48, sl:sl + 1024], 1.0 / HD, EPS,
                      ALU.mult, ALU.add, deps=dd, inc="ve")
            t4 = b.tt(var, var, tp_, ALU.subtract, deps=[t2, t3], inc="ve")
            t6 = b.act(r, var, AF.Sqrt, deps=[t4], inc="se")
            t7 = b.recip(r, r, deps=[t6], inc="ve")
            t8 = b.act(tp_, r, AF.Square, deps=[t7], inc="se")
            t9 = b.tt(tp_, tp_, var, ALU.mult, deps=[t8], inc="ve")
            t10 = b.ts(tp_, tp_, -0.5, 1.5, ALU.mult, ALU.add, deps=[t9], inc="ve")
            t11 = b.tt(r, r, tp_, ALU.mult, deps=[t10], inc="ve")
            tA = b.copy(Ar[:, sl:sl + 1024], r, deps=[t11], inc="ve")
            t12 = b.tt(tp_, mu, r, ALU.mult, deps=[t11], inc="ve")
            tB = b.ts(Br[:, sl:sl + 1024], tp_, -1.0, None, ALU.mult,
                      deps=[t12], inc="ve")
        ar16.freeblk(o_st, 8 * 1024)
        if ("kvab" in taps) and mc0 == 8:
            b.dma("sync", tap_t["kvab"], A16[0:16, o_ab:o_ab + 2 * T],
                  deps=[tA, tB], inc="out")
        b.barrier()
        o_t = ar16.alloc(512, "kv ap t")
        tu3 = []
        tev = []
        for j in range(KCH):
            E6 = v16(o_E6 + j * 128, 128, 0, 16)
            for nt in range(4):
                s = nt * 512
                ta = b.mm(psB[:, 0:512], E6, Ar[:, s:s + 512], True, True,
                          deps=[tA] + tu3, inc="mm")
                tb = b.mm(psB[:, 512:1024], E6, Br[:, s:s + 512], True, True,
                          deps=[tB], inc="mm")
                tm = v16(o_t, 512)
                u1 = b.tt(tm, v16(o_kv + j * T + s, 512), psB[:, 0:512], ALU.mult,
                          deps=[ta] + tev, inc="ve")
                u2 = b.tt(tm, tm, psB[:, 512:1024], ALU.add, deps=[u1, tb], inc="ve")
                u3a = b.ts(tm, tm, gcol_fn(j), None, ALU.mult,
                           deps=[u2, tgeff, tw[0]], inc="ve")
                u3 = b.ts(tm, tm, bcol_fn(j), None, ALU.add,
                          deps=[u3a, tbeff], inc="ve")
                tu3 = [u3]
                ttr = None
                for q4 in range(4):
                    ttr = b.trans(psC[:, 64 * q4:64 * (q4 + 1)].bitcast(F16),
                                  tm[:, 128 * q4:128 * (q4 + 1)], ident,
                                  deps=[u3] + tev, inc="mm" if q4 == 3 else None)
                tev = []
                for q4 in range(4):
                    tcn = nt * 4 + q4
                    te = b.copy(A16[:, o_tok + tcn * C + j * 128:
                                    o_tok + tcn * C + j * 128 + 128],
                                psC[:, 64 * q4:64 * (q4 + 1)].bitcast(F16),
                                deps=[ttr], inc="ve")
                    tev = [te]
                b.barrier()
        ar16.freeblk(o_t, 512)
        ar16.freeblk(o_kv, KCH * T)
        ar16.freeblk(o_ab, 2 * T)
        b.barrier()

    kv_group(8, o_ktok, lambda j: v32(o_geff + j, 1),
             lambda j: v32(o_beff + j, 1))
    kv_group(16, o_vtok, lambda j: v32(o_lngb + 2, 1),
             lambda j: v32(o_lngb + 3, 1))
    tap("ktok", A16[:, o_ktok:o_ktok + NT * C], [b.tok("ve")])
    tap("vtok", A16[:, o_vtok:o_vtok + NT * C], [b.tok("ve")])

    o_q = ar16.alloc(KCH * T, "qT")
    for j in range(KCH):
        def evq(tiles, tmm, j=j):
            toks = []
            for nt in range(4):
                te = b.act(v16(o_q + j * T + nt * 512, 512), tiles[nt], AF.Identity,
                           bias=v32(o_qkvb + j, 1), deps=[tmm, tw[2]], inc="se")
                toks.append(te)
            return toks
        mm_weight(d["qkvB"][j], xin_ch, evq)
    b.barrier()
    tap("q", A16[:, o_q:o_q + KCH * T], [b.tok("se")])
    ckpt("S2")

    tkv = None
    for h in range(H_A):
        for tcn in range(NT):
            tkv = b.mm(psA[0:64, h * 64:h * 64 + 64], kt_ch(tcn, h * 64, 64),
                       vt_ch(tcn, h * 64, 64), start=(tcn == 0), stop=(tcn == NT - 1),
                       inc="mm" if (tcn == NT - 1 and h == H_A - 1) else None)
    o_kvp = ar32.alloc(H_A * HD, "kv partials")
    tkevac = b.ts(v32(o_kvp, H_A * HD, 0, 64), psA[0:64, 0:H_A * HD],
                  1.0 / QFULL, None, ALU.mult, deps=[tkv], inc="ve")
    tout = b.dma("gpsimd", E["kv_in"][:, :], v32(o_kvp, H_A * HD, 0, 64),
                 deps=[tkevac], inc="dd")
    tcc = b.coll("AllReduce", ALU.add, PAIRS, [E["kv_in"][:, :]],
                 [E["kv_out"][:, :]], deps=[tout], inc="cc")
    tin = b.dma("gpsimd", v32(o_kvp, H_A * HD, 0, 64), E["kv_out"][:, :],
                deps=[tcc], inc="dd")
    tfin = b.tt(v32(o_kvp, H_A * HD, 0, 64), v32(o_kvp, H_A * HD, 0, 64),
                v32(o_abT, H_A * HD, 0, 64), ALU.add, deps=[tin, tw[5]], inc="ve")
    tap("kvf", A32[0:64, o_kvp:o_kvp + H_A * HD], [tfin])
    o_kvbd = ar16.alloc(8 * 128, "kv bd")
    tms = b.memset(v16(o_kvbd, 8 * 128), 0.0, deps=[], inc="gp")
    tbd = []
    for hp in range(8):
        base = o_kvbd + hp * 128
        b1 = b.copy(v16(base, 64, 0, 64), v32(o_kvp + 2 * hp * 64, 64, 0, 64),
                    deps=[tfin, tms], inc="ve")
        b2 = b.copy(v16(base + 64, 64, 64, 64),
                    v32(o_kvp + (2 * hp + 1) * 64, 64, 0, 64),
                    deps=[tfin, tms], inc="ve")
        tbd += [b1, b2]
    b.barrier()
    ar16.freeblk(o_ktok, NT * C)
    ar16.freeblk(o_vtok, NT * C)
    ar32.freeblk(o_kvp, H_A * HD)
    ckpt("S3")

    o_att = ar16.alloc(KCH * T, "att")
    tiles4 = [psB[:, 0:512], psB[:, 512:1024], psC[:, :], psD[:, :]]
    ttok4 = [None] * 4
    for hp in range(8):
        tmm = None
        for nt in range(4):
            tmm = b.mm(tiles4[nt], v16(o_kvbd + hp * 128, 128),
                       v16(o_q + hp * T + nt * 512, 512), True, True,
                       deps=[ttok4[nt]] + tbd, inc="mm" if nt == 3 else None)
        for nt in range(4):
            te = b.tt(v16(o_att + hp * T + nt * 512, 512), tiles4[nt],
                      xin_ch(hp, nt * 512, 512), ALU.add, deps=[tmm], inc="ve")
            ttok4[nt] = te
    b.barrier()
    ar16.freeblk(o_q, KCH * T)
    ar16.freeblk(o_kvbd, 8 * 128)
    tap("att", A16[:, o_att:o_att + KCH * T], [b.tok("ve")])
    ckpt("S4")

    o_h1 = ar16.alloc(KCH * T, "h1")
    att_ch = lambda k, c0, w: v16(o_att + k * T + c0, w)
    for mc in range(KCH):
        def ev1(tiles, tmm, mc=mc):
            toks = []
            for nt in range(4):
                te = b.act(v16(o_h1 + mc * T + nt * 512, 512), tiles[nt], AF.Gelu,
                           bias=v32(o_p1b + mc, 1), deps=[tmm, tw[3]], inc="se")
                toks.append(te)
            return toks
        mm_weight(d["p1B"][mc], att_ch, ev1)
    b.barrier()
    ar16.freeblk(o_att, KCH * T)

    h1_ch = lambda k, c0, w: v16(o_h1 + k * T + c0, w)
    o_xs = ar32.alloc(2 * 512, "x slc")
    o_x1s = ar32.alloc(2 * 512, "x1 slc")
    txs = [None, None]
    for mc in range(KCH):
        def ev2(tiles, tmm, mc=mc):
            toks = []
            for nt in range(4):
                par = nt % 2
                s = nt * 512
                tdx = b.dma("sync", v32(o_xs + par * 512, 512),
                            d["xT"][128 * mc:128 * (mc + 1), s:s + 512],
                            deps=[txs[par]], inc="x0" if par == 0 else "x1")
                ta = b.act(v32(o_x1s + par * 512, 512), tiles[nt], AF.Identity,
                           bias=v32(o_p2b + mc, 1), deps=[tmm, tw[4], txs[par]],
                           inc="se")
                tb = b.tt(v32(o_x1s + par * 512, 512), v32(o_x1s + par * 512, 512),
                          xin_ch(mc, s, 512), ALU.add, deps=[ta], inc="ve")
                tc2 = b.tt(v32(o_x1s + par * 512, 512), v32(o_x1s + par * 512, 512),
                           v32(o_xs + par * 512, 512), ALU.add, deps=[tb, tdx],
                           inc="ve")
                td = b.dma("gpsimd", E["x1d"][128 * mc:128 * (mc + 1), s:s + 512],
                           v32(o_x1s + par * 512, 512), deps=[tc2],
                           inc="d0" if par == 0 else "d1")
                txs[par] = td
                toks.append(tc2)
            return toks
        mm_weight(d["p2B"][mc], h1_ch, ev2)
    b.barrier()
    ar16.freeblk(o_h1, KCH * T)
    ar32.freeblk(o_xs, 2 * 512)
    ar32.freeblk(o_x1s, 2 * 512)
    ar16.freeblk(o_xin, KCH * T)
    tap("x1", E["x1d"][:, :], [b.tok("d0"), b.tok("d1")])
    ckpt("S6")

    o_xin1 = ar16.alloc(KCH * T, "xin1")
    o_wo = ar16.alloc(20 * 1024, "wo2 full")
    emit_ln(E["x1d"], o_xin1)
    xin1_ch = lambda k, c0, w: v16(o_xin1 + k * T + c0, w)

    o_E3 = ar16.alloc(20 * 128, "E3")
    o_E7 = ar16.alloc(48, "E7")
    b.memset(v16(o_E3, 20 * 128), 1.0)
    for cc in range(20):
        sub = v16(o_E3 + cc * 128, 128, 0, 40)
        b.affine(sub, sub, [[1, 128]], ALU.is_ge, 0.0, base=128 * cc, chan=-64)
        b.affine(sub, sub, [[-1, 128]], ALU.is_ge, 0.0, base=63 - 128 * cc, chan=64)
    b.memset(v16(o_E7, 48), 1.0)
    sub = v16(o_E7, 40, 0, 8)
    b.affine(sub, sub, [[1, 40]], ALU.is_ge, 0.0, base=0, chan=-5)
    b.affine(sub, sub, [[-1, 40]], ALU.is_ge, 0.0, base=4, chan=5)

    o_wsd = ar16.alloc(KCH * 80, "wsd")
    o_wsd32 = ar32.alloc(128, "wsd stg")
    twsd = []
    tpv = None
    for kc in range(KCH):
        tdw = b.dma("sync", v32(o_wsd32, 80), d["wsdB"][kc], deps=[tpv], inc="x")
        tpv = b.copy(v16(o_wsd + kc * 80, 80), v32(o_wsd32, 80), deps=[tdw], inc="ve")
        twsd.append(tpv)
    o_gate = ar32.alloc(NT * 80, "gates")
    tg = None
    tsg = None
    for tcn in range(NT):
        for kc in range(KCH):
            tg = b.mm(psC[:, 0:80], xin1_ch(kc, tcn * 128, 128),
                      v16(o_wsd + kc * 80, 80), start=(kc == 0), stop=(kc == KCH - 1),
                      deps=[twsd[kc], tsg], inc="mm" if kc == KCH - 1 else None)
        tsg = b.act(v32(o_gate + tcn * 80, 80), psC[:, 0:80], AF.Sigmoid,
                    deps=[tg], inc="se")
    b.barrier()
    ar16.freeblk(o_wsd, KCH * 80)
    ar32.freeblk(o_wsd32, 128)
    tap("gates", A32[:, o_gate:o_gate + NT * 80], [b.tok("se")])

    NM = NT * 8
    o_med = ar32.alloc(8 * NM, "med")
    o_msk = ar16.alloc(2 * NT * 40, "masks")
    o_sg = ar32.alloc(2 * NT * 40, "gated")
    gateR = A32[:, o_gate:o_gate + NT * 80].rearrange(
        "p (tcn hl h e) -> p tcn hl h e", tcn=NT, hl=2, h=8, e=5)
    tmask = []
    for hl in range(2):
        ex = [gateR[:, :, hl, :, e] for e in range(5)]
        m = [v32(o_med + i * NM, NM).rearrange("p (tcn h) -> p tcn h", tcn=NT, h=8)
             for i in range(8)]
        d0 = [b.tok("se")]
        u = []
        u.append(b.tt(m[0], ex[0], ex[1], ALU.min, deps=d0, inc="ve"))
        u.append(b.tt(m[1], ex[0], ex[1], ALU.max, deps=d0, inc="ve"))
        u.append(b.tt(m[2], ex[2], ex[3], ALU.min, deps=d0, inc="ve"))
        u.append(b.tt(m[3], ex[2], ex[3], ALU.max, deps=d0, inc="ve"))
        f_ = b.tt(m[4], m[0], m[2], ALU.max, deps=u[:3], inc="ve")
        g_ = b.tt(m[5], m[1], m[3], ALU.min, deps=u, inc="ve")
        q1 = b.tt(m[6], ex[4], m[4], ALU.min, deps=[f_], inc="ve")
        q2 = b.tt(m[7], ex[4], m[4], ALU.max, deps=[f_], inc="ve")
        q3 = b.tt(m[7], m[7], m[5], ALU.min, deps=[q2, g_], inc="ve")
        tmed = b.tt(m[0], m[6], m[7], ALU.max, deps=[q1, q3], inc="ve")
        mskR = A16[:, o_msk + hl * NT * 40:o_msk + (hl + 1) * NT * 40].rearrange(
            "p (tcn h e) -> p tcn h e", tcn=NT, h=8, e=5)
        sgR = A32[:, o_sg + hl * NT * 40:o_sg + (hl + 1) * NT * 40].rearrange(
            "p (tcn h e) -> p tcn h e", tcn=NT, h=8, e=5)
        tms_ = []
        for e in range(5):
            tms_.append(b.tt(mskR[:, :, :, e], ex[e], m[0], ALU.is_ge,
                             deps=[tmed], inc="ve"))
        tgt = b.tt(sgR, gateR[:, :, hl, :, :], mskR, ALU.mult, deps=tms_, inc="ve")
        tmask.append(tgt)
    ar32.freeblk(o_med, 8 * NM)
    ar32.freeblk(o_gate, NT * 80)
    b.barrier()
    tap("md", A16[:, o_msk + NT * 40:o_msk + 2 * NT * 40], [b.tok("ve")])
    tap("sdg", A32[:, o_sg + NT * 40:o_sg + 2 * NT * 40], [b.tok("ve")])
    ckpt("S8")

    def msk_col(hl, tcn, h, e):
        return v16(o_msk + hl * NT * 40 + tcn * 40 + h * 5 + e, 1)

    def sg_col(hl, tcn, h, e):
        return v32(o_sg + hl * NT * 40 + tcn * 40 + h * 5 + e, 1)

    o_ktf = ar16.alloc(NT * 512, "ktokF")
    o_vex = ar16.alloc(NT * 520, "vext")
    o_wg = ar16.alloc(2 * KCH * 640, "w group")
    o_wg32 = ar32.alloc(2 * 640, "wg stage")
    o_g5 = ar16.alloc(640, "group out")
    twg_par = [None, None]
    o_kT = ar16.alloc(4 * T, "kT")
    tsend = []

    tones = []
    for tcn in range(NT):
        vv = A16[:, o_vex + tcn * 520:o_vex + (tcn + 1) * 520].rearrange(
            "p (hp c) -> p hp c", hp=4, c=130)
        b.memset(vv[:, :, 64:65], 1.0)
        tones.append(b.memset(vv[:, :, 129:130], 1.0, inc="gp"))

    wg_stage_guard = [None, None]

    def wgroup_mm(blkB, og, wpar, dst_emit, gated_dep):
        twl = []
        for kc in range(KCH):
            s32 = v32(o_wg32 + (kc % 2) * 640, 640)
            tdw = b.dma("sync", s32, blkB[kc],
                        deps=[twl[-2] if len(twl) >= 2 else
                              wg_stage_guard[kc % 2], twg_par[wpar]],
                        inc="wg0" if kc % 2 == 0 else "wg1")
            twl.append(b.copy(v16(o_wg + wpar * KCH * 640 + kc * 640, 640), s32,
                              deps=[tdw], inc="ve"))
        wg_stage_guard[0] = twl[-2]
        wg_stage_guard[1] = twl[-1]
        ptiles = [(psB[:, 0:512], psB[:, 512:640]), (psA[:, 0:512], psA[:, 512:640])]
        ptok = [None, None]
        tglast = None
        for tcn in range(NT):
            pp = tcn % 2
            tmm = None
            for kc in range(KCH):
                wap = v16(o_wg + wpar * KCH * 640 + kc * 640, 640)
                b.mm(ptiles[pp][0], xin1_ch(kc, tcn * 128, 128), wap[:, 0:512],
                     start=(kc == 0), stop=(kc == KCH - 1),
                     deps=[twl[kc], ptok[pp]])
                tmm = b.mm(ptiles[pp][1], xin1_ch(kc, tcn * 128, 128),
                           wap[:, 512:640], start=(kc == 0), stop=(kc == KCH - 1),
                           inc="mm" if kc == KCH - 1 else None)
            te1 = b.act(v16(o_g5, 512), ptiles[pp][0], AF.Identity,
                        deps=[tmm, tglast], inc="se")
            te2 = b.act(v16(o_g5 + 512, 128), ptiles[pp][1], AF.Identity,
                        deps=[tmm, tglast], inc="se")
            ptok[pp] = te2
            tglast = dst_emit(og, tcn, [te1, te2, gated_dep])
        twg_par[wpar] = tglast

    def gate_kv(og, tcn, deps):
        kv = og // 4
        hp = og % 4
        tl = None
        for hh in range(2):
            h = 2 * hp + hh
            if kv == 0:
                dst = v16(o_ktf + tcn * 512 + hp * 128 + hh * 64, 64)
            else:
                dst = v16(o_vex + tcn * 520 + hp * 130 + (65 if hh else 0), 64)
            src_e = lambda e: v16(o_g5 + hh * 320 + e * 64, 64)
            t0 = b.ts(dst, src_e(0), sg_col(0, tcn, h, 0), None, ALU.mult,
                      deps=deps + [tmask[0]], inc="ve")
            for e in range(1, 5):
                t0 = b.stt(dst, src_e(e), sg_col(0, tcn, h, e), dst,
                           ALU.mult, ALU.add, deps=[t0], inc="ve")
            tl = t0
        return tl

    for og in range(8):
        wgroup_mm(d["wkvB"][og], og, og % 2, gate_kv, None)
    b.barrier()

    tktr = []
    ttr = None
    tevk = []
    for cc4 in range(4):
        for tcn in range(NT):
            pv = (psC if tcn % 2 == 0 else psD)[:, 0:64].bitcast(F16)
            ttr = b.trans(pv, v16(o_ktf + tcn * 512 + cc4 * 128, 128), ident,
                          deps=tevk[-1:], inc="mm")
            te = b.copy(v16(o_kT + cc4 * T + tcn * 128, 128), pv,
                        deps=[ttr], inc="ve")
            tevk.append(te)
        tsend.append(b.dma("gpsimd",
                           E["ag_in"][KOFF + cc4 * 128 * T:KOFF + (cc4 + 1) * 128 * T]
                           .rearrange("(p c) -> p c", c=T),
                           v16(o_kT + cc4 * T, T), deps=[tevk[-1]], inc="dd"))
    for tcn in range(NT):
        tsend.append(b.dma("gpsimd",
                           E["ag_in"][VOFF + tcn * 128 * 520:VOFF + (tcn + 1) * 128 * 520]
                           .rearrange("(p c) -> p c", c=520),
                           A16[:, o_vex + tcn * 520:o_vex + (tcn + 1) * 520],
                           deps=[tones[tcn], b.tok("ve")], inc="dd"))
    tag = b.coll("AllGather", ALU.bypass, PAIRS,
                 [E["ag_in"][:]], [E["ag_out"][0:2 * AGN]], deps=tsend, inc="cc")
    tap("kT", A16[:, o_kT:o_kT + 4 * T], [b.tok("ve")])
    tap("vext", A16[:, o_vex:o_vex + NT * 520], [b.tok("ve")])
    ckpt("S9")
    ar16.freeblk(o_kT, 4 * T)
    ar16.freeblk(o_ktf, NT * 512)

    o_qtok = ar16.alloc(NT * 512, "q tok")

    def gate_q(og, tcn, deps):
        tl = None
        for hh in range(2):
            h = 2 * og + hh
            dst = v16(o_qtok + tcn * 512 + og * 128 + hh * 64, 64)
            src_e = lambda e: v16(o_g5 + hh * 320 + e * 64, 64)
            t0 = b.ts(dst, src_e(0), sg_col(1, tcn, h, 0), None, ALU.mult,
                      deps=deps + [tmask[1]], inc="ve")
            for e in range(1, 5):
                t0 = b.stt(dst, src_e(e), sg_col(1, tcn, h, e), dst,
                           ALU.mult, ALU.add, deps=[t0], inc="ve")
            tl = t0
        return tl

    for og in range(4):
        wgroup_mm(d["wqB"][og], og, og % 2, gate_q, None)
    b.barrier()
    o_qT = ar16.alloc(4 * T, "qT ffn")
    tevq = []
    for cc4 in range(4):
        for tcn in range(NT):
            pv = (psC if tcn % 2 == 0 else psD)[:, 0:64].bitcast(F16)
            ttr = b.trans(pv, v16(o_qtok + tcn * 512 + cc4 * 128, 128), ident,
                          deps=tevq[-1:], inc="mm")
            te = b.copy(v16(o_qT + cc4 * T + tcn * 128, 128), pv,
                        deps=[ttr], inc="ve")
            tevq.append(te)
    ar16.freeblk(o_qtok, NT * 512)
    ar16.freeblk(o_wg, 2 * KCH * 640)
    ar16.freeblk(o_g5, 640)
    ar32.freeblk(o_wg32, 2 * 640)
    tap("qf", A16[:, o_qT:o_qT + 4 * T], [b.tok("ve")])
    ckpt("S10")

    o_wo32 = ar32.alloc(2 * 128, "wo2 stg")
    tpw = [None, None]
    two = []
    for cc in range(20):
        for mc2 in range(KCH):
            par = (cc * KCH + mc2) % 2
            s32 = v32(o_wo32 + par * 128, 128)
            tdw = b.dma("sync", s32, d["wo2B"][cc, mc2], deps=[tpw[par]],
                        inc="w0" if par == 0 else "w1")
            tpw[par] = b.copy(v16(o_wo + cc * 1024 + mc2 * 128, 128), s32,
                              deps=[tdw], inc="ve")
            two.append(tpw[par])
    ar32.freeblk(o_wo32, 2 * 128)
    ar16.freeblk(o_xin1, KCH * T)
    b.barrier()

    o_khp = ar16.alloc(2 * QFULL, "k hp")
    o_vhp = ar16.alloc(2 * 32 * 130, "v hp")
    o_P = ar16.alloc(2 * 1024, "P bufs")
    o_avT = ar16.alloc(4 * T, "avT")
    o_sst = ar32.alloc(1024, "sums stage")
    tpre = [None, None]

    def prefetch_hp(hp, pp, deps):
        toks = []
        for sl in range(2):
            base = sl * AGN + KOFF + hp * 128 * QFULL // 2
            toks.append(b.dma(
                "gpsimd",
                v16(o_khp + pp * QFULL + sl * (QFULL // 2), QFULL // 2),
                E["ag_out"][base:base + 128 * 2048].rearrange("(p c) -> p c", c=2048),
                deps=deps, inc="kh0" if pp == 0 else "kh1"))
        for sl in range(2):
            for ch in range(16):
                src0 = sl * AGN + VOFF + ch * 128 * 520
                toks.append(b.dma(
                    "gpsimd",
                    v16(o_vhp + pp * 32 * 130 + (sl * 16 + ch) * 130, 130),
                    E["ag_out"][src0:src0 + 128 * 520]
                    .rearrange("(p c) -> p c", c=520)[:, hp * 130:hp * 130 + 130],
                    deps=deps, inc="kh0" if pp == 0 else "kh1"))
        return toks

    tpre[0] = prefetch_hp(0, 0, [tag])
    o_sums_rd = ar32.alloc(T, "sums rd")
    tev_av = []
    tsum_dma = []
    for hp in range(4):
        pp = hp % 2
        if hp < 3:
            tpre[(hp + 1) % 2] = prefetch_hp(hp + 1, (hp + 1) % 2,
                                             [tag] + tev_av[-1:])
        kdeps = tpre[pp]
        for tc4 in range(4):
            ts_ = tc4 * 512
            texp = [None, None]
            tavmm = None
            tav_par = [None, None]
            tSmm = [None, None]
            for jc in range(32):
                par = jc % 2
                if par == 0:
                    So = (psB[:, 0:512], psB[:, 512:1024])
                    Sfull = psB[:, 0:1024]
                else:
                    So = (psC[:, :], psD[:, :])
                    Sfull = None
                sdeps = kdeps + [texp[par]] + (tev_av if jc < 2 else [])
                b.mm(So[0], v16(o_khp + pp * QFULL + jc * 128, 128, 0, 64),
                     v16(o_qT + hp * T + ts_, 512, 0, 64), True, True,
                     deps=sdeps, tp=(0, 0))
                tSmm[par] = b.mm(So[1],
                                 v16(o_khp + pp * QFULL + jc * 128, 128, 64, 64),
                                 v16(o_qT + hp * T + ts_, 512, 64, 64), True, True,
                                 tp=(64, 0), inc="mm")
                if Sfull is not None:
                    texp[par] = b.act(v16(o_P + par * 1024, 1024), Sfull, AF.Exp,
                                      scale=SCALE_F, deps=[tSmm[par], tav_par[par]],
                                      inc="se")
                else:
                    b.act(v16(o_P + par * 1024, 512), So[0], AF.Exp,
                          scale=SCALE_F, deps=[tSmm[par], tav_par[par]], inc="se")
                    texp[par] = b.act(v16(o_P + par * 1024 + 512, 512), So[1],
                                      AF.Exp, scale=SCALE_F, inc="se")
                vch = v16(o_vhp + pp * 32 * 130 + jc * 130, 130)
                b.mm(psA[0:65, 0:512], vch[:, 0:65], v16(o_P + par * 1024, 512),
                     start=(jc == 0), stop=(jc == 31),
                     deps=[texp[par]] + (tev_av if jc == 0 else []))
                tavmm = b.mm(psA[0:65, 512:1024], vch[:, 65:130],
                             v16(o_P + par * 1024 + 512, 512),
                             start=(jc == 0), stop=(jc == 31), inc="mm")
                tav_par[par] = tavmm
            e1 = b.copy(v16(o_avT + hp * T + ts_, 512, 0, 64), psA[0:64, 0:512],
                        deps=[tavmm], inc="ve")
            e2 = b.copy(v16(o_avT + hp * T + ts_, 512, 64, 64),
                        psA[0:64, 512:1024], deps=[tavmm], inc="ve")
            s1 = b.copy(v32(o_sst, 512, 64, 1), psA[64:65, 0:512],
                        deps=[tavmm] + tsum_dma, inc="ve")
            s2 = b.copy(v32(o_sst + 512, 512, 64, 1), psA[64:65, 512:1024],
                        deps=[tavmm] + tsum_dma, inc="ve")
            tsum_dma = [
                b.dma("gpsimd", E["sums_d"][2 * hp:2 * hp + 1, ts_:ts_ + 512],
                      v32(o_sst, 512, 64, 1), deps=[s1], inc="d0"),
                b.dma("gpsimd", E["sums_d"][2 * hp + 1:2 * hp + 2, ts_:ts_ + 512],
                      v32(o_sst + 512, 512, 64, 1), deps=[s2], inc="d1"),
            ]
            tev_av = [e1, e2, s1, s2]
    b.barrier()
    ar16.freeblk(o_khp, 2 * QFULL)
    ar16.freeblk(o_vhp, 2 * 32 * 130)
    ar16.freeblk(o_P, 2 * 1024)
    ar32.freeblk(o_sst, 1024)
    o_E4 = ar16.alloc(40 * 128, "E4")
    b.memset(v16(o_E4, 40 * 128), 0.0)
    for g in range(40):
        h = g // 5
        s0 = (h % 2) * 64
        p0 = (g % 2) * 64
        sub = v16(o_E4 + g * 128 + p0, 64, s0, 64)
        b.affine(sub, sub, [[-1, 64]], ALU.not_equal, 1.0, base=0, chan=1)
    tap("avT", A16[:, o_avT:o_avT + 4 * T], [b.tok("ve")])
    ckpt("S11")

    trd = b.dma("sync", v32(o_sums_rd, T, 0, 8), E["sums_d"][:, :],
                deps=[b.tok("d0"), b.tok("d1")], inc="x")
    o_rs = ar16.alloc(T, "recip16")
    tr1 = b.recip(v32(o_sums_rd, T, 0, 8), v32(o_sums_rd, T, 0, 8),
                  deps=[trd], inc="ve")
    tr2 = b.copy(v16(o_rs, T, 0, 8), v32(o_sums_rd, T, 0, 8), deps=[tr1], inc="ve")
    ar32.freeblk(o_sums_rd, T)
    o_mdT = ar16.alloc(T, "mdT")
    tmdT = []
    for tcn in range(NT):
        pv = psD[0:40, 0:64].bitcast(F16)
        tt_ = b.trans(pv, v16(o_msk + NT * 40 + tcn * 40, 40), ident,
                      deps=tmdT[-1:], inc="mm")
        tmdT.append(b.copy(v16(o_mdT + tcn * 128, 128, 0, 40), pv,
                           deps=[tt_], inc="ve"))
    o_gT = ar16.alloc(T, "gT")
    tgT = []
    for tc4 in range(4):
        s = tc4 * 512
        tm1 = b.mm(psC[0:40, :], v16(o_E7, 40, 0, 8), v16(o_rs + s, 512, 0, 8),
                   True, True, deps=[tr2, tsetup] + tmdT[-1:] + tgT[-1:], inc="mm")
        tgT.append(b.tt(v16(o_gT + s, 512, 0, 40), v16(o_mdT + s, 512, 0, 40),
                        psC[0:40, :], ALU.mult, deps=[tm1] + tmdT, inc="ve"))
    tap("gT", A16[0:40, o_gT:o_gT + T], [b.tok("ve")])
    b.barrier()
    ar16.freeblk(o_mdT, T)
    ar16.freeblk(o_rs, T)

    o_om = ar16.alloc(20 * 512, "OM")
    o_gbc = ar16.alloc(512, "gbc sbuf")
    o_x1s2 = ar32.alloc(2 * 512, "x1 slc f")
    o_yout = ar32.alloc(2 * 512, "y out")
    tx1 = [None, None]
    ty = [None, None]
    tomlast = []
    n_out = 0
    for tc4 in range(4):
        s = tc4 * 512
        tgb_prev = None
        tom = []
        for cc in range(20):
            tgb = b.mm(psD[:, :], v16(o_E3 + cc * 128, 128, 0, 40),
                       v16(o_gT + s, 512, 0, 40), True, True,
                       deps=tgT + [tgb_prev], inc="mm")
            tgbe = b.act(v16(o_gbc, 512), psD[:, :], AF.Identity,
                         deps=[tgb, tom[-1:][0] if tom else None], inc="se")
            tgb_prev = tgbe
            tab = None
            for half in range(2):
                gidx = 2 * cc + half
                hh = gidx // 5
                hp = hh // 2
                tab = b.mm(psC[:, :], v16(o_E4 + gidx * 128, 128),
                           v16(o_avT + hp * T + s, 512),
                           start=(half == 0), stop=(half == 1),
                           deps=[tom[-1] if tom else None] if half == 0 else None,
                           inc="mm" if half == 1 else None)
            tom.append(b.tt(v16(o_om + cc * 512, 512), v16(o_gbc, 512),
                            psC[:, :], ALU.mult, deps=[tgbe, tab], inc="ve"))
        for mc in range(KCH):
            par = mc % 2
            tdx = b.dma("sync", v32(o_x1s2 + par * 512, 512),
                        E["x1d"][128 * mc:128 * (mc + 1), s:s + 512],
                        deps=[ty[par]], inc="x0" if par == 0 else "x1")
            ptile = psB[:, 0:512] if par == 0 else psB[:, 512:1024]
            tmm = None
            for cc in range(20):
                tmm = b.mm(ptile, v16(o_wo + cc * 1024 + mc * 128, 128),
                           v16(o_om + cc * 512, 512),
                           start=(cc == 0), stop=(cc == 19),
                           deps=[tom[cc], ty[par]] + two[-1:],
                           inc="mm" if cc == 19 else None)
            tyv = b.tt(v32(o_yout + par * 512, 512), ptile,
                       v32(o_x1s2 + par * 512, 512), ALU.add,
                       deps=[tmm, tdx], inc="ve")
            ty[par] = b.dma("gpsimd", E["y"][128 * mc:128 * (mc + 1), s:s + 512],
                            v32(o_yout + par * 512, 512), deps=[tyv], inc="out")
            n_out += 1
        tomlast = tom
        if "om0" in taps and tc4 == 0:
            tap("om0", v16(o_om, 512), [tomlast[0]])
    b.final_wait([b.tok("out")])
    return ar32.peak, ar16.peak


TAP_SHAPES = {
    "xin": ([C, T], F16), "q": ([C, T], F16),
    "ktok": ([128, NT * C], F16), "vtok": ([128, NT * C], F16),
    "kvf": ([64, H_A * HD], F32), "att": ([C, T], F16),
    "x1": ([C, T], F32), "gates": ([128, NT * 80], F32),
    "sdg": ([128, NT * 40], F32), "md": ([128, NT * 40], F16),
    "kT": ([128, 4 * T], F16), "vext": ([128, NT * 520], F16),
    "qf": ([128, 4 * T], F16), "avT": ([128, 4 * T], F16),
    "gT": ([40, T], F16), "om0": ([128, 512], F16),
    "lnab": ([1, 2 * T], F16),
    "kvab": ([16, 2 * T], F16),
    "kvsums": ([64, T], F32),
    "lnsums": ([2, T], F32),
}


def build_nc(taps=(), stop=None):
    nc = bass.Bass("TRN2", target_bir_lowering=False, debug=False)
    es = ExitStack()
    d = {}
    for name, shape in [
        ("xT", [C, T]), ("qkvB", [24, KCH, 128, 128]), ("qkvb", [128, 24]),
        ("p1B", [KCH, KCH, 128, 128]), ("p1b", [128, KCH]),
        ("p2B", [KCH, KCH, 128, 128]), ("p2b", [128, KCH]),
        ("lngb", [128, 4]), ("tmpc", [128, KCH]), ("abT", [64, H_A * HD]),
        ("wkvB", [8, KCH, 128, 640]), ("wqB", [4, KCH, 128, 640]),
        ("wsdB", [KCH, 128, 80]), ("wo2B", [20, KCH, 128, 128]),
    ]:
        d[name] = nc.dram_tensor(name, shape, F32, kind="ExternalInput").ap()

    E = {"d": d}
    E["y"] = nc.dram_tensor("y", [C, T], F32, kind="ExternalOutput").ap()
    E["x1d"] = nc.dram_tensor("x1d", [C, T], F32).ap()
    E["ag_in"] = nc.dram_tensor("ag_in", [AGN], F16).ap()
    E["ag_out"] = nc.dram_tensor("ag_out", [2 * AGN + 2048], F16).ap()
    E["kv_in"] = nc.dram_tensor("kv_in", [64, H_A * HD], F32).ap()
    E["kv_out"] = nc.dram_tensor("kv_out", [64, H_A * HD], F32).ap()
    E["sums_d"] = nc.dram_tensor("sums_d", [8, T], F32).ap()
    E["tap_t"] = {}
    for t in taps:
        shp, dt_ = TAP_SHAPES[t]
        E["tap_t"][t] = nc.dram_tensor("tap_" + t, list(shp), dt_,
                                       kind="ExternalOutput").ap()

    E["A32"] = es.enter_context(nc.sbuf_tensor("af32", [128, NF32], F32))
    E["A16"] = es.enter_context(nc.sbuf_tensor("af16", [128, NF16], F16))
    E["psA"] = es.enter_context(nc.psum_tensor("psA", [128, 2048], F32))
    E["psB"] = es.enter_context(nc.psum_tensor("psB", [128, 1024], F32))
    E["psC"] = es.enter_context(nc.psum_tensor("psC", [128, 512], F32))
    E["psD"] = es.enter_context(nc.psum_tensor("psD", [128, 512], F32))
    sems = {s: es.enter_context(nc.semaphore(s)) for s in SEMS}

    ref_cnt = {}
    with nc.Block() as blk:
        def mk(role):
            def f(eng):
                cnt = {s: 0 for s in SEMS}
                bb = B(nc, role, sems, cnt)
                try:
                    _program(bb, E, set(taps), stop=stop)
                except Exception as ex:
                    if type(ex).__name__ != "_Stop":
                        raise
                if not ref_cnt:
                    ref_cnt.update(cnt)
                else:
                    assert ref_cnt == cnt, (role, ref_cnt, cnt)
            return f
        blk.tensor(mk("tensor"))
        blk.vector(mk("vector"))
        blk.scalar(mk("scalar"))
        blk.gpsimd(mk("gpsimd"))
        blk.sync(mk("sync"))
    es.close()
    return nc


def layout_inputs(inputs):
    f = lambda a: np.ascontiguousarray(np.asarray(a, dtype=np.float32))
    x = f(inputs["L0_attn_x"])
    qkv_w = f(inputs["L0_attn_qkv_w"])
    qkv_b = f(inputs["L0_attn_qkv_b"])
    p1_w, p1_b = f(inputs["L0_attn_p1_w"]), f(inputs["L0_attn_p1_b"])
    p2_w, p2_b = f(inputs["L0_attn_p2_w"]), f(inputs["L0_attn_p2_b"])
    kn_g, kn_b = f(inputs["L0_attn_kn_g"]), f(inputs["L0_attn_kn_b"])
    vn_g, vn_b = f(inputs["L0_attn_vn_g"]), f(inputs["L0_attn_vn_b"])
    temp = f(inputs["L0_attn_temperature"]).reshape(H_A)
    ab = f(inputs["L0_attn_attn_bias"])
    Wq, Wkv = f(inputs["L0_ffn_Wq"]), f(inputs["L0_ffn_Wkv"])
    Ws, Wd, Wo = f(inputs["L0_ffn_Ws"]), f(inputs["L0_ffn_Wd"]), f(inputs["L0_ffn_Wo"])

    blk = lambda w, mcs: np.ascontiguousarray(
        w.reshape(KCH, 128, mcs, 128).transpose(2, 0, 1, 3))
    shared = {
        "qkvB": blk(qkv_w.T, 24),
        "qkvb": np.ascontiguousarray(qkv_b.reshape(24, 128).T),
        "p1B": blk(p1_w.T, KCH),
        "p1b": np.ascontiguousarray(p1_b.reshape(KCH, 128).T),
        "p2B": blk(p2_w.T, KCH),
        "p2b": np.ascontiguousarray(p2_b.reshape(KCH, 128).T),
        "lngb": np.ascontiguousarray(np.stack(
            [np.tile(kn_g, 2), np.tile(kn_b, 2), np.tile(vn_g, 2),
             np.tile(vn_b, 2)], axis=1)),
        "tmpc": np.ascontiguousarray(np.repeat(temp, HD).reshape(KCH, 128).T),
        "abT": np.ascontiguousarray(ab.transpose(1, 0, 2).reshape(64, H_A * HD)),
        "wkvB": np.ascontiguousarray(
            Wkv.reshape(KCH, 128, 8, 640).transpose(2, 0, 1, 3)),
        "wqB": np.ascontiguousarray(
            Wq.reshape(KCH, 128, 4, 640).transpose(2, 0, 1, 3)),
        "wsdB": np.ascontiguousarray(
            np.concatenate([Ws, Wd], axis=1).reshape(KCH, 128, 80)),
        "wo2B": np.ascontiguousarray(
            np.tile(Wo.reshape(64, 5, 1024).transpose(1, 0, 2).reshape(320, 1024),
                    (8, 1)).reshape(20, 128, KCH, 128).transpose(0, 2, 1, 3)),
    }
    in_maps = []
    for c in range(8):
        bq, half = c // 2, c % 2
        m = dict(shared)
        m["xT"] = np.ascontiguousarray(x[bq, half * T:(half + 1) * T, :].T)
        in_maps.append(m)
    return in_maps


def kernel(**inputs):
    in_maps = layout_inputs(inputs)
    nc = build_nc()
    res = run_bass_kernel_spmd(nc, in_maps, core_ids=list(range(8)))
    out = np.empty((4, QFULL, C), dtype=np.float32)
    for c in range(8):
        bq, half = c // 2, c % 2
        out[bq, half * T:(half + 1) * T, :] = res.results[c]["y"].T
    return out


# version 02
# speedup vs baseline: 1.0552x; 1.0552x over previous
from contextlib import ExitStack

import numpy as np
import concourse.bass as bass
import concourse.mybir as mybir
from concourse.bass_utils import run_bass_kernel_spmd

F32 = mybir.dt.float32
F16 = mybir.dt.float16
AF = mybir.ActivationFunctionType
ALU = mybir.AluOpType

T = 2048
NT = T // 128
C = 1024
KCH = C // 128
QFULL = 4096
EPS = 1e-5
H_A, HD = 16, 64
H_F, E_F, DH = 8, 5, 64
SCALE_F = DH ** -0.5
KOFF = 0
VOFF = 512 * 2048
AGN = VOFF + 2048 * 520
PAIRS = [[0, 1], [2, 3], [4, 5], [6, 7]]
SEMS = ["wc", "w0", "w1", "w2", "w3", "wg0", "wg1", "x", "x0", "x1",
        "d0", "d1", "kh0", "kh1", "mm", "ve", "se", "gp", "dd", "cc",
        "out", "bar"]
NF32 = 8192
NF16 = 88064


class Alloc:
    def __init__(self, ncols, name):
        self.free = [(0, ncols)]
        self.ncols = ncols
        self.peak = 0
        self.name = name

    def alloc(self, n, tag=""):
        n = -(-n // 16) * 16
        for i, (s, ln) in enumerate(self.free):
            if ln >= n:
                if ln == n:
                    self.free.pop(i)
                else:
                    self.free[i] = (s + n, ln - n)
                self.peak = max(self.peak, self.ncols - sum(l for _, l in self.free))
                return s
        raise MemoryError(f"{self.name} OOM: {n} cols for {tag}; free={self.free}")

    def freeblk(self, s, n):
        n = -(-n // 16) * 16
        self.free.append((s, n))
        self.free.sort()
        out = []
        for blk in self.free:
            if out and out[-1][0] + out[-1][1] == blk[0]:
                out[-1] = (out[-1][0], out[-1][1] + blk[1])
            else:
                out.append(list(blk))
        self.free = [tuple(xx) for xx in out]


class B:

    def __init__(self, nc, role, sems, cnt):
        self.nc, self.role, self.sems, self.cnt = nc, role, sems, cnt
        self.wm = {}
        self.gp_chain = None

    def tok(self, sem):
        return (sem, self.cnt[sem])

    DMA_SEMS = ("wc", "w0", "w1", "w2", "w3", "wg0", "wg1", "x", "x0",
                "x1", "d0", "d1", "kh0", "kh1", "dd", "out")

    def _wait(self, eng, deps):
        for dd in deps or ():
            if dd is None:
                continue
            sem, val = dd
            if sem in self.DMA_SEMS:
                val = self.cnt[sem]
            if val <= 0 or self.wm.get(sem, 0) >= val:
                continue
            eng.wait_ge(self.sems[sem], val)
            self.wm[sem] = val

    def _run(self, owner, fn, deps, inc, n):
        ins = None
        if self.role == owner:
            self._wait(getattr(self.nc, owner), deps)
            ins = fn()
        if inc is not None:
            self.cnt[inc] += n
            if ins is not None:
                ins.then_inc(self.sems[inc], n)
            return (inc, self.cnt[inc])
        return None

    def dma(self, q, out, in_, deps=None, inc=None):
        return self._run(q, lambda: getattr(self.nc, q).dma_start(out=out, in_=in_),
                         deps, inc, 16)

    def mm(self, out, lhsT, rhs, start, stop, deps=None, inc=None, tp=None):
        return self._run("tensor",
                         lambda: self.nc.tensor.matmul(out, lhsT, rhs, start=start,
                                                       stop=stop, tile_position=tp),
                         deps, inc, 1)

    def trans(self, out, in_, ident, deps=None, inc=None):
        return self._run("tensor", lambda: self.nc.tensor.transpose(out, in_, ident),
                         deps, inc, 1)

    def act(self, out, in_, func, bias=None, scale=None, accum_out=None,
            deps=None, inc=None):
        kw = {}
        if bias is not None:
            kw["bias"] = bias
        if scale is not None:
            kw["scale"] = scale
        if accum_out is not None:
            kw["accum_out"] = accum_out
        return self._run("scalar",
                         lambda: self.nc.scalar.activation(out, in_, func, **kw),
                         deps, inc, 1)

    def tt(self, out, a, c, op, deps=None, inc=None, eng="vector"):
        return self._run(eng, lambda: getattr(self.nc, eng).tensor_tensor(out, a, c, op),
                         deps, inc, 1)

    def ts(self, out, in0, s1, s2, op0, op1=None, deps=None, inc=None, eng="vector"):
        def f():
            e = getattr(self.nc, eng)
            if op1 is None:
                return e.tensor_scalar(out, in0, s1, None, op0)
            return e.tensor_scalar(out, in0, s1, s2, op0, op1)
        return self._run(eng, f, deps, inc, 1)

    def stt(self, out, in0, scalar, in1, op0, op1, deps=None, inc=None, eng="vector"):
        return self._run(eng,
                         lambda: getattr(self.nc, eng).scalar_tensor_tensor(
                             out, in0, scalar, in1, op0, op1),
                         deps, inc, 1)

    def copy(self, out, in_, deps=None, inc=None, eng="vector"):
        return self._run(eng, lambda: getattr(self.nc, eng).tensor_copy(out, in_),
                         deps, inc, 1)

    def recip(self, out, in_, deps=None, inc=None):
        return self._run("vector", lambda: self.nc.vector.reciprocal(out, in_),
                         deps, inc, 1)

    def memset(self, ap, v, deps=None, inc=None):
        deps = list(deps or []) + [self.gp_chain]
        tok = self._run("gpsimd", lambda: self.nc.gpsimd.memset(ap, v), deps,
                        "gp", 1)
        self.gp_chain = tok
        return tok

    def affine(self, out, in_, pattern, cmp, fill, base=None, chan=None,
               deps=None, inc=None):
        deps = list(deps or []) + [self.gp_chain]
        tok = self._run("gpsimd",
                        lambda: self.nc.gpsimd.affine_select(
                            out=out, in_=in_, pattern=pattern, compare_op=cmp,
                            fill=fill, base=base, channel_multiplier=chan),
                        deps, "gp", 1)
        self.gp_chain = tok
        return tok

    def coll(self, kind, op, groups, ins, outs, deps=None, inc=None):
        return self._run("gpsimd",
                         lambda: self.nc.gpsimd.collective_compute(
                             kind, op, replica_groups=groups, ins=ins, outs=outs),
                         deps, inc, 1)

    def barrier(self):
        self.cnt["bar"] += 4
        tgt = self.cnt["bar"]
        if self.role in ("tensor", "vector", "scalar", "gpsimd"):
            eng = getattr(self.nc, self.role)
            self._wait(eng, [(s, self.cnt[s]) for s in self.DMA_SEMS])
            eng.drain().then_inc(self.sems["bar"], 1)
            eng.wait_ge(self.sems["bar"], tgt)
            self.wm["bar"] = tgt
        elif self.role == "sync":
            eng = self.nc.sync
            self._wait(eng, [(s, self.cnt[s]) for s in self.DMA_SEMS])
            eng.wait_ge(self.sems["bar"], tgt)
            self.wm["bar"] = tgt
        return ("bar", tgt)

    def final_wait(self, deps):
        if self.role == "gpsimd":
            self._wait(self.nc.gpsimd, deps)


def _program(b, E, taps, stop=None):
    d = E["d"]
    A32, A16 = E["A32"], E["A16"]
    psA, psB, psC, psD = E["psA"], E["psB"], E["psC"], E["psD"]
    psE = E["psE"]
    tap_t = E["tap_t"]
    ar32 = Alloc(NF32, "af32")
    ar16 = Alloc(NF16, "af16")

    def v32(off, w, p0=0, pn=128):
        return A32[p0:p0 + pn, off:off + w]

    def v16(off, w, p0=0, pn=128):
        return A16[p0:p0 + pn, off:off + w]

    def tap(name, ap, deps):
        if name in taps:
            b.dma("sync", tap_t[name], ap, deps=deps, inc="out")

    class _Stop(Exception):
        pass

    def ckpt(name):
        if stop == name:
            b.barrier()
            b.final_wait([b.tok("out")])
            raise _Stop()

    o_id = ar16.alloc(128, "ident")
    o_ones_c = ar16.alloc(16, "ones col")
    o_ones_r = ar16.alloc(128, "ones row")
    o_Ekv = ar16.alloc(8 * 128, "Ekv")
    o_E6 = ar16.alloc(8 * 128, "E6")
    o_lngb = ar32.alloc(4, "lngb")
    o_geff = ar32.alloc(KCH, "geff")
    o_beff = ar32.alloc(KCH, "beff")
    o_tmpc = ar32.alloc(KCH, "tmpc")
    o_qkvb = ar32.alloc(24, "qkvb")
    o_p1b = ar32.alloc(KCH, "p1b")
    o_p2b = ar32.alloc(KCH, "p2b")
    o_abT = ar32.alloc(H_A * HD, "abT")
    ident = v16(o_id, 128)

    tw = [
        b.dma("sync", v32(o_lngb, 4), d["lngb"][:, :], inc="wc"),
        b.dma("sync", v32(o_tmpc, KCH), d["tmpc"][:, :], inc="wc"),
        b.dma("sync", v32(o_qkvb, 24), d["qkvb"][:, :], inc="wc"),
        b.dma("sync", v32(o_p1b, KCH), d["p1b"][:, :], inc="wc"),
        b.dma("sync", v32(o_p2b, KCH), d["p2b"][:, :], inc="wc"),
        b.dma("sync", v32(o_abT, H_A * HD, 0, 64), d["abT"][:, :], inc="wc"),
    ]

    b.memset(ident[:, :], 0.0)
    b.affine(ident[:, :], ident[:, :], [[-1, 128]], ALU.not_equal, 1.0, base=0, chan=1)
    b.memset(v16(o_ones_c, 1), 1.0)
    b.memset(v16(o_ones_r, 128, 0, 1), 1.0)
    b.memset(v16(o_Ekv, 8 * 128), 0.0)
    for j in range(8):
        base = o_Ekv + j * 128
        b.memset(v16(base + 2 * j, 1, 0, 64), 1.0)
        b.memset(v16(base + 2 * j + 1, 1, 64, 64), 1.0)
        b.memset(v16(base + 64 + 32 + 2 * j, 1, 0, 64), 1.0)
        b.memset(v16(base + 64 + 32 + 2 * j + 1, 1, 64, 64), 1.0)
    b.memset(v16(o_E6, 8 * 128), 1.0)
    for j in range(8):
        sub = v16(o_E6 + j * 128, 128, 0, 16)
        b.affine(sub, sub, [[1, 128]], ALU.is_ge, 0.0, base=128 * j, chan=-64)
        b.affine(sub, sub, [[-1, 128]], ALU.is_ge, 0.0, base=63 - 128 * j, chan=64)
    tsetup = b.tok("gp")

    tgeff = b.ts(v32(o_geff, KCH), v32(o_tmpc, KCH), v32(o_lngb, 1), None,
                 ALU.mult, deps=[tw[0], tw[1]], inc="ve")
    tbeff = b.ts(v32(o_beff, KCH), v32(o_tmpc, KCH), v32(o_lngb + 1, 1), None,
                 ALU.mult, deps=[tw[0], tw[1]], inc="ve")

    def emit_ln(src_dram, o_out16, tapon=False):
        o_stg = ar32.alloc(T, "ln stage")
        o_cst = ar16.alloc(2 * T, "ln cast")
        tmm = None
        tprev = []
        for k in range(KCH):
            tdx = b.dma("sync", v32(o_stg, T), src_dram[128 * k:128 * (k + 1), :],
                        deps=tprev, inc="x")
            xc = v16(o_cst, T)
            xq = v16(o_cst + T, T)
            tc1 = b.copy(xc, v32(o_stg, T), deps=[tdx, tmm], inc="ve")
            tc2 = b.act(xq, v32(o_stg, T), AF.Square, deps=[tdx, tmm], inc="se")
            tprev = [tc1, tc2]
            for nt in range(4):
                s = nt * 512
                b.mm(psA[0:1, s:s + 512], v16(o_ones_c, 1), xc[:, s:s + 512],
                     start=(k == 0), stop=(k == KCH - 1), deps=[tc1, tsetup])
                tmm = b.mm(psA[32:33, s:s + 512], v16(o_ones_c, 1),
                           xq[:, s:s + 512],
                           start=(k == 0), stop=(k == KCH - 1), deps=[tc2],
                           inc="mm" if nt == 3 else None)
        ar16.freeblk(o_cst, 2 * T)
        o_st = ar16.alloc(8 * 1024, "ln st")
        def q32(i, w, p0, pn):
            return A16[p0:p0 + pn, o_st + i * 2048:o_st + i * 2048 + 2 * w]\
                .bitcast(F32)
        o_ab = ar16.alloc(2 * T, "ln ab")
        Arow = v16(o_ab, T, 0, 1)
        Brow = v16(o_ab + T, T, 0, 1)
        tA = tB = None
        for hf in range(2):
            sl = hf * 1024
            mu = q32(0, 1024, 0, 1)
            var = q32(1, 1024, 0, 1)
            r = q32(2, 1024, 0, 1)
            tp_ = q32(3, 1024, 0, 1)
            dd = [tmm, tB]
            t1 = b.ts(mu, psA[0:1, sl:sl + 1024], 1.0 / C, None, ALU.mult,
                      deps=dd, inc="ve")
            t3 = b.act(tp_, mu, AF.Square, deps=[t1, tB], inc="se")
            t2 = b.ts(var, psA[32:33, sl:sl + 1024], 1.0 / C, EPS,
                      ALU.mult, ALU.add, deps=dd, inc="ve")
            t4 = b.tt(var, var, tp_, ALU.subtract, deps=[t2, t3], inc="ve")
            t6 = b.act(r, var, AF.Sqrt, deps=[t4], inc="se")
            t7 = b.recip(r, r, deps=[t6], inc="ve")
            t8 = b.act(tp_, r, AF.Square, deps=[t7], inc="se")
            t9 = b.tt(tp_, tp_, var, ALU.mult, deps=[t8], inc="ve")
            t10 = b.ts(tp_, tp_, -0.5, 1.5, ALU.mult, ALU.add, deps=[t9], inc="ve")
            t11 = b.tt(r, r, tp_, ALU.mult, deps=[t10], inc="ve")
            tA = b.copy(Arow[:, sl:sl + 1024], r, deps=[t11], inc="ve")
            t12 = b.tt(tp_, mu, r, ALU.mult, deps=[t11], inc="ve")
            tB = b.ts(Brow[:, sl:sl + 1024], tp_, -1.0, None, ALU.mult,
                      deps=[t12], inc="ve")
        if tapon and "lnab" in taps:
            tk_ = b.dma("sync", tap_t["lnab"], A16[0:1, o_ab:o_ab + 2 * T],
                        deps=[tA, tB], inc="out")
            b.final_wait([tk_])
        if tapon and "lnsums" in taps:
            o_sdbg = ar32.alloc(2 * T, "sums dbg")
            ts1 = b.copy(v32(o_sdbg, T, 0, 1), psA[0:1, 0:T], deps=[tmm], inc="ve")
            ts2 = b.copy(v32(o_sdbg + T, T, 0, 1), psA[32:33, 0:T], deps=[tmm],
                         inc="ve")
            b.dma("sync", tap_t["lnsums"][0:1, :], v32(o_sdbg, T, 0, 1),
                  deps=[ts1], inc="out")
            b.dma("sync", tap_t["lnsums"][1:2, :], v32(o_sdbg + T, T, 0, 1),
                  deps=[ts2], inc="out")
            ar32.freeblk(o_sdbg, 2 * T)
        for nt in range(4):
            s = nt * 512
            ta = b.mm(psB[:, 0:512], v16(o_ones_r, 128, 0, 1), Arow[:, s:s + 512],
                      True, True, deps=[tA] + tprev, inc="mm")
            tb = b.mm(psB[:, 512:1024], v16(o_ones_r, 128, 0, 1), Brow[:, s:s + 512],
                      True, True, deps=[tB] + tprev, inc="mm")
            for k in range(KCH):
                tdx = b.dma("sync", v32(o_stg, 512),
                            src_dram[128 * k:128 * (k + 1), s:s + 512],
                            deps=tprev, inc="x")
                o_t = ar16.alloc(512, "ln t")
                u1 = b.tt(v16(o_t, 512), v32(o_stg, 512), psB[:, 0:512], ALU.mult,
                          deps=[ta, tdx], inc="ve")
                u2 = b.tt(v16(o_out16 + k * T + s, 512), v16(o_t, 512),
                          psB[:, 512:1024], ALU.add, deps=[u1, tb], inc="ve")
                tprev = [u2]
                ar16.freeblk(o_t, 512)
        ar16.freeblk(o_st, 8 * 1024)
        ar32.freeblk(o_stg, T)
        ar16.freeblk(o_ab, 2 * T)
        b.barrier()

    o_xin = ar16.alloc(KCH * T, "xin")
    emit_ln(d["xT"], o_xin, tapon=True)
    tap("xin", A16[:, o_xin:o_xin + KCH * T], [b.tok("ve")])
    xin_ch = lambda k, c0, w: v16(o_xin + k * T + c0, w)
    ckpt("S1")

    o_wstg = ar32.alloc(4 * 128, "wstg")
    o_wblk = ar16.alloc(4 * 128, "wblk")
    wpar_tok = [None, None, None, None]
    WSEM = ("w0", "w1", "w2", "w3")

    def wblock(src_ap, par):
        s32 = v32(o_wstg + par * 128, 128)
        s16 = v16(o_wblk + par * 128, 128)
        td = b.dma("sync", s32, src_ap, deps=[wpar_tok[par]], inc=WSEM[par])
        tc = b.copy(s16, s32, deps=[td], inc="ve")
        return s16, tc

    tile_tok = [None] * 4

    def mm_weight(blkB, src_ch, out_evac):
        tiles = [psB[:, 0:512], psB[:, 512:1024], psC[:, :], psD[:, :]]
        tmm = None
        for kc in range(KCH):
            par = kc % 4
            w16, tcst = wblock(blkB[kc], par)
            for nt in range(4):
                tmm = b.mm(tiles[nt], w16, src_ch(kc, nt * 512, 512),
                           start=(kc == 0), stop=(kc == KCH - 1),
                           deps=[tcst, tile_tok[nt]],
                           inc="mm" if nt == 3 else None)
            wpar_tok[par] = tmm
        toks = out_evac(tiles, tmm)
        for nt in range(4):
            tile_tok[nt] = toks[nt]

    o_ktok = ar16.alloc(NT * C, "ktok")
    o_vtok = ar16.alloc(NT * C, "vtok")
    kt_ch = lambda tcn, c0, w: v16(o_ktok + tcn * C + c0, w)
    vt_ch = lambda tcn, c0, w: v16(o_vtok + tcn * C + c0, w)

    def kv_group(mc0, o_tok, gcol_fn, bcol_fn):
        o_kv = ar16.alloc(KCH * T, "kvT")
        o_sq = ar16.alloc(T, "sq")
        tstat = None
        tsq = None
        for j in range(KCH):
            mcx = mc0 + j

            def evac(tiles, tmm, j=j, mcx=mcx):
                toks = []
                for nt in range(4):
                    te = b.act(v16(o_kv + j * T + nt * 512, 512), tiles[nt],
                               AF.Identity, bias=v32(o_qkvb + mcx, 1),
                               deps=[tmm, tw[2]], inc="se")
                    toks.append(te)
                return toks

            mm_weight(d["qkvB"][mcx], xin_ch, evac)
            tevs = list(tile_tok)
            tsq = b.act(v16(o_sq, T), v16(o_kv + j * T, T), AF.Square,
                        deps=tevs + [tstat], inc="se")
            Ev = v16(o_Ekv + j * 128, 64)
            Eq = v16(o_Ekv + j * 128 + 64, 64)
            for nt in range(4):
                s = nt * 512
                b.mm(psA[0:64, s:s + 512], Ev, v16(o_kv + j * T + s, 512),
                     start=(j == 0), stop=False, deps=tevs + [tsetup])
                tstat = b.mm(psA[0:64, s:s + 512], Eq, v16(o_sq + s, 512),
                             start=False, stop=(j == KCH - 1), deps=[tsq],
                             inc="mm" if nt == 3 else None)
        ar16.freeblk(o_sq, T)
        if ("kvsums" in taps) and mc0 == 8:
            o_sdb = ar32.alloc(T, "kvs dbg")
            td1 = b.copy(v32(o_sdb, T, 0, 64), psA[0:64, 0:T], deps=[tstat], inc="ve")
            b.dma("sync", tap_t["kvsums"], v32(o_sdb, T, 0, 64), deps=[td1],
                  inc="out")
            ar32.freeblk(o_sdb, T)
        o_st = ar16.alloc(8 * 1024, "kv st")
        def q32(i, w):
            return A16[0:16, o_st + i * 2048:o_st + i * 2048 + 2 * w].bitcast(F32)
        o_ab = ar16.alloc(2 * T, "kv ab")
        Ar = v16(o_ab, T, 0, 16)
        Br = v16(o_ab + T, T, 0, 16)
        tA = tB = None
        for hf in range(2):
            sl = hf * 1024
            mu = q32(0, 1024)
            var = q32(1, 1024)
            r = q32(2, 1024)
            tp_ = q32(3, 1024)
            dd = [tstat, tB]
            t1 = b.ts(mu, psA[0:16, sl:sl + 1024], 1.0 / HD, None, ALU.mult,
                      deps=dd, inc="ve")
            t3 = b.act(tp_, mu, AF.Square, deps=[t1, tB], inc="se")
            t2 = b.ts(var, psA[32:48, sl:sl + 1024], 1.0 / HD, EPS,
                      ALU.mult, ALU.add, deps=dd, inc="ve")
            t4 = b.tt(var, var, tp_, ALU.subtract, deps=[t2, t3], inc="ve")
            t6 = b.act(r, var, AF.Sqrt, deps=[t4], inc="se")
            t7 = b.recip(r, r, deps=[t6], inc="ve")
            t8 = b.act(tp_, r, AF.Square, deps=[t7], inc="se")
            t9 = b.tt(tp_, tp_, var, ALU.mult, deps=[t8], inc="ve")
            t10 = b.ts(tp_, tp_, -0.5, 1.5, ALU.mult, ALU.add, deps=[t9], inc="ve")
            t11 = b.tt(r, r, tp_, ALU.mult, deps=[t10], inc="ve")
            tA = b.copy(Ar[:, sl:sl + 1024], r, deps=[t11], inc="ve")
            t12 = b.tt(tp_, mu, r, ALU.mult, deps=[t11], inc="ve")
            tB = b.ts(Br[:, sl:sl + 1024], tp_, -1.0, None, ALU.mult,
                      deps=[t12], inc="ve")
        ar16.freeblk(o_st, 8 * 1024)
        if ("kvab" in taps) and mc0 == 8:
            b.dma("sync", tap_t["kvab"], A16[0:16, o_ab:o_ab + 2 * T],
                  deps=[tA, tB], inc="out")
        b.barrier()
        o_t = ar16.alloc(512, "kv ap t")
        tu3 = []
        tev = []
        for j in range(KCH):
            E6 = v16(o_E6 + j * 128, 128, 0, 16)
            for nt in range(4):
                s = nt * 512
                ta = b.mm(psB[:, 0:512], E6, Ar[:, s:s + 512], True, True,
                          deps=[tA] + tu3, inc="mm")
                tb = b.mm(psB[:, 512:1024], E6, Br[:, s:s + 512], True, True,
                          deps=[tB], inc="mm")
                tm = v16(o_t, 512)
                u1 = b.tt(tm, v16(o_kv + j * T + s, 512), psB[:, 0:512], ALU.mult,
                          deps=[ta] + tev, inc="ve")
                u2 = b.tt(tm, tm, psB[:, 512:1024], ALU.add, deps=[u1, tb], inc="ve")
                u3a = b.ts(tm, tm, gcol_fn(j), None, ALU.mult,
                           deps=[u2, tgeff, tw[0]], inc="ve")
                u3 = b.ts(tm, tm, bcol_fn(j), None, ALU.add,
                          deps=[u3a, tbeff], inc="ve")
                tu3 = [u3]
                ttr = None
                for q4 in range(4):
                    ttr = b.trans(psC[:, 64 * q4:64 * (q4 + 1)].bitcast(F16),
                                  tm[:, 128 * q4:128 * (q4 + 1)], ident,
                                  deps=[u3] + tev, inc="mm" if q4 == 3 else None)
                tev = []
                for q4 in range(4):
                    tcn = nt * 4 + q4
                    te = b.copy(A16[:, o_tok + tcn * C + j * 128:
                                    o_tok + tcn * C + j * 128 + 128],
                                psC[:, 64 * q4:64 * (q4 + 1)].bitcast(F16),
                                deps=[ttr], inc="ve")
                    tev = [te]
                b.barrier()
        ar16.freeblk(o_t, 512)
        ar16.freeblk(o_kv, KCH * T)
        ar16.freeblk(o_ab, 2 * T)
        b.barrier()

    kv_group(8, o_ktok, lambda j: v32(o_geff + j, 1),
             lambda j: v32(o_beff + j, 1))
    kv_group(16, o_vtok, lambda j: v32(o_lngb + 2, 1),
             lambda j: v32(o_lngb + 3, 1))
    tap("ktok", A16[:, o_ktok:o_ktok + NT * C], [b.tok("ve")])
    tap("vtok", A16[:, o_vtok:o_vtok + NT * C], [b.tok("ve")])

    o_q = ar16.alloc(KCH * T, "qT")
    for j in range(KCH):
        def evq(tiles, tmm, j=j):
            toks = []
            for nt in range(4):
                te = b.act(v16(o_q + j * T + nt * 512, 512), tiles[nt], AF.Identity,
                           bias=v32(o_qkvb + j, 1), deps=[tmm, tw[2]], inc="se")
                toks.append(te)
            return toks
        mm_weight(d["qkvB"][j], xin_ch, evq)
    b.barrier()
    tap("q", A16[:, o_q:o_q + KCH * T], [b.tok("se")])
    ckpt("S2")

    tkv = None
    for h in range(H_A):
        for tcn in range(NT):
            tkv = b.mm(psA[0:64, h * 64:h * 64 + 64], kt_ch(tcn, h * 64, 64),
                       vt_ch(tcn, h * 64, 64), start=(tcn == 0), stop=(tcn == NT - 1),
                       inc="mm" if (tcn == NT - 1 and h == H_A - 1) else None)
    o_kvp = ar32.alloc(H_A * HD, "kv partials")
    tkevac = b.ts(v32(o_kvp, H_A * HD, 0, 64), psA[0:64, 0:H_A * HD],
                  1.0 / QFULL, None, ALU.mult, deps=[tkv], inc="ve")
    tout = b.dma("gpsimd", E["kv_in"][:, :], v32(o_kvp, H_A * HD, 0, 64),
                 deps=[tkevac], inc="dd")
    tcc = b.coll("AllReduce", ALU.add, PAIRS, [E["kv_in"][:, :]],
                 [E["kv_out"][:, :]], deps=[tout], inc="cc")
    tin = b.dma("gpsimd", v32(o_kvp, H_A * HD, 0, 64), E["kv_out"][:, :],
                deps=[tcc], inc="dd")
    tfin = b.tt(v32(o_kvp, H_A * HD, 0, 64), v32(o_kvp, H_A * HD, 0, 64),
                v32(o_abT, H_A * HD, 0, 64), ALU.add, deps=[tin, tw[5]], inc="ve")
    tap("kvf", A32[0:64, o_kvp:o_kvp + H_A * HD], [tfin])
    o_kvbd = ar16.alloc(8 * 128, "kv bd")
    tms = b.memset(v16(o_kvbd, 8 * 128), 0.0, deps=[], inc="gp")
    tbd = []
    for hp in range(8):
        base = o_kvbd + hp * 128
        b1 = b.copy(v16(base, 64, 0, 64), v32(o_kvp + 2 * hp * 64, 64, 0, 64),
                    deps=[tfin, tms], inc="ve")
        b2 = b.copy(v16(base + 64, 64, 64, 64),
                    v32(o_kvp + (2 * hp + 1) * 64, 64, 0, 64),
                    deps=[tfin, tms], inc="ve")
        tbd += [b1, b2]
    b.barrier()
    ar16.freeblk(o_ktok, NT * C)
    ar16.freeblk(o_vtok, NT * C)
    ar32.freeblk(o_kvp, H_A * HD)
    ckpt("S3")

    o_att = ar16.alloc(KCH * T, "att")
    tiles4 = [psB[:, 0:512], psB[:, 512:1024], psC[:, :], psD[:, :]]
    ttok4 = [None] * 4
    for hp in range(8):
        tmm = None
        for nt in range(4):
            tmm = b.mm(tiles4[nt], v16(o_kvbd + hp * 128, 128),
                       v16(o_q + hp * T + nt * 512, 512), True, True,
                       deps=[ttok4[nt]] + tbd, inc="mm" if nt == 3 else None)
        for nt in range(4):
            te = b.tt(v16(o_att + hp * T + nt * 512, 512), tiles4[nt],
                      xin_ch(hp, nt * 512, 512), ALU.add, deps=[tmm], inc="ve")
            ttok4[nt] = te
    b.barrier()
    ar16.freeblk(o_q, KCH * T)
    ar16.freeblk(o_kvbd, 8 * 128)
    tap("att", A16[:, o_att:o_att + KCH * T], [b.tok("ve")])
    ckpt("S4")

    o_h1 = ar16.alloc(KCH * T, "h1")
    att_ch = lambda k, c0, w: v16(o_att + k * T + c0, w)
    for mc in range(KCH):
        def ev1(tiles, tmm, mc=mc):
            toks = []
            for nt in range(4):
                te = b.act(v16(o_h1 + mc * T + nt * 512, 512), tiles[nt], AF.Gelu,
                           bias=v32(o_p1b + mc, 1), deps=[tmm, tw[3]], inc="se")
                toks.append(te)
            return toks
        mm_weight(d["p1B"][mc], att_ch, ev1)
    b.barrier()
    ar16.freeblk(o_att, KCH * T)

    h1_ch = lambda k, c0, w: v16(o_h1 + k * T + c0, w)
    o_xs = ar32.alloc(2 * 512, "x slc")
    o_x1s = ar32.alloc(2 * 512, "x1 slc")
    txs = [None, None]
    for mc in range(KCH):
        def ev2(tiles, tmm, mc=mc):
            toks = []
            for nt in range(4):
                par = nt % 2
                s = nt * 512
                tdx = b.dma("sync", v32(o_xs + par * 512, 512),
                            d["xT"][128 * mc:128 * (mc + 1), s:s + 512],
                            deps=[txs[par]], inc="x0" if par == 0 else "x1")
                ta = b.act(v32(o_x1s + par * 512, 512), tiles[nt], AF.Identity,
                           bias=v32(o_p2b + mc, 1), deps=[tmm, tw[4], txs[par]],
                           inc="se")
                tb = b.tt(v32(o_x1s + par * 512, 512), v32(o_x1s + par * 512, 512),
                          xin_ch(mc, s, 512), ALU.add, deps=[ta], inc="ve")
                tc2 = b.tt(v32(o_x1s + par * 512, 512), v32(o_x1s + par * 512, 512),
                           v32(o_xs + par * 512, 512), ALU.add, deps=[tb, tdx],
                           inc="ve")
                td = b.dma("gpsimd", E["x1d"][128 * mc:128 * (mc + 1), s:s + 512],
                           v32(o_x1s + par * 512, 512), deps=[tc2],
                           inc="d0" if par == 0 else "d1")
                txs[par] = td
                toks.append(tc2)
            return toks
        mm_weight(d["p2B"][mc], h1_ch, ev2)
    b.barrier()
    ar16.freeblk(o_h1, KCH * T)
    ar32.freeblk(o_xs, 2 * 512)
    ar32.freeblk(o_x1s, 2 * 512)
    ar16.freeblk(o_xin, KCH * T)
    tap("x1", E["x1d"][:, :], [b.tok("d0"), b.tok("d1")])
    ckpt("S6")

    o_xin1 = ar16.alloc(KCH * T, "xin1")
    o_wo = ar16.alloc(20 * 1024, "wo2 full")
    emit_ln(E["x1d"], o_xin1)
    xin1_ch = lambda k, c0, w: v16(o_xin1 + k * T + c0, w)

    o_E3 = ar16.alloc(20 * 128, "E3")
    o_E7 = ar16.alloc(48, "E7")
    b.memset(v16(o_E3, 20 * 128), 1.0)
    for cc in range(20):
        sub = v16(o_E3 + cc * 128, 128, 0, 40)
        b.affine(sub, sub, [[1, 128]], ALU.is_ge, 0.0, base=128 * cc, chan=-64)
        b.affine(sub, sub, [[-1, 128]], ALU.is_ge, 0.0, base=63 - 128 * cc, chan=64)
    b.memset(v16(o_E7, 48), 1.0)
    sub = v16(o_E7, 40, 0, 8)
    b.affine(sub, sub, [[1, 40]], ALU.is_ge, 0.0, base=0, chan=-5)
    b.affine(sub, sub, [[-1, 40]], ALU.is_ge, 0.0, base=4, chan=5)

    o_wsd = ar16.alloc(KCH * 80, "wsd")
    o_wsd32 = ar32.alloc(128, "wsd stg")
    twsd = []
    tpv = None
    for kc in range(KCH):
        tdw = b.dma("sync", v32(o_wsd32, 80), d["wsdB"][kc], deps=[tpv], inc="x")
        tpv = b.copy(v16(o_wsd + kc * 80, 80), v32(o_wsd32, 80), deps=[tdw], inc="ve")
        twsd.append(tpv)
    o_gate = ar32.alloc(NT * 80, "gates")
    tg = None
    tsg = None
    for tcn in range(NT):
        for kc in range(KCH):
            tg = b.mm(psC[:, 0:80], xin1_ch(kc, tcn * 128, 128),
                      v16(o_wsd + kc * 80, 80), start=(kc == 0), stop=(kc == KCH - 1),
                      deps=[twsd[kc], tsg], inc="mm" if kc == KCH - 1 else None)
        tsg = b.act(v32(o_gate + tcn * 80, 80), psC[:, 0:80], AF.Sigmoid,
                    deps=[tg], inc="se")
    b.barrier()
    ar16.freeblk(o_wsd, KCH * 80)
    ar32.freeblk(o_wsd32, 128)
    tap("gates", A32[:, o_gate:o_gate + NT * 80], [b.tok("se")])

    NM = NT * 8
    o_med = ar32.alloc(8 * NM, "med")
    o_msk = ar16.alloc(2 * NT * 40, "masks")
    o_sg = ar32.alloc(2 * NT * 40, "gated")
    gateR = A32[:, o_gate:o_gate + NT * 80].rearrange(
        "p (tcn hl h e) -> p tcn hl h e", tcn=NT, hl=2, h=8, e=5)
    tmask = []
    for hl in range(2):
        ex = [gateR[:, :, hl, :, e] for e in range(5)]
        m = [v32(o_med + i * NM, NM).rearrange("p (tcn h) -> p tcn h", tcn=NT, h=8)
             for i in range(8)]
        d0 = [b.tok("se")]
        u = []
        u.append(b.tt(m[0], ex[0], ex[1], ALU.min, deps=d0, inc="ve"))
        u.append(b.tt(m[1], ex[0], ex[1], ALU.max, deps=d0, inc="ve"))
        u.append(b.tt(m[2], ex[2], ex[3], ALU.min, deps=d0, inc="ve"))
        u.append(b.tt(m[3], ex[2], ex[3], ALU.max, deps=d0, inc="ve"))
        f_ = b.tt(m[4], m[0], m[2], ALU.max, deps=u[:3], inc="ve")
        g_ = b.tt(m[5], m[1], m[3], ALU.min, deps=u, inc="ve")
        q1 = b.tt(m[6], ex[4], m[4], ALU.min, deps=[f_], inc="ve")
        q2 = b.tt(m[7], ex[4], m[4], ALU.max, deps=[f_], inc="ve")
        q3 = b.tt(m[7], m[7], m[5], ALU.min, deps=[q2, g_], inc="ve")
        tmed = b.tt(m[0], m[6], m[7], ALU.max, deps=[q1, q3], inc="ve")
        mskR = A16[:, o_msk + hl * NT * 40:o_msk + (hl + 1) * NT * 40].rearrange(
            "p (tcn h e) -> p tcn h e", tcn=NT, h=8, e=5)
        sgR = A32[:, o_sg + hl * NT * 40:o_sg + (hl + 1) * NT * 40].rearrange(
            "p (tcn h e) -> p tcn h e", tcn=NT, h=8, e=5)
        tms_ = []
        for e in range(5):
            tms_.append(b.tt(mskR[:, :, :, e], ex[e], m[0], ALU.is_ge,
                             deps=[tmed], inc="ve"))
        tgt = b.tt(sgR, gateR[:, :, hl, :, :], mskR, ALU.mult, deps=tms_, inc="ve")
        tmask.append(tgt)
    ar32.freeblk(o_med, 8 * NM)
    ar32.freeblk(o_gate, NT * 80)
    b.barrier()
    tap("md", A16[:, o_msk + NT * 40:o_msk + 2 * NT * 40], [b.tok("ve")])
    tap("sdg", A32[:, o_sg + NT * 40:o_sg + 2 * NT * 40], [b.tok("ve")])
    ckpt("S8")

    def msk_col(hl, tcn, h, e):
        return v16(o_msk + hl * NT * 40 + tcn * 40 + h * 5 + e, 1)

    def sg_col(hl, tcn, h, e):
        return v32(o_sg + hl * NT * 40 + tcn * 40 + h * 5 + e, 1)

    o_ktf = ar16.alloc(NT * 512, "ktokF")
    o_vex = ar16.alloc(NT * 520, "vext")
    o_wg = ar16.alloc(2 * KCH * 640, "w group")
    o_wg32 = ar32.alloc(2 * 640, "wg stage")
    o_g5 = ar16.alloc(640, "group out")
    twg_par = [None, None]
    o_kT = ar16.alloc(4 * T, "kT")
    tsend = []

    tones = []
    for tcn in range(NT):
        vv = A16[:, o_vex + tcn * 520:o_vex + (tcn + 1) * 520].rearrange(
            "p (hp c) -> p hp c", hp=4, c=130)
        b.memset(vv[:, :, 64:65], 1.0)
        tones.append(b.memset(vv[:, :, 129:130], 1.0, inc="gp"))

    wg_stage_guard = [None, None]

    def wgroup_mm(blkB, og, wpar, dst_emit, gated_dep):
        twl = []
        for kc in range(KCH):
            s32 = v32(o_wg32 + (kc % 2) * 640, 640)
            tdw = b.dma("sync", s32, blkB[kc],
                        deps=[twl[-2] if len(twl) >= 2 else
                              wg_stage_guard[kc % 2], twg_par[wpar]],
                        inc="wg0" if kc % 2 == 0 else "wg1")
            twl.append(b.copy(v16(o_wg + wpar * KCH * 640 + kc * 640, 640), s32,
                              deps=[tdw], inc="ve"))
        wg_stage_guard[0] = twl[-2]
        wg_stage_guard[1] = twl[-1]
        ptiles = [(psB[:, 0:512], psB[:, 512:640]), (psA[:, 0:512], psA[:, 512:640])]
        ptok = [None, None]
        tglast = None
        for tcn in range(NT):
            pp = tcn % 2
            tmm = None
            for kc in range(KCH):
                wap = v16(o_wg + wpar * KCH * 640 + kc * 640, 640)
                b.mm(ptiles[pp][0], xin1_ch(kc, tcn * 128, 128), wap[:, 0:512],
                     start=(kc == 0), stop=(kc == KCH - 1),
                     deps=[twl[kc], ptok[pp]])
                tmm = b.mm(ptiles[pp][1], xin1_ch(kc, tcn * 128, 128),
                           wap[:, 512:640], start=(kc == 0), stop=(kc == KCH - 1),
                           inc="mm" if kc == KCH - 1 else None)
            te1 = b.act(v16(o_g5, 512), ptiles[pp][0], AF.Identity,
                        deps=[tmm, tglast], inc="se")
            te2 = b.act(v16(o_g5 + 512, 128), ptiles[pp][1], AF.Identity,
                        deps=[tmm, tglast], inc="se")
            ptok[pp] = te2
            tglast = dst_emit(og, tcn, [te1, te2, gated_dep])
        twg_par[wpar] = tglast

    o_gt4 = ar16.alloc(4 * 64, "gate tmp")
    _gprev = [None]

    def gate_sum(dst, src_e, scol, deps):
        deps = deps + [_gprev[0]]
        p0 = b.ts(dst, src_e(0), scol(0), None, ALU.mult, deps=deps, inc="ve")
        pt = []
        for e in range(1, 5):
            pt.append(b.ts(v16(o_gt4 + (e - 1) * 64, 64), src_e(e), scol(e),
                           None, ALU.mult, deps=deps, inc="ve"))
        a1 = b.tt(v16(o_gt4, 64), v16(o_gt4, 64), v16(o_gt4 + 64, 64), ALU.add,
                  deps=pt[:2], inc="ve")
        a2 = b.tt(v16(o_gt4 + 128, 64), v16(o_gt4 + 128, 64),
                  v16(o_gt4 + 192, 64), ALU.add, deps=pt[2:], inc="ve")
        a3 = b.tt(v16(o_gt4, 64), v16(o_gt4, 64), v16(o_gt4 + 128, 64), ALU.add,
                  deps=[a1, a2], inc="ve")
        t = b.tt(dst, dst, v16(o_gt4, 64), ALU.add, deps=[p0, a3], inc="ve")
        _gprev[0] = t
        return t

    def gate_kv(og, tcn, deps):
        kv = og // 4
        hp = og % 4
        tl = None
        for hh in range(2):
            h = 2 * hp + hh
            if kv == 0:
                dst = v16(o_ktf + tcn * 512 + hp * 128 + hh * 64, 64)
            else:
                dst = v16(o_vex + tcn * 520 + hp * 130 + (65 if hh else 0), 64)
            src_e = lambda e: v16(o_g5 + hh * 320 + e * 64, 64)
            tl = gate_sum(dst, src_e, lambda e: sg_col(0, tcn, h, e),
                          deps + [tmask[0]])
        return tl

    for og in range(8):
        wgroup_mm(d["wkvB"][og], og, og % 2, gate_kv, None)
    b.barrier()

    tktr = []
    ttr = None
    tevk = []
    for cc4 in range(4):
        for tcn in range(NT):
            pv = (psC if tcn % 2 == 0 else psD)[:, 0:64].bitcast(F16)
            ttr = b.trans(pv, v16(o_ktf + tcn * 512 + cc4 * 128, 128), ident,
                          deps=tevk[-1:], inc="mm")
            te = b.copy(v16(o_kT + cc4 * T + tcn * 128, 128), pv,
                        deps=[ttr], inc="ve")
            tevk.append(te)
        tsend.append(b.dma("gpsimd",
                           E["ag_in"][KOFF + cc4 * 128 * T:KOFF + (cc4 + 1) * 128 * T]
                           .rearrange("(p c) -> p c", c=T),
                           v16(o_kT + cc4 * T, T), deps=[tevk[-1]], inc="dd"))
    for tcn in range(NT):
        tsend.append(b.dma("gpsimd",
                           E["ag_in"][VOFF + tcn * 128 * 520:VOFF + (tcn + 1) * 128 * 520]
                           .rearrange("(p c) -> p c", c=520),
                           A16[:, o_vex + tcn * 520:o_vex + (tcn + 1) * 520],
                           deps=[tones[tcn], b.tok("ve")], inc="dd"))
    tag = b.coll("AllGather", ALU.bypass, PAIRS,
                 [E["ag_in"][:]], [E["ag_out"][0:2 * AGN]], deps=tsend, inc="cc")
    tap("kT", A16[:, o_kT:o_kT + 4 * T], [b.tok("ve")])
    tap("vext", A16[:, o_vex:o_vex + NT * 520], [b.tok("ve")])
    ckpt("S9")
    ar16.freeblk(o_kT, 4 * T)
    ar16.freeblk(o_ktf, NT * 512)

    o_qtok = ar16.alloc(NT * 512, "q tok")

    def gate_q(og, tcn, deps):
        tl = None
        for hh in range(2):
            h = 2 * og + hh
            dst = v16(o_qtok + tcn * 512 + og * 128 + hh * 64, 64)
            src_e = lambda e: v16(o_g5 + hh * 320 + e * 64, 64)
            tl = gate_sum(dst, src_e, lambda e: sg_col(1, tcn, h, e),
                          deps + [tmask[1]])
        return tl

    for og in range(4):
        wgroup_mm(d["wqB"][og], og, og % 2, gate_q, None)
    b.barrier()
    o_qT = ar16.alloc(4 * T, "qT ffn")
    tevq = []
    for cc4 in range(4):
        for tcn in range(NT):
            pv = (psC if tcn % 2 == 0 else psD)[:, 0:64].bitcast(F16)
            ttr = b.trans(pv, v16(o_qtok + tcn * 512 + cc4 * 128, 128), ident,
                          deps=tevq[-1:], inc="mm")
            te = b.copy(v16(o_qT + cc4 * T + tcn * 128, 128), pv,
                        deps=[ttr], inc="ve")
            tevq.append(te)
    ar16.freeblk(o_qtok, NT * 512)
    ar16.freeblk(o_wg, 2 * KCH * 640)
    ar16.freeblk(o_g5, 640)
    ar32.freeblk(o_wg32, 2 * 640)
    tap("qf", A16[:, o_qT:o_qT + 4 * T], [b.tok("ve")])
    ckpt("S10")

    o_wo32 = ar32.alloc(2 * 128, "wo2 stg")
    tpw = [None, None]
    two = []
    for cc in range(20):
        for mc2 in range(KCH):
            par = (cc * KCH + mc2) % 2
            s32 = v32(o_wo32 + par * 128, 128)
            tdw = b.dma("sync", s32, d["wo2B"][cc, mc2], deps=[tpw[par]],
                        inc="w0" if par == 0 else "w1")
            tpw[par] = b.copy(v16(o_wo + cc * 1024 + mc2 * 128, 128), s32,
                              deps=[tdw], inc="ve")
            two.append(tpw[par])
    ar32.freeblk(o_wo32, 2 * 128)
    ar16.freeblk(o_xin1, KCH * T)
    b.barrier()

    o_khp = ar16.alloc(2 * QFULL, "k hp")
    o_vhp = ar16.alloc(2 * 32 * 130, "v hp")
    o_P = ar16.alloc(3 * 1024, "P bufs")
    o_avT = ar16.alloc(4 * T, "avT")
    o_sst = ar32.alloc(1024, "sums stage")
    tpre = [None, None]

    def prefetch_hp(hp, pp, deps):
        toks = []
        for sl in range(2):
            base = sl * AGN + KOFF + hp * 128 * QFULL // 2
            toks.append(b.dma(
                "gpsimd",
                v16(o_khp + pp * QFULL + sl * (QFULL // 2), QFULL // 2),
                E["ag_out"][base:base + 128 * 2048].rearrange("(p c) -> p c", c=2048),
                deps=deps, inc="kh0" if pp == 0 else "kh1"))
        for sl in range(2):
            for ch in range(16):
                src0 = sl * AGN + VOFF + ch * 128 * 520
                toks.append(b.dma(
                    "gpsimd",
                    v16(o_vhp + pp * 32 * 130 + (sl * 16 + ch) * 130, 130),
                    E["ag_out"][src0:src0 + 128 * 520]
                    .rearrange("(p c) -> p c", c=520)[:, hp * 130:hp * 130 + 130],
                    deps=deps, inc="kh0" if pp == 0 else "kh1"))
        return toks

    tpre[0] = prefetch_hp(0, 0, [tag])
    o_sums_rd = ar32.alloc(T, "sums rd")
    tev_av = []
    tsum_dma = []
    for hp in range(4):
        pp = hp % 2
        if hp < 3:
            tpre[(hp + 1) % 2] = prefetch_hp(hp + 1, (hp + 1) % 2,
                                             [tag] + tev_av[-1:])
        kdeps = tpre[pp]
        for tc4 in range(4):
            ts_ = tc4 * 512
            texp = [None, None, None]
            tavmm = None
            tav_par = [None, None, None]
            tSmm = [None, None, None]
            SPAIR = [(psB[:, 0:512], psB[:, 512:1024], psB[:, 0:1024]),
                     (psE[:, 0:512], psE[:, 512:1024], psE[:, 0:1024]),
                     (psA[:, 1024:1536], psA[:, 1536:2048], psA[:, 1024:2048])]
            for jc in range(32):
                par = jc % 3
                So = SPAIR[par]
                sdeps = kdeps + [texp[par]] + (tev_av if jc < 3 else [])
                b.mm(So[0], v16(o_khp + pp * QFULL + jc * 128, 128, 0, 64),
                     v16(o_qT + hp * T + ts_, 512, 0, 64), True, True,
                     deps=sdeps, tp=(0, 0))
                tSmm[par] = b.mm(So[1],
                                 v16(o_khp + pp * QFULL + jc * 128, 128, 64, 64),
                                 v16(o_qT + hp * T + ts_, 512, 64, 64), True, True,
                                 tp=(64, 0), inc="mm")
                texp[par] = b.act(v16(o_P + par * 1024, 1024), So[2], AF.Exp,
                                  scale=SCALE_F, deps=[tSmm[par], tav_par[par]],
                                  inc="se")
                vch = v16(o_vhp + pp * 32 * 130 + jc * 130, 130)
                b.mm(psA[0:65, 0:512], vch[:, 0:65], v16(o_P + par * 1024, 512),
                     start=(jc == 0), stop=(jc == 31),
                     deps=[texp[par]] + (tev_av if jc == 0 else []))
                tavmm = b.mm(psA[0:65, 512:1024], vch[:, 65:130],
                             v16(o_P + par * 1024 + 512, 512),
                             start=(jc == 0), stop=(jc == 31), inc="mm")
                tav_par[par] = tavmm
            e1 = b.copy(v16(o_avT + hp * T + ts_, 512, 0, 64), psA[0:64, 0:512],
                        deps=[tavmm], inc="ve")
            e2 = b.copy(v16(o_avT + hp * T + ts_, 512, 64, 64),
                        psA[0:64, 512:1024], deps=[tavmm], inc="ve")
            s1 = b.copy(v32(o_sst, 512, 64, 1), psA[64:65, 0:512],
                        deps=[tavmm] + tsum_dma, inc="ve")
            s2 = b.copy(v32(o_sst + 512, 512, 64, 1), psA[64:65, 512:1024],
                        deps=[tavmm] + tsum_dma, inc="ve")
            tsum_dma = [
                b.dma("gpsimd", E["sums_d"][2 * hp:2 * hp + 1, ts_:ts_ + 512],
                      v32(o_sst, 512, 64, 1), deps=[s1], inc="d0"),
                b.dma("gpsimd", E["sums_d"][2 * hp + 1:2 * hp + 2, ts_:ts_ + 512],
                      v32(o_sst + 512, 512, 64, 1), deps=[s2], inc="d1"),
            ]
            tev_av = [e1, e2, s1, s2]
    b.barrier()
    ar16.freeblk(o_khp, 2 * QFULL)
    ar16.freeblk(o_vhp, 2 * 32 * 130)
    ar16.freeblk(o_P, 3 * 1024)
    ar32.freeblk(o_sst, 1024)
    o_E4 = ar16.alloc(40 * 128, "E4")
    b.memset(v16(o_E4, 40 * 128), 0.0)
    for g in range(40):
        h = g // 5
        s0 = (h % 2) * 64
        p0 = (g % 2) * 64
        sub = v16(o_E4 + g * 128 + p0, 64, s0, 64)
        b.affine(sub, sub, [[-1, 64]], ALU.not_equal, 1.0, base=0, chan=1)
    tap("avT", A16[:, o_avT:o_avT + 4 * T], [b.tok("ve")])
    ckpt("S11")

    trd = b.dma("sync", v32(o_sums_rd, T, 0, 8), E["sums_d"][:, :],
                deps=[b.tok("d0"), b.tok("d1")], inc="x")
    o_rs = ar16.alloc(T, "recip16")
    tr1 = b.recip(v32(o_sums_rd, T, 0, 8), v32(o_sums_rd, T, 0, 8),
                  deps=[trd], inc="ve")
    tr2 = b.copy(v16(o_rs, T, 0, 8), v32(o_sums_rd, T, 0, 8), deps=[tr1], inc="ve")
    ar32.freeblk(o_sums_rd, T)
    o_mdT = ar16.alloc(T, "mdT")
    tmdT = []
    for tcn in range(NT):
        pv = psD[0:40, 0:64].bitcast(F16)
        tt_ = b.trans(pv, v16(o_msk + NT * 40 + tcn * 40, 40), ident,
                      deps=tmdT[-1:], inc="mm")
        tmdT.append(b.copy(v16(o_mdT + tcn * 128, 128, 0, 40), pv,
                           deps=[tt_], inc="ve"))
    o_gT = ar16.alloc(T, "gT")
    tgT = []
    for tc4 in range(4):
        s = tc4 * 512
        tm1 = b.mm(psC[0:40, :], v16(o_E7, 40, 0, 8), v16(o_rs + s, 512, 0, 8),
                   True, True, deps=[tr2, tsetup] + tmdT[-1:] + tgT[-1:], inc="mm")
        tgT.append(b.tt(v16(o_gT + s, 512, 0, 40), v16(o_mdT + s, 512, 0, 40),
                        psC[0:40, :], ALU.mult, deps=[tm1] + tmdT, inc="ve"))
    tap("gT", A16[0:40, o_gT:o_gT + T], [b.tok("ve")])
    b.barrier()
    ar16.freeblk(o_mdT, T)
    ar16.freeblk(o_rs, T)

    o_om = ar16.alloc(20 * 512, "OM")
    o_gbc = ar16.alloc(512, "gbc sbuf")
    o_x1s2 = ar32.alloc(2 * 512, "x1 slc f")
    o_yout = ar32.alloc(2 * 512, "y out")
    tx1 = [None, None]
    ty = [None, None]
    tomlast = []
    n_out = 0
    for tc4 in range(4):
        s = tc4 * 512
        tgb_prev = None
        tom = []
        for cc in range(20):
            tgb = b.mm(psD[:, :], v16(o_E3 + cc * 128, 128, 0, 40),
                       v16(o_gT + s, 512, 0, 40), True, True,
                       deps=tgT + [tgb_prev], inc="mm")
            tgbe = b.act(v16(o_gbc, 512), psD[:, :], AF.Identity,
                         deps=[tgb, tom[-1:][0] if tom else None], inc="se")
            tgb_prev = tgbe
            tab = None
            for half in range(2):
                gidx = 2 * cc + half
                hh = gidx // 5
                hp = hh // 2
                tab = b.mm(psC[:, :], v16(o_E4 + gidx * 128, 128),
                           v16(o_avT + hp * T + s, 512),
                           start=(half == 0), stop=(half == 1),
                           deps=[tom[-1] if tom else None] if half == 0 else None,
                           inc="mm" if half == 1 else None)
            tom.append(b.tt(v16(o_om + cc * 512, 512), v16(o_gbc, 512),
                            psC[:, :], ALU.mult, deps=[tgbe, tab], inc="ve"))
        for mc in range(KCH):
            par = mc % 2
            tdx = b.dma("sync", v32(o_x1s2 + par * 512, 512),
                        E["x1d"][128 * mc:128 * (mc + 1), s:s + 512],
                        deps=[ty[par]], inc="x0" if par == 0 else "x1")
            ptile = psB[:, 0:512] if par == 0 else psB[:, 512:1024]
            tmm = None
            for cc in range(20):
                tmm = b.mm(ptile, v16(o_wo + cc * 1024 + mc * 128, 128),
                           v16(o_om + cc * 512, 512),
                           start=(cc == 0), stop=(cc == 19),
                           deps=[tom[cc], ty[par]] + two[-1:],
                           inc="mm" if cc == 19 else None)
            tyv = b.tt(v32(o_yout + par * 512, 512), ptile,
                       v32(o_x1s2 + par * 512, 512), ALU.add,
                       deps=[tmm, tdx], inc="ve")
            ty[par] = b.dma("gpsimd", E["y"][128 * mc:128 * (mc + 1), s:s + 512],
                            v32(o_yout + par * 512, 512), deps=[tyv], inc="out")
            n_out += 1
        tomlast = tom
        if "om0" in taps and tc4 == 0:
            tap("om0", v16(o_om, 512), [tomlast[0]])
    b.final_wait([b.tok("out")])
    return ar32.peak, ar16.peak


TAP_SHAPES = {
    "xin": ([C, T], F16), "q": ([C, T], F16),
    "ktok": ([128, NT * C], F16), "vtok": ([128, NT * C], F16),
    "kvf": ([64, H_A * HD], F32), "att": ([C, T], F16),
    "x1": ([C, T], F32), "gates": ([128, NT * 80], F32),
    "sdg": ([128, NT * 40], F32), "md": ([128, NT * 40], F16),
    "kT": ([128, 4 * T], F16), "vext": ([128, NT * 520], F16),
    "qf": ([128, 4 * T], F16), "avT": ([128, 4 * T], F16),
    "gT": ([40, T], F16), "om0": ([128, 512], F16),
    "lnab": ([1, 2 * T], F16),
    "kvab": ([16, 2 * T], F16),
    "kvsums": ([64, T], F32),
    "lnsums": ([2, T], F32),
}


def build_nc(taps=(), stop=None):
    nc = bass.Bass("TRN2", target_bir_lowering=False, debug=False)
    es = ExitStack()
    d = {}
    for name, shape in [
        ("xT", [C, T]), ("qkvB", [24, KCH, 128, 128]), ("qkvb", [128, 24]),
        ("p1B", [KCH, KCH, 128, 128]), ("p1b", [128, KCH]),
        ("p2B", [KCH, KCH, 128, 128]), ("p2b", [128, KCH]),
        ("lngb", [128, 4]), ("tmpc", [128, KCH]), ("abT", [64, H_A * HD]),
        ("wkvB", [8, KCH, 128, 640]), ("wqB", [4, KCH, 128, 640]),
        ("wsdB", [KCH, 128, 80]), ("wo2B", [20, KCH, 128, 128]),
    ]:
        d[name] = nc.dram_tensor(name, shape, F32, kind="ExternalInput").ap()

    E = {"d": d}
    E["y"] = nc.dram_tensor("y", [C, T], F32, kind="ExternalOutput").ap()
    E["x1d"] = nc.dram_tensor("x1d", [C, T], F32).ap()
    E["ag_in"] = nc.dram_tensor("ag_in", [AGN], F16).ap()
    E["ag_out"] = nc.dram_tensor("ag_out", [2 * AGN + 2048], F16).ap()
    E["kv_in"] = nc.dram_tensor("kv_in", [64, H_A * HD], F32).ap()
    E["kv_out"] = nc.dram_tensor("kv_out", [64, H_A * HD], F32).ap()
    E["sums_d"] = nc.dram_tensor("sums_d", [8, T], F32).ap()
    E["tap_t"] = {}
    for t in taps:
        shp, dt_ = TAP_SHAPES[t]
        E["tap_t"][t] = nc.dram_tensor("tap_" + t, list(shp), dt_,
                                       kind="ExternalOutput").ap()

    E["A32"] = es.enter_context(nc.sbuf_tensor("af32", [128, NF32], F32))
    E["A16"] = es.enter_context(nc.sbuf_tensor("af16", [128, NF16], F16))
    E["psA"] = es.enter_context(nc.psum_tensor("psA", [128, 2048], F32))
    E["psB"] = es.enter_context(nc.psum_tensor("psB", [128, 1024], F32))
    E["psE"] = es.enter_context(nc.psum_tensor("psE", [128, 1024], F32))
    E["psC"] = E["psE"][:, 0:512]
    E["psD"] = E["psE"][:, 512:1024]
    sems = {s: es.enter_context(nc.semaphore(s)) for s in SEMS}

    ref_cnt = {}
    with nc.Block() as blk:
        def mk(role):
            def f(eng):
                cnt = {s: 0 for s in SEMS}
                bb = B(nc, role, sems, cnt)
                try:
                    _program(bb, E, set(taps), stop=stop)
                except Exception as ex:
                    if type(ex).__name__ != "_Stop":
                        raise
                if not ref_cnt:
                    ref_cnt.update(cnt)
                else:
                    assert ref_cnt == cnt, (role, ref_cnt, cnt)
            return f
        blk.tensor(mk("tensor"))
        blk.vector(mk("vector"))
        blk.scalar(mk("scalar"))
        blk.gpsimd(mk("gpsimd"))
        blk.sync(mk("sync"))
    es.close()
    return nc


def layout_inputs(inputs):
    f = lambda a: np.ascontiguousarray(np.asarray(a, dtype=np.float32))
    x = f(inputs["L0_attn_x"])
    qkv_w = f(inputs["L0_attn_qkv_w"])
    qkv_b = f(inputs["L0_attn_qkv_b"])
    p1_w, p1_b = f(inputs["L0_attn_p1_w"]), f(inputs["L0_attn_p1_b"])
    p2_w, p2_b = f(inputs["L0_attn_p2_w"]), f(inputs["L0_attn_p2_b"])
    kn_g, kn_b = f(inputs["L0_attn_kn_g"]), f(inputs["L0_attn_kn_b"])
    vn_g, vn_b = f(inputs["L0_attn_vn_g"]), f(inputs["L0_attn_vn_b"])
    temp = f(inputs["L0_attn_temperature"]).reshape(H_A)
    ab = f(inputs["L0_attn_attn_bias"])
    Wq, Wkv = f(inputs["L0_ffn_Wq"]), f(inputs["L0_ffn_Wkv"])
    Ws, Wd, Wo = f(inputs["L0_ffn_Ws"]), f(inputs["L0_ffn_Wd"]), f(inputs["L0_ffn_Wo"])

    blk = lambda w, mcs: np.ascontiguousarray(
        w.reshape(KCH, 128, mcs, 128).transpose(2, 0, 1, 3))
    shared = {
        "qkvB": blk(qkv_w.T, 24),
        "qkvb": np.ascontiguousarray(qkv_b.reshape(24, 128).T),
        "p1B": blk(p1_w.T, KCH),
        "p1b": np.ascontiguousarray(p1_b.reshape(KCH, 128).T),
        "p2B": blk(p2_w.T, KCH),
        "p2b": np.ascontiguousarray(p2_b.reshape(KCH, 128).T),
        "lngb": np.ascontiguousarray(np.stack(
            [np.tile(kn_g, 2), np.tile(kn_b, 2), np.tile(vn_g, 2),
             np.tile(vn_b, 2)], axis=1)),
        "tmpc": np.ascontiguousarray(np.repeat(temp, HD).reshape(KCH, 128).T),
        "abT": np.ascontiguousarray(ab.transpose(1, 0, 2).reshape(64, H_A * HD)),
        "wkvB": np.ascontiguousarray(
            Wkv.reshape(KCH, 128, 8, 640).transpose(2, 0, 1, 3)),
        "wqB": np.ascontiguousarray(
            Wq.reshape(KCH, 128, 4, 640).transpose(2, 0, 1, 3)),
        "wsdB": np.ascontiguousarray(
            np.concatenate([Ws, Wd], axis=1).reshape(KCH, 128, 80)),
        "wo2B": np.ascontiguousarray(
            np.tile(Wo.reshape(64, 5, 1024).transpose(1, 0, 2).reshape(320, 1024),
                    (8, 1)).reshape(20, 128, KCH, 128).transpose(0, 2, 1, 3)),
    }
    in_maps = []
    for c in range(8):
        bq, half = c // 2, c % 2
        m = dict(shared)
        m["xT"] = np.ascontiguousarray(x[bq, half * T:(half + 1) * T, :].T)
        in_maps.append(m)
    return in_maps


def kernel(**inputs):
    in_maps = layout_inputs(inputs)
    nc = build_nc()
    res = run_bass_kernel_spmd(nc, in_maps, core_ids=list(range(8)))
    out = np.empty((4, QFULL, C), dtype=np.float32)
    for c in range(8):
        bq, half = c // 2, c % 2
        out[bq, half * T:(half + 1) * T, :] = res.results[c]["y"].T
    return out


# version 03
# speedup vs baseline: 1.1153x; 1.0570x over previous
from contextlib import ExitStack

import numpy as np
import concourse.bass as bass
import concourse.mybir as mybir
from concourse.bass_utils import run_bass_kernel_spmd

F32 = mybir.dt.float32
F16 = mybir.dt.float16
AF = mybir.ActivationFunctionType
ALU = mybir.AluOpType

T = 2048
NT = T // 128
C = 1024
KCH = C // 128
QFULL = 4096
EPS = 1e-5
H_A, HD = 16, 64
H_F, E_F, DH = 8, 5, 64
SCALE_F = DH ** -0.5
KOFF = 0
VOFF = 512 * 2048
AGN = VOFF + 2048 * 520
PAIRS = [[0, 1], [2, 3], [4, 5], [6, 7]]
SEMS = ["wc", "w0", "w1", "w2", "w3", "wg0", "wg1", "x", "x0", "x1",
        "d0", "d1", "kh0", "kh1", "mm", "ve", "se", "gp", "dd", "cc",
        "out", "bar"]
NF32 = 8192
NF16 = 88064


class Alloc:
    def __init__(self, ncols, name):
        self.free = [(0, ncols)]
        self.ncols = ncols
        self.peak = 0
        self.name = name

    def alloc(self, n, tag=""):
        n = -(-n // 16) * 16
        for i, (s, ln) in enumerate(self.free):
            if ln >= n:
                if ln == n:
                    self.free.pop(i)
                else:
                    self.free[i] = (s + n, ln - n)
                self.peak = max(self.peak, self.ncols - sum(l for _, l in self.free))
                return s
        raise MemoryError(f"{self.name} OOM: {n} cols for {tag}; free={self.free}")

    def freeblk(self, s, n):
        n = -(-n // 16) * 16
        self.free.append((s, n))
        self.free.sort()
        out = []
        for blk in self.free:
            if out and out[-1][0] + out[-1][1] == blk[0]:
                out[-1] = (out[-1][0], out[-1][1] + blk[1])
            else:
                out.append(list(blk))
        self.free = [tuple(xx) for xx in out]


class B:

    def __init__(self, nc, role, sems, cnt):
        self.nc, self.role, self.sems, self.cnt = nc, role, sems, cnt
        self.wm = {}
        self.gp_chain = None

    def tok(self, sem):
        return (sem, self.cnt[sem])

    DMA_SEMS = ("wc", "w0", "w1", "w2", "w3", "wg0", "wg1", "x", "x0",
                "x1", "d0", "d1", "kh0", "kh1", "dd", "out")

    def _wait(self, eng, deps):
        flat = []
        for dd in deps or ():
            if isinstance(dd, list):
                flat.extend(dd)
            else:
                flat.append(dd)
        for dd in flat:
            if dd is None:
                continue
            sem, val = dd
            if sem in self.DMA_SEMS:
                val = self.cnt[sem]
            if val <= 0 or self.wm.get(sem, 0) >= val:
                continue
            eng.wait_ge(self.sems[sem], val)
            self.wm[sem] = val

    def _run(self, owner, fn, deps, inc, n):
        ins = None
        if self.role == owner:
            self._wait(getattr(self.nc, owner), deps)
            ins = fn()
        if inc is not None:
            self.cnt[inc] += n
            if ins is not None:
                ins.then_inc(self.sems[inc], n)
            return (inc, self.cnt[inc])
        return None

    def dma(self, q, out, in_, deps=None, inc=None):
        return self._run(q, lambda: getattr(self.nc, q).dma_start(out=out, in_=in_),
                         deps, inc, 16)

    def mm(self, out, lhsT, rhs, start, stop, deps=None, inc=None, tp=None):
        return self._run("tensor",
                         lambda: self.nc.tensor.matmul(out, lhsT, rhs, start=start,
                                                       stop=stop, tile_position=tp),
                         deps, inc, 1)

    def trans(self, out, in_, ident, deps=None, inc=None):
        return self._run("tensor", lambda: self.nc.tensor.transpose(out, in_, ident),
                         deps, inc, 1)

    def act(self, out, in_, func, bias=None, scale=None, accum_out=None,
            deps=None, inc=None):
        kw = {}
        if bias is not None:
            kw["bias"] = bias
        if scale is not None:
            kw["scale"] = scale
        if accum_out is not None:
            kw["accum_out"] = accum_out
        return self._run("scalar",
                         lambda: self.nc.scalar.activation(out, in_, func, **kw),
                         deps, inc, 1)

    def tt(self, out, a, c, op, deps=None, inc=None, eng="vector"):
        return self._run(eng, lambda: getattr(self.nc, eng).tensor_tensor(out, a, c, op),
                         deps, inc, 1)

    def ts(self, out, in0, s1, s2, op0, op1=None, deps=None, inc=None, eng="vector"):
        def f():
            e = getattr(self.nc, eng)
            if op1 is None:
                return e.tensor_scalar(out, in0, s1, None, op0)
            return e.tensor_scalar(out, in0, s1, s2, op0, op1)
        return self._run(eng, f, deps, inc, 1)

    def stt(self, out, in0, scalar, in1, op0, op1, deps=None, inc=None, eng="vector"):
        return self._run(eng,
                         lambda: getattr(self.nc, eng).scalar_tensor_tensor(
                             out, in0, scalar, in1, op0, op1),
                         deps, inc, 1)

    def copy(self, out, in_, deps=None, inc=None, eng="vector"):
        return self._run(eng, lambda: getattr(self.nc, eng).tensor_copy(out, in_),
                         deps, inc, 1)

    def recip(self, out, in_, deps=None, inc=None):
        return self._run("vector", lambda: self.nc.vector.reciprocal(out, in_),
                         deps, inc, 1)

    def memset(self, ap, v, deps=None, inc=None):
        deps = list(deps or []) + [self.gp_chain]
        tok = self._run("gpsimd", lambda: self.nc.gpsimd.memset(ap, v), deps,
                        "gp", 1)
        self.gp_chain = tok
        return tok

    def affine(self, out, in_, pattern, cmp, fill, base=None, chan=None,
               deps=None, inc=None):
        deps = list(deps or []) + [self.gp_chain]
        tok = self._run("gpsimd",
                        lambda: self.nc.gpsimd.affine_select(
                            out=out, in_=in_, pattern=pattern, compare_op=cmp,
                            fill=fill, base=base, channel_multiplier=chan),
                        deps, "gp", 1)
        self.gp_chain = tok
        return tok

    def coll(self, kind, op, groups, ins, outs, deps=None, inc=None):
        return self._run("gpsimd",
                         lambda: self.nc.gpsimd.collective_compute(
                             kind, op, replica_groups=groups, ins=ins, outs=outs),
                         deps, inc, 1)

    def barrier(self):
        self.cnt["bar"] += 4
        tgt = self.cnt["bar"]
        if self.role in ("tensor", "vector", "scalar", "gpsimd"):
            eng = getattr(self.nc, self.role)
            self._wait(eng, [(s, self.cnt[s]) for s in self.DMA_SEMS])
            eng.drain().then_inc(self.sems["bar"], 1)
            eng.wait_ge(self.sems["bar"], tgt)
            self.wm["bar"] = tgt
        elif self.role == "sync":
            eng = self.nc.sync
            self._wait(eng, [(s, self.cnt[s]) for s in self.DMA_SEMS])
            eng.wait_ge(self.sems["bar"], tgt)
            self.wm["bar"] = tgt
        return ("bar", tgt)

    def final_wait(self, deps):
        if self.role == "gpsimd":
            self._wait(self.nc.gpsimd, deps)


def _program(b, E, taps, stop=None):
    d = E["d"]
    A32, A16 = E["A32"], E["A16"]
    psA, psB, psC, psD = E["psA"], E["psB"], E["psC"], E["psD"]
    psE = E["psE"]
    tap_t = E["tap_t"]
    ar32 = Alloc(NF32, "af32")
    ar16 = Alloc(NF16, "af16")

    def v32(off, w, p0=0, pn=128):
        return A32[p0:p0 + pn, off:off + w]

    def v16(off, w, p0=0, pn=128):
        return A16[p0:p0 + pn, off:off + w]

    def tap(name, ap, deps):
        if name in taps:
            b.dma("sync", tap_t[name], ap, deps=deps, inc="out")

    class _Stop(Exception):
        pass

    def ckpt(name):
        if stop == name:
            b.barrier()
            b.final_wait([b.tok("out")])
            raise _Stop()

    o_id = ar16.alloc(128, "ident")
    o_ones_c = ar16.alloc(16, "ones col")
    o_ones_r = ar16.alloc(128, "ones row")
    o_Ekv = ar16.alloc(8 * 128, "Ekv")
    o_E6 = ar16.alloc(8 * 128, "E6")
    o_lngb = ar32.alloc(4, "lngb")
    o_geff = ar32.alloc(KCH, "geff")
    o_beff = ar32.alloc(KCH, "beff")
    o_tmpc = ar32.alloc(KCH, "tmpc")
    o_qkvb = ar32.alloc(24, "qkvb")
    o_p1b = ar32.alloc(KCH, "p1b")
    o_p2b = ar32.alloc(KCH, "p2b")
    o_abT = ar32.alloc(H_A * HD, "abT")
    ident = v16(o_id, 128)

    tw = [
        b.dma("sync", v32(o_lngb, 4), d["lngb"][:, :], inc="wc"),
        b.dma("sync", v32(o_tmpc, KCH), d["tmpc"][:, :], inc="wc"),
        b.dma("sync", v32(o_qkvb, 24), d["qkvb"][:, :], inc="wc"),
        b.dma("sync", v32(o_p1b, KCH), d["p1b"][:, :], inc="wc"),
        b.dma("sync", v32(o_p2b, KCH), d["p2b"][:, :], inc="wc"),
        b.dma("sync", v32(o_abT, H_A * HD, 0, 64), d["abT"][:, :], inc="wc"),
    ]

    b.memset(ident[:, :], 0.0)
    b.affine(ident[:, :], ident[:, :], [[-1, 128]], ALU.not_equal, 1.0, base=0, chan=1)
    b.memset(v16(o_ones_c, 1), 1.0)
    b.memset(v16(o_ones_r, 128, 0, 1), 1.0)
    b.memset(v16(o_Ekv, 8 * 128), 0.0)
    for j in range(8):
        base = o_Ekv + j * 128
        b.memset(v16(base + 2 * j, 1, 0, 64), 1.0)
        b.memset(v16(base + 2 * j + 1, 1, 64, 64), 1.0)
        b.memset(v16(base + 64 + 32 + 2 * j, 1, 0, 64), 1.0)
        b.memset(v16(base + 64 + 32 + 2 * j + 1, 1, 64, 64), 1.0)
    b.memset(v16(o_E6, 8 * 128), 1.0)
    for j in range(8):
        sub = v16(o_E6 + j * 128, 128, 0, 16)
        b.affine(sub, sub, [[1, 128]], ALU.is_ge, 0.0, base=128 * j, chan=-64)
        b.affine(sub, sub, [[-1, 128]], ALU.is_ge, 0.0, base=63 - 128 * j, chan=64)
    tsetup = b.tok("gp")

    tgeff = b.ts(v32(o_geff, KCH), v32(o_tmpc, KCH), v32(o_lngb, 1), None,
                 ALU.mult, deps=[tw[0], tw[1]], inc="ve")
    tbeff = b.ts(v32(o_beff, KCH), v32(o_tmpc, KCH), v32(o_lngb + 1, 1), None,
                 ALU.mult, deps=[tw[0], tw[1]], inc="ve")

    def emit_ln(src_dram, o_out16, tapon=False):
        o_stg = ar32.alloc(T, "ln stage")
        o_cst = ar16.alloc(2 * T, "ln cast")
        tmm = None
        tprev = []
        for k in range(KCH):
            tdx = b.dma("sync", v32(o_stg, T), src_dram[128 * k:128 * (k + 1), :],
                        deps=tprev, inc="x")
            xc = v16(o_cst, T)
            xq = v16(o_cst + T, T)
            tc1 = b.copy(xc, v32(o_stg, T), deps=[tdx, tmm], inc="ve")
            tc2 = b.act(xq, v32(o_stg, T), AF.Square, deps=[tdx, tmm], inc="se")
            tprev = [tc1, tc2]
            for nt in range(4):
                s = nt * 512
                b.mm(psA[0:1, s:s + 512], v16(o_ones_c, 1), xc[:, s:s + 512],
                     start=(k == 0), stop=(k == KCH - 1), deps=[tc1, tsetup])
                tmm = b.mm(psA[32:33, s:s + 512], v16(o_ones_c, 1),
                           xq[:, s:s + 512],
                           start=(k == 0), stop=(k == KCH - 1), deps=[tc2],
                           inc="mm" if nt == 3 else None)
        ar16.freeblk(o_cst, 2 * T)
        o_st = ar16.alloc(8 * 1024, "ln st")
        def q32(i, w, p0, pn):
            return A16[p0:p0 + pn, o_st + i * 2048:o_st + i * 2048 + 2 * w]\
                .bitcast(F32)
        o_ab = ar16.alloc(2 * T, "ln ab")
        Arow = v16(o_ab, T, 0, 1)
        Brow = v16(o_ab + T, T, 0, 1)
        tA = tB = None
        for hf in range(2):
            sl = hf * 1024
            mu = q32(0, 1024, 0, 1)
            var = q32(1, 1024, 0, 1)
            r = q32(2, 1024, 0, 1)
            tp_ = q32(3, 1024, 0, 1)
            dd = [tmm, tB]
            t1 = b.ts(mu, psA[0:1, sl:sl + 1024], 1.0 / C, None, ALU.mult,
                      deps=dd, inc="ve")
            t3 = b.act(tp_, mu, AF.Square, deps=[t1, tB], inc="se")
            t2 = b.ts(var, psA[32:33, sl:sl + 1024], 1.0 / C, EPS,
                      ALU.mult, ALU.add, deps=dd, inc="ve")
            t4 = b.tt(var, var, tp_, ALU.subtract, deps=[t2, t3], inc="ve")
            t6 = b.act(r, var, AF.Sqrt, deps=[t4], inc="se")
            t7 = b.recip(r, r, deps=[t6], inc="ve")
            t8 = b.act(tp_, r, AF.Square, deps=[t7], inc="se")
            t9 = b.tt(tp_, tp_, var, ALU.mult, deps=[t8], inc="ve")
            t10 = b.ts(tp_, tp_, -0.5, 1.5, ALU.mult, ALU.add, deps=[t9], inc="ve")
            t11 = b.tt(r, r, tp_, ALU.mult, deps=[t10], inc="ve")
            tA = b.copy(Arow[:, sl:sl + 1024], r, deps=[t11], inc="ve")
            t12 = b.tt(tp_, mu, r, ALU.mult, deps=[t11], inc="ve")
            tB = b.ts(Brow[:, sl:sl + 1024], tp_, -1.0, None, ALU.mult,
                      deps=[t12], inc="ve")
        if tapon and "lnab" in taps:
            tk_ = b.dma("sync", tap_t["lnab"], A16[0:1, o_ab:o_ab + 2 * T],
                        deps=[tA, tB], inc="out")
            b.final_wait([tk_])
        if tapon and "lnsums" in taps:
            o_sdbg = ar32.alloc(2 * T, "sums dbg")
            ts1 = b.copy(v32(o_sdbg, T, 0, 1), psA[0:1, 0:T], deps=[tmm], inc="ve")
            ts2 = b.copy(v32(o_sdbg + T, T, 0, 1), psA[32:33, 0:T], deps=[tmm],
                         inc="ve")
            b.dma("sync", tap_t["lnsums"][0:1, :], v32(o_sdbg, T, 0, 1),
                  deps=[ts1], inc="out")
            b.dma("sync", tap_t["lnsums"][1:2, :], v32(o_sdbg + T, T, 0, 1),
                  deps=[ts2], inc="out")
            ar32.freeblk(o_sdbg, 2 * T)
        for nt in range(4):
            s = nt * 512
            ta = b.mm(psB[:, 0:512], v16(o_ones_r, 128, 0, 1), Arow[:, s:s + 512],
                      True, True, deps=[tA] + tprev, inc="mm")
            tb = b.mm(psB[:, 512:1024], v16(o_ones_r, 128, 0, 1), Brow[:, s:s + 512],
                      True, True, deps=[tB] + tprev, inc="mm")
            for k in range(KCH):
                tdx = b.dma("sync", v32(o_stg, 512),
                            src_dram[128 * k:128 * (k + 1), s:s + 512],
                            deps=tprev, inc="x")
                o_t = ar16.alloc(512, "ln t")
                u1 = b.tt(v16(o_t, 512), v32(o_stg, 512), psB[:, 0:512], ALU.mult,
                          deps=[ta, tdx], inc="ve")
                u2 = b.tt(v16(o_out16 + k * T + s, 512), v16(o_t, 512),
                          psB[:, 512:1024], ALU.add, deps=[u1, tb], inc="ve")
                tprev = [u2]
                ar16.freeblk(o_t, 512)
        ar16.freeblk(o_st, 8 * 1024)
        ar32.freeblk(o_stg, T)
        ar16.freeblk(o_ab, 2 * T)
        b.barrier()

    o_xin = ar16.alloc(KCH * T, "xin")
    emit_ln(d["xT"], o_xin, tapon=True)
    tap("xin", A16[:, o_xin:o_xin + KCH * T], [b.tok("ve")])
    xin_ch = lambda k, c0, w: v16(o_xin + k * T + c0, w)
    ckpt("S1")

    o_wstg = ar32.alloc(4 * 128, "wstg")
    o_wblk = ar16.alloc(4 * 128, "wblk")
    wpar_tok = [None, None, None, None]
    WSEM = ("w0", "w1", "w2", "w3")

    def wblock(src_ap, par):
        s32 = v32(o_wstg + par * 128, 128)
        s16 = v16(o_wblk + par * 128, 128)
        td = b.dma("sync", s32, src_ap, deps=[wpar_tok[par]], inc=WSEM[par])
        tc = b.copy(s16, s32, deps=[td], inc="ve")
        return s16, tc

    tile_tok = [None] * 4

    def mm_weight(blkB, src_ch, out_evac):
        tiles = [psB[:, 0:512], psB[:, 512:1024], psC[:, :], psD[:, :]]
        tmm = None
        for kc in range(KCH):
            par = kc % 4
            w16, tcst = wblock(blkB[kc], par)
            for nt in range(4):
                tmm = b.mm(tiles[nt], w16, src_ch(kc, nt * 512, 512),
                           start=(kc == 0), stop=(kc == KCH - 1),
                           deps=[tcst, tile_tok[nt]],
                           inc="mm" if nt == 3 else None)
            wpar_tok[par] = tmm
        toks = out_evac(tiles, tmm)
        for nt in range(4):
            tile_tok[nt] = toks[nt]

    o_ktok = ar16.alloc(NT * C, "ktok")
    o_vtok = ar16.alloc(NT * C, "vtok")
    kt_ch = lambda tcn, c0, w: v16(o_ktok + tcn * C + c0, w)
    vt_ch = lambda tcn, c0, w: v16(o_vtok + tcn * C + c0, w)

    def kv_group(mc0, o_tok, gcol_fn, bcol_fn):
        o_kv = ar16.alloc(KCH * T, "kvT")
        o_sq = ar16.alloc(T, "sq")
        tstat = None
        tsq = None
        for j in range(KCH):
            mcx = mc0 + j

            def evac(tiles, tmm, j=j, mcx=mcx):
                toks = []
                for nt in range(4):
                    te = b.act(v16(o_kv + j * T + nt * 512, 512), tiles[nt],
                               AF.Identity, bias=v32(o_qkvb + mcx, 1),
                               deps=[tmm, tw[2]], inc="se")
                    toks.append(te)
                return toks

            mm_weight(d["qkvB"][mcx], xin_ch, evac)
            tevs = list(tile_tok)
            tsq = b.act(v16(o_sq, T), v16(o_kv + j * T, T), AF.Square,
                        deps=tevs + [tstat], inc="se")
            Ev = v16(o_Ekv + j * 128, 64)
            Eq = v16(o_Ekv + j * 128 + 64, 64)
            for nt in range(4):
                s = nt * 512
                b.mm(psA[0:64, s:s + 512], Ev, v16(o_kv + j * T + s, 512),
                     start=(j == 0), stop=False, deps=tevs + [tsetup])
                tstat = b.mm(psA[0:64, s:s + 512], Eq, v16(o_sq + s, 512),
                             start=False, stop=(j == KCH - 1), deps=[tsq],
                             inc="mm" if nt == 3 else None)
        ar16.freeblk(o_sq, T)
        if ("kvsums" in taps) and mc0 == 8:
            o_sdb = ar32.alloc(T, "kvs dbg")
            td1 = b.copy(v32(o_sdb, T, 0, 64), psA[0:64, 0:T], deps=[tstat], inc="ve")
            b.dma("sync", tap_t["kvsums"], v32(o_sdb, T, 0, 64), deps=[td1],
                  inc="out")
            ar32.freeblk(o_sdb, T)
        o_st = ar16.alloc(8 * 1024, "kv st")
        def q32(i, w):
            return A16[0:16, o_st + i * 2048:o_st + i * 2048 + 2 * w].bitcast(F32)
        o_ab = ar16.alloc(2 * T, "kv ab")
        Ar = v16(o_ab, T, 0, 16)
        Br = v16(o_ab + T, T, 0, 16)
        tA = tB = None
        for hf in range(2):
            sl = hf * 1024
            mu = q32(0, 1024)
            var = q32(1, 1024)
            r = q32(2, 1024)
            tp_ = q32(3, 1024)
            dd = [tstat, tB]
            t1 = b.ts(mu, psA[0:16, sl:sl + 1024], 1.0 / HD, None, ALU.mult,
                      deps=dd, inc="ve")
            t3 = b.act(tp_, mu, AF.Square, deps=[t1, tB], inc="se")
            t2 = b.ts(var, psA[32:48, sl:sl + 1024], 1.0 / HD, EPS,
                      ALU.mult, ALU.add, deps=dd, inc="ve")
            t4 = b.tt(var, var, tp_, ALU.subtract, deps=[t2, t3], inc="ve")
            t6 = b.act(r, var, AF.Sqrt, deps=[t4], inc="se")
            t7 = b.recip(r, r, deps=[t6], inc="ve")
            t8 = b.act(tp_, r, AF.Square, deps=[t7], inc="se")
            t9 = b.tt(tp_, tp_, var, ALU.mult, deps=[t8], inc="ve")
            t10 = b.ts(tp_, tp_, -0.5, 1.5, ALU.mult, ALU.add, deps=[t9], inc="ve")
            t11 = b.tt(r, r, tp_, ALU.mult, deps=[t10], inc="ve")
            tA = b.copy(Ar[:, sl:sl + 1024], r, deps=[t11], inc="ve")
            t12 = b.tt(tp_, mu, r, ALU.mult, deps=[t11], inc="ve")
            tB = b.ts(Br[:, sl:sl + 1024], tp_, -1.0, None, ALU.mult,
                      deps=[t12], inc="ve")
        ar16.freeblk(o_st, 8 * 1024)
        if ("kvab" in taps) and mc0 == 8:
            b.dma("sync", tap_t["kvab"], A16[0:16, o_ab:o_ab + 2 * T],
                  deps=[tA, tB], inc="out")
        b.barrier()
        o_t = ar16.alloc(512, "kv ap t")
        tu3 = []
        tev = []
        for j in range(KCH):
            E6 = v16(o_E6 + j * 128, 128, 0, 16)
            for nt in range(4):
                s = nt * 512
                ta = b.mm(psB[:, 0:512], E6, Ar[:, s:s + 512], True, True,
                          deps=[tA] + tu3, inc="mm")
                tb = b.mm(psB[:, 512:1024], E6, Br[:, s:s + 512], True, True,
                          deps=[tB], inc="mm")
                tm = v16(o_t, 512)
                u1 = b.tt(tm, v16(o_kv + j * T + s, 512), psB[:, 0:512], ALU.mult,
                          deps=[ta] + tev, inc="ve")
                u2 = b.tt(tm, tm, psB[:, 512:1024], ALU.add, deps=[u1, tb], inc="ve")
                u3a = b.ts(tm, tm, gcol_fn(j), None, ALU.mult,
                           deps=[u2, tgeff, tw[0]], inc="ve")
                u3 = b.ts(tm, tm, bcol_fn(j), None, ALU.add,
                          deps=[u3a, tbeff], inc="ve")
                tu3 = [u3]
                ttr = None
                for q4 in range(4):
                    ttr = b.trans(psC[:, 64 * q4:64 * (q4 + 1)].bitcast(F16),
                                  tm[:, 128 * q4:128 * (q4 + 1)], ident,
                                  deps=[u3] + tev, inc="mm" if q4 == 3 else None)
                tev = []
                for q4 in range(4):
                    tcn = nt * 4 + q4
                    te = b.copy(A16[:, o_tok + tcn * C + j * 128:
                                    o_tok + tcn * C + j * 128 + 128],
                                psC[:, 64 * q4:64 * (q4 + 1)].bitcast(F16),
                                deps=[ttr], inc="ve")
                    tev = [te]
                b.barrier()
        ar16.freeblk(o_t, 512)
        ar16.freeblk(o_kv, KCH * T)
        ar16.freeblk(o_ab, 2 * T)
        b.barrier()

    kv_group(8, o_ktok, lambda j: v32(o_geff + j, 1),
             lambda j: v32(o_beff + j, 1))
    kv_group(16, o_vtok, lambda j: v32(o_lngb + 2, 1),
             lambda j: v32(o_lngb + 3, 1))
    tap("ktok", A16[:, o_ktok:o_ktok + NT * C], [b.tok("ve")])
    tap("vtok", A16[:, o_vtok:o_vtok + NT * C], [b.tok("ve")])

    o_q = ar16.alloc(KCH * T, "qT")
    for j in range(KCH):
        def evq(tiles, tmm, j=j):
            toks = []
            for nt in range(4):
                te = b.act(v16(o_q + j * T + nt * 512, 512), tiles[nt], AF.Identity,
                           bias=v32(o_qkvb + j, 1), deps=[tmm, tw[2]], inc="se")
                toks.append(te)
            return toks
        mm_weight(d["qkvB"][j], xin_ch, evq)
    b.barrier()
    tap("q", A16[:, o_q:o_q + KCH * T], [b.tok("se")])
    ckpt("S2")

    tkv = None
    for h in range(H_A):
        for tcn in range(NT):
            tkv = b.mm(psA[0:64, h * 64:h * 64 + 64], kt_ch(tcn, h * 64, 64),
                       vt_ch(tcn, h * 64, 64), start=(tcn == 0), stop=(tcn == NT - 1),
                       inc="mm" if (tcn == NT - 1 and h == H_A - 1) else None)
    o_kvp = ar32.alloc(H_A * HD, "kv partials")
    tkevac = b.ts(v32(o_kvp, H_A * HD, 0, 64), psA[0:64, 0:H_A * HD],
                  1.0 / QFULL, None, ALU.mult, deps=[tkv], inc="ve")
    tout = b.dma("gpsimd", E["kv_in"][:, :], v32(o_kvp, H_A * HD, 0, 64),
                 deps=[tkevac], inc="dd")
    tcc = b.coll("AllReduce", ALU.add, PAIRS, [E["kv_in"][:, :]],
                 [E["kv_out"][:, :]], deps=[tout], inc="cc")
    tin = b.dma("gpsimd", v32(o_kvp, H_A * HD, 0, 64), E["kv_out"][:, :],
                deps=[tcc], inc="dd")
    tfin = b.tt(v32(o_kvp, H_A * HD, 0, 64), v32(o_kvp, H_A * HD, 0, 64),
                v32(o_abT, H_A * HD, 0, 64), ALU.add, deps=[tin, tw[5]], inc="ve")
    tap("kvf", A32[0:64, o_kvp:o_kvp + H_A * HD], [tfin])
    o_kvbd = ar16.alloc(8 * 128, "kv bd")
    tms = b.memset(v16(o_kvbd, 8 * 128), 0.0, deps=[], inc="gp")
    tbd = []
    for hp in range(8):
        base = o_kvbd + hp * 128
        b1 = b.copy(v16(base, 64, 0, 64), v32(o_kvp + 2 * hp * 64, 64, 0, 64),
                    deps=[tfin, tms], inc="ve")
        b2 = b.copy(v16(base + 64, 64, 64, 64),
                    v32(o_kvp + (2 * hp + 1) * 64, 64, 0, 64),
                    deps=[tfin, tms], inc="ve")
        tbd += [b1, b2]
    b.barrier()
    ar16.freeblk(o_ktok, NT * C)
    ar16.freeblk(o_vtok, NT * C)
    ar32.freeblk(o_kvp, H_A * HD)
    ckpt("S3")

    o_att = ar16.alloc(KCH * T, "att")
    tiles4 = [psB[:, 0:512], psB[:, 512:1024], psC[:, :], psD[:, :]]
    ttok4 = [None] * 4
    for hp in range(8):
        tmm = None
        for nt in range(4):
            tmm = b.mm(tiles4[nt], v16(o_kvbd + hp * 128, 128),
                       v16(o_q + hp * T + nt * 512, 512), True, True,
                       deps=[ttok4[nt]] + tbd, inc="mm" if nt == 3 else None)
        for nt in range(4):
            te = b.tt(v16(o_att + hp * T + nt * 512, 512), tiles4[nt],
                      xin_ch(hp, nt * 512, 512), ALU.add, deps=[tmm], inc="ve")
            ttok4[nt] = te
    b.barrier()
    ar16.freeblk(o_q, KCH * T)
    ar16.freeblk(o_kvbd, 8 * 128)
    tap("att", A16[:, o_att:o_att + KCH * T], [b.tok("ve")])
    ckpt("S4")

    o_h1 = ar16.alloc(KCH * T, "h1")
    att_ch = lambda k, c0, w: v16(o_att + k * T + c0, w)
    for mc in range(KCH):
        def ev1(tiles, tmm, mc=mc):
            toks = []
            for nt in range(4):
                te = b.act(v16(o_h1 + mc * T + nt * 512, 512), tiles[nt], AF.Gelu,
                           bias=v32(o_p1b + mc, 1), deps=[tmm, tw[3]], inc="se")
                toks.append(te)
            return toks
        mm_weight(d["p1B"][mc], att_ch, ev1)
    b.barrier()
    ar16.freeblk(o_att, KCH * T)

    h1_ch = lambda k, c0, w: v16(o_h1 + k * T + c0, w)
    o_xs = ar32.alloc(2 * 512, "x slc")
    o_x1s = ar32.alloc(2 * 512, "x1 slc")
    txs = [None, None]
    for mc in range(KCH):
        def ev2(tiles, tmm, mc=mc):
            toks = []
            for nt in range(4):
                par = nt % 2
                s = nt * 512
                tdx = b.dma("sync", v32(o_xs + par * 512, 512),
                            d["xT"][128 * mc:128 * (mc + 1), s:s + 512],
                            deps=[txs[par]], inc="x0" if par == 0 else "x1")
                ta = b.act(v32(o_x1s + par * 512, 512), tiles[nt], AF.Identity,
                           bias=v32(o_p2b + mc, 1), deps=[tmm, tw[4], txs[par]],
                           inc="se")
                tb = b.tt(v32(o_x1s + par * 512, 512), v32(o_x1s + par * 512, 512),
                          xin_ch(mc, s, 512), ALU.add, deps=[ta], inc="ve")
                tc2 = b.tt(v32(o_x1s + par * 512, 512), v32(o_x1s + par * 512, 512),
                           v32(o_xs + par * 512, 512), ALU.add, deps=[tb, tdx],
                           inc="ve")
                td = b.dma("gpsimd", E["x1d"][128 * mc:128 * (mc + 1), s:s + 512],
                           v32(o_x1s + par * 512, 512), deps=[tc2],
                           inc="d0" if par == 0 else "d1")
                txs[par] = td
                toks.append(tc2)
            return toks
        mm_weight(d["p2B"][mc], h1_ch, ev2)
    b.barrier()
    ar16.freeblk(o_h1, KCH * T)
    ar32.freeblk(o_xs, 2 * 512)
    ar32.freeblk(o_x1s, 2 * 512)
    ar16.freeblk(o_xin, KCH * T)
    tap("x1", E["x1d"][:, :], [b.tok("d0"), b.tok("d1")])
    ckpt("S6")

    o_xin1 = ar16.alloc(KCH * T, "xin1")
    o_wo = ar16.alloc(20 * 1024, "wo2 full")
    emit_ln(E["x1d"], o_xin1)
    xin1_ch = lambda k, c0, w: v16(o_xin1 + k * T + c0, w)

    o_E3 = ar16.alloc(20 * 128, "E3")
    o_E7 = ar16.alloc(48, "E7")
    b.memset(v16(o_E3, 20 * 128), 1.0)
    for cc in range(20):
        sub = v16(o_E3 + cc * 128, 128, 0, 40)
        b.affine(sub, sub, [[1, 128]], ALU.is_ge, 0.0, base=128 * cc, chan=-64)
        b.affine(sub, sub, [[-1, 128]], ALU.is_ge, 0.0, base=63 - 128 * cc, chan=64)
    b.memset(v16(o_E7, 48), 1.0)
    sub = v16(o_E7, 40, 0, 8)
    b.affine(sub, sub, [[1, 40]], ALU.is_ge, 0.0, base=0, chan=-5)
    b.affine(sub, sub, [[-1, 40]], ALU.is_ge, 0.0, base=4, chan=5)

    o_wsd = ar16.alloc(KCH * 80, "wsd")
    o_wsd32 = ar32.alloc(128, "wsd stg")
    twsd = []
    tpv = None
    for kc in range(KCH):
        tdw = b.dma("sync", v32(o_wsd32, 80), d["wsdB"][kc], deps=[tpv], inc="x")
        tpv = b.copy(v16(o_wsd + kc * 80, 80), v32(o_wsd32, 80), deps=[tdw], inc="ve")
        twsd.append(tpv)
    o_gate = ar32.alloc(NT * 80, "gates")
    tg = None
    tsg = None
    for tcn in range(NT):
        for kc in range(KCH):
            tg = b.mm(psC[:, 0:80], xin1_ch(kc, tcn * 128, 128),
                      v16(o_wsd + kc * 80, 80), start=(kc == 0), stop=(kc == KCH - 1),
                      deps=[twsd[kc], tsg], inc="mm" if kc == KCH - 1 else None)
        tsg = b.act(v32(o_gate + tcn * 80, 80), psC[:, 0:80], AF.Sigmoid,
                    deps=[tg], inc="se")
    b.barrier()
    ar16.freeblk(o_wsd, KCH * 80)
    ar32.freeblk(o_wsd32, 128)
    tap("gates", A32[:, o_gate:o_gate + NT * 80], [b.tok("se")])

    NM = NT * 8
    o_med = ar32.alloc(8 * NM, "med")
    o_msk = ar16.alloc(2 * NT * 40, "masks")
    o_sg = ar32.alloc(2 * NT * 40, "gated")
    gateR = A32[:, o_gate:o_gate + NT * 80].rearrange(
        "p (tcn hl h e) -> p tcn hl h e", tcn=NT, hl=2, h=8, e=5)
    tmask = []
    for hl in range(2):
        ex = [gateR[:, :, hl, :, e] for e in range(5)]
        m = [v32(o_med + i * NM, NM).rearrange("p (tcn h) -> p tcn h", tcn=NT, h=8)
             for i in range(8)]
        d0 = [b.tok("se")]
        u = []
        u.append(b.tt(m[0], ex[0], ex[1], ALU.min, deps=d0, inc="ve"))
        u.append(b.tt(m[1], ex[0], ex[1], ALU.max, deps=d0, inc="ve"))
        u.append(b.tt(m[2], ex[2], ex[3], ALU.min, deps=d0, inc="ve"))
        u.append(b.tt(m[3], ex[2], ex[3], ALU.max, deps=d0, inc="ve"))
        f_ = b.tt(m[4], m[0], m[2], ALU.max, deps=u[:3], inc="ve")
        g_ = b.tt(m[5], m[1], m[3], ALU.min, deps=u, inc="ve")
        q1 = b.tt(m[6], ex[4], m[4], ALU.min, deps=[f_], inc="ve")
        q2 = b.tt(m[7], ex[4], m[4], ALU.max, deps=[f_], inc="ve")
        q3 = b.tt(m[7], m[7], m[5], ALU.min, deps=[q2, g_], inc="ve")
        tmed = b.tt(m[0], m[6], m[7], ALU.max, deps=[q1, q3], inc="ve")
        mskR = A16[:, o_msk + hl * NT * 40:o_msk + (hl + 1) * NT * 40].rearrange(
            "p (tcn h e) -> p tcn h e", tcn=NT, h=8, e=5)
        sgR = A32[:, o_sg + hl * NT * 40:o_sg + (hl + 1) * NT * 40].rearrange(
            "p (tcn h e) -> p tcn h e", tcn=NT, h=8, e=5)
        tms_ = []
        for e in range(5):
            tms_.append(b.tt(mskR[:, :, :, e], ex[e], m[0], ALU.is_ge,
                             deps=[tmed], inc="ve"))
        tgt = b.tt(sgR, gateR[:, :, hl, :, :], mskR, ALU.mult, deps=tms_, inc="ve")
        tmask.append(tgt)
    ar32.freeblk(o_med, 8 * NM)
    ar32.freeblk(o_gate, NT * 80)
    b.barrier()
    tap("md", A16[:, o_msk + NT * 40:o_msk + 2 * NT * 40], [b.tok("ve")])
    tap("sdg", A32[:, o_sg + NT * 40:o_sg + 2 * NT * 40], [b.tok("ve")])
    ckpt("S8")

    def msk_col(hl, tcn, h, e):
        return v16(o_msk + hl * NT * 40 + tcn * 40 + h * 5 + e, 1)

    def sg_col(hl, tcn, h, e):
        return v32(o_sg + hl * NT * 40 + tcn * 40 + h * 5 + e, 1)

    o_ktf = ar16.alloc(NT * 512, "ktokF")
    o_vex = ar16.alloc(NT * 520, "vext")
    o_wg = ar16.alloc(2 * KCH * 640, "w group")
    o_wg32 = ar32.alloc(2 * 640, "wg stage")
    o_g5 = ar16.alloc(640, "group out")
    twg_par = [None, None]
    o_kT = ar16.alloc(4 * T, "kT")
    tsend = []

    tones = []
    for tcn in range(NT):
        vv = A16[:, o_vex + tcn * 520:o_vex + (tcn + 1) * 520].rearrange(
            "p (hp c) -> p hp c", hp=4, c=130)
        b.memset(vv[:, :, 64:65], 1.0)
        tones.append(b.memset(vv[:, :, 129:130], 1.0, inc="gp"))

    wg_stage_guard = [None, None]
    wg_ptok = [None, None]
    wg_glast = [None]

    def wgroup_mm(blkB, og, wpar, dst_emit, gated_dep):
        twl = []
        for kc in range(KCH):
            s32 = v32(o_wg32 + (kc % 2) * 640, 640)
            tdw = b.dma("sync", s32, blkB[kc],
                        deps=[twl[-2] if len(twl) >= 2 else
                              wg_stage_guard[kc % 2], twg_par[wpar]],
                        inc="wg0" if kc % 2 == 0 else "wg1")
            twl.append(b.copy(v16(o_wg + wpar * KCH * 640 + kc * 640, 640), s32,
                              deps=[tdw], inc="ve"))
        wg_stage_guard[0] = twl[-2]
        wg_stage_guard[1] = twl[-1]
        ptiles = [(psB[:, 0:512], psB[:, 512:640]), (psA[:, 0:512], psA[:, 512:640])]
        ptok = wg_ptok
        tglast = wg_glast[0]
        for tcn in range(NT):
            pp = tcn % 2
            tmm = None
            for kc in range(KCH):
                wap = v16(o_wg + wpar * KCH * 640 + kc * 640, 640)
                b.mm(ptiles[pp][0], xin1_ch(kc, tcn * 128, 128), wap[:, 0:512],
                     start=(kc == 0), stop=(kc == KCH - 1),
                     deps=[twl[kc], ptok[pp]])
                tmm = b.mm(ptiles[pp][1], xin1_ch(kc, tcn * 128, 128),
                           wap[:, 512:640], start=(kc == 0), stop=(kc == KCH - 1),
                           inc="mm" if kc == KCH - 1 else None)
            te1 = b.act(v16(o_g5, 512), ptiles[pp][0], AF.Identity,
                        deps=[tmm, tglast], inc="se")
            te2 = b.act(v16(o_g5 + 512, 128), ptiles[pp][1], AF.Identity,
                        deps=[tmm, tglast], inc="se")
            ptok[pp] = te2
            tglast = dst_emit(og, tcn, [te1, te2, gated_dep])
        wg_glast[0] = tglast
        twg_par[wpar] = tglast

    o_gt4 = ar16.alloc(2 * 4 * 64, "gate tmp")
    _gprev = [None, None]

    def gate_sum(dst, src_e, scol, deps, lane):
        eng = "vector" if lane == 0 else "gpsimd"
        sem = "ve" if lane == 0 else "gp"
        base = o_gt4 + lane * 256
        deps = deps + [_gprev[lane]]
        p0 = b.ts(dst, src_e(0), scol(0), None, ALU.mult, deps=deps, inc=sem,
                  eng=eng)
        pt = []
        for e in range(1, 5):
            pt.append(b.ts(v16(base + (e - 1) * 64, 64), src_e(e), scol(e),
                           None, ALU.mult, deps=deps, inc=sem, eng=eng))
        a1 = b.tt(v16(base, 64), v16(base, 64), v16(base + 64, 64), ALU.add,
                  deps=pt[:2], inc=sem, eng=eng)
        a2 = b.tt(v16(base + 128, 64), v16(base + 128, 64),
                  v16(base + 192, 64), ALU.add, deps=pt[2:], inc=sem, eng=eng)
        a3 = b.tt(v16(base, 64), v16(base, 64), v16(base + 128, 64), ALU.add,
                  deps=[a1, a2], inc=sem, eng=eng)
        t = b.tt(dst, dst, v16(base, 64), ALU.add, deps=[p0, a3], inc=sem,
                 eng=eng)
        _gprev[lane] = t
        return t

    def gate_kv(og, tcn, deps):
        kv = og // 4
        hp = og % 4
        tl = []
        for hh in range(2):
            h = 2 * hp + hh
            if kv == 0:
                dst = v16(o_ktf + tcn * 512 + hp * 128 + hh * 64, 64)
            else:
                dst = v16(o_vex + tcn * 520 + hp * 130 + (65 if hh else 0), 64)
            src_e = lambda e: v16(o_g5 + hh * 320 + e * 64, 64)
            tl.append(gate_sum(dst, src_e, lambda e: sg_col(0, tcn, h, e),
                                deps + [tmask[0]], hh))
        return tl

    for og in range(8):
        wgroup_mm(d["wkvB"][og], og, og % 2, gate_kv, None)
    b.barrier()

    tktr = []
    ttr = None
    tevk = []
    for cc4 in range(4):
        for tcn in range(NT):
            pv = (psC if tcn % 2 == 0 else psD)[:, 0:64].bitcast(F16)
            ttr = b.trans(pv, v16(o_ktf + tcn * 512 + cc4 * 128, 128), ident,
                          deps=tevk[-1:], inc="mm")
            te = b.copy(v16(o_kT + cc4 * T + tcn * 128, 128), pv,
                        deps=[ttr], inc="ve")
            tevk.append(te)
        tsend.append(b.dma("gpsimd",
                           E["ag_in"][KOFF + cc4 * 128 * T:KOFF + (cc4 + 1) * 128 * T]
                           .rearrange("(p c) -> p c", c=T),
                           v16(o_kT + cc4 * T, T), deps=[tevk[-1]], inc="dd"))
    for tcn in range(NT):
        tsend.append(b.dma("gpsimd",
                           E["ag_in"][VOFF + tcn * 128 * 520:VOFF + (tcn + 1) * 128 * 520]
                           .rearrange("(p c) -> p c", c=520),
                           A16[:, o_vex + tcn * 520:o_vex + (tcn + 1) * 520],
                           deps=[tones[tcn], b.tok("ve")], inc="dd"))
    tag = b.coll("AllGather", ALU.bypass, PAIRS,
                 [E["ag_in"][:]], [E["ag_out"][0:2 * AGN]], deps=tsend, inc="cc")
    tap("kT", A16[:, o_kT:o_kT + 4 * T], [b.tok("ve")])
    tap("vext", A16[:, o_vex:o_vex + NT * 520], [b.tok("ve")])
    ckpt("S9")
    ar16.freeblk(o_kT, 4 * T)
    ar16.freeblk(o_ktf, NT * 512)

    o_qtok = ar16.alloc(NT * 512, "q tok")

    def gate_q(og, tcn, deps):
        tl = []
        for hh in range(2):
            h = 2 * og + hh
            dst = v16(o_qtok + tcn * 512 + og * 128 + hh * 64, 64)
            src_e = lambda e: v16(o_g5 + hh * 320 + e * 64, 64)
            tl.append(gate_sum(dst, src_e, lambda e: sg_col(1, tcn, h, e),
                                deps + [tmask[1]], hh))
        return tl

    for og in range(4):
        wgroup_mm(d["wqB"][og], og, og % 2, gate_q, None)
    b.barrier()
    o_qT = ar16.alloc(4 * T, "qT ffn")
    tevq = []
    for cc4 in range(4):
        for tcn in range(NT):
            pv = (psC if tcn % 2 == 0 else psD)[:, 0:64].bitcast(F16)
            ttr = b.trans(pv, v16(o_qtok + tcn * 512 + cc4 * 128, 128), ident,
                          deps=tevq[-1:], inc="mm")
            te = b.copy(v16(o_qT + cc4 * T + tcn * 128, 128), pv,
                        deps=[ttr], inc="ve")
            tevq.append(te)
    ar16.freeblk(o_qtok, NT * 512)
    ar16.freeblk(o_wg, 2 * KCH * 640)
    ar16.freeblk(o_g5, 640)
    ar32.freeblk(o_wg32, 2 * 640)
    tap("qf", A16[:, o_qT:o_qT + 4 * T], [b.tok("ve")])
    ckpt("S10")

    o_wo32 = ar32.alloc(2 * 128, "wo2 stg")
    tpw = [None, None]
    two = []
    for cc in range(20):
        for mc2 in range(KCH):
            par = (cc * KCH + mc2) % 2
            s32 = v32(o_wo32 + par * 128, 128)
            tdw = b.dma("sync", s32, d["wo2B"][cc, mc2], deps=[tpw[par]],
                        inc="w0" if par == 0 else "w1")
            tpw[par] = b.copy(v16(o_wo + cc * 1024 + mc2 * 128, 128), s32,
                              deps=[tdw], inc="ve")
            two.append(tpw[par])
    ar32.freeblk(o_wo32, 2 * 128)
    ar16.freeblk(o_xin1, KCH * T)
    b.barrier()

    o_khp = ar16.alloc(2 * QFULL, "k hp")
    o_vhp = ar16.alloc(2 * 32 * 130, "v hp")
    o_P = ar16.alloc(3 * 1024, "P bufs")
    o_avT = ar16.alloc(4 * T, "avT")
    o_sst = ar32.alloc(1024, "sums stage")
    tpre = [None, None]

    def prefetch_hp(hp, pp, deps):
        toks = []
        for sl in range(2):
            base = sl * AGN + KOFF + hp * 128 * QFULL // 2
            toks.append(b.dma(
                "gpsimd",
                v16(o_khp + pp * QFULL + sl * (QFULL // 2), QFULL // 2),
                E["ag_out"][base:base + 128 * 2048].rearrange("(p c) -> p c", c=2048),
                deps=deps, inc="kh0" if pp == 0 else "kh1"))
        for sl in range(2):
            for ch in range(16):
                src0 = sl * AGN + VOFF + ch * 128 * 520
                toks.append(b.dma(
                    "gpsimd",
                    v16(o_vhp + pp * 32 * 130 + (sl * 16 + ch) * 130, 130),
                    E["ag_out"][src0:src0 + 128 * 520]
                    .rearrange("(p c) -> p c", c=520)[:, hp * 130:hp * 130 + 130],
                    deps=deps, inc="kh0" if pp == 0 else "kh1"))
        return toks

    tpre[0] = prefetch_hp(0, 0, [tag])
    o_sums_rd = ar32.alloc(T, "sums rd")
    tev_av = []
    tsum_dma = []
    for hp in range(4):
        pp = hp % 2
        if hp < 3:
            tpre[(hp + 1) % 2] = prefetch_hp(hp + 1, (hp + 1) % 2,
                                             [tag] + tev_av[-1:])
        kdeps = tpre[pp]
        for tc4 in range(4):
            ts_ = tc4 * 512
            texp = [None, None, None]
            tavmm_c = [None]
            tav_par = [None, None, None]
            tSmm = [None, None, None]
            SPAIR = [(psB[:, 0:512], psB[:, 512:1024], psB[:, 0:1024]),
                     (psE[:, 0:512], psE[:, 512:1024], psE[:, 0:1024]),
                     (psA[:, 1024:1536], psA[:, 1536:2048], psA[:, 1024:2048])]
            def emit_S(jc):
                par = jc % 3
                So = SPAIR[par]
                sdeps = kdeps + [texp[par]] + (tev_av if jc < 3 else [])
                b.mm(So[0], v16(o_khp + pp * QFULL + jc * 128, 128, 0, 64),
                     v16(o_qT + hp * T + ts_, 512, 0, 64), True, True,
                     deps=sdeps, tp=(0, 0))
                tSmm[par] = b.mm(So[1],
                                 v16(o_khp + pp * QFULL + jc * 128, 128, 64, 64),
                                 v16(o_qT + hp * T + ts_, 512, 64, 64), True, True,
                                 tp=(64, 0), inc="mm")
                texp[par] = b.act(v16(o_P + par * 1024, 1024), So[2], AF.Exp,
                                  scale=SCALE_F, deps=[tSmm[par], tav_par[par]],
                                  inc="se")

            def emit_AV(jc):
                par = jc % 3
                vch = v16(o_vhp + pp * 32 * 130 + jc * 130, 130)
                b.mm(psA[0:65, 0:512], vch[:, 0:65], v16(o_P + par * 1024, 512),
                     start=(jc == 0), stop=(jc == 31),
                     deps=[texp[par]] + (tev_av if jc == 0 else []))
                tavmm_c[0] = b.mm(psA[0:65, 512:1024], vch[:, 65:130],
                                  v16(o_P + par * 1024 + 512, 512),
                                  start=(jc == 0), stop=(jc == 31), inc="mm")
                tav_par[par] = tavmm_c[0]

            emit_S(0)
            emit_S(1)
            for jc in range(32):
                if jc + 2 < 32:
                    emit_S(jc + 2)
                emit_AV(jc)
            tavmm = tavmm_c[0]
            e1 = b.copy(v16(o_avT + hp * T + ts_, 512, 0, 64), psA[0:64, 0:512],
                        deps=[tavmm], inc="ve")
            e2 = b.copy(v16(o_avT + hp * T + ts_, 512, 64, 64),
                        psA[0:64, 512:1024], deps=[tavmm], inc="ve")
            s1 = b.copy(v32(o_sst, 512, 64, 1), psA[64:65, 0:512],
                        deps=[tavmm] + tsum_dma, inc="ve")
            s2 = b.copy(v32(o_sst + 512, 512, 64, 1), psA[64:65, 512:1024],
                        deps=[tavmm] + tsum_dma, inc="ve")
            tsum_dma = [
                b.dma("gpsimd", E["sums_d"][2 * hp:2 * hp + 1, ts_:ts_ + 512],
                      v32(o_sst, 512, 64, 1), deps=[s1], inc="d0"),
                b.dma("gpsimd", E["sums_d"][2 * hp + 1:2 * hp + 2, ts_:ts_ + 512],
                      v32(o_sst + 512, 512, 64, 1), deps=[s2], inc="d1"),
            ]
            tev_av = [e1, e2, s1, s2]
    b.barrier()
    ar16.freeblk(o_khp, 2 * QFULL)
    ar16.freeblk(o_vhp, 2 * 32 * 130)
    ar16.freeblk(o_P, 3 * 1024)
    ar32.freeblk(o_sst, 1024)
    o_E4 = ar16.alloc(40 * 128, "E4")
    b.memset(v16(o_E4, 40 * 128), 0.0)
    for g in range(40):
        h = g // 5
        s0 = (h % 2) * 64
        p0 = (g % 2) * 64
        sub = v16(o_E4 + g * 128 + p0, 64, s0, 64)
        b.affine(sub, sub, [[-1, 64]], ALU.not_equal, 1.0, base=0, chan=1)
    tap("avT", A16[:, o_avT:o_avT + 4 * T], [b.tok("ve")])
    ckpt("S11")

    trd = b.dma("sync", v32(o_sums_rd, T, 0, 8), E["sums_d"][:, :],
                deps=[b.tok("d0"), b.tok("d1")], inc="x")
    o_rs = ar16.alloc(T, "recip16")
    tr1 = b.recip(v32(o_sums_rd, T, 0, 8), v32(o_sums_rd, T, 0, 8),
                  deps=[trd], inc="ve")
    tr2 = b.copy(v16(o_rs, T, 0, 8), v32(o_sums_rd, T, 0, 8), deps=[tr1], inc="ve")
    ar32.freeblk(o_sums_rd, T)
    o_mdT = ar16.alloc(T, "mdT")
    tmdT = []
    for tcn in range(NT):
        pv = psD[0:40, 0:64].bitcast(F16)
        tt_ = b.trans(pv, v16(o_msk + NT * 40 + tcn * 40, 40), ident,
                      deps=tmdT[-1:], inc="mm")
        tmdT.append(b.copy(v16(o_mdT + tcn * 128, 128, 0, 40), pv,
                           deps=[tt_], inc="ve"))
    o_gT = ar16.alloc(T, "gT")
    tgT = []
    for tc4 in range(4):
        s = tc4 * 512
        tm1 = b.mm(psC[0:40, :], v16(o_E7, 40, 0, 8), v16(o_rs + s, 512, 0, 8),
                   True, True, deps=[tr2, tsetup] + tmdT[-1:] + tgT[-1:], inc="mm")
        tgT.append(b.tt(v16(o_gT + s, 512, 0, 40), v16(o_mdT + s, 512, 0, 40),
                        psC[0:40, :], ALU.mult, deps=[tm1] + tmdT, inc="ve"))
    tap("gT", A16[0:40, o_gT:o_gT + T], [b.tok("ve")])
    b.barrier()
    ar16.freeblk(o_mdT, T)
    ar16.freeblk(o_rs, T)

    o_om = ar16.alloc(20 * 512, "OM")
    o_gbc = ar16.alloc(512, "gbc sbuf")
    o_x1s2 = ar32.alloc(2 * 512, "x1 slc f")
    o_yout = ar32.alloc(2 * 512, "y out")
    tx1 = [None, None]
    ty = [None, None]
    tomlast = []
    n_out = 0
    for tc4 in range(4):
        s = tc4 * 512
        tgb_prev = None
        tom = []
        for cc in range(20):
            tgb = b.mm(psD[:, :], v16(o_E3 + cc * 128, 128, 0, 40),
                       v16(o_gT + s, 512, 0, 40), True, True,
                       deps=tgT + [tgb_prev], inc="mm")
            tgbe = b.act(v16(o_gbc, 512), psD[:, :], AF.Identity,
                         deps=[tgb, tom[-1:][0] if tom else None], inc="se")
            tgb_prev = tgbe
            tab = None
            for half in range(2):
                gidx = 2 * cc + half
                hh = gidx // 5
                hp = hh // 2
                tab = b.mm(psC[:, :], v16(o_E4 + gidx * 128, 128),
                           v16(o_avT + hp * T + s, 512),
                           start=(half == 0), stop=(half == 1),
                           deps=[tom[-1] if tom else None] if half == 0 else None,
                           inc="mm" if half == 1 else None)
            tom.append(b.tt(v16(o_om + cc * 512, 512), v16(o_gbc, 512),
                            psC[:, :], ALU.mult, deps=[tgbe, tab], inc="ve"))
        for mc in range(KCH):
            par = mc % 2
            tdx = b.dma("sync", v32(o_x1s2 + par * 512, 512),
                        E["x1d"][128 * mc:128 * (mc + 1), s:s + 512],
                        deps=[ty[par]], inc="x0" if par == 0 else "x1")
            ptile = psB[:, 0:512] if par == 0 else psB[:, 512:1024]
            tmm = None
            for cc in range(20):
                tmm = b.mm(ptile, v16(o_wo + cc * 1024 + mc * 128, 128),
                           v16(o_om + cc * 512, 512),
                           start=(cc == 0), stop=(cc == 19),
                           deps=[tom[cc], ty[par]] + two[-1:],
                           inc="mm" if cc == 19 else None)
            tyv = b.tt(v32(o_yout + par * 512, 512), ptile,
                       v32(o_x1s2 + par * 512, 512), ALU.add,
                       deps=[tmm, tdx], inc="ve")
            ty[par] = b.dma("gpsimd", E["y"][128 * mc:128 * (mc + 1), s:s + 512],
                            v32(o_yout + par * 512, 512), deps=[tyv], inc="out")
            n_out += 1
        tomlast = tom
        if "om0" in taps and tc4 == 0:
            tap("om0", v16(o_om, 512), [tomlast[0]])
    b.final_wait([b.tok("out")])
    return ar32.peak, ar16.peak


TAP_SHAPES = {
    "xin": ([C, T], F16), "q": ([C, T], F16),
    "ktok": ([128, NT * C], F16), "vtok": ([128, NT * C], F16),
    "kvf": ([64, H_A * HD], F32), "att": ([C, T], F16),
    "x1": ([C, T], F32), "gates": ([128, NT * 80], F32),
    "sdg": ([128, NT * 40], F32), "md": ([128, NT * 40], F16),
    "kT": ([128, 4 * T], F16), "vext": ([128, NT * 520], F16),
    "qf": ([128, 4 * T], F16), "avT": ([128, 4 * T], F16),
    "gT": ([40, T], F16), "om0": ([128, 512], F16),
    "lnab": ([1, 2 * T], F16),
    "kvab": ([16, 2 * T], F16),
    "kvsums": ([64, T], F32),
    "lnsums": ([2, T], F32),
}


def build_nc(taps=(), stop=None):
    nc = bass.Bass("TRN2", target_bir_lowering=False, debug=False)
    es = ExitStack()
    d = {}
    for name, shape in [
        ("xT", [C, T]), ("qkvB", [24, KCH, 128, 128]), ("qkvb", [128, 24]),
        ("p1B", [KCH, KCH, 128, 128]), ("p1b", [128, KCH]),
        ("p2B", [KCH, KCH, 128, 128]), ("p2b", [128, KCH]),
        ("lngb", [128, 4]), ("tmpc", [128, KCH]), ("abT", [64, H_A * HD]),
        ("wkvB", [8, KCH, 128, 640]), ("wqB", [4, KCH, 128, 640]),
        ("wsdB", [KCH, 128, 80]), ("wo2B", [20, KCH, 128, 128]),
    ]:
        d[name] = nc.dram_tensor(name, shape, F32, kind="ExternalInput").ap()

    E = {"d": d}
    E["y"] = nc.dram_tensor("y", [C, T], F32, kind="ExternalOutput").ap()
    E["x1d"] = nc.dram_tensor("x1d", [C, T], F32).ap()
    E["ag_in"] = nc.dram_tensor("ag_in", [AGN], F16).ap()
    E["ag_out"] = nc.dram_tensor("ag_out", [2 * AGN + 2048], F16).ap()
    E["kv_in"] = nc.dram_tensor("kv_in", [64, H_A * HD], F32).ap()
    E["kv_out"] = nc.dram_tensor("kv_out", [64, H_A * HD], F32).ap()
    E["sums_d"] = nc.dram_tensor("sums_d", [8, T], F32).ap()
    E["tap_t"] = {}
    for t in taps:
        shp, dt_ = TAP_SHAPES[t]
        E["tap_t"][t] = nc.dram_tensor("tap_" + t, list(shp), dt_,
                                       kind="ExternalOutput").ap()

    E["A32"] = es.enter_context(nc.sbuf_tensor("af32", [128, NF32], F32))
    E["A16"] = es.enter_context(nc.sbuf_tensor("af16", [128, NF16], F16))
    E["psA"] = es.enter_context(nc.psum_tensor("psA", [128, 2048], F32))
    E["psB"] = es.enter_context(nc.psum_tensor("psB", [128, 1024], F32))
    E["psE"] = es.enter_context(nc.psum_tensor("psE", [128, 1024], F32))
    E["psC"] = E["psE"][:, 0:512]
    E["psD"] = E["psE"][:, 512:1024]
    sems = {s: es.enter_context(nc.semaphore(s)) for s in SEMS}

    ref_cnt = {}
    with nc.Block() as blk:
        def mk(role):
            def f(eng):
                cnt = {s: 0 for s in SEMS}
                bb = B(nc, role, sems, cnt)
                try:
                    _program(bb, E, set(taps), stop=stop)
                except Exception as ex:
                    if type(ex).__name__ != "_Stop":
                        raise
                if not ref_cnt:
                    ref_cnt.update(cnt)
                else:
                    assert ref_cnt == cnt, (role, ref_cnt, cnt)
            return f
        blk.tensor(mk("tensor"))
        blk.vector(mk("vector"))
        blk.scalar(mk("scalar"))
        blk.gpsimd(mk("gpsimd"))
        blk.sync(mk("sync"))
    es.close()
    return nc


def layout_inputs(inputs):
    f = lambda a: np.ascontiguousarray(np.asarray(a, dtype=np.float32))
    x = f(inputs["L0_attn_x"])
    qkv_w = f(inputs["L0_attn_qkv_w"])
    qkv_b = f(inputs["L0_attn_qkv_b"])
    p1_w, p1_b = f(inputs["L0_attn_p1_w"]), f(inputs["L0_attn_p1_b"])
    p2_w, p2_b = f(inputs["L0_attn_p2_w"]), f(inputs["L0_attn_p2_b"])
    kn_g, kn_b = f(inputs["L0_attn_kn_g"]), f(inputs["L0_attn_kn_b"])
    vn_g, vn_b = f(inputs["L0_attn_vn_g"]), f(inputs["L0_attn_vn_b"])
    temp = f(inputs["L0_attn_temperature"]).reshape(H_A)
    ab = f(inputs["L0_attn_attn_bias"])
    Wq, Wkv = f(inputs["L0_ffn_Wq"]), f(inputs["L0_ffn_Wkv"])
    Ws, Wd, Wo = f(inputs["L0_ffn_Ws"]), f(inputs["L0_ffn_Wd"]), f(inputs["L0_ffn_Wo"])

    blk = lambda w, mcs: np.ascontiguousarray(
        w.reshape(KCH, 128, mcs, 128).transpose(2, 0, 1, 3))
    shared = {
        "qkvB": blk(qkv_w.T, 24),
        "qkvb": np.ascontiguousarray(qkv_b.reshape(24, 128).T),
        "p1B": blk(p1_w.T, KCH),
        "p1b": np.ascontiguousarray(p1_b.reshape(KCH, 128).T),
        "p2B": blk(p2_w.T, KCH),
        "p2b": np.ascontiguousarray(p2_b.reshape(KCH, 128).T),
        "lngb": np.ascontiguousarray(np.stack(
            [np.tile(kn_g, 2), np.tile(kn_b, 2), np.tile(vn_g, 2),
             np.tile(vn_b, 2)], axis=1)),
        "tmpc": np.ascontiguousarray(np.repeat(temp, HD).reshape(KCH, 128).T),
        "abT": np.ascontiguousarray(ab.transpose(1, 0, 2).reshape(64, H_A * HD)),
        "wkvB": np.ascontiguousarray(
            Wkv.reshape(KCH, 128, 8, 640).transpose(2, 0, 1, 3)),
        "wqB": np.ascontiguousarray(
            Wq.reshape(KCH, 128, 4, 640).transpose(2, 0, 1, 3)),
        "wsdB": np.ascontiguousarray(
            np.concatenate([Ws, Wd], axis=1).reshape(KCH, 128, 80)),
        "wo2B": np.ascontiguousarray(
            np.tile(Wo.reshape(64, 5, 1024).transpose(1, 0, 2).reshape(320, 1024),
                    (8, 1)).reshape(20, 128, KCH, 128).transpose(0, 2, 1, 3)),
    }
    in_maps = []
    for c in range(8):
        bq, half = c // 2, c % 2
        m = dict(shared)
        m["xT"] = np.ascontiguousarray(x[bq, half * T:(half + 1) * T, :].T)
        in_maps.append(m)
    return in_maps


def kernel(**inputs):
    in_maps = layout_inputs(inputs)
    nc = build_nc()
    res = run_bass_kernel_spmd(nc, in_maps, core_ids=list(range(8)))
    out = np.empty((4, QFULL, C), dtype=np.float32)
    for c in range(8):
        bq, half = c // 2, c % 2
        out[bq, half * T:(half + 1) * T, :] = res.results[c]["y"].T
    return out


# version 04
# speedup vs baseline: 1.3596x; 1.2191x over previous
from contextlib import ExitStack

import numpy as np
import concourse.bass as bass
import concourse.mybir as mybir
from concourse.bass_utils import run_bass_kernel_spmd

F32 = mybir.dt.float32
F16 = mybir.dt.float16
AF = mybir.ActivationFunctionType
ALU = mybir.AluOpType

T = 2048
NT = T // 128
C = 1024
KCH = C // 128
QFULL = 4096
EPS = 1e-5
H_A, HD = 16, 64
H_F, E_F, DH = 8, 5, 64
SCALE_F = DH ** -0.5
KOFF = 0
VOFF = 512 * 2048
AGN = VOFF + 2048 * 520
PAIRS = [[0, 1], [2, 3], [4, 5], [6, 7]]
SEMS = ["wc", "w0", "w1", "w2", "w3", "wg0", "wg1", "x", "x0", "x1",
        "d0", "d1", "kh0", "kh1", "mm", "ve", "se", "gp", "dd", "cc",
        "out", "bar"]
NF32 = 8192
NF16 = 88064


class Alloc:
    def __init__(self, ncols, name):
        self.free = [(0, ncols)]
        self.ncols = ncols
        self.peak = 0
        self.name = name

    def alloc(self, n, tag=""):
        n = -(-n // 16) * 16
        for i, (s, ln) in enumerate(self.free):
            if ln >= n:
                if ln == n:
                    self.free.pop(i)
                else:
                    self.free[i] = (s + n, ln - n)
                self.peak = max(self.peak, self.ncols - sum(l for _, l in self.free))
                return s
        raise MemoryError(f"{self.name} OOM: {n} cols for {tag}; free={self.free}")

    def freeblk(self, s, n):
        n = -(-n // 16) * 16
        self.free.append((s, n))
        self.free.sort()
        out = []
        for blk in self.free:
            if out and out[-1][0] + out[-1][1] == blk[0]:
                out[-1] = (out[-1][0], out[-1][1] + blk[1])
            else:
                out.append(list(blk))
        self.free = [tuple(xx) for xx in out]


class B:

    def __init__(self, nc, role, sems, cnt):
        self.nc, self.role, self.sems, self.cnt = nc, role, sems, cnt
        self.wm = {}
        self.gp_chain = None

    def tok(self, sem):
        return (sem, self.cnt[sem])

    DMA_SEMS = ("wc", "w0", "w1", "w2", "w3", "wg0", "wg1", "x", "x0",
                "x1", "d0", "d1", "kh0", "kh1", "dd", "out")

    def _wait(self, eng, deps):
        flat = []
        for dd in deps or ():
            if isinstance(dd, list):
                flat.extend(dd)
            else:
                flat.append(dd)
        for dd in flat:
            if dd is None:
                continue
            sem, val = dd
            if sem in self.DMA_SEMS:
                val = self.cnt[sem]
            if val <= 0 or self.wm.get(sem, 0) >= val:
                continue
            eng.wait_ge(self.sems[sem], val)
            self.wm[sem] = val

    def _run(self, owner, fn, deps, inc, n):
        ins = None
        if self.role == owner:
            self._wait(getattr(self.nc, owner), deps)
            ins = fn()
        if inc is not None:
            self.cnt[inc] += n
            if ins is not None:
                ins.then_inc(self.sems[inc], n)
            return (inc, self.cnt[inc])
        return None

    def dma(self, q, out, in_, deps=None, inc=None):
        return self._run(q, lambda: getattr(self.nc, q).dma_start(out=out, in_=in_),
                         deps, inc, 16)

    def mm(self, out, lhsT, rhs, start, stop, deps=None, inc=None, tp=None):
        return self._run("tensor",
                         lambda: self.nc.tensor.matmul(out, lhsT, rhs, start=start,
                                                       stop=stop, tile_position=tp),
                         deps, inc, 1)

    def trans(self, out, in_, ident, deps=None, inc=None):
        return self._run("tensor", lambda: self.nc.tensor.transpose(out, in_, ident),
                         deps, inc, 1)

    def act(self, out, in_, func, bias=None, scale=None, accum_out=None,
            deps=None, inc=None):
        kw = {}
        if bias is not None:
            kw["bias"] = bias
        if scale is not None:
            kw["scale"] = scale
        if accum_out is not None:
            kw["accum_out"] = accum_out
        return self._run("scalar",
                         lambda: self.nc.scalar.activation(out, in_, func, **kw),
                         deps, inc, 1)

    def tt(self, out, a, c, op, deps=None, inc=None, eng="vector"):
        return self._run(eng, lambda: getattr(self.nc, eng).tensor_tensor(out, a, c, op),
                         deps, inc, 1)

    def ts(self, out, in0, s1, s2, op0, op1=None, deps=None, inc=None, eng="vector"):
        def f():
            e = getattr(self.nc, eng)
            if op1 is None:
                return e.tensor_scalar(out, in0, s1, None, op0)
            return e.tensor_scalar(out, in0, s1, s2, op0, op1)
        return self._run(eng, f, deps, inc, 1)

    def stt(self, out, in0, scalar, in1, op0, op1, deps=None, inc=None, eng="vector"):
        return self._run(eng,
                         lambda: getattr(self.nc, eng).scalar_tensor_tensor(
                             out, in0, scalar, in1, op0, op1),
                         deps, inc, 1)

    def copy(self, out, in_, deps=None, inc=None, eng="vector"):
        return self._run(eng, lambda: getattr(self.nc, eng).tensor_copy(out, in_),
                         deps, inc, 1)

    def recip(self, out, in_, deps=None, inc=None):
        return self._run("vector", lambda: self.nc.vector.reciprocal(out, in_),
                         deps, inc, 1)

    def memset(self, ap, v, deps=None, inc=None):
        deps = list(deps or []) + [self.gp_chain]
        tok = self._run("gpsimd", lambda: self.nc.gpsimd.memset(ap, v), deps,
                        "gp", 1)
        self.gp_chain = tok
        return tok

    def affine(self, out, in_, pattern, cmp, fill, base=None, chan=None,
               deps=None, inc=None):
        deps = list(deps or []) + [self.gp_chain]
        tok = self._run("gpsimd",
                        lambda: self.nc.gpsimd.affine_select(
                            out=out, in_=in_, pattern=pattern, compare_op=cmp,
                            fill=fill, base=base, channel_multiplier=chan),
                        deps, "gp", 1)
        self.gp_chain = tok
        return tok

    def coll(self, kind, op, groups, ins, outs, deps=None, inc=None):
        return self._run("gpsimd",
                         lambda: self.nc.gpsimd.collective_compute(
                             kind, op, replica_groups=groups, ins=ins, outs=outs),
                         deps, inc, 1)

    def barrier(self):
        self.cnt["bar"] += 4
        tgt = self.cnt["bar"]
        if self.role in ("tensor", "vector", "scalar", "gpsimd"):
            eng = getattr(self.nc, self.role)
            self._wait(eng, [(s, self.cnt[s]) for s in self.DMA_SEMS])
            eng.drain().then_inc(self.sems["bar"], 1)
            eng.wait_ge(self.sems["bar"], tgt)
            self.wm["bar"] = tgt
        elif self.role == "sync":
            eng = self.nc.sync
            self._wait(eng, [(s, self.cnt[s]) for s in self.DMA_SEMS])
            eng.wait_ge(self.sems["bar"], tgt)
            self.wm["bar"] = tgt
        return ("bar", tgt)

    def final_wait(self, deps):
        if self.role == "gpsimd":
            self._wait(self.nc.gpsimd, deps)


def _program(b, E, taps, stop=None):
    d = E["d"]
    A32, A16 = E["A32"], E["A16"]
    psA, psB, psC, psD = E["psA"], E["psB"], E["psC"], E["psD"]
    psE = E["psE"]
    tap_t = E["tap_t"]
    ar32 = Alloc(NF32, "af32")
    ar16 = Alloc(NF16, "af16")

    def v32(off, w, p0=0, pn=128):
        return A32[p0:p0 + pn, off:off + w]

    def v16(off, w, p0=0, pn=128):
        return A16[p0:p0 + pn, off:off + w]

    def tap(name, ap, deps):
        if name in taps:
            b.dma("sync", tap_t[name], ap, deps=deps, inc="out")

    class _Stop(Exception):
        pass

    def ckpt(name):
        if stop == name:
            b.barrier()
            b.final_wait([b.tok("out")])
            raise _Stop()

    o_id = ar16.alloc(128, "ident")
    o_ones_c = ar16.alloc(16, "ones col")
    o_ones_r = ar16.alloc(128, "ones row")
    o_Ekv = ar16.alloc(8 * 128, "Ekv")
    o_E6 = ar16.alloc(8 * 128, "E6")
    o_lngb = ar32.alloc(4, "lngb")
    o_geff = ar32.alloc(KCH, "geff")
    o_beff = ar32.alloc(KCH, "beff")
    o_tmpc = ar32.alloc(KCH, "tmpc")
    o_qkvb = ar32.alloc(24, "qkvb")
    o_p1b = ar32.alloc(KCH, "p1b")
    o_p2b = ar32.alloc(KCH, "p2b")
    o_abT = ar32.alloc(H_A * HD, "abT")
    ident = v16(o_id, 128)

    tw = [
        b.dma("sync", v32(o_lngb, 4), d["lngb"][:, :], inc="wc"),
        b.dma("sync", v32(o_tmpc, KCH), d["tmpc"][:, :], inc="wc"),
        b.dma("sync", v32(o_qkvb, 24), d["qkvb"][:, :], inc="wc"),
        b.dma("sync", v32(o_p1b, KCH), d["p1b"][:, :], inc="wc"),
        b.dma("sync", v32(o_p2b, KCH), d["p2b"][:, :], inc="wc"),
        b.dma("sync", v32(o_abT, H_A * HD, 0, 64), d["abT"][:, :], inc="wc"),
    ]

    b.memset(ident[:, :], 0.0)
    b.affine(ident[:, :], ident[:, :], [[-1, 128]], ALU.not_equal, 1.0, base=0, chan=1)
    b.memset(v16(o_ones_c, 1), 1.0)
    b.memset(v16(o_ones_r, 128, 0, 1), 1.0)
    b.memset(v16(o_Ekv, 8 * 128), 0.0)
    for j in range(8):
        base = o_Ekv + j * 128
        b.memset(v16(base + 2 * j, 1, 0, 64), 1.0)
        b.memset(v16(base + 2 * j + 1, 1, 64, 64), 1.0)
        b.memset(v16(base + 64 + 32 + 2 * j, 1, 0, 64), 1.0)
        b.memset(v16(base + 64 + 32 + 2 * j + 1, 1, 64, 64), 1.0)
    b.memset(v16(o_E6, 8 * 128), 1.0)
    for j in range(8):
        sub = v16(o_E6 + j * 128, 128, 0, 16)
        b.affine(sub, sub, [[1, 128]], ALU.is_ge, 0.0, base=128 * j, chan=-64)
        b.affine(sub, sub, [[-1, 128]], ALU.is_ge, 0.0, base=63 - 128 * j, chan=64)
    tsetup = b.tok("gp")

    tgeff = b.ts(v32(o_geff, KCH), v32(o_tmpc, KCH), v32(o_lngb, 1), None,
                 ALU.mult, deps=[tw[0], tw[1]], inc="ve")
    tbeff = b.ts(v32(o_beff, KCH), v32(o_tmpc, KCH), v32(o_lngb + 1, 1), None,
                 ALU.mult, deps=[tw[0], tw[1]], inc="ve")

    def emit_ln(src_dram, o_out16, tapon=False):
        o_stg = ar32.alloc(T, "ln stage")
        o_cst = ar16.alloc(2 * T, "ln cast")
        tmm = None
        tprev = []
        for k in range(KCH):
            tdx = b.dma("sync", v32(o_stg, T), src_dram[128 * k:128 * (k + 1), :],
                        deps=tprev, inc="x")
            xc = v16(o_cst, T)
            xq = v16(o_cst + T, T)
            tc1 = b.copy(xc, v32(o_stg, T), deps=[tdx, tmm], inc="ve")
            tc2 = b.act(xq, v32(o_stg, T), AF.Square, deps=[tdx, tmm], inc="se")
            tprev = [tc1, tc2]
            for nt in range(4):
                s = nt * 512
                b.mm(psA[0:1, s:s + 512], v16(o_ones_c, 1), xc[:, s:s + 512],
                     start=(k == 0), stop=(k == KCH - 1), deps=[tc1, tsetup])
                tmm = b.mm(psA[32:33, s:s + 512], v16(o_ones_c, 1),
                           xq[:, s:s + 512],
                           start=(k == 0), stop=(k == KCH - 1), deps=[tc2],
                           inc="mm" if nt == 3 else None)
        ar16.freeblk(o_cst, 2 * T)
        o_st = ar16.alloc(8 * 1024, "ln st")
        def q32(i, w, p0, pn):
            return A16[p0:p0 + pn, o_st + i * 2048:o_st + i * 2048 + 2 * w]\
                .bitcast(F32)
        o_ab = ar16.alloc(2 * T, "ln ab")
        Arow = v16(o_ab, T, 0, 1)
        Brow = v16(o_ab + T, T, 0, 1)
        tA = tB = None
        for hf in range(2):
            sl = hf * 1024
            mu = q32(0, 1024, 0, 1)
            var = q32(1, 1024, 0, 1)
            r = q32(2, 1024, 0, 1)
            tp_ = q32(3, 1024, 0, 1)
            dd = [tmm, tB]
            t1 = b.ts(mu, psA[0:1, sl:sl + 1024], 1.0 / C, None, ALU.mult,
                      deps=dd, inc="ve")
            t3 = b.act(tp_, mu, AF.Square, deps=[t1, tB], inc="se")
            t2 = b.ts(var, psA[32:33, sl:sl + 1024], 1.0 / C, EPS,
                      ALU.mult, ALU.add, deps=dd, inc="ve")
            t4 = b.tt(var, var, tp_, ALU.subtract, deps=[t2, t3], inc="ve")
            t6 = b.act(r, var, AF.Sqrt, deps=[t4], inc="se")
            t7 = b.recip(r, r, deps=[t6], inc="ve")
            t8 = b.act(tp_, r, AF.Square, deps=[t7], inc="se")
            t9 = b.tt(tp_, tp_, var, ALU.mult, deps=[t8], inc="ve")
            t10 = b.ts(tp_, tp_, -0.5, 1.5, ALU.mult, ALU.add, deps=[t9], inc="ve")
            t11 = b.tt(r, r, tp_, ALU.mult, deps=[t10], inc="ve")
            tA = b.copy(Arow[:, sl:sl + 1024], r, deps=[t11], inc="ve")
            t12 = b.tt(tp_, mu, r, ALU.mult, deps=[t11], inc="ve")
            tB = b.ts(Brow[:, sl:sl + 1024], tp_, -1.0, None, ALU.mult,
                      deps=[t12], inc="ve")
        if tapon and "lnab" in taps:
            tk_ = b.dma("sync", tap_t["lnab"], A16[0:1, o_ab:o_ab + 2 * T],
                        deps=[tA, tB], inc="out")
            b.final_wait([tk_])
        if tapon and "lnsums" in taps:
            o_sdbg = ar32.alloc(2 * T, "sums dbg")
            ts1 = b.copy(v32(o_sdbg, T, 0, 1), psA[0:1, 0:T], deps=[tmm], inc="ve")
            ts2 = b.copy(v32(o_sdbg + T, T, 0, 1), psA[32:33, 0:T], deps=[tmm],
                         inc="ve")
            b.dma("sync", tap_t["lnsums"][0:1, :], v32(o_sdbg, T, 0, 1),
                  deps=[ts1], inc="out")
            b.dma("sync", tap_t["lnsums"][1:2, :], v32(o_sdbg + T, T, 0, 1),
                  deps=[ts2], inc="out")
            ar32.freeblk(o_sdbg, 2 * T)
        for nt in range(4):
            s = nt * 512
            ta = b.mm(psB[:, 0:512], v16(o_ones_r, 128, 0, 1), Arow[:, s:s + 512],
                      True, True, deps=[tA] + tprev, inc="mm")
            tb = b.mm(psB[:, 512:1024], v16(o_ones_r, 128, 0, 1), Brow[:, s:s + 512],
                      True, True, deps=[tB] + tprev, inc="mm")
            for k in range(KCH):
                tdx = b.dma("sync", v32(o_stg, 512),
                            src_dram[128 * k:128 * (k + 1), s:s + 512],
                            deps=tprev, inc="x")
                o_t = ar16.alloc(512, "ln t")
                u1 = b.tt(v16(o_t, 512), v32(o_stg, 512), psB[:, 0:512], ALU.mult,
                          deps=[ta, tdx], inc="ve")
                u2 = b.tt(v16(o_out16 + k * T + s, 512), v16(o_t, 512),
                          psB[:, 512:1024], ALU.add, deps=[u1, tb], inc="ve")
                tprev = [u2]
                ar16.freeblk(o_t, 512)
        ar16.freeblk(o_st, 8 * 1024)
        ar32.freeblk(o_stg, T)
        ar16.freeblk(o_ab, 2 * T)
        b.barrier()

    o_xin = ar16.alloc(KCH * T, "xin")
    emit_ln(d["xT"], o_xin, tapon=True)
    tap("xin", A16[:, o_xin:o_xin + KCH * T], [b.tok("ve")])
    xin_ch = lambda k, c0, w: v16(o_xin + k * T + c0, w)
    ckpt("S1")

    o_wstg = ar32.alloc(4 * 128, "wstg")
    o_wblk = ar16.alloc(4 * 128, "wblk")
    wpar_tok = [None, None, None, None]
    WSEM = ("w0", "w1", "w2", "w3")

    def wblock(src_ap, par):
        s32 = v32(o_wstg + par * 128, 128)
        s16 = v16(o_wblk + par * 128, 128)
        td = b.dma("sync", s32, src_ap, deps=[wpar_tok[par]], inc=WSEM[par])
        tc = b.copy(s16, s32, deps=[td], inc="ve")
        return s16, tc

    tile_tok = [None] * 4

    def mm_weight(blkB, src_ch, out_evac):
        tiles = [psB[:, 0:512], psB[:, 512:1024], psC[:, :], psD[:, :]]
        tmm = None
        for kc in range(KCH):
            par = kc % 4
            w16, tcst = wblock(blkB[kc], par)
            for nt in range(4):
                tmm = b.mm(tiles[nt], w16, src_ch(kc, nt * 512, 512),
                           start=(kc == 0), stop=(kc == KCH - 1),
                           deps=[tcst, tile_tok[nt]],
                           inc="mm" if nt == 3 else None)
            wpar_tok[par] = tmm
        toks = out_evac(tiles, tmm)
        for nt in range(4):
            tile_tok[nt] = toks[nt]

    o_ktok = ar16.alloc(NT * C, "ktok")
    o_vtok = ar16.alloc(NT * C, "vtok")
    kt_ch = lambda tcn, c0, w: v16(o_ktok + tcn * C + c0, w)
    vt_ch = lambda tcn, c0, w: v16(o_vtok + tcn * C + c0, w)

    def kv_group(mc0, o_tok, gcol_fn, bcol_fn):
        o_kv = ar16.alloc(KCH * T, "kvT")
        o_sq = ar16.alloc(T, "sq")
        tstat = None
        tsq = None
        for j in range(KCH):
            mcx = mc0 + j

            def evac(tiles, tmm, j=j, mcx=mcx):
                toks = []
                for nt in range(4):
                    te = b.act(v16(o_kv + j * T + nt * 512, 512), tiles[nt],
                               AF.Identity, bias=v32(o_qkvb + mcx, 1),
                               deps=[tmm, tw[2]], inc="se")
                    toks.append(te)
                return toks

            mm_weight(d["qkvB"][mcx], xin_ch, evac)
            tevs = list(tile_tok)
            tsq = b.act(v16(o_sq, T), v16(o_kv + j * T, T), AF.Square,
                        deps=tevs + [tstat], inc="se")
            Ev = v16(o_Ekv + j * 128, 64)
            Eq = v16(o_Ekv + j * 128 + 64, 64)
            for nt in range(4):
                s = nt * 512
                b.mm(psA[0:64, s:s + 512], Ev, v16(o_kv + j * T + s, 512),
                     start=(j == 0), stop=False, deps=tevs + [tsetup])
                tstat = b.mm(psA[0:64, s:s + 512], Eq, v16(o_sq + s, 512),
                             start=False, stop=(j == KCH - 1), deps=[tsq],
                             inc="mm" if nt == 3 else None)
        ar16.freeblk(o_sq, T)
        if ("kvsums" in taps) and mc0 == 8:
            o_sdb = ar32.alloc(T, "kvs dbg")
            td1 = b.copy(v32(o_sdb, T, 0, 64), psA[0:64, 0:T], deps=[tstat], inc="ve")
            b.dma("sync", tap_t["kvsums"], v32(o_sdb, T, 0, 64), deps=[td1],
                  inc="out")
            ar32.freeblk(o_sdb, T)
        o_st = ar16.alloc(8 * 1024, "kv st")
        def q32(i, w):
            return A16[0:16, o_st + i * 2048:o_st + i * 2048 + 2 * w].bitcast(F32)
        o_ab = ar16.alloc(2 * T, "kv ab")
        Ar = v16(o_ab, T, 0, 16)
        Br = v16(o_ab + T, T, 0, 16)
        tA = tB = None
        for hf in range(2):
            sl = hf * 1024
            mu = q32(0, 1024)
            var = q32(1, 1024)
            r = q32(2, 1024)
            tp_ = q32(3, 1024)
            dd = [tstat, tB]
            t1 = b.ts(mu, psA[0:16, sl:sl + 1024], 1.0 / HD, None, ALU.mult,
                      deps=dd, inc="ve")
            t3 = b.act(tp_, mu, AF.Square, deps=[t1, tB], inc="se")
            t2 = b.ts(var, psA[32:48, sl:sl + 1024], 1.0 / HD, EPS,
                      ALU.mult, ALU.add, deps=dd, inc="ve")
            t4 = b.tt(var, var, tp_, ALU.subtract, deps=[t2, t3], inc="ve")
            t6 = b.act(r, var, AF.Sqrt, deps=[t4], inc="se")
            t7 = b.recip(r, r, deps=[t6], inc="ve")
            t8 = b.act(tp_, r, AF.Square, deps=[t7], inc="se")
            t9 = b.tt(tp_, tp_, var, ALU.mult, deps=[t8], inc="ve")
            t10 = b.ts(tp_, tp_, -0.5, 1.5, ALU.mult, ALU.add, deps=[t9], inc="ve")
            t11 = b.tt(r, r, tp_, ALU.mult, deps=[t10], inc="ve")
            tA = b.copy(Ar[:, sl:sl + 1024], r, deps=[t11], inc="ve")
            t12 = b.tt(tp_, mu, r, ALU.mult, deps=[t11], inc="ve")
            tB = b.ts(Br[:, sl:sl + 1024], tp_, -1.0, None, ALU.mult,
                      deps=[t12], inc="ve")
        ar16.freeblk(o_st, 8 * 1024)
        if ("kvab" in taps) and mc0 == 8:
            b.dma("sync", tap_t["kvab"], A16[0:16, o_ab:o_ab + 2 * T],
                  deps=[tA, tB], inc="out")
        b.barrier()
        o_t = ar16.alloc(512, "kv ap t")
        tu3 = []
        tev = []
        for j in range(KCH):
            E6 = v16(o_E6 + j * 128, 128, 0, 16)
            for nt in range(4):
                s = nt * 512
                ta = b.mm(psB[:, 0:512], E6, Ar[:, s:s + 512], True, True,
                          deps=[tA] + tu3, inc="mm")
                tb = b.mm(psB[:, 512:1024], E6, Br[:, s:s + 512], True, True,
                          deps=[tB], inc="mm")
                tm = v16(o_t, 512)
                u1 = b.tt(tm, v16(o_kv + j * T + s, 512), psB[:, 0:512], ALU.mult,
                          deps=[ta] + tev, inc="ve")
                u2 = b.tt(tm, tm, psB[:, 512:1024], ALU.add, deps=[u1, tb], inc="ve")
                u3a = b.ts(tm, tm, gcol_fn(j), None, ALU.mult,
                           deps=[u2, tgeff, tw[0]], inc="ve")
                u3 = b.ts(tm, tm, bcol_fn(j), None, ALU.add,
                          deps=[u3a, tbeff], inc="ve")
                tu3 = [u3]
                ttr = None
                for q4 in range(4):
                    ttr = b.trans(psC[:, 64 * q4:64 * (q4 + 1)].bitcast(F16),
                                  tm[:, 128 * q4:128 * (q4 + 1)], ident,
                                  deps=[u3] + tev, inc="mm" if q4 == 3 else None)
                tev = []
                for q4 in range(4):
                    tcn = nt * 4 + q4
                    te = b.copy(A16[:, o_tok + tcn * C + j * 128:
                                    o_tok + tcn * C + j * 128 + 128],
                                psC[:, 64 * q4:64 * (q4 + 1)].bitcast(F16),
                                deps=[ttr], inc="ve")
                    tev = [te]
                b.barrier()
        ar16.freeblk(o_t, 512)
        ar16.freeblk(o_kv, KCH * T)
        ar16.freeblk(o_ab, 2 * T)
        b.barrier()

    kv_group(8, o_ktok, lambda j: v32(o_geff + j, 1),
             lambda j: v32(o_beff + j, 1))
    kv_group(16, o_vtok, lambda j: v32(o_lngb + 2, 1),
             lambda j: v32(o_lngb + 3, 1))
    tap("ktok", A16[:, o_ktok:o_ktok + NT * C], [b.tok("ve")])
    tap("vtok", A16[:, o_vtok:o_vtok + NT * C], [b.tok("ve")])

    o_q = ar16.alloc(KCH * T, "qT")
    for j in range(KCH):
        def evq(tiles, tmm, j=j):
            toks = []
            for nt in range(4):
                te = b.act(v16(o_q + j * T + nt * 512, 512), tiles[nt], AF.Identity,
                           bias=v32(o_qkvb + j, 1), deps=[tmm, tw[2]], inc="se")
                toks.append(te)
            return toks
        mm_weight(d["qkvB"][j], xin_ch, evq)
    b.barrier()
    tap("q", A16[:, o_q:o_q + KCH * T], [b.tok("se")])
    ckpt("S2")

    tkv = None
    for h in range(H_A):
        for tcn in range(NT):
            tkv = b.mm(psA[0:64, h * 64:h * 64 + 64], kt_ch(tcn, h * 64, 64),
                       vt_ch(tcn, h * 64, 64), start=(tcn == 0), stop=(tcn == NT - 1),
                       inc="mm" if (tcn == NT - 1 and h == H_A - 1) else None)
    o_kvp = ar32.alloc(H_A * HD, "kv partials")
    tkevac = b.ts(v32(o_kvp, H_A * HD, 0, 64), psA[0:64, 0:H_A * HD],
                  1.0 / QFULL, None, ALU.mult, deps=[tkv], inc="ve")
    tout = b.dma("gpsimd", E["kv_in"][:, :], v32(o_kvp, H_A * HD, 0, 64),
                 deps=[tkevac], inc="dd")
    tcc = b.coll("AllReduce", ALU.add, PAIRS, [E["kv_in"][:, :]],
                 [E["kv_out"][:, :]], deps=[tout], inc="cc")
    tin = b.dma("gpsimd", v32(o_kvp, H_A * HD, 0, 64), E["kv_out"][:, :],
                deps=[tcc], inc="dd")
    tfin = b.tt(v32(o_kvp, H_A * HD, 0, 64), v32(o_kvp, H_A * HD, 0, 64),
                v32(o_abT, H_A * HD, 0, 64), ALU.add, deps=[tin, tw[5]], inc="ve")
    tap("kvf", A32[0:64, o_kvp:o_kvp + H_A * HD], [tfin])
    o_kvbd = ar16.alloc(8 * 128, "kv bd")
    tms = b.memset(v16(o_kvbd, 8 * 128), 0.0, deps=[], inc="gp")
    tbd = []
    for hp in range(8):
        base = o_kvbd + hp * 128
        b1 = b.copy(v16(base, 64, 0, 64), v32(o_kvp + 2 * hp * 64, 64, 0, 64),
                    deps=[tfin, tms], inc="ve")
        b2 = b.copy(v16(base + 64, 64, 64, 64),
                    v32(o_kvp + (2 * hp + 1) * 64, 64, 0, 64),
                    deps=[tfin, tms], inc="ve")
        tbd += [b1, b2]
    b.barrier()
    ar16.freeblk(o_ktok, NT * C)
    ar16.freeblk(o_vtok, NT * C)
    ar32.freeblk(o_kvp, H_A * HD)
    ckpt("S3")

    o_att = ar16.alloc(KCH * T, "att")
    tiles4 = [psB[:, 0:512], psB[:, 512:1024], psC[:, :], psD[:, :]]
    ttok4 = [None] * 4
    for hp in range(8):
        tmm = None
        for nt in range(4):
            tmm = b.mm(tiles4[nt], v16(o_kvbd + hp * 128, 128),
                       v16(o_q + hp * T + nt * 512, 512), True, True,
                       deps=[ttok4[nt]] + tbd, inc="mm" if nt == 3 else None)
        for nt in range(4):
            te = b.tt(v16(o_att + hp * T + nt * 512, 512), tiles4[nt],
                      xin_ch(hp, nt * 512, 512), ALU.add, deps=[tmm], inc="ve")
            ttok4[nt] = te
    b.barrier()
    ar16.freeblk(o_q, KCH * T)
    ar16.freeblk(o_kvbd, 8 * 128)
    tap("att", A16[:, o_att:o_att + KCH * T], [b.tok("ve")])
    ckpt("S4")

    o_h1 = ar16.alloc(KCH * T, "h1")
    att_ch = lambda k, c0, w: v16(o_att + k * T + c0, w)
    for mc in range(KCH):
        def ev1(tiles, tmm, mc=mc):
            toks = []
            for nt in range(4):
                te = b.act(v16(o_h1 + mc * T + nt * 512, 512), tiles[nt], AF.Gelu,
                           bias=v32(o_p1b + mc, 1), deps=[tmm, tw[3]], inc="se")
                toks.append(te)
            return toks
        mm_weight(d["p1B"][mc], att_ch, ev1)
    b.barrier()
    ar16.freeblk(o_att, KCH * T)

    h1_ch = lambda k, c0, w: v16(o_h1 + k * T + c0, w)
    o_xs = ar32.alloc(2 * 512, "x slc")
    o_x1s = ar32.alloc(2 * 512, "x1 slc")
    txs = [None, None]
    for mc in range(KCH):
        def ev2(tiles, tmm, mc=mc):
            toks = []
            for nt in range(4):
                par = nt % 2
                s = nt * 512
                tdx = b.dma("sync", v32(o_xs + par * 512, 512),
                            d["xT"][128 * mc:128 * (mc + 1), s:s + 512],
                            deps=[txs[par]], inc="x0" if par == 0 else "x1")
                ta = b.act(v32(o_x1s + par * 512, 512), tiles[nt], AF.Identity,
                           bias=v32(o_p2b + mc, 1), deps=[tmm, tw[4], txs[par]],
                           inc="se")
                tb = b.tt(v32(o_x1s + par * 512, 512), v32(o_x1s + par * 512, 512),
                          xin_ch(mc, s, 512), ALU.add, deps=[ta], inc="ve")
                tc2 = b.tt(v32(o_x1s + par * 512, 512), v32(o_x1s + par * 512, 512),
                           v32(o_xs + par * 512, 512), ALU.add, deps=[tb, tdx],
                           inc="ve")
                td = b.dma("gpsimd", E["x1d"][128 * mc:128 * (mc + 1), s:s + 512],
                           v32(o_x1s + par * 512, 512), deps=[tc2],
                           inc="d0" if par == 0 else "d1")
                txs[par] = td
                toks.append(tc2)
            return toks
        mm_weight(d["p2B"][mc], h1_ch, ev2)
    b.barrier()
    ar16.freeblk(o_h1, KCH * T)
    ar32.freeblk(o_xs, 2 * 512)
    ar32.freeblk(o_x1s, 2 * 512)
    ar16.freeblk(o_xin, KCH * T)
    tap("x1", E["x1d"][:, :], [b.tok("d0"), b.tok("d1")])
    ckpt("S6")

    o_xin1 = ar16.alloc(KCH * T, "xin1")
    o_wo = ar16.alloc(20 * 1024, "wo2 full")
    emit_ln(E["x1d"], o_xin1)
    xin1_ch = lambda k, c0, w: v16(o_xin1 + k * T + c0, w)

    o_E3 = ar16.alloc(20 * 128, "E3")
    o_E7 = ar16.alloc(48, "E7")
    b.memset(v16(o_E3, 20 * 128), 1.0)
    for cc in range(20):
        sub = v16(o_E3 + cc * 128, 128, 0, 40)
        b.affine(sub, sub, [[1, 128]], ALU.is_ge, 0.0, base=128 * cc, chan=-64)
        b.affine(sub, sub, [[-1, 128]], ALU.is_ge, 0.0, base=63 - 128 * cc, chan=64)
    b.memset(v16(o_E7, 48), 1.0)
    sub = v16(o_E7, 40, 0, 8)
    b.affine(sub, sub, [[1, 40]], ALU.is_ge, 0.0, base=0, chan=-5)
    b.affine(sub, sub, [[-1, 40]], ALU.is_ge, 0.0, base=4, chan=5)

    o_wsd = ar16.alloc(KCH * 80, "wsd")
    o_wsd32 = ar32.alloc(128, "wsd stg")
    twsd = []
    tpv = None
    for kc in range(KCH):
        tdw = b.dma("sync", v32(o_wsd32, 80), d["wsdB"][kc], deps=[tpv], inc="x")
        tpv = b.copy(v16(o_wsd + kc * 80, 80), v32(o_wsd32, 80), deps=[tdw], inc="ve")
        twsd.append(tpv)
    o_gate = ar32.alloc(NT * 80, "gates")
    tg = None
    tsg = None
    for tcn in range(NT):
        for kc in range(KCH):
            tg = b.mm(psC[:, 0:80], xin1_ch(kc, tcn * 128, 128),
                      v16(o_wsd + kc * 80, 80), start=(kc == 0), stop=(kc == KCH - 1),
                      deps=[twsd[kc], tsg], inc="mm" if kc == KCH - 1 else None)
        tsg = b.act(v32(o_gate + tcn * 80, 80), psC[:, 0:80], AF.Sigmoid,
                    deps=[tg], inc="se")
    b.barrier()
    ar16.freeblk(o_wsd, KCH * 80)
    ar32.freeblk(o_wsd32, 128)
    tap("gates", A32[:, o_gate:o_gate + NT * 80], [b.tok("se")])

    NM = NT * 8
    o_med = ar32.alloc(8 * NM, "med")
    o_msk = ar16.alloc(2 * NT * 40, "masks")
    o_sg = ar32.alloc(2 * NT * 40, "gated")
    gateR = A32[:, o_gate:o_gate + NT * 80].rearrange(
        "p (tcn hl h e) -> p tcn hl h e", tcn=NT, hl=2, h=8, e=5)
    tmask = []
    for hl in range(2):
        ex = [gateR[:, :, hl, :, e] for e in range(5)]
        m = [v32(o_med + i * NM, NM).rearrange("p (tcn h) -> p tcn h", tcn=NT, h=8)
             for i in range(8)]
        d0 = [b.tok("se")]
        u = []
        u.append(b.tt(m[0], ex[0], ex[1], ALU.min, deps=d0, inc="ve"))
        u.append(b.tt(m[1], ex[0], ex[1], ALU.max, deps=d0, inc="ve"))
        u.append(b.tt(m[2], ex[2], ex[3], ALU.min, deps=d0, inc="ve"))
        u.append(b.tt(m[3], ex[2], ex[3], ALU.max, deps=d0, inc="ve"))
        f_ = b.tt(m[4], m[0], m[2], ALU.max, deps=u[:3], inc="ve")
        g_ = b.tt(m[5], m[1], m[3], ALU.min, deps=u, inc="ve")
        q1 = b.tt(m[6], ex[4], m[4], ALU.min, deps=[f_], inc="ve")
        q2 = b.tt(m[7], ex[4], m[4], ALU.max, deps=[f_], inc="ve")
        q3 = b.tt(m[7], m[7], m[5], ALU.min, deps=[q2, g_], inc="ve")
        tmed = b.tt(m[0], m[6], m[7], ALU.max, deps=[q1, q3], inc="ve")
        mskR = A16[:, o_msk + hl * NT * 40:o_msk + (hl + 1) * NT * 40].rearrange(
            "p (tcn h e) -> p tcn h e", tcn=NT, h=8, e=5)
        sgR = A32[:, o_sg + hl * NT * 40:o_sg + (hl + 1) * NT * 40].rearrange(
            "p (tcn h e) -> p tcn h e", tcn=NT, h=8, e=5)
        tms_ = []
        for e in range(5):
            tms_.append(b.tt(mskR[:, :, :, e], ex[e], m[0], ALU.is_ge,
                             deps=[tmed], inc="ve"))
        tgt = b.tt(sgR, gateR[:, :, hl, :, :], mskR, ALU.mult, deps=tms_, inc="ve")
        tmask.append(tgt)
    ar32.freeblk(o_med, 8 * NM)
    ar32.freeblk(o_gate, NT * 80)
    b.barrier()
    tap("md", A16[:, o_msk + NT * 40:o_msk + 2 * NT * 40], [b.tok("ve")])
    tap("sdg", A32[:, o_sg + NT * 40:o_sg + 2 * NT * 40], [b.tok("ve")])
    ckpt("S8")

    def msk_col(hl, tcn, h, e):
        return v16(o_msk + hl * NT * 40 + tcn * 40 + h * 5 + e, 1)

    def sg_col(hl, tcn, h, e):
        return v32(o_sg + hl * NT * 40 + tcn * 40 + h * 5 + e, 1)

    o_ktf = ar16.alloc(NT * 512, "ktokF")
    o_vex = ar16.alloc(NT * 520, "vext")
    o_wg = ar16.alloc(2 * KCH * 640, "w group")
    o_wg32 = ar32.alloc(2 * 640, "wg stage")
    o_g5 = ar16.alloc(NT * 640, "group out")
    twg_par = [None, None]
    o_kT = ar16.alloc(T, "kT")
    tsend = []

    tones = []
    for tcn in range(NT):
        vv = A16[:, o_vex + tcn * 520:o_vex + (tcn + 1) * 520].rearrange(
            "p (hp c) -> p hp c", hp=4, c=130)
        b.memset(vv[:, :, 64:65], 1.0)
        tones.append(b.memset(vv[:, :, 129:130], 1.0, inc="gp"))

    wg_stage_guard = [None, None]
    wg_ptok = [None, None]
    wg_glast = [None]
    wg_slot_guard = [None] * NT

    def wgroup_mm(blkB, og, wpar, dst_emit, gated_dep):
        twl = []
        for kc in range(KCH):
            s32 = v32(o_wg32 + (kc % 2) * 640, 640)
            tdw = b.dma("sync", s32, blkB[kc],
                        deps=[twl[-2] if len(twl) >= 2 else
                              wg_stage_guard[kc % 2], twg_par[wpar]],
                        inc="wg0" if kc % 2 == 0 else "wg1")
            twl.append(b.copy(v16(o_wg + wpar * KCH * 640 + kc * 640, 640), s32,
                              deps=[tdw], inc="ve"))
        wg_stage_guard[0] = twl[-2]
        wg_stage_guard[1] = twl[-1]
        ptiles = [(psB[:, 0:512], psB[:, 512:640]), (psA[:, 0:512], psA[:, 512:640])]
        ptok = wg_ptok
        tglast = wg_glast[0]
        for tcn in range(NT):
            pp = tcn % 2
            tmm = None
            for kc in range(KCH):
                wap = v16(o_wg + wpar * KCH * 640 + kc * 640, 640)
                b.mm(ptiles[pp][0], xin1_ch(kc, tcn * 128, 128), wap[:, 0:512],
                     start=(kc == 0), stop=(kc == KCH - 1),
                     deps=[twl[kc], ptok[pp]])
                tmm = b.mm(ptiles[pp][1], xin1_ch(kc, tcn * 128, 128),
                           wap[:, 512:640], start=(kc == 0), stop=(kc == KCH - 1),
                           inc="mm" if kc == KCH - 1 else None)
            gslot = o_g5 + tcn * 640
            te1 = b.act(v16(gslot, 512), ptiles[pp][0], AF.Identity,
                        deps=[tmm, wg_slot_guard[tcn]], inc="se")
            te2 = b.act(v16(gslot + 512, 128), ptiles[pp][1], AF.Identity,
                        deps=[tmm, wg_slot_guard[tcn]], inc="se")
            ptok[pp] = te2
            tglast = dst_emit(og, tcn, [te1, te2, gated_dep])
            wg_slot_guard[tcn] = tglast
        wg_glast[0] = tglast
        twg_par[wpar] = tglast

    o_gt4 = ar16.alloc(2 * 4 * 64, "gate tmp")
    _gprev = [None, None]

    def gate_sum(dst, src_e, scol, deps, lane):
        eng = "vector"
        sem = "ve"
        base = o_gt4 + lane * 256
        deps = deps + [_gprev[lane]]
        p0 = b.ts(dst, src_e(0), scol(0), None, ALU.mult, deps=deps, inc=sem,
                  eng=eng)
        pt = []
        for e in range(1, 5):
            pt.append(b.ts(v16(base + (e - 1) * 64, 64), src_e(e), scol(e),
                           None, ALU.mult, deps=deps, inc=sem, eng=eng))
        a1 = b.tt(v16(base, 64), v16(base, 64), v16(base + 64, 64), ALU.add,
                  deps=pt[:2], inc=sem, eng=eng)
        a2 = b.tt(v16(base + 128, 64), v16(base + 128, 64),
                  v16(base + 192, 64), ALU.add, deps=pt[2:], inc=sem, eng=eng)
        a3 = b.tt(v16(base, 64), v16(base, 64), v16(base + 128, 64), ALU.add,
                  deps=[a1, a2], inc=sem, eng=eng)
        t = b.tt(dst, dst, v16(base, 64), ALU.add, deps=[p0, a3], inc=sem,
                 eng=eng)
        _gprev[lane] = t
        return t

    def gate_kv(og, tcn, deps):
        kv = og // 4
        hp = og % 4
        tl = []
        for hh in range(2):
            h = 2 * hp + hh
            if kv == 0:
                dst = v16(o_ktf + tcn * 512 + hp * 128 + hh * 64, 64)
            else:
                dst = v16(o_vex + tcn * 520 + hp * 130 + (65 if hh else 0), 64)
            src_e = lambda e: v16(o_g5 + tcn * 640 + hh * 320 + e * 64, 64)
            tl.append(gate_sum(dst, src_e, lambda e: sg_col(0, tcn, h, e),
                                deps + [tmask[0]], hh))
        return tl

    for og in range(4):
        wgroup_mm(d["wkvB"][og], og, og % 2, gate_kv, None)

    ttr = None
    tevk = []
    tsendk = []
    for cc4 in range(4):
        for tcn in range(NT):
            pv = (psC if tcn % 2 == 0 else psD)[:, 0:64].bitcast(F16)
            ttr = b.trans(pv, v16(o_ktf + tcn * 512 + cc4 * 128, 128), ident,
                          deps=tevk[-1:] + [wg_slot_guard[tcn], wg_ptok[0],
                                            wg_ptok[1]], inc="mm")
            te = b.copy(v16(o_kT + tcn * 128, 128), pv,
                        deps=[ttr] + tsendk[-1:], inc="ve")
            tevk.append(te)
        tsendk.append(b.dma("gpsimd",
                            E["agk_in"][cc4 * 128 * T:(cc4 + 1) * 128 * T]
                            .rearrange("(p c) -> p c", c=T),
                            v16(o_kT, T), deps=[tevk[-1]], inc="dd"))
    tag_k = b.coll("AllGather", ALU.bypass, PAIRS,
                   [E["agk_in"][:]], [E["agk_out"][:]], deps=tsendk, inc="cc")

    for og in range(4, 8):
        wgroup_mm(d["wkvB"][og], og, og % 2, gate_kv, None)
    b.barrier()
    for tcn in range(NT):
        tsend.append(b.dma("gpsimd",
                           E["agv_in"][tcn * 128 * 520:(tcn + 1) * 128 * 520]
                           .rearrange("(p c) -> p c", c=520),
                           A16[:, o_vex + tcn * 520:o_vex + (tcn + 1) * 520],
                           deps=[tones[tcn], b.tok("ve")], inc="dd"))
    tag = b.coll("AllGather", ALU.bypass, PAIRS,
                 [E["agv_in"][:]], [E["agv_out"][0:2 * 2048 * 520]],
                 deps=tsend, inc="cc")
    tap("kT", A16[:, o_kT:o_kT + T], [b.tok("ve")])
    tap("vext", A16[:, o_vex:o_vex + NT * 520], [b.tok("ve")])
    ckpt("S9")
    ar16.freeblk(o_kT, T)
    ar16.freeblk(o_ktf, NT * 512)

    o_qtok = ar16.alloc(NT * 512, "q tok")

    def gate_q(og, tcn, deps):
        tl = []
        for hh in range(2):
            h = 2 * og + hh
            dst = v16(o_qtok + tcn * 512 + og * 128 + hh * 64, 64)
            src_e = lambda e: v16(o_g5 + tcn * 640 + hh * 320 + e * 64, 64)
            tl.append(gate_sum(dst, src_e, lambda e: sg_col(1, tcn, h, e),
                                deps + [tmask[1]], hh))
        return tl

    for og in range(4):
        wgroup_mm(d["wqB"][og], og, og % 2, gate_q, None)
    b.barrier()
    ar16.freeblk(o_wg, 2 * KCH * 640)
    ar16.freeblk(o_g5, NT * 640)
    ar32.freeblk(o_wg32, 2 * 640)
    o_qT = ar16.alloc(4 * T, "qT ffn")
    tevq = []
    for cc4 in range(4):
        for tcn in range(NT):
            pv = (psC if tcn % 2 == 0 else psD)[:, 0:64].bitcast(F16)
            ttr = b.trans(pv, v16(o_qtok + tcn * 512 + cc4 * 128, 128), ident,
                          deps=tevq[-1:], inc="mm")
            te = b.copy(v16(o_qT + cc4 * T + tcn * 128, 128), pv,
                        deps=[ttr], inc="ve")
            tevq.append(te)
    ar16.freeblk(o_qtok, NT * 512)
    tap("qf", A16[:, o_qT:o_qT + 4 * T], [b.tok("ve")])
    ckpt("S10")

    o_wo32 = ar32.alloc(2 * 128, "wo2 stg")
    tpw = [None, None]
    two = []
    for cc in range(20):
        for mc2 in range(KCH):
            par = (cc * KCH + mc2) % 2
            s32 = v32(o_wo32 + par * 128, 128)
            tdw = b.dma("sync", s32, d["wo2B"][cc, mc2], deps=[tpw[par]],
                        inc="w0" if par == 0 else "w1")
            tpw[par] = b.copy(v16(o_wo + cc * 1024 + mc2 * 128, 128), s32,
                              deps=[tdw], inc="ve")
            two.append(tpw[par])
    ar32.freeblk(o_wo32, 2 * 128)
    ar16.freeblk(o_xin1, KCH * T)
    b.barrier()

    o_khp = ar16.alloc(2 * QFULL, "k hp")
    o_vhp = ar16.alloc(2 * 32 * 130, "v hp")
    o_P = ar16.alloc(3 * 1024, "P bufs")
    o_avT = ar16.alloc(4 * T, "avT")
    o_sst = ar32.alloc(1024, "sums stage")
    tpre = [None, None]

    def prefetch_hp(hp, pp, deps):
        toks = []
        for sl in range(2):
            base = sl * 512 * 2048 + hp * 128 * 2048
            toks.append(b.dma(
                "gpsimd",
                v16(o_khp + pp * QFULL + sl * (QFULL // 2), QFULL // 2),
                E["agk_out"][base:base + 128 * 2048].rearrange("(p c) -> p c", c=2048),
                deps=deps, inc="kh0" if pp == 0 else "kh1"))
        for sl in range(2):
            for ch in range(16):
                src0 = sl * 2048 * 520 + ch * 128 * 520
                toks.append(b.dma(
                    "gpsimd",
                    v16(o_vhp + pp * 32 * 130 + (sl * 16 + ch) * 130, 130),
                    E["agv_out"][src0:src0 + 128 * 520]
                    .rearrange("(p c) -> p c", c=520)[:, hp * 130:hp * 130 + 130],
                    deps=deps, inc="kh0" if pp == 0 else "kh1"))
        return toks

    tpre[0] = prefetch_hp(0, 0, [tag_k, tag])
    o_sums_rd = ar32.alloc(T, "sums rd")
    tev_av = []
    tsum_dma = []
    for hp in range(4):
        pp = hp % 2
        if hp < 3:
            tpre[(hp + 1) % 2] = prefetch_hp(hp + 1, (hp + 1) % 2,
                                             [tag] + tev_av[-1:])
        kdeps = tpre[pp]
        for tc4 in range(4):
            ts_ = tc4 * 512
            texp = [None, None, None]
            tavmm_c = [None]
            tav_par = [None, None, None]
            tSmm = [None, None, None]
            SPAIR = [(psB[:, 0:512], psB[:, 512:1024], psB[:, 0:1024]),
                     (psE[:, 0:512], psE[:, 512:1024], psE[:, 0:1024]),
                     (psA[:, 1024:1536], psA[:, 1536:2048], psA[:, 1024:2048])]
            def emit_S(jc):
                par = jc % 3
                So = SPAIR[par]
                sdeps = kdeps + [texp[par]] + (tev_av if jc < 3 else [])
                b.mm(So[0], v16(o_khp + pp * QFULL + jc * 128, 128, 0, 64),
                     v16(o_qT + hp * T + ts_, 512, 0, 64), True, True,
                     deps=sdeps, tp=(0, 0))
                tSmm[par] = b.mm(So[1],
                                 v16(o_khp + pp * QFULL + jc * 128, 128, 64, 64),
                                 v16(o_qT + hp * T + ts_, 512, 64, 64), True, True,
                                 tp=(64, 0), inc="mm")
                texp[par] = b.act(v16(o_P + par * 1024, 1024), So[2], AF.Exp,
                                  scale=SCALE_F, deps=[tSmm[par], tav_par[par]],
                                  inc="se")

            def emit_AV(jc):
                par = jc % 3
                vch = v16(o_vhp + pp * 32 * 130 + jc * 130, 130)
                b.mm(psA[0:65, 0:512], vch[:, 0:65], v16(o_P + par * 1024, 512),
                     start=(jc == 0), stop=(jc == 31),
                     deps=[texp[par]] + (tev_av if jc == 0 else []))
                tavmm_c[0] = b.mm(psA[0:65, 512:1024], vch[:, 65:130],
                                  v16(o_P + par * 1024 + 512, 512),
                                  start=(jc == 0), stop=(jc == 31), inc="mm")
                tav_par[par] = tavmm_c[0]

            emit_S(0)
            emit_S(1)
            for jc in range(32):
                if jc + 2 < 32:
                    emit_S(jc + 2)
                emit_AV(jc)
            tavmm = tavmm_c[0]
            e1 = b.copy(v16(o_avT + hp * T + ts_, 512, 0, 64), psA[0:64, 0:512],
                        deps=[tavmm], inc="ve")
            e2 = b.copy(v16(o_avT + hp * T + ts_, 512, 64, 64),
                        psA[0:64, 512:1024], deps=[tavmm], inc="ve")
            s1 = b.copy(v32(o_sst, 512, 64, 1), psA[64:65, 0:512],
                        deps=[tavmm] + tsum_dma, inc="ve")
            s2 = b.copy(v32(o_sst + 512, 512, 64, 1), psA[64:65, 512:1024],
                        deps=[tavmm] + tsum_dma, inc="ve")
            tsum_dma = [
                b.dma("gpsimd", E["sums_d"][2 * hp:2 * hp + 1, ts_:ts_ + 512],
                      v32(o_sst, 512, 64, 1), deps=[s1], inc="d0"),
                b.dma("gpsimd", E["sums_d"][2 * hp + 1:2 * hp + 2, ts_:ts_ + 512],
                      v32(o_sst + 512, 512, 64, 1), deps=[s2], inc="d1"),
            ]
            tev_av = [e1, e2, s1, s2]
    b.barrier()
    ar16.freeblk(o_khp, 2 * QFULL)
    ar16.freeblk(o_vhp, 2 * 32 * 130)
    ar16.freeblk(o_P, 3 * 1024)
    ar32.freeblk(o_sst, 1024)
    o_E4 = ar16.alloc(40 * 128, "E4")
    b.memset(v16(o_E4, 40 * 128), 0.0)
    for g in range(40):
        h = g // 5
        s0 = (h % 2) * 64
        p0 = (g % 2) * 64
        sub = v16(o_E4 + g * 128 + p0, 64, s0, 64)
        b.affine(sub, sub, [[-1, 64]], ALU.not_equal, 1.0, base=0, chan=1)
    tap("avT", A16[:, o_avT:o_avT + 4 * T], [b.tok("ve")])
    ckpt("S11")

    trd = b.dma("sync", v32(o_sums_rd, T, 0, 8), E["sums_d"][:, :],
                deps=[b.tok("d0"), b.tok("d1")], inc="x")
    o_rs = ar16.alloc(T, "recip16")
    tr1 = b.recip(v32(o_sums_rd, T, 0, 8), v32(o_sums_rd, T, 0, 8),
                  deps=[trd], inc="ve")
    tr2 = b.copy(v16(o_rs, T, 0, 8), v32(o_sums_rd, T, 0, 8), deps=[tr1], inc="ve")
    ar32.freeblk(o_sums_rd, T)
    o_mdT = ar16.alloc(T, "mdT")
    tmdT = []
    for tcn in range(NT):
        pv = psD[0:40, 0:64].bitcast(F16)
        tt_ = b.trans(pv, v16(o_msk + NT * 40 + tcn * 40, 40), ident,
                      deps=tmdT[-1:], inc="mm")
        tmdT.append(b.copy(v16(o_mdT + tcn * 128, 128, 0, 40), pv,
                           deps=[tt_], inc="ve"))
    o_gT = ar16.alloc(T, "gT")
    tgT = []
    for tc4 in range(4):
        s = tc4 * 512
        tm1 = b.mm(psC[0:40, :], v16(o_E7, 40, 0, 8), v16(o_rs + s, 512, 0, 8),
                   True, True, deps=[tr2, tsetup] + tmdT[-1:] + tgT[-1:], inc="mm")
        tgT.append(b.tt(v16(o_gT + s, 512, 0, 40), v16(o_mdT + s, 512, 0, 40),
                        psC[0:40, :], ALU.mult, deps=[tm1] + tmdT, inc="ve"))
    tap("gT", A16[0:40, o_gT:o_gT + T], [b.tok("ve")])
    b.barrier()
    ar16.freeblk(o_mdT, T)
    ar16.freeblk(o_rs, T)

    o_om = ar16.alloc(20 * 512, "OM")
    o_gbc = ar16.alloc(512, "gbc sbuf")
    o_x1s2 = ar32.alloc(2 * 512, "x1 slc f")
    o_yout = ar32.alloc(2 * 512, "y out")
    tx1 = [None, None]
    ty = [None, None]
    tomlast = []
    n_out = 0
    for tc4 in range(4):
        s = tc4 * 512
        tgb_prev = None
        tom = []
        for cc in range(20):
            tgb = b.mm(psD[:, :], v16(o_E3 + cc * 128, 128, 0, 40),
                       v16(o_gT + s, 512, 0, 40), True, True,
                       deps=tgT + [tgb_prev], inc="mm")
            tgbe = b.act(v16(o_gbc, 512), psD[:, :], AF.Identity,
                         deps=[tgb, tom[-1:][0] if tom else None], inc="se")
            tgb_prev = tgbe
            tab = None
            for half in range(2):
                gidx = 2 * cc + half
                hh = gidx // 5
                hp = hh // 2
                tab = b.mm(psC[:, :], v16(o_E4 + gidx * 128, 128),
                           v16(o_avT + hp * T + s, 512),
                           start=(half == 0), stop=(half == 1),
                           deps=[tom[-1] if tom else None] if half == 0 else None,
                           inc="mm" if half == 1 else None)
            tom.append(b.tt(v16(o_om + cc * 512, 512), v16(o_gbc, 512),
                            psC[:, :], ALU.mult, deps=[tgbe, tab], inc="ve"))
        for mc in range(KCH):
            par = mc % 2
            tdx = b.dma("sync", v32(o_x1s2 + par * 512, 512),
                        E["x1d"][128 * mc:128 * (mc + 1), s:s + 512],
                        deps=[ty[par]], inc="x0" if par == 0 else "x1")
            ptile = psB[:, 0:512] if par == 0 else psB[:, 512:1024]
            tmm = None
            for cc in range(20):
                tmm = b.mm(ptile, v16(o_wo + cc * 1024 + mc * 128, 128),
                           v16(o_om + cc * 512, 512),
                           start=(cc == 0), stop=(cc == 19),
                           deps=[tom[cc], ty[par]] + two[-1:],
                           inc="mm" if cc == 19 else None)
            tyv = b.tt(v32(o_yout + par * 512, 512), ptile,
                       v32(o_x1s2 + par * 512, 512), ALU.add,
                       deps=[tmm, tdx], inc="ve")
            ty[par] = b.dma("gpsimd", E["y"][128 * mc:128 * (mc + 1), s:s + 512],
                            v32(o_yout + par * 512, 512), deps=[tyv], inc="out")
            n_out += 1
        tomlast = tom
        if "om0" in taps and tc4 == 0:
            tap("om0", v16(o_om, 512), [tomlast[0]])
    b.final_wait([b.tok("out")])
    return ar32.peak, ar16.peak


TAP_SHAPES = {
    "xin": ([C, T], F16), "q": ([C, T], F16),
    "ktok": ([128, NT * C], F16), "vtok": ([128, NT * C], F16),
    "kvf": ([64, H_A * HD], F32), "att": ([C, T], F16),
    "x1": ([C, T], F32), "gates": ([128, NT * 80], F32),
    "sdg": ([128, NT * 40], F32), "md": ([128, NT * 40], F16),
    "kT": ([128, T], F16), "vext": ([128, NT * 520], F16),
    "qf": ([128, 4 * T], F16), "avT": ([128, 4 * T], F16),
    "gT": ([40, T], F16), "om0": ([128, 512], F16),
    "lnab": ([1, 2 * T], F16),
    "kvab": ([16, 2 * T], F16),
    "kvsums": ([64, T], F32),
    "lnsums": ([2, T], F32),
}


def build_nc(taps=(), stop=None):
    nc = bass.Bass("TRN2", target_bir_lowering=False, debug=False)
    es = ExitStack()
    d = {}
    for name, shape in [
        ("xT", [C, T]), ("qkvB", [24, KCH, 128, 128]), ("qkvb", [128, 24]),
        ("p1B", [KCH, KCH, 128, 128]), ("p1b", [128, KCH]),
        ("p2B", [KCH, KCH, 128, 128]), ("p2b", [128, KCH]),
        ("lngb", [128, 4]), ("tmpc", [128, KCH]), ("abT", [64, H_A * HD]),
        ("wkvB", [8, KCH, 128, 640]), ("wqB", [4, KCH, 128, 640]),
        ("wsdB", [KCH, 128, 80]), ("wo2B", [20, KCH, 128, 128]),
    ]:
        d[name] = nc.dram_tensor(name, shape, F32, kind="ExternalInput").ap()

    E = {"d": d}
    E["y"] = nc.dram_tensor("y", [C, T], F32, kind="ExternalOutput").ap()
    E["x1d"] = nc.dram_tensor("x1d", [C, T], F32).ap()
    E["agk_in"] = nc.dram_tensor("agk_in", [512 * 2048], F16).ap()
    E["agk_out"] = nc.dram_tensor("agk_out", [2 * 512 * 2048], F16).ap()
    E["agv_in"] = nc.dram_tensor("agv_in", [2048 * 520], F16).ap()
    E["agv_out"] = nc.dram_tensor("agv_out", [2 * 2048 * 520 + 2048], F16).ap()
    E["kv_in"] = nc.dram_tensor("kv_in", [64, H_A * HD], F32).ap()
    E["kv_out"] = nc.dram_tensor("kv_out", [64, H_A * HD], F32).ap()
    E["sums_d"] = nc.dram_tensor("sums_d", [8, T], F32).ap()
    E["tap_t"] = {}
    for t in taps:
        shp, dt_ = TAP_SHAPES[t]
        E["tap_t"][t] = nc.dram_tensor("tap_" + t, list(shp), dt_,
                                       kind="ExternalOutput").ap()

    E["A32"] = es.enter_context(nc.sbuf_tensor("af32", [128, NF32], F32))
    E["A16"] = es.enter_context(nc.sbuf_tensor("af16", [128, NF16], F16))
    E["psA"] = es.enter_context(nc.psum_tensor("psA", [128, 2048], F32))
    E["psB"] = es.enter_context(nc.psum_tensor("psB", [128, 1024], F32))
    E["psE"] = es.enter_context(nc.psum_tensor("psE", [128, 1024], F32))
    E["psC"] = E["psE"][:, 0:512]
    E["psD"] = E["psE"][:, 512:1024]
    sems = {s: es.enter_context(nc.semaphore(s)) for s in SEMS}

    ref_cnt = {}
    with nc.Block() as blk:
        def mk(role):
            def f(eng):
                cnt = {s: 0 for s in SEMS}
                bb = B(nc, role, sems, cnt)
                try:
                    _program(bb, E, set(taps), stop=stop)
                except Exception as ex:
                    if type(ex).__name__ != "_Stop":
                        raise
                if not ref_cnt:
                    ref_cnt.update(cnt)
                else:
                    assert ref_cnt == cnt, (role, ref_cnt, cnt)
            return f
        blk.tensor(mk("tensor"))
        blk.vector(mk("vector"))
        blk.scalar(mk("scalar"))
        blk.gpsimd(mk("gpsimd"))
        blk.sync(mk("sync"))
    es.close()
    return nc


def layout_inputs(inputs):
    f = lambda a: np.ascontiguousarray(np.asarray(a, dtype=np.float32))
    x = f(inputs["L0_attn_x"])
    qkv_w = f(inputs["L0_attn_qkv_w"])
    qkv_b = f(inputs["L0_attn_qkv_b"])
    p1_w, p1_b = f(inputs["L0_attn_p1_w"]), f(inputs["L0_attn_p1_b"])
    p2_w, p2_b = f(inputs["L0_attn_p2_w"]), f(inputs["L0_attn_p2_b"])
    kn_g, kn_b = f(inputs["L0_attn_kn_g"]), f(inputs["L0_attn_kn_b"])
    vn_g, vn_b = f(inputs["L0_attn_vn_g"]), f(inputs["L0_attn_vn_b"])
    temp = f(inputs["L0_attn_temperature"]).reshape(H_A)
    ab = f(inputs["L0_attn_attn_bias"])
    Wq, Wkv = f(inputs["L0_ffn_Wq"]), f(inputs["L0_ffn_Wkv"])
    Ws, Wd, Wo = f(inputs["L0_ffn_Ws"]), f(inputs["L0_ffn_Wd"]), f(inputs["L0_ffn_Wo"])

    blk = lambda w, mcs: np.ascontiguousarray(
        w.reshape(KCH, 128, mcs, 128).transpose(2, 0, 1, 3))
    shared = {
        "qkvB": blk(qkv_w.T, 24),
        "qkvb": np.ascontiguousarray(qkv_b.reshape(24, 128).T),
        "p1B": blk(p1_w.T, KCH),
        "p1b": np.ascontiguousarray(p1_b.reshape(KCH, 128).T),
        "p2B": blk(p2_w.T, KCH),
        "p2b": np.ascontiguousarray(p2_b.reshape(KCH, 128).T),
        "lngb": np.ascontiguousarray(np.stack(
            [np.tile(kn_g, 2), np.tile(kn_b, 2), np.tile(vn_g, 2),
             np.tile(vn_b, 2)], axis=1)),
        "tmpc": np.ascontiguousarray(np.repeat(temp, HD).reshape(KCH, 128).T),
        "abT": np.ascontiguousarray(ab.transpose(1, 0, 2).reshape(64, H_A * HD)),
        "wkvB": np.ascontiguousarray(
            Wkv.reshape(KCH, 128, 8, 640).transpose(2, 0, 1, 3)),
        "wqB": np.ascontiguousarray(
            Wq.reshape(KCH, 128, 4, 640).transpose(2, 0, 1, 3)),
        "wsdB": np.ascontiguousarray(
            np.concatenate([Ws, Wd], axis=1).reshape(KCH, 128, 80)),
        "wo2B": np.ascontiguousarray(
            np.tile(Wo.reshape(64, 5, 1024).transpose(1, 0, 2).reshape(320, 1024),
                    (8, 1)).reshape(20, 128, KCH, 128).transpose(0, 2, 1, 3)),
    }
    in_maps = []
    for c in range(8):
        bq, half = c // 2, c % 2
        m = dict(shared)
        m["xT"] = np.ascontiguousarray(x[bq, half * T:(half + 1) * T, :].T)
        in_maps.append(m)
    return in_maps


def kernel(**inputs):
    in_maps = layout_inputs(inputs)
    nc = build_nc()
    res = run_bass_kernel_spmd(nc, in_maps, core_ids=list(range(8)))
    out = np.empty((4, QFULL, C), dtype=np.float32)
    for c in range(8):
        bq, half = c // 2, c % 2
        out[bq, half * T:(half + 1) * T, :] = res.results[c]["y"].T
    return out


# version 06
# speedup vs baseline: 1.4070x; 1.0348x over previous
from contextlib import ExitStack

import numpy as np
import concourse.bass as bass
import concourse.mybir as mybir
from concourse.bass_utils import run_bass_kernel_spmd

F32 = mybir.dt.float32
F16 = mybir.dt.float16
AF = mybir.ActivationFunctionType
ALU = mybir.AluOpType

T = 2048
NT = T // 128
C = 1024
KCH = C // 128
QFULL = 4096
EPS = 1e-5
H_A, HD = 16, 64
H_F, E_F, DH = 8, 5, 64
SCALE_F = DH ** -0.5
KOFF = 0
VOFF = 512 * 2048
AGN = VOFF + 2048 * 520
PAIRS = [[0, 1], [2, 3], [4, 5], [6, 7]]
SEMS = ["wc", "w0", "w1", "w2", "w3", "wg0", "wg1", "x", "x0", "x1",
        "d0", "d1", "kh0", "kh1", "mm", "ve", "se", "gp", "dd", "cc",
        "out", "bar"]
NF32 = 8192
NF16 = 88064


class Alloc:
    def __init__(self, ncols, name):
        self.free = [(0, ncols)]
        self.ncols = ncols
        self.peak = 0
        self.name = name

    def alloc(self, n, tag=""):
        n = -(-n // 16) * 16
        for i, (s, ln) in enumerate(self.free):
            if ln >= n:
                if ln == n:
                    self.free.pop(i)
                else:
                    self.free[i] = (s + n, ln - n)
                self.peak = max(self.peak, self.ncols - sum(l for _, l in self.free))
                return s
        raise MemoryError(f"{self.name} OOM: {n} cols for {tag}; free={self.free}")

    def freeblk(self, s, n):
        n = -(-n // 16) * 16
        self.free.append((s, n))
        self.free.sort()
        out = []
        for blk in self.free:
            if out and out[-1][0] + out[-1][1] == blk[0]:
                out[-1] = (out[-1][0], out[-1][1] + blk[1])
            else:
                out.append(list(blk))
        self.free = [tuple(xx) for xx in out]


class B:

    def __init__(self, nc, role, sems, cnt):
        self.nc, self.role, self.sems, self.cnt = nc, role, sems, cnt
        self.wm = {}
        self.gp_chain = None

    def tok(self, sem):
        return (sem, self.cnt[sem])

    DMA_SEMS = ("wc", "w0", "w1", "w2", "w3", "wg0", "wg1", "x", "x0",
                "x1", "d0", "d1", "kh0", "kh1", "dd", "out")

    def _wait(self, eng, deps):
        flat = []
        for dd in deps or ():
            if isinstance(dd, list):
                flat.extend(dd)
            else:
                flat.append(dd)
        for dd in flat:
            if dd is None:
                continue
            sem, val = dd
            if sem in self.DMA_SEMS:
                val = self.cnt[sem]
            if val <= 0 or self.wm.get(sem, 0) >= val:
                continue
            eng.wait_ge(self.sems[sem], val)
            self.wm[sem] = val

    def _run(self, owner, fn, deps, inc, n):
        ins = None
        if self.role == owner:
            self._wait(getattr(self.nc, owner), deps)
            ins = fn()
        if inc is not None:
            self.cnt[inc] += n
            if ins is not None:
                ins.then_inc(self.sems[inc], n)
            return (inc, self.cnt[inc])
        return None

    def dma(self, q, out, in_, deps=None, inc=None):
        return self._run(q, lambda: getattr(self.nc, q).dma_start(out=out, in_=in_),
                         deps, inc, 16)

    def mm(self, out, lhsT, rhs, start, stop, deps=None, inc=None, tp=None):
        return self._run("tensor",
                         lambda: self.nc.tensor.matmul(out, lhsT, rhs, start=start,
                                                       stop=stop, tile_position=tp),
                         deps, inc, 1)

    def trans(self, out, in_, ident, deps=None, inc=None):
        return self._run("tensor", lambda: self.nc.tensor.transpose(out, in_, ident),
                         deps, inc, 1)

    def act(self, out, in_, func, bias=None, scale=None, accum_out=None,
            deps=None, inc=None):
        kw = {}
        if bias is not None:
            kw["bias"] = bias
        if scale is not None:
            kw["scale"] = scale
        if accum_out is not None:
            kw["accum_out"] = accum_out
        return self._run("scalar",
                         lambda: self.nc.scalar.activation(out, in_, func, **kw),
                         deps, inc, 1)

    def tt(self, out, a, c, op, deps=None, inc=None, eng="vector"):
        return self._run(eng, lambda: getattr(self.nc, eng).tensor_tensor(out, a, c, op),
                         deps, inc, 1)

    def ts(self, out, in0, s1, s2, op0, op1=None, deps=None, inc=None, eng="vector"):
        def f():
            e = getattr(self.nc, eng)
            if op1 is None:
                return e.tensor_scalar(out, in0, s1, None, op0)
            return e.tensor_scalar(out, in0, s1, s2, op0, op1)
        return self._run(eng, f, deps, inc, 1)

    def stt(self, out, in0, scalar, in1, op0, op1, deps=None, inc=None, eng="vector"):
        return self._run(eng,
                         lambda: getattr(self.nc, eng).scalar_tensor_tensor(
                             out, in0, scalar, in1, op0, op1),
                         deps, inc, 1)

    def copy(self, out, in_, deps=None, inc=None, eng="vector"):
        return self._run(eng, lambda: getattr(self.nc, eng).tensor_copy(out, in_),
                         deps, inc, 1)

    def recip(self, out, in_, deps=None, inc=None):
        return self._run("vector", lambda: self.nc.vector.reciprocal(out, in_),
                         deps, inc, 1)

    def memset(self, ap, v, deps=None, inc=None):
        deps = list(deps or []) + [self.gp_chain]
        tok = self._run("gpsimd", lambda: self.nc.gpsimd.memset(ap, v), deps,
                        "gp", 1)
        self.gp_chain = tok
        return tok

    def affine(self, out, in_, pattern, cmp, fill, base=None, chan=None,
               deps=None, inc=None):
        deps = list(deps or []) + [self.gp_chain]
        tok = self._run("gpsimd",
                        lambda: self.nc.gpsimd.affine_select(
                            out=out, in_=in_, pattern=pattern, compare_op=cmp,
                            fill=fill, base=base, channel_multiplier=chan),
                        deps, "gp", 1)
        self.gp_chain = tok
        return tok

    def coll(self, kind, op, groups, ins, outs, deps=None, inc=None):
        return self._run("gpsimd",
                         lambda: self.nc.gpsimd.collective_compute(
                             kind, op, replica_groups=groups, ins=ins, outs=outs),
                         deps, inc, 1)

    def barrier(self):
        self.cnt["bar"] += 4
        tgt = self.cnt["bar"]
        if self.role in ("tensor", "vector", "scalar", "gpsimd"):
            eng = getattr(self.nc, self.role)
            self._wait(eng, [(s, self.cnt[s]) for s in self.DMA_SEMS])
            eng.drain().then_inc(self.sems["bar"], 1)
            eng.wait_ge(self.sems["bar"], tgt)
            self.wm["bar"] = tgt
        elif self.role == "sync":
            eng = self.nc.sync
            self._wait(eng, [(s, self.cnt[s]) for s in self.DMA_SEMS])
            eng.wait_ge(self.sems["bar"], tgt)
            self.wm["bar"] = tgt
        return ("bar", tgt)

    def final_wait(self, deps):
        if self.role == "gpsimd":
            self._wait(self.nc.gpsimd, deps)


def _program(b, E, taps, stop=None):
    d = E["d"]
    A32, A16 = E["A32"], E["A16"]
    psA, psB, psC, psD = E["psA"], E["psB"], E["psC"], E["psD"]
    psE = E["psE"]
    tap_t = E["tap_t"]
    ar32 = Alloc(NF32, "af32")
    ar16 = Alloc(NF16, "af16")

    def v32(off, w, p0=0, pn=128):
        return A32[p0:p0 + pn, off:off + w]

    def v16(off, w, p0=0, pn=128):
        return A16[p0:p0 + pn, off:off + w]

    def tap(name, ap, deps):
        if name in taps:
            b.dma("sync", tap_t[name], ap, deps=deps, inc="out")

    class _Stop(Exception):
        pass

    def ckpt(name):
        if stop == name:
            b.barrier()
            b.final_wait([b.tok("out")])
            raise _Stop()

    o_id = ar16.alloc(128, "ident")
    o_ones_c = ar16.alloc(16, "ones col")
    o_ones_r = ar16.alloc(128, "ones row")
    o_Ekv = ar16.alloc(8 * 128, "Ekv")
    o_E6 = ar16.alloc(8 * 128, "E6")
    o_lngb = ar32.alloc(4, "lngb")
    o_geff = ar32.alloc(KCH, "geff")
    o_beff = ar32.alloc(KCH, "beff")
    o_tmpc = ar32.alloc(KCH, "tmpc")
    o_qkvb = ar32.alloc(24, "qkvb")
    o_p1b = ar32.alloc(KCH, "p1b")
    o_p2b = ar32.alloc(KCH, "p2b")
    o_abT = ar32.alloc(H_A * HD, "abT")
    ident = v16(o_id, 128)

    tw = [
        b.dma("sync", v32(o_lngb, 4), d["lngb"][:, :], inc="wc"),
        b.dma("sync", v32(o_tmpc, KCH), d["tmpc"][:, :], inc="wc"),
        b.dma("sync", v32(o_qkvb, 24), d["qkvb"][:, :], inc="wc"),
        b.dma("sync", v32(o_p1b, KCH), d["p1b"][:, :], inc="wc"),
        b.dma("sync", v32(o_p2b, KCH), d["p2b"][:, :], inc="wc"),
        b.dma("sync", v32(o_abT, H_A * HD, 0, 64), d["abT"][:, :], inc="wc"),
    ]

    b.memset(ident[:, :], 0.0)
    b.affine(ident[:, :], ident[:, :], [[-1, 128]], ALU.not_equal, 1.0, base=0, chan=1)
    b.memset(v16(o_ones_c, 1), 1.0)
    b.memset(v16(o_ones_r, 128, 0, 1), 1.0)
    b.memset(v16(o_Ekv, 8 * 128), 0.0)
    for j in range(8):
        base = o_Ekv + j * 128
        b.memset(v16(base + 2 * j, 1, 0, 64), 1.0)
        b.memset(v16(base + 2 * j + 1, 1, 64, 64), 1.0)
        b.memset(v16(base + 64 + 32 + 2 * j, 1, 0, 64), 1.0)
        b.memset(v16(base + 64 + 32 + 2 * j + 1, 1, 64, 64), 1.0)
    b.memset(v16(o_E6, 8 * 128), 1.0)
    for j in range(8):
        sub = v16(o_E6 + j * 128, 128, 0, 16)
        b.affine(sub, sub, [[1, 128]], ALU.is_ge, 0.0, base=128 * j, chan=-64)
        b.affine(sub, sub, [[-1, 128]], ALU.is_ge, 0.0, base=63 - 128 * j, chan=64)
    tsetup = b.tok("gp")

    tgeff = b.ts(v32(o_geff, KCH), v32(o_tmpc, KCH), v32(o_lngb, 1), None,
                 ALU.mult, deps=[tw[0], tw[1]], inc="ve")
    tbeff = b.ts(v32(o_beff, KCH), v32(o_tmpc, KCH), v32(o_lngb + 1, 1), None,
                 ALU.mult, deps=[tw[0], tw[1]], inc="ve")

    def emit_ln(src_dram, o_out16, tapon=False):
        o_stg = ar32.alloc(2 * T, "ln stage")
        o_cst = ar16.alloc(4 * T, "ln cast")
        tmm = [None, None]
        tcst = [None, None, None, None]
        tdxs = [None, None]
        for k in range(KCH):
            par = k % 2
            tdx = b.dma("sync", v32(o_stg + par * T, T),
                        src_dram[128 * k:128 * (k + 1), :],
                        deps=[tcst[par * 2], tcst[par * 2 + 1]],
                        inc="x0" if par == 0 else "x1")
            xc = v16(o_cst + par * 2 * T, T)
            xq = v16(o_cst + par * 2 * T + T, T)
            tc1 = b.copy(xc, v32(o_stg + par * T, T), deps=[tdx, tmm[par]], inc="ve")
            tc2 = b.act(xq, v32(o_stg + par * T, T), AF.Square, deps=[tdx, tmm[par]],
                        inc="se")
            tcst[par * 2] = tc1
            tcst[par * 2 + 1] = tc2
            for nt in range(4):
                s = nt * 512
                b.mm(psA[0:1, s:s + 512], v16(o_ones_c, 1), xc[:, s:s + 512],
                     start=(k == 0), stop=(k == KCH - 1), deps=[tc1, tsetup])
                t_ = b.mm(psA[32:33, s:s + 512], v16(o_ones_c, 1),
                          xq[:, s:s + 512],
                          start=(k == 0), stop=(k == KCH - 1), deps=[tc2],
                          inc="mm" if nt == 3 else None)
                if nt == 3:
                    tmm[par] = t_
        ar16.freeblk(o_cst, 4 * T)
        tstat = tmm[(KCH - 1) % 2]
        o_st = ar16.alloc(8 * 1024, "ln st")
        def q32(i, w, p0, pn):
            return A16[p0:p0 + pn, o_st + i * 2048:o_st + i * 2048 + 2 * w]\
                .bitcast(F32)
        o_ab = ar16.alloc(2 * T, "ln ab")
        Arow = v16(o_ab, T, 0, 1)
        Brow = v16(o_ab + T, T, 0, 1)
        tA = tB = None
        for hf in range(2):
            sl = hf * 1024
            mu = q32(0, 1024, 0, 1)
            var = q32(1, 1024, 0, 1)
            r = q32(2, 1024, 0, 1)
            tp_ = q32(3, 1024, 0, 1)
            dd = [tstat, tB]
            t1 = b.ts(mu, psA[0:1, sl:sl + 1024], 1.0 / C, None, ALU.mult,
                      deps=dd, inc="ve")
            t3 = b.act(tp_, mu, AF.Square, deps=[t1, tB], inc="se")
            t2 = b.ts(var, psA[32:33, sl:sl + 1024], 1.0 / C, EPS,
                      ALU.mult, ALU.add, deps=dd, inc="ve")
            t4 = b.tt(var, var, tp_, ALU.subtract, deps=[t2, t3], inc="ve")
            t6 = b.act(r, var, AF.Sqrt, deps=[t4], inc="se")
            t7 = b.recip(r, r, deps=[t6], inc="ve")
            t8 = b.act(tp_, r, AF.Square, deps=[t7], inc="se")
            t9 = b.tt(tp_, tp_, var, ALU.mult, deps=[t8], inc="ve")
            t10 = b.ts(tp_, tp_, -0.5, 1.5, ALU.mult, ALU.add, deps=[t9], inc="ve")
            t11 = b.tt(r, r, tp_, ALU.mult, deps=[t10], inc="ve")
            tA = b.copy(Arow[:, sl:sl + 1024], r, deps=[t11], inc="ve")
            t12 = b.tt(tp_, mu, r, ALU.mult, deps=[t11], inc="ve")
            tB = b.ts(Brow[:, sl:sl + 1024], tp_, -1.0, None, ALU.mult,
                      deps=[t12], inc="ve")
        if tapon and "lnab" in taps:
            tk_ = b.dma("sync", tap_t["lnab"], A16[0:1, o_ab:o_ab + 2 * T],
                        deps=[tA, tB], inc="out")
            b.final_wait([tk_])
        o_bsb = ar32.alloc(T, "ln bbc")
        tba = []
        tbb = []
        for nt in range(4):
            s = nt * 512
            tba.append(b.mm(psA[:, s:s + 512], v16(o_ones_r, 128, 0, 1),
                            Arow[:, s:s + 512], True, True,
                            deps=[tA, tstat], inc="mm"))
            pdst = psB[:, s:s + 512] if nt < 2 else psE[:, s - 1024:s - 512]
            tbb.append(b.mm(pdst, v16(o_ones_r, 128, 0, 1), Brow[:, s:s + 512],
                            True, True, deps=[tB], inc="mm"))
        tb1 = b.copy(v32(o_bsb, 1024), psB[:, 0:1024], deps=tbb[:2], inc="ve")
        tb2 = b.copy(v32(o_bsb + 1024, 1024), psE[:, 0:1024], deps=tbb[2:], inc="ve")
        o_t = ar16.alloc(2 * T, "ln t")
        tprev = [[tcst[0], tcst[1]], [tcst[2], tcst[3]]]
        for k in range(KCH):
            par = k % 2
            tdx = b.dma("sync", v32(o_stg + par * T, T),
                        src_dram[128 * k:128 * (k + 1), :],
                        deps=[tprev[par]],
                        inc="x0" if par == 0 else "x1")
            u1 = b.tt(v16(o_t + par * T, T), v32(o_stg + par * T, T),
                      psA[:, 0:T], ALU.mult, deps=[tdx, tprev[par]] + tba,
                      inc="ve")
            u2 = b.tt(v16(o_out16 + k * T, T), v16(o_t + par * T, T),
                      v32(o_bsb, T), ALU.add, deps=[u1, tb1, tb2], inc="ve")
            tprev[par] = u1
        ar16.freeblk(o_t, 2 * T)
        ar32.freeblk(o_bsb, T)
        ar16.freeblk(o_st, 8 * 1024)
        ar32.freeblk(o_stg, 2 * T)
        ar16.freeblk(o_ab, 2 * T)
        b.barrier()

    o_xin = ar16.alloc(KCH * T, "xin")
    emit_ln(d["xT"], o_xin, tapon=True)
    tap("xin", A16[:, o_xin:o_xin + KCH * T], [b.tok("ve")])
    xin_ch = lambda k, c0, w: v16(o_xin + k * T + c0, w)
    ckpt("S1")

    o_wstg = ar32.alloc(4 * 128, "wstg")
    o_wblk = ar16.alloc(4 * 128, "wblk")
    wpar_tok = [None, None, None, None]
    WSEM = ("w0", "w1", "w2", "w3")

    def wblock(src_ap, par):
        s32 = v32(o_wstg + par * 128, 128)
        s16 = v16(o_wblk + par * 128, 128)
        td = b.dma("sync", s32, src_ap, deps=[wpar_tok[par]], inc=WSEM[par])
        tc = b.copy(s16, s32, deps=[td], inc="ve")
        return s16, tc

    tile_tok = [None] * 4

    def mm_weight(blkB, src_ch, out_evac):
        tiles = [psB[:, 0:512], psB[:, 512:1024], psC[:, :], psD[:, :]]
        tmm = None
        for kc in range(KCH):
            par = kc % 4
            w16, tcst = wblock(blkB[kc], par)
            for nt in range(4):
                tmm = b.mm(tiles[nt], w16, src_ch(kc, nt * 512, 512),
                           start=(kc == 0), stop=(kc == KCH - 1),
                           deps=[tcst, tile_tok[nt]],
                           inc="mm" if nt == 3 else None)
            wpar_tok[par] = tmm
        toks = out_evac(tiles, tmm)
        for nt in range(4):
            tile_tok[nt] = toks[nt]

    o_ktok = ar16.alloc(NT * C, "ktok")
    o_vtok = ar16.alloc(NT * C, "vtok")
    kt_ch = lambda tcn, c0, w: v16(o_ktok + tcn * C + c0, w)
    vt_ch = lambda tcn, c0, w: v16(o_vtok + tcn * C + c0, w)

    def kv_group(mc0, o_tok, gcol_fn, bcol_fn):
        o_kv = ar16.alloc(KCH * T, "kvT")
        o_sq = ar16.alloc(T, "sq")
        tstat = None
        tsq = None
        for j in range(KCH):
            mcx = mc0 + j

            def evac(tiles, tmm, j=j, mcx=mcx):
                toks = []
                for nt in range(4):
                    te = b.act(v16(o_kv + j * T + nt * 512, 512), tiles[nt],
                               AF.Identity, bias=v32(o_qkvb + mcx, 1),
                               deps=[tmm, tw[2]], inc="se")
                    toks.append(te)
                return toks

            mm_weight(d["qkvB"][mcx], xin_ch, evac)
            tevs = list(tile_tok)
            tsq = b.act(v16(o_sq, T), v16(o_kv + j * T, T), AF.Square,
                        deps=tevs + [tstat], inc="se")
            Ev = v16(o_Ekv + j * 128, 64)
            Eq = v16(o_Ekv + j * 128 + 64, 64)
            for nt in range(4):
                s = nt * 512
                b.mm(psA[0:64, s:s + 512], Ev, v16(o_kv + j * T + s, 512),
                     start=(j == 0), stop=False, deps=tevs + [tsetup])
                tstat = b.mm(psA[0:64, s:s + 512], Eq, v16(o_sq + s, 512),
                             start=False, stop=(j == KCH - 1), deps=[tsq],
                             inc="mm" if nt == 3 else None)
        ar16.freeblk(o_sq, T)
        if ("kvsums" in taps) and mc0 == 8:
            o_sdb = ar32.alloc(T, "kvs dbg")
            td1 = b.copy(v32(o_sdb, T, 0, 64), psA[0:64, 0:T], deps=[tstat], inc="ve")
            b.dma("sync", tap_t["kvsums"], v32(o_sdb, T, 0, 64), deps=[td1],
                  inc="out")
            ar32.freeblk(o_sdb, T)
        o_st = ar16.alloc(8 * 1024, "kv st")
        def q32(i, w):
            return A16[0:16, o_st + i * 2048:o_st + i * 2048 + 2 * w].bitcast(F32)
        o_ab = ar16.alloc(2 * T, "kv ab")
        Ar = v16(o_ab, T, 0, 16)
        Br = v16(o_ab + T, T, 0, 16)
        tA = tB = None
        for hf in range(2):
            sl = hf * 1024
            mu = q32(0, 1024)
            var = q32(1, 1024)
            r = q32(2, 1024)
            tp_ = q32(3, 1024)
            dd = [tstat, tB]
            t1 = b.ts(mu, psA[0:16, sl:sl + 1024], 1.0 / HD, None, ALU.mult,
                      deps=dd, inc="ve")
            t3 = b.act(tp_, mu, AF.Square, deps=[t1, tB], inc="se")
            t2 = b.ts(var, psA[32:48, sl:sl + 1024], 1.0 / HD, EPS,
                      ALU.mult, ALU.add, deps=dd, inc="ve")
            t4 = b.tt(var, var, tp_, ALU.subtract, deps=[t2, t3], inc="ve")
            t6 = b.act(r, var, AF.Sqrt, deps=[t4], inc="se")
            t7 = b.recip(r, r, deps=[t6], inc="ve")
            t8 = b.act(tp_, r, AF.Square, deps=[t7], inc="se")
            t9 = b.tt(tp_, tp_, var, ALU.mult, deps=[t8], inc="ve")
            t10 = b.ts(tp_, tp_, -0.5, 1.5, ALU.mult, ALU.add, deps=[t9], inc="ve")
            t11 = b.tt(r, r, tp_, ALU.mult, deps=[t10], inc="ve")
            tA = b.copy(Ar[:, sl:sl + 1024], r, deps=[t11], inc="ve")
            t12 = b.tt(tp_, mu, r, ALU.mult, deps=[t11], inc="ve")
            tB = b.ts(Br[:, sl:sl + 1024], tp_, -1.0, None, ALU.mult,
                      deps=[t12], inc="ve")
        ar16.freeblk(o_st, 8 * 1024)
        if ("kvab" in taps) and mc0 == 8:
            b.dma("sync", tap_t["kvab"], A16[0:16, o_ab:o_ab + 2 * T],
                  deps=[tA, tB], inc="out")
        b.barrier()
        o_t = ar16.alloc(512, "kv ap t")
        tu3 = []
        tev = []
        for j in range(KCH):
            E6 = v16(o_E6 + j * 128, 128, 0, 16)
            for nt in range(4):
                s = nt * 512
                ta = b.mm(psB[:, 0:512], E6, Ar[:, s:s + 512], True, True,
                          deps=[tA] + tu3, inc="mm")
                tb = b.mm(psB[:, 512:1024], E6, Br[:, s:s + 512], True, True,
                          deps=[tB], inc="mm")
                tm = v16(o_t, 512)
                u1 = b.tt(tm, v16(o_kv + j * T + s, 512), psB[:, 0:512], ALU.mult,
                          deps=[ta] + tev, inc="ve")
                u2 = b.tt(tm, tm, psB[:, 512:1024], ALU.add, deps=[u1, tb], inc="ve")
                u3a = b.ts(tm, tm, gcol_fn(j), None, ALU.mult,
                           deps=[u2, tgeff, tw[0]], inc="ve")
                u3 = b.ts(tm, tm, bcol_fn(j), None, ALU.add,
                          deps=[u3a, tbeff], inc="ve")
                tu3 = [u3]
                ttr = None
                for q4 in range(4):
                    ttr = b.trans(psC[:, 64 * q4:64 * (q4 + 1)].bitcast(F16),
                                  tm[:, 128 * q4:128 * (q4 + 1)], ident,
                                  deps=[u3] + tev, inc="mm" if q4 == 3 else None)
                tev = []
                for q4 in range(4):
                    tcn = nt * 4 + q4
                    te = b.copy(A16[:, o_tok + tcn * C + j * 128:
                                    o_tok + tcn * C + j * 128 + 128],
                                psC[:, 64 * q4:64 * (q4 + 1)].bitcast(F16),
                                deps=[ttr], inc="ve")
                    tev = [te]
                b.barrier()
        ar16.freeblk(o_t, 512)
        ar16.freeblk(o_kv, KCH * T)
        ar16.freeblk(o_ab, 2 * T)
        b.barrier()

    kv_group(8, o_ktok, lambda j: v32(o_geff + j, 1),
             lambda j: v32(o_beff + j, 1))
    kv_group(16, o_vtok, lambda j: v32(o_lngb + 2, 1),
             lambda j: v32(o_lngb + 3, 1))
    tap("ktok", A16[:, o_ktok:o_ktok + NT * C], [b.tok("ve")])
    tap("vtok", A16[:, o_vtok:o_vtok + NT * C], [b.tok("ve")])

    o_q = ar16.alloc(KCH * T, "qT")
    for j in range(KCH):
        def evq(tiles, tmm, j=j):
            toks = []
            for nt in range(4):
                te = b.act(v16(o_q + j * T + nt * 512, 512), tiles[nt], AF.Identity,
                           bias=v32(o_qkvb + j, 1), deps=[tmm, tw[2]], inc="se")
                toks.append(te)
            return toks
        mm_weight(d["qkvB"][j], xin_ch, evq)
    b.barrier()
    tap("q", A16[:, o_q:o_q + KCH * T], [b.tok("se")])
    ckpt("S2")

    tkv = None
    for h in range(H_A):
        for tcn in range(NT):
            tkv = b.mm(psA[0:64, h * 64:h * 64 + 64], kt_ch(tcn, h * 64, 64),
                       vt_ch(tcn, h * 64, 64), start=(tcn == 0), stop=(tcn == NT - 1),
                       inc="mm" if (tcn == NT - 1 and h == H_A - 1) else None)
    o_kvp = ar32.alloc(H_A * HD, "kv partials")
    tkevac = b.ts(v32(o_kvp, H_A * HD, 0, 64), psA[0:64, 0:H_A * HD],
                  1.0 / QFULL, None, ALU.mult, deps=[tkv], inc="ve")
    tout = b.dma("gpsimd", E["kv_in"][:, :], v32(o_kvp, H_A * HD, 0, 64),
                 deps=[tkevac], inc="dd")
    tcc = b.coll("AllReduce", ALU.add, PAIRS, [E["kv_in"][:, :]],
                 [E["kv_out"][:, :]], deps=[tout], inc="cc")
    tin = b.dma("gpsimd", v32(o_kvp, H_A * HD, 0, 64), E["kv_out"][:, :],
                deps=[tcc], inc="dd")
    tfin = b.tt(v32(o_kvp, H_A * HD, 0, 64), v32(o_kvp, H_A * HD, 0, 64),
                v32(o_abT, H_A * HD, 0, 64), ALU.add, deps=[tin, tw[5]], inc="ve")
    tap("kvf", A32[0:64, o_kvp:o_kvp + H_A * HD], [tfin])
    o_kvbd = ar16.alloc(8 * 128, "kv bd")
    tms = b.memset(v16(o_kvbd, 8 * 128), 0.0, deps=[], inc="gp")
    tbd = []
    for hp in range(8):
        base = o_kvbd + hp * 128
        b1 = b.copy(v16(base, 64, 0, 64), v32(o_kvp + 2 * hp * 64, 64, 0, 64),
                    deps=[tfin, tms], inc="ve")
        b2 = b.copy(v16(base + 64, 64, 64, 64),
                    v32(o_kvp + (2 * hp + 1) * 64, 64, 0, 64),
                    deps=[tfin, tms], inc="ve")
        tbd += [b1, b2]
    b.barrier()
    ar16.freeblk(o_ktok, NT * C)
    ar16.freeblk(o_vtok, NT * C)
    ar32.freeblk(o_kvp, H_A * HD)
    ckpt("S3")

    o_att = ar16.alloc(KCH * T, "att")
    tiles4 = [psB[:, 0:512], psB[:, 512:1024], psC[:, :], psD[:, :]]
    ttok4 = [None] * 4
    for hp in range(8):
        tmm = None
        for nt in range(4):
            tmm = b.mm(tiles4[nt], v16(o_kvbd + hp * 128, 128),
                       v16(o_q + hp * T + nt * 512, 512), True, True,
                       deps=[ttok4[nt]] + tbd, inc="mm" if nt == 3 else None)
        for nt in range(4):
            te = b.tt(v16(o_att + hp * T + nt * 512, 512), tiles4[nt],
                      xin_ch(hp, nt * 512, 512), ALU.add, deps=[tmm], inc="ve")
            ttok4[nt] = te
    b.barrier()
    ar16.freeblk(o_q, KCH * T)
    ar16.freeblk(o_kvbd, 8 * 128)
    tap("att", A16[:, o_att:o_att + KCH * T], [b.tok("ve")])
    ckpt("S4")

    o_h1 = ar16.alloc(KCH * T, "h1")
    att_ch = lambda k, c0, w: v16(o_att + k * T + c0, w)
    for mc in range(KCH):
        def ev1(tiles, tmm, mc=mc):
            toks = []
            for nt in range(4):
                te = b.act(v16(o_h1 + mc * T + nt * 512, 512), tiles[nt], AF.Gelu,
                           bias=v32(o_p1b + mc, 1), deps=[tmm, tw[3]], inc="se")
                toks.append(te)
            return toks
        mm_weight(d["p1B"][mc], att_ch, ev1)
    b.barrier()
    ar16.freeblk(o_att, KCH * T)

    h1_ch = lambda k, c0, w: v16(o_h1 + k * T + c0, w)
    o_xs = ar32.alloc(2 * 512, "x slc")
    o_x1s = ar32.alloc(2 * 512, "x1 slc")
    txs = [None, None]
    for mc in range(KCH):
        def ev2(tiles, tmm, mc=mc):
            toks = []
            for nt in range(4):
                par = nt % 2
                s = nt * 512
                tdx = b.dma("sync", v32(o_xs + par * 512, 512),
                            d["xT"][128 * mc:128 * (mc + 1), s:s + 512],
                            deps=[txs[par]], inc="x0" if par == 0 else "x1")
                ta = b.act(v32(o_x1s + par * 512, 512), tiles[nt], AF.Identity,
                           bias=v32(o_p2b + mc, 1), deps=[tmm, tw[4], txs[par]],
                           inc="se")
                tb = b.tt(v32(o_x1s + par * 512, 512), v32(o_x1s + par * 512, 512),
                          xin_ch(mc, s, 512), ALU.add, deps=[ta], inc="ve")
                tc2 = b.tt(v32(o_x1s + par * 512, 512), v32(o_x1s + par * 512, 512),
                           v32(o_xs + par * 512, 512), ALU.add, deps=[tb, tdx],
                           inc="ve")
                td = b.dma("gpsimd", E["x1d"][128 * mc:128 * (mc + 1), s:s + 512],
                           v32(o_x1s + par * 512, 512), deps=[tc2],
                           inc="d0" if par == 0 else "d1")
                txs[par] = td
                toks.append(tc2)
            return toks
        mm_weight(d["p2B"][mc], h1_ch, ev2)
    b.barrier()
    ar16.freeblk(o_h1, KCH * T)
    ar32.freeblk(o_xs, 2 * 512)
    ar32.freeblk(o_x1s, 2 * 512)
    ar16.freeblk(o_xin, KCH * T)
    tap("x1", E["x1d"][:, :], [b.tok("d0"), b.tok("d1")])
    ckpt("S6")

    o_xin1 = ar16.alloc(KCH * T, "xin1")
    o_wo = ar16.alloc(20 * 1024, "wo2 full")
    emit_ln(E["x1d"], o_xin1)
    xin1_ch = lambda k, c0, w: v16(o_xin1 + k * T + c0, w)

    o_E3 = ar16.alloc(20 * 128, "E3")
    o_E7 = ar16.alloc(48, "E7")
    b.memset(v16(o_E3, 20 * 128), 1.0)
    for cc in range(20):
        sub = v16(o_E3 + cc * 128, 128, 0, 40)
        b.affine(sub, sub, [[1, 128]], ALU.is_ge, 0.0, base=128 * cc, chan=-64)
        b.affine(sub, sub, [[-1, 128]], ALU.is_ge, 0.0, base=63 - 128 * cc, chan=64)
    b.memset(v16(o_E7, 48), 1.0)
    sub = v16(o_E7, 40, 0, 8)
    b.affine(sub, sub, [[1, 40]], ALU.is_ge, 0.0, base=0, chan=-5)
    b.affine(sub, sub, [[-1, 40]], ALU.is_ge, 0.0, base=4, chan=5)

    o_wsd = ar16.alloc(KCH * 80, "wsd")
    o_wsd32 = ar32.alloc(128, "wsd stg")
    twsd = []
    tpv = None
    for kc in range(KCH):
        tdw = b.dma("sync", v32(o_wsd32, 80), d["wsdB"][kc], deps=[tpv], inc="x")
        tpv = b.copy(v16(o_wsd + kc * 80, 80), v32(o_wsd32, 80), deps=[tdw], inc="ve")
        twsd.append(tpv)
    o_gate = ar32.alloc(NT * 80, "gates")
    tg = None
    tsg = None
    for tcn in range(NT):
        for kc in range(KCH):
            tg = b.mm(psC[:, 0:80], xin1_ch(kc, tcn * 128, 128),
                      v16(o_wsd + kc * 80, 80), start=(kc == 0), stop=(kc == KCH - 1),
                      deps=[twsd[kc], tsg], inc="mm" if kc == KCH - 1 else None)
        tsg = b.act(v32(o_gate + tcn * 80, 80), psC[:, 0:80], AF.Sigmoid,
                    deps=[tg], inc="se")
    b.barrier()
    ar16.freeblk(o_wsd, KCH * 80)
    ar32.freeblk(o_wsd32, 128)
    tap("gates", A32[:, o_gate:o_gate + NT * 80], [b.tok("se")])

    NM = NT * 8
    o_med = ar32.alloc(8 * NM, "med")
    o_msk = ar16.alloc(2 * NT * 40, "masks")
    o_sg = ar32.alloc(2 * NT * 40, "gated")
    gateR = A32[:, o_gate:o_gate + NT * 80].rearrange(
        "p (tcn hl h e) -> p tcn hl h e", tcn=NT, hl=2, h=8, e=5)
    tmask = []
    for hl in range(2):
        ex = [gateR[:, :, hl, :, e] for e in range(5)]
        m = [v32(o_med + i * NM, NM).rearrange("p (tcn h) -> p tcn h", tcn=NT, h=8)
             for i in range(8)]
        d0 = [b.tok("se")]
        u = []
        u.append(b.tt(m[0], ex[0], ex[1], ALU.min, deps=d0, inc="ve"))
        u.append(b.tt(m[1], ex[0], ex[1], ALU.max, deps=d0, inc="ve"))
        u.append(b.tt(m[2], ex[2], ex[3], ALU.min, deps=d0, inc="ve"))
        u.append(b.tt(m[3], ex[2], ex[3], ALU.max, deps=d0, inc="ve"))
        f_ = b.tt(m[4], m[0], m[2], ALU.max, deps=u[:3], inc="ve")
        g_ = b.tt(m[5], m[1], m[3], ALU.min, deps=u, inc="ve")
        q1 = b.tt(m[6], ex[4], m[4], ALU.min, deps=[f_], inc="ve")
        q2 = b.tt(m[7], ex[4], m[4], ALU.max, deps=[f_], inc="ve")
        q3 = b.tt(m[7], m[7], m[5], ALU.min, deps=[q2, g_], inc="ve")
        tmed = b.tt(m[0], m[6], m[7], ALU.max, deps=[q1, q3], inc="ve")
        mskR = A16[:, o_msk + hl * NT * 40:o_msk + (hl + 1) * NT * 40].rearrange(
            "p (tcn h e) -> p tcn h e", tcn=NT, h=8, e=5)
        sgR = A32[:, o_sg + hl * NT * 40:o_sg + (hl + 1) * NT * 40].rearrange(
            "p (tcn h e) -> p tcn h e", tcn=NT, h=8, e=5)
        tms_ = []
        for e in range(5):
            tms_.append(b.tt(mskR[:, :, :, e], ex[e], m[0], ALU.is_ge,
                             deps=[tmed], inc="ve"))
        tgt = b.tt(sgR, gateR[:, :, hl, :, :], mskR, ALU.mult, deps=tms_, inc="ve")
        tmask.append(tgt)
    ar32.freeblk(o_med, 8 * NM)
    ar32.freeblk(o_gate, NT * 80)
    b.barrier()
    tap("md", A16[:, o_msk + NT * 40:o_msk + 2 * NT * 40], [b.tok("ve")])
    tap("sdg", A32[:, o_sg + NT * 40:o_sg + 2 * NT * 40], [b.tok("ve")])
    ckpt("S8")

    def msk_col(hl, tcn, h, e):
        return v16(o_msk + hl * NT * 40 + tcn * 40 + h * 5 + e, 1)

    def sg_col(hl, tcn, h, e):
        return v32(o_sg + hl * NT * 40 + tcn * 40 + h * 5 + e, 1)

    o_ktf = ar16.alloc(NT * 512, "ktokF")
    o_vex = ar16.alloc(NT * 520, "vext")
    o_wg = ar16.alloc(2 * KCH * 640, "w group")
    o_wg32 = ar32.alloc(2 * 640, "wg stage")
    o_g5 = ar16.alloc(NT * 640, "group out")
    twg_par = [None, None]
    o_kT = ar16.alloc(T, "kT")
    tsend = []

    tones = []
    for tcn in range(NT):
        vv = A16[:, o_vex + tcn * 520:o_vex + (tcn + 1) * 520].rearrange(
            "p (hp c) -> p hp c", hp=4, c=130)
        b.memset(vv[:, :, 64:65], 1.0)
        tones.append(b.memset(vv[:, :, 129:130], 1.0, inc="gp"))

    wg_stage_guard = [None, None]
    wg_ptok = [None, None]
    wg_glast = [None]
    wg_slot_guard = [None] * NT

    def wgroup_mm(blkB, og, wpar, dst_emit, gated_dep):
        twl = []
        for kc in range(KCH):
            s32 = v32(o_wg32 + (kc % 2) * 640, 640)
            tdw = b.dma("sync", s32, blkB[kc],
                        deps=[twl[-2] if len(twl) >= 2 else
                              wg_stage_guard[kc % 2], twg_par[wpar]],
                        inc="wg0" if kc % 2 == 0 else "wg1")
            twl.append(b.copy(v16(o_wg + wpar * KCH * 640 + kc * 640, 640), s32,
                              deps=[tdw], inc="ve"))
        wg_stage_guard[0] = twl[-2]
        wg_stage_guard[1] = twl[-1]
        ptiles = [(psB[:, 0:512], psB[:, 512:640]), (psA[:, 0:512], psA[:, 512:640])]
        ptok = wg_ptok
        tglast = wg_glast[0]
        for tcn in range(NT):
            pp = tcn % 2
            tmm = None
            for kc in range(KCH):
                wap = v16(o_wg + wpar * KCH * 640 + kc * 640, 640)
                b.mm(ptiles[pp][0], xin1_ch(kc, tcn * 128, 128), wap[:, 0:512],
                     start=(kc == 0), stop=(kc == KCH - 1),
                     deps=[twl[kc], ptok[pp]])
                tmm = b.mm(ptiles[pp][1], xin1_ch(kc, tcn * 128, 128),
                           wap[:, 512:640], start=(kc == 0), stop=(kc == KCH - 1),
                           inc="mm" if kc == KCH - 1 else None)
            gslot = o_g5 + tcn * 640
            te1 = b.act(v16(gslot, 512), ptiles[pp][0], AF.Identity,
                        deps=[tmm, wg_slot_guard[tcn]], inc="se")
            te2 = b.act(v16(gslot + 512, 128), ptiles[pp][1], AF.Identity,
                        deps=[tmm, wg_slot_guard[tcn]], inc="se")
            ptok[pp] = te2
            tglast = dst_emit(og, tcn, [te1, te2, gated_dep])
            wg_slot_guard[tcn] = tglast
        wg_glast[0] = tglast
        twg_par[wpar] = tglast

    o_gt4 = ar16.alloc(2 * 4 * 64, "gate tmp")
    _gprev = [None, None]

    def gate_sum(dst, src_e, scol, deps, lane):
        base = o_gt4 + lane * 256
        deps = deps + [_gprev[lane]]
        p0 = b.act(dst, src_e(0), AF.Identity, scale=scol(0), deps=deps, inc="se")
        pt = []
        for e in range(1, 5):
            pt.append(b.act(v16(base + (e - 1) * 64, 64), src_e(e), AF.Identity,
                            scale=scol(e), deps=deps, inc="se"))
        a1 = b.tt(v16(base, 64), v16(base, 64), v16(base + 64, 64), ALU.add,
                  deps=pt[:2], inc="ve")
        a2 = b.tt(v16(base + 128, 64), v16(base + 128, 64),
                  v16(base + 192, 64), ALU.add, deps=pt[2:], inc="ve")
        a3 = b.tt(v16(base, 64), v16(base, 64), v16(base + 128, 64), ALU.add,
                  deps=[a1, a2], inc="ve")
        t = b.tt(dst, dst, v16(base, 64), ALU.add, deps=[p0, a3], inc="ve")
        _gprev[lane] = t
        return t

    def gate_kv(og, tcn, deps):
        kv = og // 4
        hp = og % 4
        tl = []
        for hh in range(2):
            h = 2 * hp + hh
            if kv == 0:
                dst = v16(o_ktf + tcn * 512 + hp * 128 + hh * 64, 64)
            else:
                dst = v16(o_vex + tcn * 520 + hp * 130 + (65 if hh else 0), 64)
            src_e = lambda e: v16(o_g5 + tcn * 640 + hh * 320 + e * 64, 64)
            tl.append(gate_sum(dst, src_e, lambda e: sg_col(0, tcn, h, e),
                                deps + [tmask[0]], hh))
        return tl

    for og in range(4, 8):
        wgroup_mm(d["wkvB"][og], og, og % 2, gate_kv, None)
    for tcn in range(NT):
        tsend.append(b.dma("gpsimd",
                           E["agv_in"][tcn * 128 * 520:(tcn + 1) * 128 * 520]
                           .rearrange("(p c) -> p c", c=520),
                           A16[:, o_vex + tcn * 520:o_vex + (tcn + 1) * 520],
                           deps=[tones[tcn], wg_slot_guard[tcn]], inc="dd"))
    tag = b.coll("AllGather", ALU.bypass, PAIRS,
                 [E["agv_in"][:]], [E["agv_out"][0:2 * 2048 * 520]],
                 deps=tsend, inc="cc")

    for og in range(4):
        wgroup_mm(d["wkvB"][og], og, og % 2, gate_kv, None)

    ttr = None
    tevk = []
    tsendk = []
    for cc4 in range(4):
        for tcn in range(NT):
            pv = (psC if tcn % 2 == 0 else psD)[:, 0:64].bitcast(F16)
            ttr = b.trans(pv, v16(o_ktf + tcn * 512 + cc4 * 128, 128), ident,
                          deps=tevk[-1:] + [wg_slot_guard[tcn], wg_ptok[0],
                                            wg_ptok[1]], inc="mm")
            te = b.copy(v16(o_kT + tcn * 128, 128), pv,
                        deps=[ttr] + tsendk[-1:], inc="ve")
            tevk.append(te)
        tsendk.append(b.dma("gpsimd",
                            E["agk_in"][cc4 * 128 * T:(cc4 + 1) * 128 * T]
                            .rearrange("(p c) -> p c", c=T),
                            v16(o_kT, T), deps=[tevk[-1]], inc="dd"))
    tag_k = b.coll("AllGather", ALU.bypass, PAIRS,
                   [E["agk_in"][:]], [E["agk_out"][:]], deps=tsendk, inc="cc")
    b.barrier()
    tap("kT", A16[:, o_kT:o_kT + T], [b.tok("ve")])
    tap("vext", A16[:, o_vex:o_vex + NT * 520], [b.tok("ve")])
    ckpt("S9")
    ar16.freeblk(o_kT, T)
    ar16.freeblk(o_ktf, NT * 512)

    o_qtok = ar16.alloc(NT * 512, "q tok")

    def gate_q(og, tcn, deps):
        tl = []
        for hh in range(2):
            h = 2 * og + hh
            dst = v16(o_qtok + tcn * 512 + og * 128 + hh * 64, 64)
            src_e = lambda e: v16(o_g5 + tcn * 640 + hh * 320 + e * 64, 64)
            tl.append(gate_sum(dst, src_e, lambda e: sg_col(1, tcn, h, e),
                                deps + [tmask[1]], hh))
        return tl

    for og in range(4):
        wgroup_mm(d["wqB"][og], og, og % 2, gate_q, None)
    b.barrier()
    ar16.freeblk(o_wg, 2 * KCH * 640)
    ar16.freeblk(o_g5, NT * 640)
    ar32.freeblk(o_wg32, 2 * 640)
    o_qT = ar16.alloc(4 * T, "qT ffn")
    tevq = []
    for cc4 in range(4):
        for tcn in range(NT):
            pv = (psC if tcn % 2 == 0 else psD)[:, 0:64].bitcast(F16)
            ttr = b.trans(pv, v16(o_qtok + tcn * 512 + cc4 * 128, 128), ident,
                          deps=tevq[-1:], inc="mm")
            te = b.copy(v16(o_qT + cc4 * T + tcn * 128, 128), pv,
                        deps=[ttr], inc="ve")
            tevq.append(te)
    ar16.freeblk(o_qtok, NT * 512)
    tap("qf", A16[:, o_qT:o_qT + 4 * T], [b.tok("ve")])
    ckpt("S10")

    o_wo32 = ar32.alloc(2 * 128, "wo2 stg")
    tpw = [None, None]
    two = []
    for cc in range(20):
        for mc2 in range(KCH):
            par = (cc * KCH + mc2) % 2
            s32 = v32(o_wo32 + par * 128, 128)
            tdw = b.dma("sync", s32, d["wo2B"][cc, mc2], deps=[tpw[par]],
                        inc="w0" if par == 0 else "w1")
            tpw[par] = b.copy(v16(o_wo + cc * 1024 + mc2 * 128, 128), s32,
                              deps=[tdw], inc="ve")
            two.append(tpw[par])
    ar32.freeblk(o_wo32, 2 * 128)
    ar16.freeblk(o_xin1, KCH * T)
    b.barrier()

    o_khp = ar16.alloc(2 * QFULL, "k hp")
    o_vhp = ar16.alloc(2 * 32 * 130, "v hp")
    o_P = ar16.alloc(3 * 1024, "P bufs")
    o_avT = ar16.alloc(4 * T, "avT")
    o_sst = ar32.alloc(1024, "sums stage")
    tpre = [None, None]

    def prefetch_hp(hp, pp, deps):
        toks = []
        for sl in range(2):
            base = sl * 512 * 2048 + hp * 128 * 2048
            toks.append(b.dma(
                "gpsimd",
                v16(o_khp + pp * QFULL + sl * (QFULL // 2), QFULL // 2),
                E["agk_out"][base:base + 128 * 2048].rearrange("(p c) -> p c", c=2048),
                deps=deps, inc="kh0" if pp == 0 else "kh1"))
        for sl in range(2):
            for ch in range(16):
                src0 = sl * 2048 * 520 + ch * 128 * 520
                toks.append(b.dma(
                    "gpsimd",
                    v16(o_vhp + pp * 32 * 130 + (sl * 16 + ch) * 130, 130),
                    E["agv_out"][src0:src0 + 128 * 520]
                    .rearrange("(p c) -> p c", c=520)[:, hp * 130:hp * 130 + 130],
                    deps=deps, inc="kh0" if pp == 0 else "kh1"))
        return toks

    tpre[0] = prefetch_hp(0, 0, [tag_k, tag])
    o_sums_rd = ar32.alloc(T, "sums rd")
    tev_av = []
    tsum_dma = []
    for hp in range(4):
        pp = hp % 2
        if hp < 3:
            tpre[(hp + 1) % 2] = prefetch_hp(hp + 1, (hp + 1) % 2,
                                             [tag] + tev_av[-1:])
        kdeps = tpre[pp]
        for tc4 in range(4):
            ts_ = tc4 * 512
            texp = [None, None, None]
            tavmm_c = [None]
            tav_par = [None, None, None]
            tSmm = [None, None, None]
            SPAIR = [(psB[:, 0:512], psB[:, 512:1024], psB[:, 0:1024]),
                     (psE[:, 0:512], psE[:, 512:1024], psE[:, 0:1024]),
                     (psA[:, 1024:1536], psA[:, 1536:2048], psA[:, 1024:2048])]
            def emit_S(jc):
                par = jc % 3
                So = SPAIR[par]
                sdeps = kdeps + [texp[par]] + (tev_av if jc < 3 else [])
                b.mm(So[0], v16(o_khp + pp * QFULL + jc * 128, 128, 0, 64),
                     v16(o_qT + hp * T + ts_, 512, 0, 64), True, True,
                     deps=sdeps, tp=(0, 0))
                tSmm[par] = b.mm(So[1],
                                 v16(o_khp + pp * QFULL + jc * 128, 128, 64, 64),
                                 v16(o_qT + hp * T + ts_, 512, 64, 64), True, True,
                                 tp=(64, 0), inc="mm")
                texp[par] = b.act(v16(o_P + par * 1024, 1024), So[2], AF.Exp,
                                  scale=SCALE_F, deps=[tSmm[par], tav_par[par]],
                                  inc="se")

            def emit_AV(jc):
                par = jc % 3
                vch = v16(o_vhp + pp * 32 * 130 + jc * 130, 130)
                b.mm(psA[0:65, 0:512], vch[:, 0:65], v16(o_P + par * 1024, 512),
                     start=(jc == 0), stop=(jc == 31),
                     deps=[texp[par]] + (tev_av if jc == 0 else []))
                tavmm_c[0] = b.mm(psA[0:65, 512:1024], vch[:, 65:130],
                                  v16(o_P + par * 1024 + 512, 512),
                                  start=(jc == 0), stop=(jc == 31), inc="mm")
                tav_par[par] = tavmm_c[0]

            emit_S(0)
            emit_S(1)
            for jc in range(32):
                if jc + 2 < 32:
                    emit_S(jc + 2)
                emit_AV(jc)
            tavmm = tavmm_c[0]
            e1 = b.copy(v16(o_avT + hp * T + ts_, 512, 0, 64), psA[0:64, 0:512],
                        deps=[tavmm], inc="ve")
            e2 = b.copy(v16(o_avT + hp * T + ts_, 512, 64, 64),
                        psA[0:64, 512:1024], deps=[tavmm], inc="ve")
            s1 = b.copy(v32(o_sst, 512, 64, 1), psA[64:65, 0:512],
                        deps=[tavmm] + tsum_dma, inc="ve")
            s2 = b.copy(v32(o_sst + 512, 512, 64, 1), psA[64:65, 512:1024],
                        deps=[tavmm] + tsum_dma, inc="ve")
            tsum_dma = [
                b.dma("gpsimd", E["sums_d"][2 * hp:2 * hp + 1, ts_:ts_ + 512],
                      v32(o_sst, 512, 64, 1), deps=[s1], inc="d0"),
                b.dma("gpsimd", E["sums_d"][2 * hp + 1:2 * hp + 2, ts_:ts_ + 512],
                      v32(o_sst + 512, 512, 64, 1), deps=[s2], inc="d1"),
            ]
            tev_av = [e1, e2, s1, s2]
    b.barrier()
    ar16.freeblk(o_khp, 2 * QFULL)
    ar16.freeblk(o_vhp, 2 * 32 * 130)
    ar16.freeblk(o_P, 3 * 1024)
    ar32.freeblk(o_sst, 1024)
    o_E4 = ar16.alloc(40 * 128, "E4")
    b.memset(v16(o_E4, 40 * 128), 0.0)
    for g in range(40):
        h = g // 5
        s0 = (h % 2) * 64
        p0 = (g % 2) * 64
        sub = v16(o_E4 + g * 128 + p0, 64, s0, 64)
        b.affine(sub, sub, [[-1, 64]], ALU.not_equal, 1.0, base=0, chan=1)
    tap("avT", A16[:, o_avT:o_avT + 4 * T], [b.tok("ve")])
    ckpt("S11")

    trd = b.dma("sync", v32(o_sums_rd, T, 0, 8), E["sums_d"][:, :],
                deps=[b.tok("d0"), b.tok("d1")], inc="x")
    o_rs = ar16.alloc(T, "recip16")
    tr1 = b.recip(v32(o_sums_rd, T, 0, 8), v32(o_sums_rd, T, 0, 8),
                  deps=[trd], inc="ve")
    tr2 = b.copy(v16(o_rs, T, 0, 8), v32(o_sums_rd, T, 0, 8), deps=[tr1], inc="ve")
    ar32.freeblk(o_sums_rd, T)
    o_mdT = ar16.alloc(T, "mdT")
    tmdT = []
    for tcn in range(NT):
        pv = psD[0:40, 0:64].bitcast(F16)
        tt_ = b.trans(pv, v16(o_msk + NT * 40 + tcn * 40, 40), ident,
                      deps=tmdT[-1:], inc="mm")
        tmdT.append(b.copy(v16(o_mdT + tcn * 128, 128, 0, 40), pv,
                           deps=[tt_], inc="ve"))
    o_gT = ar16.alloc(T, "gT")
    tgT = []
    for tc4 in range(4):
        s = tc4 * 512
        tm1 = b.mm(psC[0:40, :], v16(o_E7, 40, 0, 8), v16(o_rs + s, 512, 0, 8),
                   True, True, deps=[tr2, tsetup] + tmdT[-1:] + tgT[-1:], inc="mm")
        tgT.append(b.tt(v16(o_gT + s, 512, 0, 40), v16(o_mdT + s, 512, 0, 40),
                        psC[0:40, :], ALU.mult, deps=[tm1] + tmdT, inc="ve"))
    tap("gT", A16[0:40, o_gT:o_gT + T], [b.tok("ve")])
    b.barrier()
    ar16.freeblk(o_mdT, T)
    ar16.freeblk(o_rs, T)

    o_om = ar16.alloc(20 * 512, "OM")
    o_gbc = ar16.alloc(512, "gbc sbuf")
    o_x1s2 = ar32.alloc(2 * 512, "x1 slc f")
    o_yout = ar32.alloc(2 * 512, "y out")
    tx1 = [None, None]
    ty = [None, None]
    tomlast = []
    n_out = 0
    for tc4 in range(4):
        s = tc4 * 512
        tgb_prev = None
        tom = []
        for cc in range(20):
            tgb = b.mm(psD[:, :], v16(o_E3 + cc * 128, 128, 0, 40),
                       v16(o_gT + s, 512, 0, 40), True, True,
                       deps=tgT + [tgb_prev], inc="mm")
            tgbe = b.act(v16(o_gbc, 512), psD[:, :], AF.Identity,
                         deps=[tgb, tom[-1:][0] if tom else None], inc="se")
            tgb_prev = tgbe
            tab = None
            for half in range(2):
                gidx = 2 * cc + half
                hh = gidx // 5
                hp = hh // 2
                tab = b.mm(psC[:, :], v16(o_E4 + gidx * 128, 128),
                           v16(o_avT + hp * T + s, 512),
                           start=(half == 0), stop=(half == 1),
                           deps=[tom[-1] if tom else None] if half == 0 else None,
                           inc="mm" if half == 1 else None)
            tom.append(b.tt(v16(o_om + cc * 512, 512), v16(o_gbc, 512),
                            psC[:, :], ALU.mult, deps=[tgbe, tab], inc="ve"))
        for mc in range(KCH):
            par = mc % 2
            tdx = b.dma("sync", v32(o_x1s2 + par * 512, 512),
                        E["x1d"][128 * mc:128 * (mc + 1), s:s + 512],
                        deps=[ty[par]], inc="x0" if par == 0 else "x1")
            ptile = psB[:, 0:512] if par == 0 else psB[:, 512:1024]
            tmm = None
            for cc in range(20):
                tmm = b.mm(ptile, v16(o_wo + cc * 1024 + mc * 128, 128),
                           v16(o_om + cc * 512, 512),
                           start=(cc == 0), stop=(cc == 19),
                           deps=[tom[cc], ty[par]] + two[-1:],
                           inc="mm" if cc == 19 else None)
            tyv = b.tt(v32(o_yout + par * 512, 512), ptile,
                       v32(o_x1s2 + par * 512, 512), ALU.add,
                       deps=[tmm, tdx], inc="ve")
            ty[par] = b.dma("gpsimd", E["y"][128 * mc:128 * (mc + 1), s:s + 512],
                            v32(o_yout + par * 512, 512), deps=[tyv], inc="out")
            n_out += 1
        tomlast = tom
        if "om0" in taps and tc4 == 0:
            tap("om0", v16(o_om, 512), [tomlast[0]])
    b.final_wait([b.tok("out")])
    return ar32.peak, ar16.peak


TAP_SHAPES = {
    "xin": ([C, T], F16), "q": ([C, T], F16),
    "ktok": ([128, NT * C], F16), "vtok": ([128, NT * C], F16),
    "kvf": ([64, H_A * HD], F32), "att": ([C, T], F16),
    "x1": ([C, T], F32), "gates": ([128, NT * 80], F32),
    "sdg": ([128, NT * 40], F32), "md": ([128, NT * 40], F16),
    "kT": ([128, T], F16), "vext": ([128, NT * 520], F16),
    "qf": ([128, 4 * T], F16), "avT": ([128, 4 * T], F16),
    "gT": ([40, T], F16), "om0": ([128, 512], F16),
    "lnab": ([1, 2 * T], F16),
    "kvab": ([16, 2 * T], F16),
    "kvsums": ([64, T], F32),
    "lnsums": ([2, T], F32),
}


def build_nc(taps=(), stop=None):
    nc = bass.Bass("TRN2", target_bir_lowering=False, debug=False)
    es = ExitStack()
    d = {}
    for name, shape in [
        ("xT", [C, T]), ("qkvB", [24, KCH, 128, 128]), ("qkvb", [128, 24]),
        ("p1B", [KCH, KCH, 128, 128]), ("p1b", [128, KCH]),
        ("p2B", [KCH, KCH, 128, 128]), ("p2b", [128, KCH]),
        ("lngb", [128, 4]), ("tmpc", [128, KCH]), ("abT", [64, H_A * HD]),
        ("wkvB", [8, KCH, 128, 640]), ("wqB", [4, KCH, 128, 640]),
        ("wsdB", [KCH, 128, 80]), ("wo2B", [20, KCH, 128, 128]),
    ]:
        d[name] = nc.dram_tensor(name, shape, F32, kind="ExternalInput").ap()

    E = {"d": d}
    E["y"] = nc.dram_tensor("y", [C, T], F32, kind="ExternalOutput").ap()
    E["x1d"] = nc.dram_tensor("x1d", [C, T], F32).ap()
    E["agk_in"] = nc.dram_tensor("agk_in", [512 * 2048], F16).ap()
    E["agk_out"] = nc.dram_tensor("agk_out", [2 * 512 * 2048], F16).ap()
    E["agv_in"] = nc.dram_tensor("agv_in", [2048 * 520], F16).ap()
    E["agv_out"] = nc.dram_tensor("agv_out", [2 * 2048 * 520 + 2048], F16).ap()
    E["kv_in"] = nc.dram_tensor("kv_in", [64, H_A * HD], F32).ap()
    E["kv_out"] = nc.dram_tensor("kv_out", [64, H_A * HD], F32).ap()
    E["sums_d"] = nc.dram_tensor("sums_d", [8, T], F32).ap()
    E["tap_t"] = {}
    for t in taps:
        shp, dt_ = TAP_SHAPES[t]
        E["tap_t"][t] = nc.dram_tensor("tap_" + t, list(shp), dt_,
                                       kind="ExternalOutput").ap()

    E["A32"] = es.enter_context(nc.sbuf_tensor("af32", [128, NF32], F32))
    E["A16"] = es.enter_context(nc.sbuf_tensor("af16", [128, NF16], F16))
    E["psA"] = es.enter_context(nc.psum_tensor("psA", [128, 2048], F32))
    E["psB"] = es.enter_context(nc.psum_tensor("psB", [128, 1024], F32))
    E["psE"] = es.enter_context(nc.psum_tensor("psE", [128, 1024], F32))
    E["psC"] = E["psE"][:, 0:512]
    E["psD"] = E["psE"][:, 512:1024]
    sems = {s: es.enter_context(nc.semaphore(s)) for s in SEMS}

    ref_cnt = {}
    with nc.Block() as blk:
        def mk(role):
            def f(eng):
                cnt = {s: 0 for s in SEMS}
                bb = B(nc, role, sems, cnt)
                try:
                    _program(bb, E, set(taps), stop=stop)
                except Exception as ex:
                    if type(ex).__name__ != "_Stop":
                        raise
                if not ref_cnt:
                    ref_cnt.update(cnt)
                else:
                    assert ref_cnt == cnt, (role, ref_cnt, cnt)
            return f
        blk.tensor(mk("tensor"))
        blk.vector(mk("vector"))
        blk.scalar(mk("scalar"))
        blk.gpsimd(mk("gpsimd"))
        blk.sync(mk("sync"))
    es.close()
    return nc


def layout_inputs(inputs):
    f = lambda a: np.ascontiguousarray(np.asarray(a, dtype=np.float32))
    x = f(inputs["L0_attn_x"])
    qkv_w = f(inputs["L0_attn_qkv_w"])
    qkv_b = f(inputs["L0_attn_qkv_b"])
    p1_w, p1_b = f(inputs["L0_attn_p1_w"]), f(inputs["L0_attn_p1_b"])
    p2_w, p2_b = f(inputs["L0_attn_p2_w"]), f(inputs["L0_attn_p2_b"])
    kn_g, kn_b = f(inputs["L0_attn_kn_g"]), f(inputs["L0_attn_kn_b"])
    vn_g, vn_b = f(inputs["L0_attn_vn_g"]), f(inputs["L0_attn_vn_b"])
    temp = f(inputs["L0_attn_temperature"]).reshape(H_A)
    ab = f(inputs["L0_attn_attn_bias"])
    Wq, Wkv = f(inputs["L0_ffn_Wq"]), f(inputs["L0_ffn_Wkv"])
    Ws, Wd, Wo = f(inputs["L0_ffn_Ws"]), f(inputs["L0_ffn_Wd"]), f(inputs["L0_ffn_Wo"])

    blk = lambda w, mcs: np.ascontiguousarray(
        w.reshape(KCH, 128, mcs, 128).transpose(2, 0, 1, 3))
    shared = {
        "qkvB": blk(qkv_w.T, 24),
        "qkvb": np.ascontiguousarray(qkv_b.reshape(24, 128).T),
        "p1B": blk(p1_w.T, KCH),
        "p1b": np.ascontiguousarray(p1_b.reshape(KCH, 128).T),
        "p2B": blk(p2_w.T, KCH),
        "p2b": np.ascontiguousarray(p2_b.reshape(KCH, 128).T),
        "lngb": np.ascontiguousarray(np.stack(
            [np.tile(kn_g, 2), np.tile(kn_b, 2), np.tile(vn_g, 2),
             np.tile(vn_b, 2)], axis=1)),
        "tmpc": np.ascontiguousarray(np.repeat(temp, HD).reshape(KCH, 128).T),
        "abT": np.ascontiguousarray(ab.transpose(1, 0, 2).reshape(64, H_A * HD)),
        "wkvB": np.ascontiguousarray(
            Wkv.reshape(KCH, 128, 8, 640).transpose(2, 0, 1, 3)),
        "wqB": np.ascontiguousarray(
            Wq.reshape(KCH, 128, 4, 640).transpose(2, 0, 1, 3)),
        "wsdB": np.ascontiguousarray(
            np.concatenate([Ws, Wd], axis=1).reshape(KCH, 128, 80)),
        "wo2B": np.ascontiguousarray(
            np.tile(Wo.reshape(64, 5, 1024).transpose(1, 0, 2).reshape(320, 1024),
                    (8, 1)).reshape(20, 128, KCH, 128).transpose(0, 2, 1, 3)),
    }
    in_maps = []
    for c in range(8):
        bq, half = c // 2, c % 2
        m = dict(shared)
        m["xT"] = np.ascontiguousarray(x[bq, half * T:(half + 1) * T, :].T)
        in_maps.append(m)
    return in_maps


def kernel(**inputs):
    in_maps = layout_inputs(inputs)
    nc = build_nc()
    res = run_bass_kernel_spmd(nc, in_maps, core_ids=list(range(8)))
    out = np.empty((4, QFULL, C), dtype=np.float32)
    for c in range(8):
        bq, half = c // 2, c % 2
        out[bq, half * T:(half + 1) * T, :] = res.results[c]["y"].T
    return out


# version 07
# speedup vs baseline: 1.4502x; 1.0307x over previous
from contextlib import ExitStack

import numpy as np
import concourse.bass as bass
import concourse.mybir as mybir
from concourse.bass_utils import run_bass_kernel_spmd

F32 = mybir.dt.float32
F16 = mybir.dt.float16
AF = mybir.ActivationFunctionType
ALU = mybir.AluOpType

T = 2048
NT = T // 128
C = 1024
KCH = C // 128
QFULL = 4096
EPS = 1e-5
H_A, HD = 16, 64
H_F, E_F, DH = 8, 5, 64
SCALE_F = DH ** -0.5
KOFF = 0
VOFF = 512 * 2048
AGN = VOFF + 2048 * 520
PAIRS = [[0, 1], [2, 3], [4, 5], [6, 7]]
SEMS = ["wc", "w0", "w1", "w2", "w3", "wg0", "wg1", "x", "x0", "x1",
        "d0", "d1", "kh0", "kh1", "mm", "ve", "se", "gp", "dd", "cc",
        "out", "bar"]
NF32 = 8192
NF16 = 88064


class Alloc:
    def __init__(self, ncols, name):
        self.free = [(0, ncols)]
        self.ncols = ncols
        self.peak = 0
        self.name = name

    def alloc(self, n, tag=""):
        n = -(-n // 16) * 16
        for i, (s, ln) in enumerate(self.free):
            if ln >= n:
                if ln == n:
                    self.free.pop(i)
                else:
                    self.free[i] = (s + n, ln - n)
                self.peak = max(self.peak, self.ncols - sum(l for _, l in self.free))
                return s
        raise MemoryError(f"{self.name} OOM: {n} cols for {tag}; free={self.free}")

    def freeblk(self, s, n):
        n = -(-n // 16) * 16
        self.free.append((s, n))
        self.free.sort()
        out = []
        for blk in self.free:
            if out and out[-1][0] + out[-1][1] == blk[0]:
                out[-1] = (out[-1][0], out[-1][1] + blk[1])
            else:
                out.append(list(blk))
        self.free = [tuple(xx) for xx in out]


class B:

    def __init__(self, nc, role, sems, cnt):
        self.nc, self.role, self.sems, self.cnt = nc, role, sems, cnt
        self.wm = {}
        self.gp_chain = None

    def tok(self, sem):
        return (sem, self.cnt[sem])

    DMA_SEMS = ("wc", "w0", "w1", "w2", "w3", "wg0", "wg1", "x", "x0",
                "x1", "d0", "d1", "kh0", "kh1", "dd", "out")

    def _wait(self, eng, deps):
        flat = []
        for dd in deps or ():
            if isinstance(dd, list):
                flat.extend(dd)
            else:
                flat.append(dd)
        for dd in flat:
            if dd is None:
                continue
            sem, val = dd
            if sem in self.DMA_SEMS:
                val = self.cnt[sem]
            if val <= 0 or self.wm.get(sem, 0) >= val:
                continue
            eng.wait_ge(self.sems[sem], val)
            self.wm[sem] = val

    def _run(self, owner, fn, deps, inc, n):
        ins = None
        if self.role == owner:
            self._wait(getattr(self.nc, owner), deps)
            ins = fn()
        if inc is not None:
            self.cnt[inc] += n
            if ins is not None:
                ins.then_inc(self.sems[inc], n)
            return (inc, self.cnt[inc])
        return None

    def dma(self, q, out, in_, deps=None, inc=None):
        return self._run(q, lambda: getattr(self.nc, q).dma_start(out=out, in_=in_),
                         deps, inc, 16)

    def mm(self, out, lhsT, rhs, start, stop, deps=None, inc=None, tp=None):
        return self._run("tensor",
                         lambda: self.nc.tensor.matmul(out, lhsT, rhs, start=start,
                                                       stop=stop, tile_position=tp),
                         deps, inc, 1)

    def trans(self, out, in_, ident, deps=None, inc=None):
        return self._run("tensor", lambda: self.nc.tensor.transpose(out, in_, ident),
                         deps, inc, 1)

    def act(self, out, in_, func, bias=None, scale=None, accum_out=None,
            deps=None, inc=None):
        kw = {}
        if bias is not None:
            kw["bias"] = bias
        if scale is not None:
            kw["scale"] = scale
        if accum_out is not None:
            kw["accum_out"] = accum_out
        return self._run("scalar",
                         lambda: self.nc.scalar.activation(out, in_, func, **kw),
                         deps, inc, 1)

    def tt(self, out, a, c, op, deps=None, inc=None, eng="vector"):
        return self._run(eng, lambda: getattr(self.nc, eng).tensor_tensor(out, a, c, op),
                         deps, inc, 1)

    def ts(self, out, in0, s1, s2, op0, op1=None, deps=None, inc=None, eng="vector"):
        def f():
            e = getattr(self.nc, eng)
            if op1 is None:
                return e.tensor_scalar(out, in0, s1, None, op0)
            return e.tensor_scalar(out, in0, s1, s2, op0, op1)
        return self._run(eng, f, deps, inc, 1)

    def stt(self, out, in0, scalar, in1, op0, op1, deps=None, inc=None, eng="vector"):
        return self._run(eng,
                         lambda: getattr(self.nc, eng).scalar_tensor_tensor(
                             out, in0, scalar, in1, op0, op1),
                         deps, inc, 1)

    def copy(self, out, in_, deps=None, inc=None, eng="vector"):
        return self._run(eng, lambda: getattr(self.nc, eng).tensor_copy(out, in_),
                         deps, inc, 1)

    def recip(self, out, in_, deps=None, inc=None):
        return self._run("vector", lambda: self.nc.vector.reciprocal(out, in_),
                         deps, inc, 1)

    def memset(self, ap, v, deps=None, inc=None):
        deps = list(deps or []) + [self.gp_chain]
        tok = self._run("gpsimd", lambda: self.nc.gpsimd.memset(ap, v), deps,
                        "gp", 1)
        self.gp_chain = tok
        return tok

    def affine(self, out, in_, pattern, cmp, fill, base=None, chan=None,
               deps=None, inc=None):
        deps = list(deps or []) + [self.gp_chain]
        tok = self._run("gpsimd",
                        lambda: self.nc.gpsimd.affine_select(
                            out=out, in_=in_, pattern=pattern, compare_op=cmp,
                            fill=fill, base=base, channel_multiplier=chan),
                        deps, "gp", 1)
        self.gp_chain = tok
        return tok

    def coll(self, kind, op, groups, ins, outs, deps=None, inc=None):
        return self._run("gpsimd",
                         lambda: self.nc.gpsimd.collective_compute(
                             kind, op, replica_groups=groups, ins=ins, outs=outs),
                         deps, inc, 1)

    def barrier(self):
        self.cnt["bar"] += 4
        tgt = self.cnt["bar"]
        if self.role in ("tensor", "vector", "scalar", "gpsimd"):
            eng = getattr(self.nc, self.role)
            self._wait(eng, [(s, self.cnt[s]) for s in self.DMA_SEMS])
            eng.drain().then_inc(self.sems["bar"], 1)
            eng.wait_ge(self.sems["bar"], tgt)
            self.wm["bar"] = tgt
        elif self.role == "sync":
            eng = self.nc.sync
            self._wait(eng, [(s, self.cnt[s]) for s in self.DMA_SEMS])
            eng.wait_ge(self.sems["bar"], tgt)
            self.wm["bar"] = tgt
        return ("bar", tgt)

    def final_wait(self, deps):
        if self.role == "gpsimd":
            self._wait(self.nc.gpsimd, deps)


def _program(b, E, taps, stop=None):
    d = E["d"]
    A32, A16 = E["A32"], E["A16"]
    psA, psB, psC, psD = E["psA"], E["psB"], E["psC"], E["psD"]
    psE = E["psE"]
    tap_t = E["tap_t"]
    ar32 = Alloc(NF32, "af32")
    ar16 = Alloc(NF16, "af16")

    def v32(off, w, p0=0, pn=128):
        return A32[p0:p0 + pn, off:off + w]

    def v16(off, w, p0=0, pn=128):
        return A16[p0:p0 + pn, off:off + w]

    def tap(name, ap, deps):
        if name in taps:
            b.dma("sync", tap_t[name], ap, deps=deps, inc="out")

    class _Stop(Exception):
        pass

    def ckpt(name):
        if stop == name:
            b.barrier()
            b.final_wait([b.tok("out")])
            raise _Stop()

    o_id = ar16.alloc(128, "ident")
    o_ones_c = ar16.alloc(16, "ones col")
    o_ones_r = ar16.alloc(128, "ones row")
    o_Ekv = ar16.alloc(8 * 128, "Ekv")
    o_E6 = ar16.alloc(8 * 128, "E6")
    o_lngb = ar32.alloc(4, "lngb")
    o_geff = ar32.alloc(KCH, "geff")
    o_beff = ar32.alloc(KCH, "beff")
    o_tmpc = ar32.alloc(KCH, "tmpc")
    o_qkvb = ar32.alloc(24, "qkvb")
    o_p1b = ar32.alloc(KCH, "p1b")
    o_p2b = ar32.alloc(KCH, "p2b")
    o_abT = ar32.alloc(H_A * HD, "abT")
    ident = v16(o_id, 128)

    tw = [
        b.dma("sync", v32(o_lngb, 4), d["lngb"][:, :], inc="wc"),
        b.dma("sync", v32(o_tmpc, KCH), d["tmpc"][:, :], inc="wc"),
        b.dma("sync", v32(o_qkvb, 24), d["qkvb"][:, :], inc="wc"),
        b.dma("sync", v32(o_p1b, KCH), d["p1b"][:, :], inc="wc"),
        b.dma("sync", v32(o_p2b, KCH), d["p2b"][:, :], inc="wc"),
        b.dma("sync", v32(o_abT, H_A * HD, 0, 64), d["abT"][:, :], inc="wc"),
    ]

    b.memset(ident[:, :], 0.0)
    b.affine(ident[:, :], ident[:, :], [[-1, 128]], ALU.not_equal, 1.0, base=0, chan=1)
    b.memset(v16(o_ones_c, 1), 1.0)
    b.memset(v16(o_ones_r, 128, 0, 1), 1.0)
    b.memset(v16(o_Ekv, 8 * 128), 0.0)
    for j in range(8):
        base = o_Ekv + j * 128
        b.memset(v16(base + 2 * j, 1, 0, 64), 1.0)
        b.memset(v16(base + 2 * j + 1, 1, 64, 64), 1.0)
        b.memset(v16(base + 64 + 32 + 2 * j, 1, 0, 64), 1.0)
        b.memset(v16(base + 64 + 32 + 2 * j + 1, 1, 64, 64), 1.0)
    b.memset(v16(o_E6, 8 * 128), 1.0)
    for j in range(8):
        sub = v16(o_E6 + j * 128, 128, 0, 16)
        b.affine(sub, sub, [[1, 128]], ALU.is_ge, 0.0, base=128 * j, chan=-64)
        b.affine(sub, sub, [[-1, 128]], ALU.is_ge, 0.0, base=63 - 128 * j, chan=64)
    tsetup = b.tok("gp")

    tgeff = b.ts(v32(o_geff, KCH), v32(o_tmpc, KCH), v32(o_lngb, 1), None,
                 ALU.mult, deps=[tw[0], tw[1]], inc="ve")
    tbeff = b.ts(v32(o_beff, KCH), v32(o_tmpc, KCH), v32(o_lngb + 1, 1), None,
                 ALU.mult, deps=[tw[0], tw[1]], inc="ve")

    def emit_ln(src_dram, o_out16, tapon=False):
        o_stg = ar32.alloc(2 * T, "ln stage")
        o_cst = ar16.alloc(4 * T, "ln cast")
        tmm = [None, None]
        tcst = [None, None, None, None]
        tdxs = [None, None]
        for k in range(KCH):
            par = k % 2
            tdx = b.dma("sync", v32(o_stg + par * T, T),
                        src_dram[128 * k:128 * (k + 1), :],
                        deps=[tcst[par * 2], tcst[par * 2 + 1]],
                        inc="x0" if par == 0 else "x1")
            xc = v16(o_cst + par * 2 * T, T)
            xq = v16(o_cst + par * 2 * T + T, T)
            tc1 = b.copy(xc, v32(o_stg + par * T, T), deps=[tdx, tmm[par]], inc="ve")
            tc2 = b.act(xq, v32(o_stg + par * T, T), AF.Square, deps=[tdx, tmm[par]],
                        inc="se")
            tcst[par * 2] = tc1
            tcst[par * 2 + 1] = tc2
            for nt in range(4):
                s = nt * 512
                b.mm(psA[0:1, s:s + 512], v16(o_ones_c, 1), xc[:, s:s + 512],
                     start=(k == 0), stop=(k == KCH - 1), deps=[tc1, tsetup])
                t_ = b.mm(psA[32:33, s:s + 512], v16(o_ones_c, 1),
                          xq[:, s:s + 512],
                          start=(k == 0), stop=(k == KCH - 1), deps=[tc2],
                          inc="mm" if nt == 3 else None)
                if nt == 3:
                    tmm[par] = t_
        ar16.freeblk(o_cst, 4 * T)
        tstat = tmm[(KCH - 1) % 2]
        o_st = ar16.alloc(8 * 1024, "ln st")
        def q32(i, w, p0, pn):
            return A16[p0:p0 + pn, o_st + i * 2048:o_st + i * 2048 + 2 * w]\
                .bitcast(F32)
        o_ab = ar16.alloc(2 * T, "ln ab")
        Arow = v16(o_ab, T, 0, 1)
        Brow = v16(o_ab + T, T, 0, 1)
        tA = tB = None
        for hf in range(2):
            sl = hf * 1024
            mu = q32(0, 1024, 0, 1)
            var = q32(1, 1024, 0, 1)
            r = q32(2, 1024, 0, 1)
            tp_ = q32(3, 1024, 0, 1)
            dd = [tstat, tB]
            t1 = b.ts(mu, psA[0:1, sl:sl + 1024], 1.0 / C, None, ALU.mult,
                      deps=dd, inc="ve")
            t3 = b.act(tp_, mu, AF.Square, deps=[t1, tB], inc="se")
            t2 = b.ts(var, psA[32:33, sl:sl + 1024], 1.0 / C, EPS,
                      ALU.mult, ALU.add, deps=dd, inc="ve")
            t4 = b.tt(var, var, tp_, ALU.subtract, deps=[t2, t3], inc="ve")
            t6 = b.act(r, var, AF.Sqrt, deps=[t4], inc="se")
            t7 = b.recip(r, r, deps=[t6], inc="ve")
            t8 = b.act(tp_, r, AF.Square, deps=[t7], inc="se")
            t9 = b.tt(tp_, tp_, var, ALU.mult, deps=[t8], inc="ve")
            t10 = b.ts(tp_, tp_, -0.5, 1.5, ALU.mult, ALU.add, deps=[t9], inc="ve")
            t11 = b.tt(r, r, tp_, ALU.mult, deps=[t10], inc="ve")
            tA = b.copy(Arow[:, sl:sl + 1024], r, deps=[t11], inc="ve")
            t12 = b.tt(tp_, mu, r, ALU.mult, deps=[t11], inc="ve")
            tB = b.ts(Brow[:, sl:sl + 1024], tp_, -1.0, None, ALU.mult,
                      deps=[t12], inc="ve")
        if tapon and "lnab" in taps:
            tk_ = b.dma("sync", tap_t["lnab"], A16[0:1, o_ab:o_ab + 2 * T],
                        deps=[tA, tB], inc="out")
            b.final_wait([tk_])
        o_bsb = ar32.alloc(T, "ln bbc")
        tba = []
        tbb = []
        for nt in range(4):
            s = nt * 512
            tba.append(b.mm(psA[:, s:s + 512], v16(o_ones_r, 128, 0, 1),
                            Arow[:, s:s + 512], True, True,
                            deps=[tA, tstat], inc="mm"))
            pdst = psB[:, s:s + 512] if nt < 2 else psE[:, s - 1024:s - 512]
            tbb.append(b.mm(pdst, v16(o_ones_r, 128, 0, 1), Brow[:, s:s + 512],
                            True, True, deps=[tB], inc="mm"))
        tb1 = b.copy(v32(o_bsb, 1024), psB[:, 0:1024], deps=tbb[:2], inc="ve")
        tb2 = b.copy(v32(o_bsb + 1024, 1024), psE[:, 0:1024], deps=tbb[2:], inc="ve")
        o_t = ar16.alloc(2 * T, "ln t")
        tprev = [[tcst[0], tcst[1]], [tcst[2], tcst[3]]]
        for k in range(KCH):
            par = k % 2
            tdx = b.dma("sync", v32(o_stg + par * T, T),
                        src_dram[128 * k:128 * (k + 1), :],
                        deps=[tprev[par]],
                        inc="x0" if par == 0 else "x1")
            u1 = b.tt(v16(o_t + par * T, T), v32(o_stg + par * T, T),
                      psA[:, 0:T], ALU.mult, deps=[tdx, tprev[par]] + tba,
                      inc="ve")
            u2 = b.tt(v16(o_out16 + k * T, T), v16(o_t + par * T, T),
                      v32(o_bsb, T), ALU.add, deps=[u1, tb1, tb2], inc="ve")
            tprev[par] = u1
        ar16.freeblk(o_t, 2 * T)
        ar32.freeblk(o_bsb, T)
        ar16.freeblk(o_st, 8 * 1024)
        ar32.freeblk(o_stg, 2 * T)
        ar16.freeblk(o_ab, 2 * T)
        b.barrier()

    o_xin = ar16.alloc(KCH * T, "xin")
    emit_ln(d["xT"], o_xin, tapon=True)
    tap("xin", A16[:, o_xin:o_xin + KCH * T], [b.tok("ve")])
    xin_ch = lambda k, c0, w: v16(o_xin + k * T + c0, w)
    ckpt("S1")

    o_wstg = ar32.alloc(4 * 128, "wstg")
    o_wblk = ar16.alloc(4 * 128, "wblk")
    wpar_tok = [None, None, None, None]
    WSEM = ("w0", "w1", "w2", "w3")

    def wblock(src_ap, par):
        s32 = v32(o_wstg + par * 128, 128)
        s16 = v16(o_wblk + par * 128, 128)
        td = b.dma("sync", s32, src_ap, deps=[wpar_tok[par]], inc=WSEM[par])
        tc = b.copy(s16, s32, deps=[td], inc="ve")
        return s16, tc

    tile_tok = [None] * 4

    def mm_weight(blkB, src_ch, out_evac):
        tiles = [psB[:, 0:512], psB[:, 512:1024], psC[:, :], psD[:, :]]
        tmm = None
        for kc in range(KCH):
            par = kc % 4
            w16, tcst = wblock(blkB[kc], par)
            for nt in range(4):
                tmm = b.mm(tiles[nt], w16, src_ch(kc, nt * 512, 512),
                           start=(kc == 0), stop=(kc == KCH - 1),
                           deps=[tcst, tile_tok[nt]],
                           inc="mm" if nt == 3 else None)
            wpar_tok[par] = tmm
        toks = out_evac(tiles, tmm)
        for nt in range(4):
            tile_tok[nt] = toks[nt]

    o_ktok = ar16.alloc(NT * C, "ktok")
    o_vtok = ar16.alloc(NT * C, "vtok")
    kt_ch = lambda tcn, c0, w: v16(o_ktok + tcn * C + c0, w)
    vt_ch = lambda tcn, c0, w: v16(o_vtok + tcn * C + c0, w)

    def kv_group(mc0, o_tok, gcol_fn, bcol_fn):
        o_kv = ar16.alloc(KCH * T, "kvT")
        o_sq = ar16.alloc(T, "sq")
        tstat = None
        tsq = None
        for j in range(KCH):
            mcx = mc0 + j

            def evac(tiles, tmm, j=j, mcx=mcx):
                toks = []
                for nt in range(4):
                    te = b.act(v16(o_kv + j * T + nt * 512, 512), tiles[nt],
                               AF.Identity, bias=v32(o_qkvb + mcx, 1),
                               deps=[tmm, tw[2]], inc="se")
                    toks.append(te)
                return toks

            mm_weight(d["qkvB"][mcx], xin_ch, evac)
            tevs = list(tile_tok)
            tsq = b.act(v16(o_sq, T), v16(o_kv + j * T, T), AF.Square,
                        deps=tevs + [tstat], inc="se")
            Ev = v16(o_Ekv + j * 128, 64)
            Eq = v16(o_Ekv + j * 128 + 64, 64)
            for nt in range(4):
                s = nt * 512
                b.mm(psA[0:64, s:s + 512], Ev, v16(o_kv + j * T + s, 512),
                     start=(j == 0), stop=False, deps=tevs + [tsetup])
                tstat = b.mm(psA[0:64, s:s + 512], Eq, v16(o_sq + s, 512),
                             start=False, stop=(j == KCH - 1), deps=[tsq],
                             inc="mm" if nt == 3 else None)
        ar16.freeblk(o_sq, T)
        if ("kvsums" in taps) and mc0 == 8:
            o_sdb = ar32.alloc(T, "kvs dbg")
            td1 = b.copy(v32(o_sdb, T, 0, 64), psA[0:64, 0:T], deps=[tstat], inc="ve")
            b.dma("sync", tap_t["kvsums"], v32(o_sdb, T, 0, 64), deps=[td1],
                  inc="out")
            ar32.freeblk(o_sdb, T)
        o_st = ar16.alloc(8 * 1024, "kv st")
        def q32(i, w):
            return A16[0:16, o_st + i * 2048:o_st + i * 2048 + 2 * w].bitcast(F32)
        o_ab = ar16.alloc(2 * T, "kv ab")
        Ar = v16(o_ab, T, 0, 16)
        Br = v16(o_ab + T, T, 0, 16)
        tA = tB = None
        for hf in range(2):
            sl = hf * 1024
            mu = q32(0, 1024)
            var = q32(1, 1024)
            r = q32(2, 1024)
            tp_ = q32(3, 1024)
            dd = [tstat, tB]
            t1 = b.ts(mu, psA[0:16, sl:sl + 1024], 1.0 / HD, None, ALU.mult,
                      deps=dd, inc="ve")
            t3 = b.act(tp_, mu, AF.Square, deps=[t1, tB], inc="se")
            t2 = b.ts(var, psA[32:48, sl:sl + 1024], 1.0 / HD, EPS,
                      ALU.mult, ALU.add, deps=dd, inc="ve")
            t4 = b.tt(var, var, tp_, ALU.subtract, deps=[t2, t3], inc="ve")
            t6 = b.act(r, var, AF.Sqrt, deps=[t4], inc="se")
            t7 = b.recip(r, r, deps=[t6], inc="ve")
            t8 = b.act(tp_, r, AF.Square, deps=[t7], inc="se")
            t9 = b.tt(tp_, tp_, var, ALU.mult, deps=[t8], inc="ve")
            t10 = b.ts(tp_, tp_, -0.5, 1.5, ALU.mult, ALU.add, deps=[t9], inc="ve")
            t11 = b.tt(r, r, tp_, ALU.mult, deps=[t10], inc="ve")
            tA = b.copy(Ar[:, sl:sl + 1024], r, deps=[t11], inc="ve")
            t12 = b.tt(tp_, mu, r, ALU.mult, deps=[t11], inc="ve")
            tB = b.ts(Br[:, sl:sl + 1024], tp_, -1.0, None, ALU.mult,
                      deps=[t12], inc="ve")
        ar16.freeblk(o_st, 8 * 1024)
        if ("kvab" in taps) and mc0 == 8:
            b.dma("sync", tap_t["kvab"], A16[0:16, o_ab:o_ab + 2 * T],
                  deps=[tA, tB], inc="out")
        b.barrier()
        o_t = ar16.alloc(512, "kv ap t")
        tu3 = []
        tev = []
        for j in range(KCH):
            E6 = v16(o_E6 + j * 128, 128, 0, 16)
            for nt in range(4):
                s = nt * 512
                ta = b.mm(psB[:, 0:512], E6, Ar[:, s:s + 512], True, True,
                          deps=[tA] + tu3, inc="mm")
                tb = b.mm(psB[:, 512:1024], E6, Br[:, s:s + 512], True, True,
                          deps=[tB], inc="mm")
                tm = v16(o_t, 512)
                u1 = b.tt(tm, v16(o_kv + j * T + s, 512), psB[:, 0:512], ALU.mult,
                          deps=[ta] + tev, inc="ve")
                u2 = b.tt(tm, tm, psB[:, 512:1024], ALU.add, deps=[u1, tb], inc="ve")
                u3a = b.ts(tm, tm, gcol_fn(j), None, ALU.mult,
                           deps=[u2, tgeff, tw[0]], inc="ve")
                u3 = b.ts(tm, tm, bcol_fn(j), None, ALU.add,
                          deps=[u3a, tbeff], inc="ve")
                tu3 = [u3]
                ttr = None
                for q4 in range(4):
                    ttr = b.trans(psC[:, 64 * q4:64 * (q4 + 1)].bitcast(F16),
                                  tm[:, 128 * q4:128 * (q4 + 1)], ident,
                                  deps=[u3] + tev, inc="mm" if q4 == 3 else None)
                tev = []
                for q4 in range(4):
                    tcn = nt * 4 + q4
                    te = b.copy(A16[:, o_tok + tcn * C + j * 128:
                                    o_tok + tcn * C + j * 128 + 128],
                                psC[:, 64 * q4:64 * (q4 + 1)].bitcast(F16),
                                deps=[ttr], inc="ve")
                    tev = [te]
                b.barrier()
        ar16.freeblk(o_t, 512)
        ar16.freeblk(o_kv, KCH * T)
        ar16.freeblk(o_ab, 2 * T)
        b.barrier()

    kv_group(8, o_ktok, lambda j: v32(o_geff + j, 1),
             lambda j: v32(o_beff + j, 1))
    kv_group(16, o_vtok, lambda j: v32(o_lngb + 2, 1),
             lambda j: v32(o_lngb + 3, 1))
    tap("ktok", A16[:, o_ktok:o_ktok + NT * C], [b.tok("ve")])
    tap("vtok", A16[:, o_vtok:o_vtok + NT * C], [b.tok("ve")])

    o_q = ar16.alloc(KCH * T, "qT")
    for j in range(KCH):
        def evq(tiles, tmm, j=j):
            toks = []
            for nt in range(4):
                te = b.act(v16(o_q + j * T + nt * 512, 512), tiles[nt], AF.Identity,
                           bias=v32(o_qkvb + j, 1), deps=[tmm, tw[2]], inc="se")
                toks.append(te)
            return toks
        mm_weight(d["qkvB"][j], xin_ch, evq)
    b.barrier()
    tap("q", A16[:, o_q:o_q + KCH * T], [b.tok("se")])
    ckpt("S2")

    tkv = None
    for h in range(H_A):
        for tcn in range(NT):
            tkv = b.mm(psA[0:64, h * 64:h * 64 + 64], kt_ch(tcn, h * 64, 64),
                       vt_ch(tcn, h * 64, 64), start=(tcn == 0), stop=(tcn == NT - 1),
                       inc="mm" if (tcn == NT - 1 and h == H_A - 1) else None)
    o_kvp = ar32.alloc(H_A * HD, "kv partials")
    tkevac = b.ts(v32(o_kvp, H_A * HD, 0, 64), psA[0:64, 0:H_A * HD],
                  1.0 / QFULL, None, ALU.mult, deps=[tkv], inc="ve")
    tout = b.dma("gpsimd", E["kv_in"][:, :], v32(o_kvp, H_A * HD, 0, 64),
                 deps=[tkevac], inc="dd")
    tcc = b.coll("AllReduce", ALU.add, PAIRS, [E["kv_in"][:, :]],
                 [E["kv_out"][:, :]], deps=[tout], inc="cc")
    tin = b.dma("gpsimd", v32(o_kvp, H_A * HD, 0, 64), E["kv_out"][:, :],
                deps=[tcc], inc="dd")
    tfin = b.tt(v32(o_kvp, H_A * HD, 0, 64), v32(o_kvp, H_A * HD, 0, 64),
                v32(o_abT, H_A * HD, 0, 64), ALU.add, deps=[tin, tw[5]], inc="ve")
    tap("kvf", A32[0:64, o_kvp:o_kvp + H_A * HD], [tfin])
    o_kvbd = ar16.alloc(8 * 128, "kv bd")
    tms = b.memset(v16(o_kvbd, 8 * 128), 0.0, deps=[], inc="gp")
    tbd = []
    for hp in range(8):
        base = o_kvbd + hp * 128
        b1 = b.copy(v16(base, 64, 0, 64), v32(o_kvp + 2 * hp * 64, 64, 0, 64),
                    deps=[tfin, tms], inc="ve")
        b2 = b.copy(v16(base + 64, 64, 64, 64),
                    v32(o_kvp + (2 * hp + 1) * 64, 64, 0, 64),
                    deps=[tfin, tms], inc="ve")
        tbd += [b1, b2]
    b.barrier()
    ar16.freeblk(o_ktok, NT * C)
    ar16.freeblk(o_vtok, NT * C)
    ar32.freeblk(o_kvp, H_A * HD)
    ckpt("S3")

    o_att = ar16.alloc(KCH * T, "att")
    tiles4 = [psB[:, 0:512], psB[:, 512:1024], psC[:, :], psD[:, :]]
    ttok4 = [None] * 4
    for hp in range(8):
        tmm = None
        for nt in range(4):
            tmm = b.mm(tiles4[nt], v16(o_kvbd + hp * 128, 128),
                       v16(o_q + hp * T + nt * 512, 512), True, True,
                       deps=[ttok4[nt]] + tbd, inc="mm" if nt == 3 else None)
        for nt in range(4):
            te = b.tt(v16(o_att + hp * T + nt * 512, 512), tiles4[nt],
                      xin_ch(hp, nt * 512, 512), ALU.add, deps=[tmm], inc="ve")
            ttok4[nt] = te
    b.barrier()
    ar16.freeblk(o_q, KCH * T)
    ar16.freeblk(o_kvbd, 8 * 128)
    tap("att", A16[:, o_att:o_att + KCH * T], [b.tok("ve")])
    ckpt("S4")

    o_h1 = ar16.alloc(KCH * T, "h1")
    att_ch = lambda k, c0, w: v16(o_att + k * T + c0, w)
    for mc in range(KCH):
        def ev1(tiles, tmm, mc=mc):
            toks = []
            for nt in range(4):
                te = b.act(v16(o_h1 + mc * T + nt * 512, 512), tiles[nt], AF.Gelu,
                           bias=v32(o_p1b + mc, 1), deps=[tmm, tw[3]], inc="se")
                toks.append(te)
            return toks
        mm_weight(d["p1B"][mc], att_ch, ev1)
    b.barrier()
    ar16.freeblk(o_att, KCH * T)

    h1_ch = lambda k, c0, w: v16(o_h1 + k * T + c0, w)
    o_xs = ar32.alloc(2 * 512, "x slc")
    o_x1s = ar32.alloc(2 * 512, "x1 slc")
    txs = [None, None]
    for mc in range(KCH):
        def ev2(tiles, tmm, mc=mc):
            toks = []
            for nt in range(4):
                par = nt % 2
                s = nt * 512
                tdx = b.dma("sync", v32(o_xs + par * 512, 512),
                            d["xT"][128 * mc:128 * (mc + 1), s:s + 512],
                            deps=[txs[par]], inc="x0" if par == 0 else "x1")
                ta = b.act(v32(o_x1s + par * 512, 512), tiles[nt], AF.Identity,
                           bias=v32(o_p2b + mc, 1), deps=[tmm, tw[4], txs[par]],
                           inc="se")
                tb = b.tt(v32(o_x1s + par * 512, 512), v32(o_x1s + par * 512, 512),
                          xin_ch(mc, s, 512), ALU.add, deps=[ta], inc="ve")
                tc2 = b.tt(v32(o_x1s + par * 512, 512), v32(o_x1s + par * 512, 512),
                           v32(o_xs + par * 512, 512), ALU.add, deps=[tb, tdx],
                           inc="ve")
                td = b.dma("gpsimd", E["x1d"][128 * mc:128 * (mc + 1), s:s + 512],
                           v32(o_x1s + par * 512, 512), deps=[tc2],
                           inc="d0" if par == 0 else "d1")
                txs[par] = td
                toks.append(tc2)
            return toks
        mm_weight(d["p2B"][mc], h1_ch, ev2)
    b.barrier()
    ar16.freeblk(o_h1, KCH * T)
    ar32.freeblk(o_xs, 2 * 512)
    ar32.freeblk(o_x1s, 2 * 512)
    ar16.freeblk(o_xin, KCH * T)
    tap("x1", E["x1d"][:, :], [b.tok("d0"), b.tok("d1")])
    ckpt("S6")

    o_xin1 = ar16.alloc(KCH * T, "xin1")
    o_wo = ar16.alloc(20 * 1024, "wo2 full")
    emit_ln(E["x1d"], o_xin1)
    xin1_ch = lambda k, c0, w: v16(o_xin1 + k * T + c0, w)

    o_E3 = ar16.alloc(20 * 128, "E3")
    o_E7 = ar16.alloc(48, "E7")
    b.memset(v16(o_E3, 20 * 128), 1.0)
    for cc in range(20):
        sub = v16(o_E3 + cc * 128, 128, 0, 40)
        b.affine(sub, sub, [[1, 128]], ALU.is_ge, 0.0, base=128 * cc, chan=-64)
        b.affine(sub, sub, [[-1, 128]], ALU.is_ge, 0.0, base=63 - 128 * cc, chan=64)
    b.memset(v16(o_E7, 48), 1.0)
    sub = v16(o_E7, 40, 0, 8)
    b.affine(sub, sub, [[1, 40]], ALU.is_ge, 0.0, base=0, chan=-5)
    b.affine(sub, sub, [[-1, 40]], ALU.is_ge, 0.0, base=4, chan=5)

    o_wsd = ar16.alloc(KCH * 80, "wsd")
    o_wsd32 = ar32.alloc(128, "wsd stg")
    twsd = []
    tpv = None
    for kc in range(KCH):
        tdw = b.dma("sync", v32(o_wsd32, 80), d["wsdB"][kc], deps=[tpv], inc="x")
        tpv = b.copy(v16(o_wsd + kc * 80, 80), v32(o_wsd32, 80), deps=[tdw], inc="ve")
        twsd.append(tpv)
    o_gate = ar32.alloc(NT * 80, "gates")
    tg = None
    tsg = None
    for tcn in range(NT):
        for kc in range(KCH):
            tg = b.mm(psC[:, 0:80], xin1_ch(kc, tcn * 128, 128),
                      v16(o_wsd + kc * 80, 80), start=(kc == 0), stop=(kc == KCH - 1),
                      deps=[twsd[kc], tsg], inc="mm" if kc == KCH - 1 else None)
        tsg = b.act(v32(o_gate + tcn * 80, 80), psC[:, 0:80], AF.Sigmoid,
                    deps=[tg], inc="se")
    b.barrier()
    ar16.freeblk(o_wsd, KCH * 80)
    ar32.freeblk(o_wsd32, 128)
    tap("gates", A32[:, o_gate:o_gate + NT * 80], [b.tok("se")])

    NM = NT * 8
    o_med = ar32.alloc(8 * NM, "med")
    o_msk = ar16.alloc(2 * NT * 40, "masks")
    o_sg = ar32.alloc(2 * NT * 40, "gated")
    gateR = A32[:, o_gate:o_gate + NT * 80].rearrange(
        "p (tcn hl h e) -> p tcn hl h e", tcn=NT, hl=2, h=8, e=5)
    tmask = []
    for hl in range(2):
        ex = [gateR[:, :, hl, :, e] for e in range(5)]
        m = [v32(o_med + i * NM, NM).rearrange("p (tcn h) -> p tcn h", tcn=NT, h=8)
             for i in range(8)]
        d0 = [b.tok("se")]
        u = []
        u.append(b.tt(m[0], ex[0], ex[1], ALU.min, deps=d0, inc="ve"))
        u.append(b.tt(m[1], ex[0], ex[1], ALU.max, deps=d0, inc="ve"))
        u.append(b.tt(m[2], ex[2], ex[3], ALU.min, deps=d0, inc="ve"))
        u.append(b.tt(m[3], ex[2], ex[3], ALU.max, deps=d0, inc="ve"))
        f_ = b.tt(m[4], m[0], m[2], ALU.max, deps=u[:3], inc="ve")
        g_ = b.tt(m[5], m[1], m[3], ALU.min, deps=u, inc="ve")
        q1 = b.tt(m[6], ex[4], m[4], ALU.min, deps=[f_], inc="ve")
        q2 = b.tt(m[7], ex[4], m[4], ALU.max, deps=[f_], inc="ve")
        q3 = b.tt(m[7], m[7], m[5], ALU.min, deps=[q2, g_], inc="ve")
        tmed = b.tt(m[0], m[6], m[7], ALU.max, deps=[q1, q3], inc="ve")
        mskR = A16[:, o_msk + hl * NT * 40:o_msk + (hl + 1) * NT * 40].rearrange(
            "p (tcn h e) -> p tcn h e", tcn=NT, h=8, e=5)
        sgR = A32[:, o_sg + hl * NT * 40:o_sg + (hl + 1) * NT * 40].rearrange(
            "p (tcn h e) -> p tcn h e", tcn=NT, h=8, e=5)
        tms_ = []
        for e in range(5):
            tms_.append(b.tt(mskR[:, :, :, e], ex[e], m[0], ALU.is_ge,
                             deps=[tmed], inc="ve"))
        tgt = b.tt(sgR, gateR[:, :, hl, :, :], mskR, ALU.mult, deps=tms_, inc="ve")
        tmask.append(tgt)
    ar32.freeblk(o_med, 8 * NM)
    ar32.freeblk(o_gate, NT * 80)
    b.barrier()
    tap("md", A16[:, o_msk + NT * 40:o_msk + 2 * NT * 40], [b.tok("ve")])
    tap("sdg", A32[:, o_sg + NT * 40:o_sg + 2 * NT * 40], [b.tok("ve")])
    ckpt("S8")

    def msk_col(hl, tcn, h, e):
        return v16(o_msk + hl * NT * 40 + tcn * 40 + h * 5 + e, 1)

    def sg_col(hl, tcn, h, e):
        return v32(o_sg + hl * NT * 40 + tcn * 40 + h * 5 + e, 1)

    o_ktf = ar16.alloc(NT * 512, "ktokF")
    o_vex = ar16.alloc(NT * 520, "vext")
    o_wg = ar16.alloc(2 * KCH * 640, "w group")
    o_wg32 = ar32.alloc(2 * 640, "wg stage")
    o_g5 = ar16.alloc(NT * 640, "group out")
    twg_par = [None, None]
    o_kT = ar16.alloc(T, "kT")
    tsend = []

    tones = []
    for tcn in range(NT):
        vv = A16[:, o_vex + tcn * 520:o_vex + (tcn + 1) * 520].rearrange(
            "p (hp c) -> p hp c", hp=4, c=130)
        b.memset(vv[:, :, 64:65], 1.0)
        tones.append(b.memset(vv[:, :, 129:130], 1.0, inc="gp"))

    wg_stage_guard = [None, None]
    wg_ptok = [None, None]
    wg_glast = [None]
    wg_slot_guard = [None] * NT

    def wgroup_mm(blkB, og, wpar, dst_emit, gated_dep):
        twl = []
        for kc in range(KCH):
            s32 = v32(o_wg32 + (kc % 2) * 640, 640)
            tdw = b.dma("sync", s32, blkB[kc],
                        deps=[twl[-2] if len(twl) >= 2 else
                              wg_stage_guard[kc % 2], twg_par[wpar]],
                        inc="wg0" if kc % 2 == 0 else "wg1")
            twl.append(b.copy(v16(o_wg + wpar * KCH * 640 + kc * 640, 640), s32,
                              deps=[tdw], inc="ve"))
        wg_stage_guard[0] = twl[-2]
        wg_stage_guard[1] = twl[-1]
        ptiles = [(psB[:, 0:512], psB[:, 512:640]), (psA[:, 0:512], psA[:, 512:640])]
        ptok = wg_ptok
        tglast = wg_glast[0]
        for tcn in range(NT):
            pp = tcn % 2
            tmm = None
            for kc in range(KCH):
                wap = v16(o_wg + wpar * KCH * 640 + kc * 640, 640)
                b.mm(ptiles[pp][0], xin1_ch(kc, tcn * 128, 128), wap[:, 0:512],
                     start=(kc == 0), stop=(kc == KCH - 1),
                     deps=[twl[kc], ptok[pp]])
                tmm = b.mm(ptiles[pp][1], xin1_ch(kc, tcn * 128, 128),
                           wap[:, 512:640], start=(kc == 0), stop=(kc == KCH - 1),
                           inc="mm" if kc == KCH - 1 else None)
            gslot = o_g5 + tcn * 640
            te1 = b.act(v16(gslot, 512), ptiles[pp][0], AF.Identity,
                        deps=[tmm, wg_slot_guard[tcn]], inc="se")
            te2 = b.act(v16(gslot + 512, 128), ptiles[pp][1], AF.Identity,
                        deps=[tmm, wg_slot_guard[tcn]], inc="se")
            ptok[pp] = te2
            tglast = dst_emit(og, tcn, [te1, te2, gated_dep])
            wg_slot_guard[tcn] = tglast
        wg_glast[0] = tglast
        twg_par[wpar] = tglast

    o_gt4 = ar16.alloc(2 * 4 * 64, "gate tmp")
    _gprev = [None, None]

    def gate_sum(dst, src_e, scol, deps, lane):
        base = o_gt4 + lane * 256
        deps = deps + [_gprev[lane]]
        p0 = b.ts(dst, src_e(0), scol(0), None, ALU.mult, deps=deps, inc="ve")
        pt = []
        for e in range(1, 5):
            pt.append(b.ts(v16(base + (e - 1) * 64, 64), src_e(e), scol(e),
                           None, ALU.mult, deps=deps, inc="ve"))
        a1 = b.tt(v16(base, 64), v16(base, 64), v16(base + 64, 64), ALU.add,
                  deps=pt[:2], inc="ve")
        a2 = b.tt(v16(base + 128, 64), v16(base + 128, 64),
                  v16(base + 192, 64), ALU.add, deps=pt[2:], inc="ve")
        a3 = b.tt(v16(base, 64), v16(base, 64), v16(base + 128, 64), ALU.add,
                  deps=[a1, a2], inc="ve")
        t = b.tt(dst, dst, v16(base, 64), ALU.add, deps=[p0, a3], inc="ve")
        _gprev[lane] = t
        return t

    def gate_kv(og, tcn, deps):
        kv = og // 4
        hp = og % 4
        tl = []
        for hh in range(2):
            h = 2 * hp + hh
            if kv == 0:
                dst = v16(o_ktf + tcn * 512 + hp * 128 + hh * 64, 64)
            else:
                dst = v16(o_vex + tcn * 520 + hp * 130 + (65 if hh else 0), 64)
            src_e = lambda e: v16(o_g5 + tcn * 640 + hh * 320 + e * 64, 64)
            tl.append(gate_sum(dst, src_e, lambda e: sg_col(0, tcn, h, e),
                                deps + [tmask[0]], hh))
        return tl

    for og in range(4):
        wgroup_mm(d["wkvB"][og], og, og % 2, gate_kv, None)

    ttr = None
    tevk = []
    tsendk = []
    for cc4 in range(4):
        for tcn in range(NT):
            pv = (psC if tcn % 2 == 0 else psD)[:, 0:64].bitcast(F16)
            ttr = b.trans(pv, v16(o_ktf + tcn * 512 + cc4 * 128, 128), ident,
                          deps=tevk[-1:] + [wg_slot_guard[tcn], wg_ptok[0],
                                            wg_ptok[1]], inc="mm")
            te = b.copy(v16(o_kT + tcn * 128, 128), pv,
                        deps=[ttr] + tsendk[-1:], inc="ve")
            tevk.append(te)
        tsendk.append(b.dma("gpsimd",
                            E["agk_in"][cc4 * 128 * T:(cc4 + 1) * 128 * T]
                            .rearrange("(p c) -> p c", c=T),
                            v16(o_kT, T), deps=[tevk[-1]], inc="dd"))
    tag_k = b.coll("AllGather", ALU.bypass, PAIRS,
                   [E["agk_in"][:]], [E["agk_out"][:]], deps=tsendk, inc="cc")

    for og in range(4, 8):
        wgroup_mm(d["wkvB"][og], og, og % 2, gate_kv, None)
    for tcn in range(NT):
        tsend.append(b.dma("gpsimd",
                           E["agv_in"][tcn * 128 * 520:(tcn + 1) * 128 * 520]
                           .rearrange("(p c) -> p c", c=520),
                           A16[:, o_vex + tcn * 520:o_vex + (tcn + 1) * 520],
                           deps=[tones[tcn], wg_slot_guard[tcn]], inc="dd"))
    tag = b.coll("AllGather", ALU.bypass, PAIRS,
                 [E["agv_in"][:]], [E["agv_out"][0:2 * 2048 * 520]],
                 deps=tsend, inc="cc")
    b.barrier()
    tap("kT", A16[:, o_kT:o_kT + T], [b.tok("ve")])
    tap("vext", A16[:, o_vex:o_vex + NT * 520], [b.tok("ve")])
    ckpt("S9")
    ar16.freeblk(o_kT, T)
    ar16.freeblk(o_ktf, NT * 512)

    o_qtok = ar16.alloc(NT * 512, "q tok")

    def gate_q(og, tcn, deps):
        tl = []
        for hh in range(2):
            h = 2 * og + hh
            dst = v16(o_qtok + tcn * 512 + og * 128 + hh * 64, 64)
            src_e = lambda e: v16(o_g5 + tcn * 640 + hh * 320 + e * 64, 64)
            tl.append(gate_sum(dst, src_e, lambda e: sg_col(1, tcn, h, e),
                                deps + [tmask[1]], hh))
        return tl

    for og in range(4):
        wgroup_mm(d["wqB"][og], og, og % 2, gate_q, None)
    b.barrier()
    ar16.freeblk(o_wg, 2 * KCH * 640)
    ar16.freeblk(o_g5, NT * 640)
    ar32.freeblk(o_wg32, 2 * 640)
    o_qT = ar16.alloc(4 * T, "qT ffn")
    tevq = []
    for cc4 in range(4):
        for tcn in range(NT):
            pv = (psC if tcn % 2 == 0 else psD)[:, 0:64].bitcast(F16)
            ttr = b.trans(pv, v16(o_qtok + tcn * 512 + cc4 * 128, 128), ident,
                          deps=tevq[-1:], inc="mm")
            te = b.copy(v16(o_qT + cc4 * T + tcn * 128, 128), pv,
                        deps=[ttr], inc="ve")
            tevq.append(te)
    ar16.freeblk(o_qtok, NT * 512)
    tap("qf", A16[:, o_qT:o_qT + 4 * T], [b.tok("ve")])
    ckpt("S10")

    o_wo32 = ar32.alloc(2 * 128, "wo2 stg")
    tpw = [None, None]
    two = []
    for cc in range(20):
        for mc2 in range(KCH):
            par = (cc * KCH + mc2) % 2
            s32 = v32(o_wo32 + par * 128, 128)
            tdw = b.dma("sync", s32, d["wo2B"][cc, mc2], deps=[tpw[par]],
                        inc="w0" if par == 0 else "w1")
            tpw[par] = b.copy(v16(o_wo + cc * 1024 + mc2 * 128, 128), s32,
                              deps=[tdw], inc="ve")
            two.append(tpw[par])
    ar32.freeblk(o_wo32, 2 * 128)
    ar16.freeblk(o_xin1, KCH * T)
    b.barrier()

    o_khp = ar16.alloc(2 * QFULL, "k hp")
    o_vhp = ar16.alloc(2 * 32 * 130, "v hp")
    o_P = ar16.alloc(3 * 1024, "P bufs")
    o_avT = ar16.alloc(4 * T, "avT")
    o_sst = ar32.alloc(1024, "sums stage")
    tpre = [None, None]

    def prefetch_hp(hp, pp, deps):
        toks = []
        for sl in range(2):
            base = sl * 512 * 2048 + hp * 128 * 2048
            toks.append(b.dma(
                "gpsimd",
                v16(o_khp + pp * QFULL + sl * (QFULL // 2), QFULL // 2),
                E["agk_out"][base:base + 128 * 2048].rearrange("(p c) -> p c", c=2048),
                deps=deps, inc="kh0" if pp == 0 else "kh1"))
        for sl in range(2):
            for ch in range(16):
                src0 = sl * 2048 * 520 + ch * 128 * 520
                toks.append(b.dma(
                    "gpsimd",
                    v16(o_vhp + pp * 32 * 130 + (sl * 16 + ch) * 130, 130),
                    E["agv_out"][src0:src0 + 128 * 520]
                    .rearrange("(p c) -> p c", c=520)[:, hp * 130:hp * 130 + 130],
                    deps=deps, inc="kh0" if pp == 0 else "kh1"))
        return toks

    tpre[0] = prefetch_hp(0, 0, [tag_k, tag])
    o_sums_rd = ar32.alloc(T, "sums rd")
    tev_av = []
    tsum_dma = []
    for hp in range(4):
        pp = hp % 2
        if hp < 3:
            tpre[(hp + 1) % 2] = prefetch_hp(hp + 1, (hp + 1) % 2,
                                             [tag] + tev_av[-1:])
        kdeps = tpre[pp]
        for tc4 in range(4):
            ts_ = tc4 * 512
            texp = [None, None, None]
            tavmm_c = [None]
            tav_par = [None, None, None]
            tSmm = [None, None, None]
            SPAIR = [(psB[:, 0:512], psB[:, 512:1024], psB[:, 0:1024]),
                     (psE[:, 0:512], psE[:, 512:1024], psE[:, 0:1024]),
                     (psA[:, 1024:1536], psA[:, 1536:2048], psA[:, 1024:2048])]
            def emit_S(jc):
                par = jc % 3
                So = SPAIR[par]
                sdeps = kdeps + [texp[par]] + (tev_av if jc < 3 else [])
                b.mm(So[0], v16(o_khp + pp * QFULL + jc * 128, 128, 0, 64),
                     v16(o_qT + hp * T + ts_, 512, 0, 64), True, True,
                     deps=sdeps, tp=(0, 0))
                tSmm[par] = b.mm(So[1],
                                 v16(o_khp + pp * QFULL + jc * 128, 128, 64, 64),
                                 v16(o_qT + hp * T + ts_, 512, 64, 64), True, True,
                                 tp=(64, 0), inc="mm")
                texp[par] = b.act(v16(o_P + par * 1024, 1024), So[2], AF.Exp,
                                  scale=SCALE_F, deps=[tSmm[par], tav_par[par]],
                                  inc="se")

            def emit_AV(jc):
                par = jc % 3
                vch = v16(o_vhp + pp * 32 * 130 + jc * 130, 130)
                b.mm(psA[0:65, 0:512], vch[:, 0:65], v16(o_P + par * 1024, 512),
                     start=(jc == 0), stop=(jc == 31),
                     deps=[texp[par]] + (tev_av if jc == 0 else []))
                tavmm_c[0] = b.mm(psA[0:65, 512:1024], vch[:, 65:130],
                                  v16(o_P + par * 1024 + 512, 512),
                                  start=(jc == 0), stop=(jc == 31), inc="mm")
                tav_par[par] = tavmm_c[0]

            emit_S(0)
            emit_S(1)
            for jc in range(32):
                if jc + 2 < 32:
                    emit_S(jc + 2)
                emit_AV(jc)
            tavmm = tavmm_c[0]
            e1 = b.copy(v16(o_avT + hp * T + ts_, 512, 0, 64), psA[0:64, 0:512],
                        deps=[tavmm], inc="ve")
            e2 = b.copy(v16(o_avT + hp * T + ts_, 512, 64, 64),
                        psA[0:64, 512:1024], deps=[tavmm], inc="ve")
            s1 = b.copy(v32(o_sst, 512, 64, 1), psA[64:65, 0:512],
                        deps=[tavmm] + tsum_dma, inc="ve")
            s2 = b.copy(v32(o_sst + 512, 512, 64, 1), psA[64:65, 512:1024],
                        deps=[tavmm] + tsum_dma, inc="ve")
            tsum_dma = [
                b.dma("gpsimd", E["sums_d"][2 * hp:2 * hp + 1, ts_:ts_ + 512],
                      v32(o_sst, 512, 64, 1), deps=[s1], inc="d0"),
                b.dma("gpsimd", E["sums_d"][2 * hp + 1:2 * hp + 2, ts_:ts_ + 512],
                      v32(o_sst + 512, 512, 64, 1), deps=[s2], inc="d1"),
            ]
            tev_av = [e1, e2, s1, s2]
    b.barrier()
    ar16.freeblk(o_khp, 2 * QFULL)
    ar16.freeblk(o_vhp, 2 * 32 * 130)
    ar16.freeblk(o_P, 3 * 1024)
    ar32.freeblk(o_sst, 1024)
    o_E4 = ar16.alloc(40 * 128, "E4")
    b.memset(v16(o_E4, 40 * 128), 0.0)
    for g in range(40):
        h = g // 5
        s0 = (h % 2) * 64
        p0 = (g % 2) * 64
        sub = v16(o_E4 + g * 128 + p0, 64, s0, 64)
        b.affine(sub, sub, [[-1, 64]], ALU.not_equal, 1.0, base=0, chan=1)
    tap("avT", A16[:, o_avT:o_avT + 4 * T], [b.tok("ve")])
    ckpt("S11")

    trd = b.dma("sync", v32(o_sums_rd, T, 0, 8), E["sums_d"][:, :],
                deps=[b.tok("d0"), b.tok("d1")], inc="x")
    o_rs = ar16.alloc(T, "recip16")
    tr1 = b.recip(v32(o_sums_rd, T, 0, 8), v32(o_sums_rd, T, 0, 8),
                  deps=[trd], inc="ve")
    tr2 = b.copy(v16(o_rs, T, 0, 8), v32(o_sums_rd, T, 0, 8), deps=[tr1], inc="ve")
    ar32.freeblk(o_sums_rd, T)
    o_mdT = ar16.alloc(T, "mdT")
    tmdT = []
    for tcn in range(NT):
        pv = psD[0:40, 0:64].bitcast(F16)
        tt_ = b.trans(pv, v16(o_msk + NT * 40 + tcn * 40, 40), ident,
                      deps=tmdT[-1:], inc="mm")
        tmdT.append(b.copy(v16(o_mdT + tcn * 128, 128, 0, 40), pv,
                           deps=[tt_], inc="ve"))
    o_gT = ar16.alloc(T, "gT")
    tgT = []
    for tc4 in range(4):
        s = tc4 * 512
        tm1 = b.mm(psC[0:40, :], v16(o_E7, 40, 0, 8), v16(o_rs + s, 512, 0, 8),
                   True, True, deps=[tr2, tsetup] + tmdT[-1:] + tgT[-1:], inc="mm")
        tgT.append(b.tt(v16(o_gT + s, 512, 0, 40), v16(o_mdT + s, 512, 0, 40),
                        psC[0:40, :], ALU.mult, deps=[tm1] + tmdT, inc="ve"))
    tap("gT", A16[0:40, o_gT:o_gT + T], [b.tok("ve")])
    b.barrier()
    ar16.freeblk(o_mdT, T)
    ar16.freeblk(o_rs, T)

    o_om = ar16.alloc(20 * 512, "OM")
    o_gbc = ar16.alloc(512, "gbc sbuf")
    o_x1s2 = ar32.alloc(2 * 512, "x1 slc f")
    o_yout = ar32.alloc(2 * 512, "y out")
    tx1 = [None, None]
    ty = [None, None]
    tomlast = []
    n_out = 0
    for tc4 in range(4):
        s = tc4 * 512
        tgb_prev = None
        tom = []
        for cc in range(20):
            tgb = b.mm(psD[:, :], v16(o_E3 + cc * 128, 128, 0, 40),
                       v16(o_gT + s, 512, 0, 40), True, True,
                       deps=tgT + [tgb_prev], inc="mm")
            tgbe = b.act(v16(o_gbc, 512), psD[:, :], AF.Identity,
                         deps=[tgb, tom[-1:][0] if tom else None], inc="se")
            tgb_prev = tgbe
            tab = None
            for half in range(2):
                gidx = 2 * cc + half
                hh = gidx // 5
                hp = hh // 2
                tab = b.mm(psC[:, :], v16(o_E4 + gidx * 128, 128),
                           v16(o_avT + hp * T + s, 512),
                           start=(half == 0), stop=(half == 1),
                           deps=[tom[-1] if tom else None] if half == 0 else None,
                           inc="mm" if half == 1 else None)
            tom.append(b.tt(v16(o_om + cc * 512, 512), v16(o_gbc, 512),
                            psC[:, :], ALU.mult, deps=[tgbe, tab], inc="ve"))
        for mc in range(KCH):
            par = mc % 2
            tdx = b.dma("sync", v32(o_x1s2 + par * 512, 512),
                        E["x1d"][128 * mc:128 * (mc + 1), s:s + 512],
                        deps=[ty[par]], inc="x0" if par == 0 else "x1")
            ptile = psB[:, 0:512] if par == 0 else psB[:, 512:1024]
            tmm = None
            for cc in range(20):
                tmm = b.mm(ptile, v16(o_wo + cc * 1024 + mc * 128, 128),
                           v16(o_om + cc * 512, 512),
                           start=(cc == 0), stop=(cc == 19),
                           deps=[tom[cc], ty[par]] + two[-1:],
                           inc="mm" if cc == 19 else None)
            tyv = b.tt(v32(o_yout + par * 512, 512), ptile,
                       v32(o_x1s2 + par * 512, 512), ALU.add,
                       deps=[tmm, tdx], inc="ve")
            ty[par] = b.dma("gpsimd", E["y"][128 * mc:128 * (mc + 1), s:s + 512],
                            v32(o_yout + par * 512, 512), deps=[tyv], inc="out")
            n_out += 1
        tomlast = tom
        if "om0" in taps and tc4 == 0:
            tap("om0", v16(o_om, 512), [tomlast[0]])
    b.final_wait([b.tok("out")])
    return ar32.peak, ar16.peak


TAP_SHAPES = {
    "xin": ([C, T], F16), "q": ([C, T], F16),
    "ktok": ([128, NT * C], F16), "vtok": ([128, NT * C], F16),
    "kvf": ([64, H_A * HD], F32), "att": ([C, T], F16),
    "x1": ([C, T], F32), "gates": ([128, NT * 80], F32),
    "sdg": ([128, NT * 40], F32), "md": ([128, NT * 40], F16),
    "kT": ([128, T], F16), "vext": ([128, NT * 520], F16),
    "qf": ([128, 4 * T], F16), "avT": ([128, 4 * T], F16),
    "gT": ([40, T], F16), "om0": ([128, 512], F16),
    "lnab": ([1, 2 * T], F16),
    "kvab": ([16, 2 * T], F16),
    "kvsums": ([64, T], F32),
    "lnsums": ([2, T], F32),
}


def build_nc(taps=(), stop=None):
    nc = bass.Bass("TRN2", target_bir_lowering=False, debug=False)
    es = ExitStack()
    d = {}
    for name, shape in [
        ("xT", [C, T]), ("qkvB", [24, KCH, 128, 128]), ("qkvb", [128, 24]),
        ("p1B", [KCH, KCH, 128, 128]), ("p1b", [128, KCH]),
        ("p2B", [KCH, KCH, 128, 128]), ("p2b", [128, KCH]),
        ("lngb", [128, 4]), ("tmpc", [128, KCH]), ("abT", [64, H_A * HD]),
        ("wkvB", [8, KCH, 128, 640]), ("wqB", [4, KCH, 128, 640]),
        ("wsdB", [KCH, 128, 80]), ("wo2B", [20, KCH, 128, 128]),
    ]:
        d[name] = nc.dram_tensor(name, shape, F32, kind="ExternalInput").ap()

    E = {"d": d}
    E["y"] = nc.dram_tensor("y", [C, T], F32, kind="ExternalOutput").ap()
    E["x1d"] = nc.dram_tensor("x1d", [C, T], F32).ap()
    E["agk_in"] = nc.dram_tensor("agk_in", [512 * 2048], F16).ap()
    E["agk_out"] = nc.dram_tensor("agk_out", [2 * 512 * 2048], F16).ap()
    E["agv_in"] = nc.dram_tensor("agv_in", [2048 * 520], F16).ap()
    E["agv_out"] = nc.dram_tensor("agv_out", [2 * 2048 * 520 + 2048], F16).ap()
    E["kv_in"] = nc.dram_tensor("kv_in", [64, H_A * HD], F32).ap()
    E["kv_out"] = nc.dram_tensor("kv_out", [64, H_A * HD], F32).ap()
    E["sums_d"] = nc.dram_tensor("sums_d", [8, T], F32).ap()
    E["tap_t"] = {}
    for t in taps:
        shp, dt_ = TAP_SHAPES[t]
        E["tap_t"][t] = nc.dram_tensor("tap_" + t, list(shp), dt_,
                                       kind="ExternalOutput").ap()

    E["A32"] = es.enter_context(nc.sbuf_tensor("af32", [128, NF32], F32))
    E["A16"] = es.enter_context(nc.sbuf_tensor("af16", [128, NF16], F16))
    E["psA"] = es.enter_context(nc.psum_tensor("psA", [128, 2048], F32))
    E["psB"] = es.enter_context(nc.psum_tensor("psB", [128, 1024], F32))
    E["psE"] = es.enter_context(nc.psum_tensor("psE", [128, 1024], F32))
    E["psC"] = E["psE"][:, 0:512]
    E["psD"] = E["psE"][:, 512:1024]
    sems = {s: es.enter_context(nc.semaphore(s)) for s in SEMS}

    ref_cnt = {}
    with nc.Block() as blk:
        def mk(role):
            def f(eng):
                cnt = {s: 0 for s in SEMS}
                bb = B(nc, role, sems, cnt)
                try:
                    _program(bb, E, set(taps), stop=stop)
                except Exception as ex:
                    if type(ex).__name__ != "_Stop":
                        raise
                if not ref_cnt:
                    ref_cnt.update(cnt)
                else:
                    assert ref_cnt == cnt, (role, ref_cnt, cnt)
            return f
        blk.tensor(mk("tensor"))
        blk.vector(mk("vector"))
        blk.scalar(mk("scalar"))
        blk.gpsimd(mk("gpsimd"))
        blk.sync(mk("sync"))
    es.close()
    return nc


def layout_inputs(inputs):
    f = lambda a: np.ascontiguousarray(np.asarray(a, dtype=np.float32))
    x = f(inputs["L0_attn_x"])
    qkv_w = f(inputs["L0_attn_qkv_w"])
    qkv_b = f(inputs["L0_attn_qkv_b"])
    p1_w, p1_b = f(inputs["L0_attn_p1_w"]), f(inputs["L0_attn_p1_b"])
    p2_w, p2_b = f(inputs["L0_attn_p2_w"]), f(inputs["L0_attn_p2_b"])
    kn_g, kn_b = f(inputs["L0_attn_kn_g"]), f(inputs["L0_attn_kn_b"])
    vn_g, vn_b = f(inputs["L0_attn_vn_g"]), f(inputs["L0_attn_vn_b"])
    temp = f(inputs["L0_attn_temperature"]).reshape(H_A)
    ab = f(inputs["L0_attn_attn_bias"])
    Wq, Wkv = f(inputs["L0_ffn_Wq"]), f(inputs["L0_ffn_Wkv"])
    Ws, Wd, Wo = f(inputs["L0_ffn_Ws"]), f(inputs["L0_ffn_Wd"]), f(inputs["L0_ffn_Wo"])

    blk = lambda w, mcs: np.ascontiguousarray(
        w.reshape(KCH, 128, mcs, 128).transpose(2, 0, 1, 3))
    shared = {
        "qkvB": blk(qkv_w.T, 24),
        "qkvb": np.ascontiguousarray(qkv_b.reshape(24, 128).T),
        "p1B": blk(p1_w.T, KCH),
        "p1b": np.ascontiguousarray(p1_b.reshape(KCH, 128).T),
        "p2B": blk(p2_w.T, KCH),
        "p2b": np.ascontiguousarray(p2_b.reshape(KCH, 128).T),
        "lngb": np.ascontiguousarray(np.stack(
            [np.tile(kn_g, 2), np.tile(kn_b, 2), np.tile(vn_g, 2),
             np.tile(vn_b, 2)], axis=1)),
        "tmpc": np.ascontiguousarray(np.repeat(temp, HD).reshape(KCH, 128).T),
        "abT": np.ascontiguousarray(ab.transpose(1, 0, 2).reshape(64, H_A * HD)),
        "wkvB": np.ascontiguousarray(
            Wkv.reshape(KCH, 128, 8, 640).transpose(2, 0, 1, 3)),
        "wqB": np.ascontiguousarray(
            Wq.reshape(KCH, 128, 4, 640).transpose(2, 0, 1, 3)),
        "wsdB": np.ascontiguousarray(
            np.concatenate([Ws, Wd], axis=1).reshape(KCH, 128, 80)),
        "wo2B": np.ascontiguousarray(
            np.tile(Wo.reshape(64, 5, 1024).transpose(1, 0, 2).reshape(320, 1024),
                    (8, 1)).reshape(20, 128, KCH, 128).transpose(0, 2, 1, 3)),
    }
    in_maps = []
    for c in range(8):
        bq, half = c // 2, c % 2
        m = dict(shared)
        m["xT"] = np.ascontiguousarray(x[bq, half * T:(half + 1) * T, :].T)
        in_maps.append(m)
    return in_maps


def kernel(**inputs):
    in_maps = layout_inputs(inputs)
    nc = build_nc()
    res = run_bass_kernel_spmd(nc, in_maps, core_ids=list(range(8)))
    out = np.empty((4, QFULL, C), dtype=np.float32)
    for c in range(8):
        bq, half = c // 2, c % 2
        out[bq, half * T:(half + 1) * T, :] = res.results[c]["y"].T
    return out


# version 08
# speedup vs baseline: 1.4984x; 1.0332x over previous
from contextlib import ExitStack

import numpy as np
import concourse.bass as bass
import concourse.mybir as mybir
from concourse.bass_utils import run_bass_kernel_spmd

F32 = mybir.dt.float32
F16 = mybir.dt.float16
AF = mybir.ActivationFunctionType
ALU = mybir.AluOpType

T = 2048
NT = T // 128
C = 1024
KCH = C // 128
QFULL = 4096
EPS = 1e-5
H_A, HD = 16, 64
H_F, E_F, DH = 8, 5, 64
SCALE_F = DH ** -0.5
KOFF = 0
VOFF = 512 * 2048
AGN = VOFF + 2048 * 520
PAIRS = [[0, 1], [2, 3], [4, 5], [6, 7]]
SEMS = ["wc", "w0", "w1", "w2", "w3", "wg0", "wg1", "x", "x0", "x1",
        "d0", "d1", "kh0", "kh1", "mm", "ve", "se", "gp", "dd", "cc",
        "out", "bar"]
NF32 = 8192
NF16 = 88064


class Alloc:
    def __init__(self, ncols, name):
        self.free = [(0, ncols)]
        self.ncols = ncols
        self.peak = 0
        self.name = name

    def alloc(self, n, tag=""):
        n = -(-n // 16) * 16
        for i, (s, ln) in enumerate(self.free):
            if ln >= n:
                if ln == n:
                    self.free.pop(i)
                else:
                    self.free[i] = (s + n, ln - n)
                self.peak = max(self.peak, self.ncols - sum(l for _, l in self.free))
                return s
        raise MemoryError(f"{self.name} OOM: {n} cols for {tag}; free={self.free}")

    def freeblk(self, s, n):
        n = -(-n // 16) * 16
        self.free.append((s, n))
        self.free.sort()
        out = []
        for blk in self.free:
            if out and out[-1][0] + out[-1][1] == blk[0]:
                out[-1] = (out[-1][0], out[-1][1] + blk[1])
            else:
                out.append(list(blk))
        self.free = [tuple(xx) for xx in out]


class B:

    def __init__(self, nc, role, sems, cnt):
        self.nc, self.role, self.sems, self.cnt = nc, role, sems, cnt
        self.wm = {}
        self.gp_chain = None

    def tok(self, sem):
        return (sem, self.cnt[sem])

    DMA_SEMS = ("wc", "w0", "w1", "w2", "w3", "wg0", "wg1", "x", "x0",
                "x1", "d0", "d1", "kh0", "kh1", "dd", "out")

    def _wait(self, eng, deps):
        flat = []
        for dd in deps or ():
            if isinstance(dd, list):
                flat.extend(dd)
            else:
                flat.append(dd)
        for dd in flat:
            if dd is None:
                continue
            sem, val = dd
            if sem in self.DMA_SEMS:
                val = self.cnt[sem]
            if val <= 0 or self.wm.get(sem, 0) >= val:
                continue
            eng.wait_ge(self.sems[sem], val)
            self.wm[sem] = val

    def _run(self, owner, fn, deps, inc, n):
        ins = None
        if self.role == owner:
            self._wait(getattr(self.nc, owner), deps)
            ins = fn()
        if inc is not None:
            self.cnt[inc] += n
            if ins is not None:
                ins.then_inc(self.sems[inc], n)
            return (inc, self.cnt[inc])
        return None

    def dma(self, q, out, in_, deps=None, inc=None):
        return self._run(q, lambda: getattr(self.nc, q).dma_start(out=out, in_=in_),
                         deps, inc, 16)

    def mm(self, out, lhsT, rhs, start, stop, deps=None, inc=None, tp=None):
        return self._run("tensor",
                         lambda: self.nc.tensor.matmul(out, lhsT, rhs, start=start,
                                                       stop=stop, tile_position=tp),
                         deps, inc, 1)

    def trans(self, out, in_, ident, deps=None, inc=None):
        return self._run("tensor", lambda: self.nc.tensor.transpose(out, in_, ident),
                         deps, inc, 1)

    def act(self, out, in_, func, bias=None, scale=None, accum_out=None,
            deps=None, inc=None):
        kw = {}
        if bias is not None:
            kw["bias"] = bias
        if scale is not None:
            kw["scale"] = scale
        if accum_out is not None:
            kw["accum_out"] = accum_out
        return self._run("scalar",
                         lambda: self.nc.scalar.activation(out, in_, func, **kw),
                         deps, inc, 1)

    def tt(self, out, a, c, op, deps=None, inc=None, eng="vector"):
        return self._run(eng, lambda: getattr(self.nc, eng).tensor_tensor(out, a, c, op),
                         deps, inc, 1)

    def ts(self, out, in0, s1, s2, op0, op1=None, deps=None, inc=None, eng="vector"):
        def f():
            e = getattr(self.nc, eng)
            if op1 is None:
                return e.tensor_scalar(out, in0, s1, None, op0)
            return e.tensor_scalar(out, in0, s1, s2, op0, op1)
        return self._run(eng, f, deps, inc, 1)

    def stt(self, out, in0, scalar, in1, op0, op1, deps=None, inc=None, eng="vector"):
        return self._run(eng,
                         lambda: getattr(self.nc, eng).scalar_tensor_tensor(
                             out, in0, scalar, in1, op0, op1),
                         deps, inc, 1)

    def copy(self, out, in_, deps=None, inc=None, eng="vector"):
        return self._run(eng, lambda: getattr(self.nc, eng).tensor_copy(out, in_),
                         deps, inc, 1)

    def recip(self, out, in_, deps=None, inc=None):
        return self._run("vector", lambda: self.nc.vector.reciprocal(out, in_),
                         deps, inc, 1)

    def memset(self, ap, v, deps=None, inc=None):
        deps = list(deps or []) + [self.gp_chain]
        tok = self._run("gpsimd", lambda: self.nc.gpsimd.memset(ap, v), deps,
                        "gp", 1)
        self.gp_chain = tok
        return tok

    def affine(self, out, in_, pattern, cmp, fill, base=None, chan=None,
               deps=None, inc=None):
        deps = list(deps or []) + [self.gp_chain]
        tok = self._run("gpsimd",
                        lambda: self.nc.gpsimd.affine_select(
                            out=out, in_=in_, pattern=pattern, compare_op=cmp,
                            fill=fill, base=base, channel_multiplier=chan),
                        deps, "gp", 1)
        self.gp_chain = tok
        return tok

    def coll(self, kind, op, groups, ins, outs, deps=None, inc=None):
        return self._run("gpsimd",
                         lambda: self.nc.gpsimd.collective_compute(
                             kind, op, replica_groups=groups, ins=ins, outs=outs),
                         deps, inc, 1)

    def barrier(self):
        self.cnt["bar"] += 4
        tgt = self.cnt["bar"]
        if self.role in ("tensor", "vector", "scalar", "gpsimd"):
            eng = getattr(self.nc, self.role)
            self._wait(eng, [(s, self.cnt[s]) for s in self.DMA_SEMS])
            eng.drain().then_inc(self.sems["bar"], 1)
            eng.wait_ge(self.sems["bar"], tgt)
            self.wm["bar"] = tgt
        elif self.role == "sync":
            eng = self.nc.sync
            self._wait(eng, [(s, self.cnt[s]) for s in self.DMA_SEMS])
            eng.wait_ge(self.sems["bar"], tgt)
            self.wm["bar"] = tgt
        return ("bar", tgt)

    def final_wait(self, deps):
        if self.role == "gpsimd":
            self._wait(self.nc.gpsimd, deps)


def _program(b, E, taps, stop=None):
    d = E["d"]
    A32, A16 = E["A32"], E["A16"]
    psA, psB, psC, psD = E["psA"], E["psB"], E["psC"], E["psD"]
    psE = E["psE"]
    tap_t = E["tap_t"]
    ar32 = Alloc(NF32, "af32")
    ar16 = Alloc(NF16, "af16")

    def v32(off, w, p0=0, pn=128):
        return A32[p0:p0 + pn, off:off + w]

    def v16(off, w, p0=0, pn=128):
        return A16[p0:p0 + pn, off:off + w]

    def tap(name, ap, deps):
        if name in taps:
            b.dma("sync", tap_t[name], ap, deps=deps, inc="out")

    class _Stop(Exception):
        pass

    def ckpt(name):
        if stop == name:
            b.barrier()
            b.final_wait([b.tok("out")])
            raise _Stop()

    o_id = ar16.alloc(128, "ident")
    o_ones_c = ar16.alloc(16, "ones col")
    o_ones_r = ar16.alloc(128, "ones row")
    o_Ekv = ar16.alloc(8 * 128, "Ekv")
    o_E6 = ar16.alloc(8 * 128, "E6")
    o_lngb = ar32.alloc(4, "lngb")
    o_geff = ar32.alloc(KCH, "geff")
    o_beff = ar32.alloc(KCH, "beff")
    o_tmpc = ar32.alloc(KCH, "tmpc")
    o_qkvb = ar32.alloc(24, "qkvb")
    o_p1b = ar32.alloc(KCH, "p1b")
    o_p2b = ar32.alloc(KCH, "p2b")
    o_abT = ar32.alloc(H_A * HD, "abT")
    ident = v16(o_id, 128)

    tw = [
        b.dma("sync", v32(o_lngb, 4), d["lngb"][:, :], inc="wc"),
        b.dma("sync", v32(o_tmpc, KCH), d["tmpc"][:, :], inc="wc"),
        b.dma("sync", v32(o_qkvb, 24), d["qkvb"][:, :], inc="wc"),
        b.dma("sync", v32(o_p1b, KCH), d["p1b"][:, :], inc="wc"),
        b.dma("sync", v32(o_p2b, KCH), d["p2b"][:, :], inc="wc"),
        b.dma("sync", v32(o_abT, H_A * HD, 0, 64), d["abT"][:, :], inc="wc"),
    ]

    b.memset(ident[:, :], 0.0)
    b.affine(ident[:, :], ident[:, :], [[-1, 128]], ALU.not_equal, 1.0, base=0, chan=1)
    b.memset(v16(o_ones_c, 1), 1.0)
    b.memset(v16(o_ones_r, 128, 0, 1), 1.0)
    b.memset(v16(o_Ekv, 8 * 128), 0.0)
    for j in range(8):
        base = o_Ekv + j * 128
        b.memset(v16(base + 2 * j, 1, 0, 64), 1.0)
        b.memset(v16(base + 2 * j + 1, 1, 64, 64), 1.0)
        b.memset(v16(base + 64 + 32 + 2 * j, 1, 0, 64), 1.0)
        b.memset(v16(base + 64 + 32 + 2 * j + 1, 1, 64, 64), 1.0)
    b.memset(v16(o_E6, 8 * 128), 1.0)
    for j in range(8):
        sub = v16(o_E6 + j * 128, 128, 0, 16)
        b.affine(sub, sub, [[1, 128]], ALU.is_ge, 0.0, base=128 * j, chan=-64)
        b.affine(sub, sub, [[-1, 128]], ALU.is_ge, 0.0, base=63 - 128 * j, chan=64)
    tsetup = b.tok("gp")

    tgeff = b.ts(v32(o_geff, KCH), v32(o_tmpc, KCH), v32(o_lngb, 1), None,
                 ALU.mult, deps=[tw[0], tw[1]], inc="ve")
    tbeff = b.ts(v32(o_beff, KCH), v32(o_tmpc, KCH), v32(o_lngb + 1, 1), None,
                 ALU.mult, deps=[tw[0], tw[1]], inc="ve")

    def emit_ln(src_dram, o_out16, tapon=False):
        o_stg = ar32.alloc(2 * T, "ln stage")
        o_cst = ar16.alloc(4 * T, "ln cast")
        tmm = [None, None]
        tcst = [None, None, None, None]
        tdxs = [None, None]
        for k in range(KCH):
            par = k % 2
            tdx = b.dma("sync", v32(o_stg + par * T, T),
                        src_dram[128 * k:128 * (k + 1), :],
                        deps=[tcst[par * 2], tcst[par * 2 + 1]],
                        inc="x0" if par == 0 else "x1")
            xc = v16(o_cst + par * 2 * T, T)
            xq = v16(o_cst + par * 2 * T + T, T)
            tc1 = b.copy(xc, v32(o_stg + par * T, T), deps=[tdx, tmm[par]], inc="ve")
            tc2 = b.act(xq, v32(o_stg + par * T, T), AF.Square, deps=[tdx, tmm[par]],
                        inc="se")
            tcst[par * 2] = tc1
            tcst[par * 2 + 1] = tc2
            for nt in range(4):
                s = nt * 512
                b.mm(psA[0:1, s:s + 512], v16(o_ones_c, 1), xc[:, s:s + 512],
                     start=(k == 0), stop=(k == KCH - 1), deps=[tc1, tsetup])
                t_ = b.mm(psA[32:33, s:s + 512], v16(o_ones_c, 1),
                          xq[:, s:s + 512],
                          start=(k == 0), stop=(k == KCH - 1), deps=[tc2],
                          inc="mm" if nt == 3 else None)
                if nt == 3:
                    tmm[par] = t_
        ar16.freeblk(o_cst, 4 * T)
        tstat = tmm[(KCH - 1) % 2]
        o_st = ar16.alloc(8 * 1024, "ln st")
        def q32(i, w, p0, pn):
            return A16[p0:p0 + pn, o_st + i * 2048:o_st + i * 2048 + 2 * w]\
                .bitcast(F32)
        o_ab = ar16.alloc(2 * T, "ln ab")
        Arow = v16(o_ab, T, 0, 1)
        Brow = v16(o_ab + T, T, 0, 1)
        tA = tB = None
        for hf in range(2):
            sl = hf * 1024
            mu = q32(0, 1024, 0, 1)
            var = q32(1, 1024, 0, 1)
            r = q32(2, 1024, 0, 1)
            tp_ = q32(3, 1024, 0, 1)
            dd = [tstat, tB]
            t1 = b.ts(mu, psA[0:1, sl:sl + 1024], 1.0 / C, None, ALU.mult,
                      deps=dd, inc="ve")
            t3 = b.act(tp_, mu, AF.Square, deps=[t1, tB], inc="se")
            t2 = b.ts(var, psA[32:33, sl:sl + 1024], 1.0 / C, EPS,
                      ALU.mult, ALU.add, deps=dd, inc="ve")
            t4 = b.tt(var, var, tp_, ALU.subtract, deps=[t2, t3], inc="ve")
            t6 = b.act(r, var, AF.Sqrt, deps=[t4], inc="se")
            t7 = b.recip(r, r, deps=[t6], inc="ve")
            t8 = b.act(tp_, r, AF.Square, deps=[t7], inc="se")
            t9 = b.tt(tp_, tp_, var, ALU.mult, deps=[t8], inc="ve")
            t10 = b.ts(tp_, tp_, -0.5, 1.5, ALU.mult, ALU.add, deps=[t9], inc="ve")
            t11 = b.tt(r, r, tp_, ALU.mult, deps=[t10], inc="ve")
            tA = b.copy(Arow[:, sl:sl + 1024], r, deps=[t11], inc="ve")
            t12 = b.tt(tp_, mu, r, ALU.mult, deps=[t11], inc="ve")
            tB = b.ts(Brow[:, sl:sl + 1024], tp_, -1.0, None, ALU.mult,
                      deps=[t12], inc="ve")
        if tapon and "lnab" in taps:
            tk_ = b.dma("sync", tap_t["lnab"], A16[0:1, o_ab:o_ab + 2 * T],
                        deps=[tA, tB], inc="out")
            b.final_wait([tk_])
        o_bsb = ar32.alloc(T, "ln bbc")
        tba = []
        tbb = []
        for nt in range(4):
            s = nt * 512
            tba.append(b.mm(psA[:, s:s + 512], v16(o_ones_r, 128, 0, 1),
                            Arow[:, s:s + 512], True, True,
                            deps=[tA, tstat], inc="mm"))
            pdst = psB[:, s:s + 512] if nt < 2 else psE[:, s - 1024:s - 512]
            tbb.append(b.mm(pdst, v16(o_ones_r, 128, 0, 1), Brow[:, s:s + 512],
                            True, True, deps=[tB], inc="mm"))
        tb1 = b.copy(v32(o_bsb, 1024), psB[:, 0:1024], deps=tbb[:2], inc="ve")
        tb2 = b.copy(v32(o_bsb + 1024, 1024), psE[:, 0:1024], deps=tbb[2:], inc="ve")
        o_t = ar16.alloc(2 * T, "ln t")
        tprev = [[tcst[0], tcst[1]], [tcst[2], tcst[3]]]
        for k in range(KCH):
            par = k % 2
            tdx = b.dma("sync", v32(o_stg + par * T, T),
                        src_dram[128 * k:128 * (k + 1), :],
                        deps=[tprev[par]],
                        inc="x0" if par == 0 else "x1")
            u1 = b.tt(v16(o_t + par * T, T), v32(o_stg + par * T, T),
                      psA[:, 0:T], ALU.mult, deps=[tdx, tprev[par]] + tba,
                      inc="ve")
            u2 = b.tt(v16(o_out16 + k * T, T), v16(o_t + par * T, T),
                      v32(o_bsb, T), ALU.add, deps=[u1, tb1, tb2], inc="ve")
            tprev[par] = u1
        ar16.freeblk(o_t, 2 * T)
        ar32.freeblk(o_bsb, T)
        ar16.freeblk(o_st, 8 * 1024)
        ar32.freeblk(o_stg, 2 * T)
        ar16.freeblk(o_ab, 2 * T)
        b.barrier()

    o_xin = ar16.alloc(KCH * T, "xin")
    emit_ln(d["xT"], o_xin, tapon=True)
    tap("xin", A16[:, o_xin:o_xin + KCH * T], [b.tok("ve")])
    xin_ch = lambda k, c0, w: v16(o_xin + k * T + c0, w)
    ckpt("S1")

    o_wstg = ar32.alloc(4 * 128, "wstg")
    o_wblk = ar16.alloc(4 * 128, "wblk")
    wpar_tok = [None, None, None, None]
    WSEM = ("w0", "w1", "w2", "w3")

    def wblock(src_ap, par):
        s32 = v32(o_wstg + par * 128, 128)
        s16 = v16(o_wblk + par * 128, 128)
        td = b.dma("sync", s32, src_ap, deps=[wpar_tok[par]], inc=WSEM[par])
        tc = b.copy(s16, s32, deps=[td], inc="ve")
        return s16, tc

    tile_tok = [None] * 4

    def mm_weight(blkB, src_ch, out_evac):
        tiles = [psB[:, 0:512], psB[:, 512:1024], psC[:, :], psD[:, :]]
        tmm = None
        for kc in range(KCH):
            par = kc % 4
            w16, tcst = wblock(blkB[kc], par)
            for nt in range(4):
                tmm = b.mm(tiles[nt], w16, src_ch(kc, nt * 512, 512),
                           start=(kc == 0), stop=(kc == KCH - 1),
                           deps=[tcst, tile_tok[nt]],
                           inc="mm" if nt == 3 else None)
            wpar_tok[par] = tmm
        toks = out_evac(tiles, tmm)
        for nt in range(4):
            tile_tok[nt] = toks[nt]

    o_ktok = ar16.alloc(NT * C, "ktok")
    o_vtok = ar16.alloc(NT * C, "vtok")
    kt_ch = lambda tcn, c0, w: v16(o_ktok + tcn * C + c0, w)
    vt_ch = lambda tcn, c0, w: v16(o_vtok + tcn * C + c0, w)

    def kv_group(mc0, o_tok, gcol_fn, bcol_fn):
        o_kv = ar16.alloc(KCH * T, "kvT")
        o_sq = ar16.alloc(T, "sq")
        tstat = None
        tsq = None
        for j in range(KCH):
            mcx = mc0 + j

            def evac(tiles, tmm, j=j, mcx=mcx):
                toks = []
                for nt in range(4):
                    te = b.act(v16(o_kv + j * T + nt * 512, 512), tiles[nt],
                               AF.Identity, bias=v32(o_qkvb + mcx, 1),
                               deps=[tmm, tw[2]], inc="se")
                    toks.append(te)
                return toks

            mm_weight(d["qkvB"][mcx], xin_ch, evac)
            tevs = list(tile_tok)
            tsq = b.act(v16(o_sq, T), v16(o_kv + j * T, T), AF.Square,
                        deps=tevs + [tstat], inc="se")
            Ev = v16(o_Ekv + j * 128, 64)
            Eq = v16(o_Ekv + j * 128 + 64, 64)
            for nt in range(4):
                s = nt * 512
                b.mm(psA[0:64, s:s + 512], Ev, v16(o_kv + j * T + s, 512),
                     start=(j == 0), stop=False, deps=tevs + [tsetup])
                tstat = b.mm(psA[0:64, s:s + 512], Eq, v16(o_sq + s, 512),
                             start=False, stop=(j == KCH - 1), deps=[tsq],
                             inc="mm" if nt == 3 else None)
        ar16.freeblk(o_sq, T)
        if ("kvsums" in taps) and mc0 == 8:
            o_sdb = ar32.alloc(T, "kvs dbg")
            td1 = b.copy(v32(o_sdb, T, 0, 64), psA[0:64, 0:T], deps=[tstat], inc="ve")
            b.dma("sync", tap_t["kvsums"], v32(o_sdb, T, 0, 64), deps=[td1],
                  inc="out")
            ar32.freeblk(o_sdb, T)
        o_st = ar16.alloc(8 * 1024, "kv st")
        def q32(i, w):
            return A16[0:16, o_st + i * 2048:o_st + i * 2048 + 2 * w].bitcast(F32)
        o_ab = ar16.alloc(2 * T, "kv ab")
        Ar = v16(o_ab, T, 0, 16)
        Br = v16(o_ab + T, T, 0, 16)
        tA = tB = None
        for hf in range(2):
            sl = hf * 1024
            mu = q32(0, 1024)
            var = q32(1, 1024)
            r = q32(2, 1024)
            tp_ = q32(3, 1024)
            dd = [tstat, tB]
            t1 = b.ts(mu, psA[0:16, sl:sl + 1024], 1.0 / HD, None, ALU.mult,
                      deps=dd, inc="ve")
            t3 = b.act(tp_, mu, AF.Square, deps=[t1, tB], inc="se")
            t2 = b.ts(var, psA[32:48, sl:sl + 1024], 1.0 / HD, EPS,
                      ALU.mult, ALU.add, deps=dd, inc="ve")
            t4 = b.tt(var, var, tp_, ALU.subtract, deps=[t2, t3], inc="ve")
            t6 = b.act(r, var, AF.Sqrt, deps=[t4], inc="se")
            t7 = b.recip(r, r, deps=[t6], inc="ve")
            t8 = b.act(tp_, r, AF.Square, deps=[t7], inc="se")
            t9 = b.tt(tp_, tp_, var, ALU.mult, deps=[t8], inc="ve")
            t10 = b.ts(tp_, tp_, -0.5, 1.5, ALU.mult, ALU.add, deps=[t9], inc="ve")
            t11 = b.tt(r, r, tp_, ALU.mult, deps=[t10], inc="ve")
            tA = b.copy(Ar[:, sl:sl + 1024], r, deps=[t11], inc="ve")
            t12 = b.tt(tp_, mu, r, ALU.mult, deps=[t11], inc="ve")
            tB = b.ts(Br[:, sl:sl + 1024], tp_, -1.0, None, ALU.mult,
                      deps=[t12], inc="ve")
        ar16.freeblk(o_st, 8 * 1024)
        if ("kvab" in taps) and mc0 == 8:
            b.dma("sync", tap_t["kvab"], A16[0:16, o_ab:o_ab + 2 * T],
                  deps=[tA, tB], inc="out")
        b.barrier()
        o_t = ar16.alloc(512, "kv ap t")
        tu3 = []
        tev = []
        for j in range(KCH):
            E6 = v16(o_E6 + j * 128, 128, 0, 16)
            for nt in range(4):
                s = nt * 512
                ta = b.mm(psB[:, 0:512], E6, Ar[:, s:s + 512], True, True,
                          deps=[tA] + tu3, inc="mm")
                tb = b.mm(psB[:, 512:1024], E6, Br[:, s:s + 512], True, True,
                          deps=[tB], inc="mm")
                tm = v16(o_t, 512)
                u1 = b.tt(tm, v16(o_kv + j * T + s, 512), psB[:, 0:512], ALU.mult,
                          deps=[ta] + tev, inc="ve")
                u2 = b.tt(tm, tm, psB[:, 512:1024], ALU.add, deps=[u1, tb], inc="ve")
                u3a = b.ts(tm, tm, gcol_fn(j), None, ALU.mult,
                           deps=[u2, tgeff, tw[0]], inc="ve")
                u3 = b.ts(tm, tm, bcol_fn(j), None, ALU.add,
                          deps=[u3a, tbeff], inc="ve")
                tu3 = [u3]
                ttr = None
                for q4 in range(4):
                    ttr = b.trans(psC[:, 64 * q4:64 * (q4 + 1)].bitcast(F16),
                                  tm[:, 128 * q4:128 * (q4 + 1)], ident,
                                  deps=[u3] + tev, inc="mm" if q4 == 3 else None)
                tev = []
                for q4 in range(4):
                    tcn = nt * 4 + q4
                    te = b.copy(A16[:, o_tok + tcn * C + j * 128:
                                    o_tok + tcn * C + j * 128 + 128],
                                psC[:, 64 * q4:64 * (q4 + 1)].bitcast(F16),
                                deps=[ttr], inc="ve")
                    tev = [te]
                b.barrier()
        ar16.freeblk(o_t, 512)
        ar16.freeblk(o_kv, KCH * T)
        ar16.freeblk(o_ab, 2 * T)
        b.barrier()

    kv_group(8, o_ktok, lambda j: v32(o_geff + j, 1),
             lambda j: v32(o_beff + j, 1))
    kv_group(16, o_vtok, lambda j: v32(o_lngb + 2, 1),
             lambda j: v32(o_lngb + 3, 1))
    tap("ktok", A16[:, o_ktok:o_ktok + NT * C], [b.tok("ve")])
    tap("vtok", A16[:, o_vtok:o_vtok + NT * C], [b.tok("ve")])

    o_q = ar16.alloc(KCH * T, "qT")
    for j in range(KCH):
        def evq(tiles, tmm, j=j):
            toks = []
            for nt in range(4):
                te = b.act(v16(o_q + j * T + nt * 512, 512), tiles[nt], AF.Identity,
                           bias=v32(o_qkvb + j, 1), deps=[tmm, tw[2]], inc="se")
                toks.append(te)
            return toks
        mm_weight(d["qkvB"][j], xin_ch, evq)
    b.barrier()
    tap("q", A16[:, o_q:o_q + KCH * T], [b.tok("se")])
    ckpt("S2")

    tkv = None
    for h in range(H_A):
        for tcn in range(NT):
            tkv = b.mm(psA[0:64, h * 64:h * 64 + 64], kt_ch(tcn, h * 64, 64),
                       vt_ch(tcn, h * 64, 64), start=(tcn == 0), stop=(tcn == NT - 1),
                       inc="mm" if (tcn == NT - 1 and h == H_A - 1) else None)
    o_kvp = ar32.alloc(H_A * HD, "kv partials")
    tkevac = b.ts(v32(o_kvp, H_A * HD, 0, 64), psA[0:64, 0:H_A * HD],
                  1.0 / QFULL, None, ALU.mult, deps=[tkv], inc="ve")
    tout = b.dma("gpsimd", E["kv_in"][:, :], v32(o_kvp, H_A * HD, 0, 64),
                 deps=[tkevac], inc="dd")
    tcc = b.coll("AllReduce", ALU.add, PAIRS, [E["kv_in"][:, :]],
                 [E["kv_out"][:, :]], deps=[tout], inc="cc")
    tin = b.dma("gpsimd", v32(o_kvp, H_A * HD, 0, 64), E["kv_out"][:, :],
                deps=[tcc], inc="dd")
    tfin = b.tt(v32(o_kvp, H_A * HD, 0, 64), v32(o_kvp, H_A * HD, 0, 64),
                v32(o_abT, H_A * HD, 0, 64), ALU.add, deps=[tin, tw[5]], inc="ve")
    tap("kvf", A32[0:64, o_kvp:o_kvp + H_A * HD], [tfin])
    o_kvbd = ar16.alloc(8 * 128, "kv bd")
    tms = b.memset(v16(o_kvbd, 8 * 128), 0.0, deps=[], inc="gp")
    tbd = []
    for hp in range(8):
        base = o_kvbd + hp * 128
        b1 = b.copy(v16(base, 64, 0, 64), v32(o_kvp + 2 * hp * 64, 64, 0, 64),
                    deps=[tfin, tms], inc="ve")
        b2 = b.copy(v16(base + 64, 64, 64, 64),
                    v32(o_kvp + (2 * hp + 1) * 64, 64, 0, 64),
                    deps=[tfin, tms], inc="ve")
        tbd += [b1, b2]
    b.barrier()
    ar16.freeblk(o_ktok, NT * C)
    ar16.freeblk(o_vtok, NT * C)
    ar32.freeblk(o_kvp, H_A * HD)
    ckpt("S3")

    o_att = ar16.alloc(KCH * T, "att")
    tiles4 = [psB[:, 0:512], psB[:, 512:1024], psC[:, :], psD[:, :]]
    ttok4 = [None] * 4
    for hp in range(8):
        tmm = None
        for nt in range(4):
            tmm = b.mm(tiles4[nt], v16(o_kvbd + hp * 128, 128),
                       v16(o_q + hp * T + nt * 512, 512), True, True,
                       deps=[ttok4[nt]] + tbd, inc="mm" if nt == 3 else None)
        for nt in range(4):
            te = b.tt(v16(o_att + hp * T + nt * 512, 512), tiles4[nt],
                      xin_ch(hp, nt * 512, 512), ALU.add, deps=[tmm], inc="ve")
            ttok4[nt] = te
    b.barrier()
    ar16.freeblk(o_q, KCH * T)
    ar16.freeblk(o_kvbd, 8 * 128)
    tap("att", A16[:, o_att:o_att + KCH * T], [b.tok("ve")])
    ckpt("S4")

    o_h1 = ar16.alloc(KCH * T, "h1")
    att_ch = lambda k, c0, w: v16(o_att + k * T + c0, w)
    for mc in range(KCH):
        def ev1(tiles, tmm, mc=mc):
            toks = []
            for nt in range(4):
                te = b.act(v16(o_h1 + mc * T + nt * 512, 512), tiles[nt], AF.Gelu,
                           bias=v32(o_p1b + mc, 1), deps=[tmm, tw[3]], inc="se")
                toks.append(te)
            return toks
        mm_weight(d["p1B"][mc], att_ch, ev1)
    b.barrier()
    ar16.freeblk(o_att, KCH * T)

    h1_ch = lambda k, c0, w: v16(o_h1 + k * T + c0, w)
    o_xs = ar32.alloc(2 * 512, "x slc")
    o_x1s = ar32.alloc(2 * 512, "x1 slc")
    txs = [None, None]
    for mc in range(KCH):
        def ev2(tiles, tmm, mc=mc):
            toks = []
            for nt in range(4):
                par = nt % 2
                s = nt * 512
                tdx = b.dma("sync", v32(o_xs + par * 512, 512),
                            d["xT"][128 * mc:128 * (mc + 1), s:s + 512],
                            deps=[txs[par]], inc="x0" if par == 0 else "x1")
                ta = b.act(v32(o_x1s + par * 512, 512), tiles[nt], AF.Identity,
                           bias=v32(o_p2b + mc, 1), deps=[tmm, tw[4], txs[par]],
                           inc="se")
                tb = b.tt(v32(o_x1s + par * 512, 512), v32(o_x1s + par * 512, 512),
                          xin_ch(mc, s, 512), ALU.add, deps=[ta], inc="ve")
                tc2 = b.tt(v32(o_x1s + par * 512, 512), v32(o_x1s + par * 512, 512),
                           v32(o_xs + par * 512, 512), ALU.add, deps=[tb, tdx],
                           inc="ve")
                td = b.dma("gpsimd", E["x1d"][128 * mc:128 * (mc + 1), s:s + 512],
                           v32(o_x1s + par * 512, 512), deps=[tc2],
                           inc="d0" if par == 0 else "d1")
                txs[par] = td
                toks.append(tc2)
            return toks
        mm_weight(d["p2B"][mc], h1_ch, ev2)
    b.barrier()
    ar16.freeblk(o_h1, KCH * T)
    ar32.freeblk(o_xs, 2 * 512)
    ar32.freeblk(o_x1s, 2 * 512)
    ar16.freeblk(o_xin, KCH * T)
    tap("x1", E["x1d"][:, :], [b.tok("d0"), b.tok("d1")])
    ckpt("S6")

    o_xin1 = ar16.alloc(KCH * T, "xin1")
    o_wo = ar16.alloc(20 * 1024, "wo2 full")
    emit_ln(E["x1d"], o_xin1)
    xin1_ch = lambda k, c0, w: v16(o_xin1 + k * T + c0, w)

    o_E3 = ar16.alloc(20 * 128, "E3")
    o_E7 = ar16.alloc(48, "E7")
    b.memset(v16(o_E3, 20 * 128), 1.0)
    for cc in range(20):
        sub = v16(o_E3 + cc * 128, 128, 0, 40)
        b.affine(sub, sub, [[1, 128]], ALU.is_ge, 0.0, base=128 * cc, chan=-64)
        b.affine(sub, sub, [[-1, 128]], ALU.is_ge, 0.0, base=63 - 128 * cc, chan=64)
    b.memset(v16(o_E7, 48), 1.0)
    sub = v16(o_E7, 40, 0, 8)
    b.affine(sub, sub, [[1, 40]], ALU.is_ge, 0.0, base=0, chan=-5)
    b.affine(sub, sub, [[-1, 40]], ALU.is_ge, 0.0, base=4, chan=5)

    o_wsd = ar16.alloc(KCH * 80, "wsd")
    o_wsd32 = ar32.alloc(128, "wsd stg")
    twsd = []
    tpv = None
    for kc in range(KCH):
        tdw = b.dma("sync", v32(o_wsd32, 80), d["wsdB"][kc], deps=[tpv], inc="x")
        tpv = b.copy(v16(o_wsd + kc * 80, 80), v32(o_wsd32, 80), deps=[tdw], inc="ve")
        twsd.append(tpv)
    o_gate = ar32.alloc(NT * 80, "gates")
    tg = None
    tsg = None
    for tcn in range(NT):
        for kc in range(KCH):
            tg = b.mm(psC[:, 0:80], xin1_ch(kc, tcn * 128, 128),
                      v16(o_wsd + kc * 80, 80), start=(kc == 0), stop=(kc == KCH - 1),
                      deps=[twsd[kc], tsg], inc="mm" if kc == KCH - 1 else None)
        tsg = b.act(v32(o_gate + tcn * 80, 80), psC[:, 0:80], AF.Sigmoid,
                    deps=[tg], inc="se")
    b.barrier()
    ar16.freeblk(o_wsd, KCH * 80)
    ar32.freeblk(o_wsd32, 128)
    tap("gates", A32[:, o_gate:o_gate + NT * 80], [b.tok("se")])

    NM = NT * 8
    o_med = ar32.alloc(8 * NM, "med")
    o_msk = ar16.alloc(2 * NT * 40, "masks")
    o_sg = ar32.alloc(2 * NT * 40, "gated")
    gateR = A32[:, o_gate:o_gate + NT * 80].rearrange(
        "p (tcn hl h e) -> p tcn hl h e", tcn=NT, hl=2, h=8, e=5)
    tmask = []
    for hl in range(2):
        ex = [gateR[:, :, hl, :, e] for e in range(5)]
        m = [v32(o_med + i * NM, NM).rearrange("p (tcn h) -> p tcn h", tcn=NT, h=8)
             for i in range(8)]
        d0 = [b.tok("se")]
        u = []
        u.append(b.tt(m[0], ex[0], ex[1], ALU.min, deps=d0, inc="ve"))
        u.append(b.tt(m[1], ex[0], ex[1], ALU.max, deps=d0, inc="ve"))
        u.append(b.tt(m[2], ex[2], ex[3], ALU.min, deps=d0, inc="ve"))
        u.append(b.tt(m[3], ex[2], ex[3], ALU.max, deps=d0, inc="ve"))
        f_ = b.tt(m[4], m[0], m[2], ALU.max, deps=u[:3], inc="ve")
        g_ = b.tt(m[5], m[1], m[3], ALU.min, deps=u, inc="ve")
        q1 = b.tt(m[6], ex[4], m[4], ALU.min, deps=[f_], inc="ve")
        q2 = b.tt(m[7], ex[4], m[4], ALU.max, deps=[f_], inc="ve")
        q3 = b.tt(m[7], m[7], m[5], ALU.min, deps=[q2, g_], inc="ve")
        tmed = b.tt(m[0], m[6], m[7], ALU.max, deps=[q1, q3], inc="ve")
        mskR = A16[:, o_msk + hl * NT * 40:o_msk + (hl + 1) * NT * 40].rearrange(
            "p (tcn h e) -> p tcn h e", tcn=NT, h=8, e=5)
        sgR = A32[:, o_sg + hl * NT * 40:o_sg + (hl + 1) * NT * 40].rearrange(
            "p (tcn h e) -> p tcn h e", tcn=NT, h=8, e=5)
        tms_ = []
        for e in range(5):
            tms_.append(b.tt(mskR[:, :, :, e], ex[e], m[0], ALU.is_ge,
                             deps=[tmed], inc="ve"))
        tgt = b.tt(sgR, gateR[:, :, hl, :, :], mskR, ALU.mult, deps=tms_, inc="ve")
        tmask.append(tgt)
    ar32.freeblk(o_med, 8 * NM)
    ar32.freeblk(o_gate, NT * 80)
    b.barrier()
    tap("md", A16[:, o_msk + NT * 40:o_msk + 2 * NT * 40], [b.tok("ve")])
    tap("sdg", A32[:, o_sg + NT * 40:o_sg + 2 * NT * 40], [b.tok("ve")])
    ckpt("S8")

    def msk_col(hl, tcn, h, e):
        return v16(o_msk + hl * NT * 40 + tcn * 40 + h * 5 + e, 1)

    def sg_col(hl, tcn, h, e):
        return v32(o_sg + hl * NT * 40 + tcn * 40 + h * 5 + e, 1)

    o_ktf = ar16.alloc(NT * 512, "ktokF")
    o_vex = ar16.alloc(NT * 520, "vext")
    o_wg = ar16.alloc(2 * KCH * 640, "w group")
    o_wg32 = ar32.alloc(2 * 640, "wg stage")
    o_g5 = ar16.alloc(NT * 640, "group out")
    twg_par = [None, None]
    o_kT = ar16.alloc(T, "kT")
    tsend = []

    tones = []
    for tcn in range(NT):
        vv = A16[:, o_vex + tcn * 520:o_vex + (tcn + 1) * 520].rearrange(
            "p (hp c) -> p hp c", hp=4, c=130)
        b.memset(vv[:, :, 64:65], 1.0)
        tones.append(b.memset(vv[:, :, 129:130], 1.0, inc="gp"))

    wg_stage_guard = [None, None]
    wg_ptok = [None, None]
    wg_glast = [None]
    wg_slot_guard = [None] * NT

    def wgroup_mm(blkB, og, wpar, dst_emit, gated_dep):
        twl = []
        for kc in range(KCH):
            s32 = v32(o_wg32 + (kc % 2) * 640, 640)
            tdw = b.dma("sync", s32, blkB[kc],
                        deps=[twl[-2] if len(twl) >= 2 else
                              wg_stage_guard[kc % 2], twg_par[wpar]],
                        inc="wg0" if kc % 2 == 0 else "wg1")
            twl.append(b.copy(v16(o_wg + wpar * KCH * 640 + kc * 640, 640), s32,
                              deps=[tdw], inc="ve"))
        wg_stage_guard[0] = twl[-2]
        wg_stage_guard[1] = twl[-1]
        ptiles = [(psB[:, 0:512], psB[:, 512:640]), (psA[:, 0:512], psA[:, 512:640])]
        ptok = wg_ptok
        tglast = wg_glast[0]
        for tcn in range(NT):
            pp = tcn % 2
            tmm = None
            for kc in range(KCH):
                wap = v16(o_wg + wpar * KCH * 640 + kc * 640, 640)
                b.mm(ptiles[pp][0], xin1_ch(kc, tcn * 128, 128), wap[:, 0:512],
                     start=(kc == 0), stop=(kc == KCH - 1),
                     deps=[twl[kc], ptok[pp]])
                tmm = b.mm(ptiles[pp][1], xin1_ch(kc, tcn * 128, 128),
                           wap[:, 512:640], start=(kc == 0), stop=(kc == KCH - 1),
                           inc="mm" if kc == KCH - 1 else None)
            gslot = o_g5 + tcn * 640
            te1 = b.act(v16(gslot, 512), ptiles[pp][0], AF.Identity,
                        deps=[tmm, wg_slot_guard[tcn]], inc="se")
            te2 = b.act(v16(gslot + 512, 128), ptiles[pp][1], AF.Identity,
                        deps=[tmm, wg_slot_guard[tcn]], inc="se")
            ptok[pp] = te2
            tglast = dst_emit(og, tcn, [te1, te2, gated_dep])
            wg_slot_guard[tcn] = tglast
        wg_glast[0] = tglast
        twg_par[wpar] = tglast

    o_gt4 = ar16.alloc(2 * 4 * 64, "gate tmp")
    _gprev = [None, None]

    def gate_sum(dst, src_e, scol, deps, lane):
        eng = "vector"
        sem = "ve"
        base = o_gt4 + lane * 256
        deps = deps + [_gprev[lane]]
        p0 = b.ts(dst, src_e(0), scol(0), None, ALU.mult, deps=deps, inc=sem,
                  eng=eng)
        pt = []
        for e in range(1, 5):
            pt.append(b.ts(v16(base + (e - 1) * 64, 64), src_e(e), scol(e),
                           None, ALU.mult, deps=deps, inc=sem, eng=eng))
        a1 = b.tt(v16(base, 64), v16(base, 64), v16(base + 64, 64), ALU.add,
                  deps=pt[:2], inc=sem, eng=eng)
        a2 = b.tt(v16(base + 128, 64), v16(base + 128, 64),
                  v16(base + 192, 64), ALU.add, deps=pt[2:], inc=sem, eng=eng)
        a3 = b.tt(v16(base, 64), v16(base, 64), v16(base + 128, 64), ALU.add,
                  deps=[a1, a2], inc=sem, eng=eng)
        t = b.tt(dst, dst, v16(base, 64), ALU.add, deps=[p0, a3], inc=sem,
                 eng=eng)
        _gprev[lane] = t
        return t

    def gate_kv(og, tcn, deps):
        kv = og // 4
        hp = og % 4
        tl = []
        for hh in range(2):
            h = 2 * hp + hh
            if kv == 0:
                dst = v16(o_ktf + tcn * 512 + hp * 128 + hh * 64, 64)
            else:
                dst = v16(o_vex + tcn * 520 + hp * 130 + (65 if hh else 0), 64)
            src_e = lambda e: v16(o_g5 + tcn * 640 + hh * 320 + e * 64, 64)
            tl.append(gate_sum(dst, src_e, lambda e: sg_col(0, tcn, h, e),
                                deps + [tmask[0]], hh))
        return tl

    for og in range(4, 8):
        wgroup_mm(d["wkvB"][og], og, og % 2, gate_kv, None)
    for tcn in range(NT):
        tsend.append(b.dma("gpsimd",
                           E["agv_in"][tcn * 128 * 520:(tcn + 1) * 128 * 520]
                           .rearrange("(p c) -> p c", c=520),
                           A16[:, o_vex + tcn * 520:o_vex + (tcn + 1) * 520],
                           deps=[tones[tcn], wg_slot_guard[tcn]], inc="dd"))
    tag = b.coll("AllGather", ALU.bypass, PAIRS,
                 [E["agv_in"][:]], [E["agv_out"][0:2 * 2048 * 520]],
                 deps=tsend, inc="cc")

    for og in range(4):
        wgroup_mm(d["wkvB"][og], og, og % 2, gate_kv, None)

    ttr = None
    tevk = []
    tsendk = []
    for cc4 in range(4):
        for tcn in range(NT):
            pv = (psC if tcn % 2 == 0 else psD)[:, 0:64].bitcast(F16)
            ttr = b.trans(pv, v16(o_ktf + tcn * 512 + cc4 * 128, 128), ident,
                          deps=tevk[-1:] + [wg_slot_guard[tcn], wg_ptok[0],
                                            wg_ptok[1]], inc="mm")
            te = b.copy(v16(o_kT + tcn * 128, 128), pv,
                        deps=[ttr] + tsendk[-1:], inc="ve")
            tevk.append(te)
        tsendk.append(b.dma("gpsimd",
                            E["agk_in"][cc4 * 128 * T:(cc4 + 1) * 128 * T]
                            .rearrange("(p c) -> p c", c=T),
                            v16(o_kT, T), deps=[tevk[-1]], inc="dd"))
    tag_k = b.coll("AllGather", ALU.bypass, PAIRS,
                   [E["agk_in"][:]], [E["agk_out"][:]], deps=tsendk, inc="cc")
    b.barrier()
    tap("kT", A16[:, o_kT:o_kT + T], [b.tok("ve")])
    tap("vext", A16[:, o_vex:o_vex + NT * 520], [b.tok("ve")])
    ckpt("S9")
    ar16.freeblk(o_kT, T)
    ar16.freeblk(o_ktf, NT * 512)

    o_qtok = ar16.alloc(NT * 512, "q tok")

    def gate_q(og, tcn, deps):
        tl = []
        for hh in range(2):
            h = 2 * og + hh
            dst = v16(o_qtok + tcn * 512 + og * 128 + hh * 64, 64)
            src_e = lambda e: v16(o_g5 + tcn * 640 + hh * 320 + e * 64, 64)
            tl.append(gate_sum(dst, src_e, lambda e: sg_col(1, tcn, h, e),
                                deps + [tmask[1]], hh))
        return tl

    for og in range(4):
        wgroup_mm(d["wqB"][og], og, og % 2, gate_q, None)
    b.barrier()
    ar16.freeblk(o_wg, 2 * KCH * 640)
    ar16.freeblk(o_g5, NT * 640)
    ar32.freeblk(o_wg32, 2 * 640)
    o_qT = ar16.alloc(4 * T, "qT ffn")
    tevq = []
    for cc4 in range(4):
        for tcn in range(NT):
            pv = (psC if tcn % 2 == 0 else psD)[:, 0:64].bitcast(F16)
            ttr = b.trans(pv, v16(o_qtok + tcn * 512 + cc4 * 128, 128), ident,
                          deps=tevq[-1:], inc="mm")
            te = b.copy(v16(o_qT + cc4 * T + tcn * 128, 128), pv,
                        deps=[ttr], inc="ve")
            tevq.append(te)
    ar16.freeblk(o_qtok, NT * 512)
    tap("qf", A16[:, o_qT:o_qT + 4 * T], [b.tok("ve")])
    ckpt("S10")

    o_wo32 = ar32.alloc(2 * 128, "wo2 stg")
    tpw = [None, None]
    two = []
    for cc in range(20):
        for mc2 in range(KCH):
            par = (cc * KCH + mc2) % 2
            s32 = v32(o_wo32 + par * 128, 128)
            tdw = b.dma("sync", s32, d["wo2B"][cc, mc2], deps=[tpw[par]],
                        inc="w0" if par == 0 else "w1")
            tpw[par] = b.copy(v16(o_wo + cc * 1024 + mc2 * 128, 128), s32,
                              deps=[tdw], inc="ve")
            two.append(tpw[par])
    ar32.freeblk(o_wo32, 2 * 128)
    ar16.freeblk(o_xin1, KCH * T)
    b.barrier()

    o_khp = ar16.alloc(2 * QFULL, "k hp")
    o_vhp = ar16.alloc(2 * 32 * 130, "v hp")
    o_P = ar16.alloc(3 * 1024, "P bufs")
    o_avT = ar16.alloc(4 * T, "avT")
    o_sst = ar32.alloc(1024, "sums stage")
    tpre = [None, None]

    def prefetch_hp(hp, pp, deps):
        toks = []
        for sl in range(2):
            base = sl * 512 * 2048 + hp * 128 * 2048
            toks.append(b.dma(
                "gpsimd",
                v16(o_khp + pp * QFULL + sl * (QFULL // 2), QFULL // 2),
                E["agk_out"][base:base + 128 * 2048].rearrange("(p c) -> p c", c=2048),
                deps=deps, inc="kh0" if pp == 0 else "kh1"))
        for sl in range(2):
            for ch in range(16):
                src0 = sl * 2048 * 520 + ch * 128 * 520
                toks.append(b.dma(
                    "gpsimd",
                    v16(o_vhp + pp * 32 * 130 + (sl * 16 + ch) * 130, 130),
                    E["agv_out"][src0:src0 + 128 * 520]
                    .rearrange("(p c) -> p c", c=520)[:, hp * 130:hp * 130 + 130],
                    deps=deps, inc="kh0" if pp == 0 else "kh1"))
        return toks

    tpre[0] = prefetch_hp(0, 0, [tag_k, tag])
    o_sums_rd = ar32.alloc(T, "sums rd")
    tev_av = []
    tsum_dma = []
    for hp in range(4):
        pp = hp % 2
        if hp < 3:
            tpre[(hp + 1) % 2] = prefetch_hp(hp + 1, (hp + 1) % 2,
                                             [tag] + tev_av[-1:])
        kdeps = tpre[pp]
        for tc4 in range(4):
            ts_ = tc4 * 512
            texp = [None, None, None]
            tavmm_c = [None]
            tav_par = [None, None, None]
            tSmm = [None, None, None]
            SPAIR = [(psB[:, 0:512], psB[:, 512:1024], psB[:, 0:1024]),
                     (psE[:, 0:512], psE[:, 512:1024], psE[:, 0:1024]),
                     (psA[:, 1024:1536], psA[:, 1536:2048], psA[:, 1024:2048])]
            def emit_S(jc):
                par = jc % 3
                So = SPAIR[par]
                sdeps = kdeps + [texp[par]] + (tev_av if jc < 3 else [])
                b.mm(So[0], v16(o_khp + pp * QFULL + jc * 128, 128, 0, 64),
                     v16(o_qT + hp * T + ts_, 512, 0, 64), True, True,
                     deps=sdeps, tp=(0, 0))
                tSmm[par] = b.mm(So[1],
                                 v16(o_khp + pp * QFULL + jc * 128, 128, 64, 64),
                                 v16(o_qT + hp * T + ts_, 512, 64, 64), True, True,
                                 tp=(64, 0), inc="mm")
                texp[par] = b.act(v16(o_P + par * 1024, 1024), So[2], AF.Exp,
                                  scale=SCALE_F, deps=[tSmm[par], tav_par[par]],
                                  inc="se")

            def emit_AV(jc):
                par = jc % 3
                vch = v16(o_vhp + pp * 32 * 130 + jc * 130, 130)
                b.mm(psA[0:65, 0:512], vch[:, 0:65], v16(o_P + par * 1024, 512),
                     start=(jc == 0), stop=(jc == 31),
                     deps=[texp[par]] + (tev_av if jc == 0 else []))
                tavmm_c[0] = b.mm(psA[0:65, 512:1024], vch[:, 65:130],
                                  v16(o_P + par * 1024 + 512, 512),
                                  start=(jc == 0), stop=(jc == 31), inc="mm")
                tav_par[par] = tavmm_c[0]

            emit_S(0)
            emit_S(1)
            for jc in range(32):
                if jc + 2 < 32:
                    emit_S(jc + 2)
                emit_AV(jc)
            tavmm = tavmm_c[0]
            e1 = b.copy(v16(o_avT + hp * T + ts_, 512, 0, 64), psA[0:64, 0:512],
                        deps=[tavmm], inc="ve")
            e2 = b.copy(v16(o_avT + hp * T + ts_, 512, 64, 64),
                        psA[0:64, 512:1024], deps=[tavmm], inc="ve")
            s1 = b.copy(v32(o_sst, 512, 64, 1), psA[64:65, 0:512],
                        deps=[tavmm] + tsum_dma, inc="ve")
            s2 = b.copy(v32(o_sst + 512, 512, 64, 1), psA[64:65, 512:1024],
                        deps=[tavmm] + tsum_dma, inc="ve")
            tsum_dma = [
                b.dma("gpsimd", E["sums_d"][2 * hp:2 * hp + 1, ts_:ts_ + 512],
                      v32(o_sst, 512, 64, 1), deps=[s1], inc="d0"),
                b.dma("gpsimd", E["sums_d"][2 * hp + 1:2 * hp + 2, ts_:ts_ + 512],
                      v32(o_sst + 512, 512, 64, 1), deps=[s2], inc="d1"),
            ]
            tev_av = [e1, e2, s1, s2]
    b.barrier()
    ar16.freeblk(o_khp, 2 * QFULL)
    ar16.freeblk(o_vhp, 2 * 32 * 130)
    ar16.freeblk(o_P, 3 * 1024)
    ar32.freeblk(o_sst, 1024)
    o_E4 = ar16.alloc(40 * 128, "E4")
    b.memset(v16(o_E4, 40 * 128), 0.0)
    for g in range(40):
        h = g // 5
        s0 = (h % 2) * 64
        p0 = (g % 2) * 64
        sub = v16(o_E4 + g * 128 + p0, 64, s0, 64)
        b.affine(sub, sub, [[-1, 64]], ALU.not_equal, 1.0, base=0, chan=1)
    tap("avT", A16[:, o_avT:o_avT + 4 * T], [b.tok("ve")])
    ckpt("S11")

    trd = b.dma("sync", v32(o_sums_rd, T, 0, 8), E["sums_d"][:, :],
                deps=[b.tok("d0"), b.tok("d1")], inc="x")
    o_rs = ar16.alloc(T, "recip16")
    tr1 = b.recip(v32(o_sums_rd, T, 0, 8), v32(o_sums_rd, T, 0, 8),
                  deps=[trd], inc="ve")
    tr2 = b.copy(v16(o_rs, T, 0, 8), v32(o_sums_rd, T, 0, 8), deps=[tr1], inc="ve")
    ar32.freeblk(o_sums_rd, T)
    o_mdT = ar16.alloc(T, "mdT")
    tmdT = []
    for tcn in range(NT):
        pv = psD[0:40, 0:64].bitcast(F16)
        tt_ = b.trans(pv, v16(o_msk + NT * 40 + tcn * 40, 40), ident,
                      deps=tmdT[-1:], inc="mm")
        tmdT.append(b.copy(v16(o_mdT + tcn * 128, 128, 0, 40), pv,
                           deps=[tt_], inc="ve"))
    o_gT = ar16.alloc(T, "gT")
    tgT = []
    for tc4 in range(4):
        s = tc4 * 512
        tm1 = b.mm(psC[0:40, :], v16(o_E7, 40, 0, 8), v16(o_rs + s, 512, 0, 8),
                   True, True, deps=[tr2, tsetup] + tmdT[-1:] + tgT[-1:], inc="mm")
        tgT.append(b.tt(v16(o_gT + s, 512, 0, 40), v16(o_mdT + s, 512, 0, 40),
                        psC[0:40, :], ALU.mult, deps=[tm1] + tmdT, inc="ve"))
    tap("gT", A16[0:40, o_gT:o_gT + T], [b.tok("ve")])
    b.barrier()
    ar16.freeblk(o_mdT, T)
    ar16.freeblk(o_rs, T)

    o_om = ar16.alloc(20 * 512, "OM")
    o_gbc = ar16.alloc(2 * 512, "gbc sbuf")
    o_x1s2 = ar32.alloc(2 * 512, "x1 slc f")
    o_yout = ar32.alloc(2 * 512, "y out")
    tx1 = [None, None]
    ty = [None, None]
    tomlast = []
    n_out = 0
    tgbe_hist = [None, None]
    tom_prev = [None, None]
    for tc4 in range(4):
        s = tc4 * 512
        tom = []
        for cc in range(20):
            par = cc % 2
            pG = psD[:, :] if par == 0 else psA[:, 1024:1536]
            pAv = psC[:, :] if par == 0 else psA[:, 1536:2048]
            tslot = tom[cc - 2] if cc >= 2 else tom_prev[par]
            tgb = b.mm(pG, v16(o_E3 + cc * 128, 128, 0, 40),
                       v16(o_gT + s, 512, 0, 40), True, True,
                       deps=tgT + [tgbe_hist[par]], inc="mm")
            tgbe = b.act(v16(o_gbc + par * 512, 512), pG, AF.Identity,
                         deps=[tgb, tslot], inc="se")
            tgbe_hist[par] = tgbe
            tab = None
            for half in range(2):
                gidx = 2 * cc + half
                hh = gidx // 5
                hp = hh // 2
                tab = b.mm(pAv, v16(o_E4 + gidx * 128, 128),
                           v16(o_avT + hp * T + s, 512),
                           start=(half == 0), stop=(half == 1),
                           deps=[tslot] if half == 0 else None,
                           inc="mm" if half == 1 else None)
            tom.append(b.tt(v16(o_om + cc * 512, 512),
                            v16(o_gbc + par * 512, 512),
                            pAv, ALU.mult, deps=[tgbe, tab], inc="ve"))
        tom_prev = [tom[18], tom[19]]
        for mc in range(KCH):
            par = mc % 2
            tdx = b.dma("sync", v32(o_x1s2 + par * 512, 512),
                        E["x1d"][128 * mc:128 * (mc + 1), s:s + 512],
                        deps=[ty[par]], inc="x0" if par == 0 else "x1")
            ptile = psB[:, 0:512] if par == 0 else psB[:, 512:1024]
            tmm = None
            for cc in range(20):
                tmm = b.mm(ptile, v16(o_wo + cc * 1024 + mc * 128, 128),
                           v16(o_om + cc * 512, 512),
                           start=(cc == 0), stop=(cc == 19),
                           deps=[tom[cc], ty[par]] + two[-1:],
                           inc="mm" if cc == 19 else None)
            tyv = b.tt(v32(o_yout + par * 512, 512), ptile,
                       v32(o_x1s2 + par * 512, 512), ALU.add,
                       deps=[tmm, tdx], inc="ve")
            ty[par] = b.dma("gpsimd", E["y"][128 * mc:128 * (mc + 1), s:s + 512],
                            v32(o_yout + par * 512, 512), deps=[tyv], inc="out")
            n_out += 1
        tomlast = tom
        if "om0" in taps and tc4 == 0:
            tap("om0", v16(o_om, 512), [tomlast[0]])
    b.final_wait([b.tok("out")])
    return ar32.peak, ar16.peak


TAP_SHAPES = {
    "xin": ([C, T], F16), "q": ([C, T], F16),
    "ktok": ([128, NT * C], F16), "vtok": ([128, NT * C], F16),
    "kvf": ([64, H_A * HD], F32), "att": ([C, T], F16),
    "x1": ([C, T], F32), "gates": ([128, NT * 80], F32),
    "sdg": ([128, NT * 40], F32), "md": ([128, NT * 40], F16),
    "kT": ([128, T], F16), "vext": ([128, NT * 520], F16),
    "qf": ([128, 4 * T], F16), "avT": ([128, 4 * T], F16),
    "gT": ([40, T], F16), "om0": ([128, 512], F16),
    "lnab": ([1, 2 * T], F16),
    "kvab": ([16, 2 * T], F16),
    "kvsums": ([64, T], F32),
    "lnsums": ([2, T], F32),
}


def build_nc(taps=(), stop=None):
    nc = bass.Bass("TRN2", target_bir_lowering=False, debug=False)
    es = ExitStack()
    d = {}
    for name, shape in [
        ("xT", [C, T]), ("qkvB", [24, KCH, 128, 128]), ("qkvb", [128, 24]),
        ("p1B", [KCH, KCH, 128, 128]), ("p1b", [128, KCH]),
        ("p2B", [KCH, KCH, 128, 128]), ("p2b", [128, KCH]),
        ("lngb", [128, 4]), ("tmpc", [128, KCH]), ("abT", [64, H_A * HD]),
        ("wkvB", [8, KCH, 128, 640]), ("wqB", [4, KCH, 128, 640]),
        ("wsdB", [KCH, 128, 80]), ("wo2B", [20, KCH, 128, 128]),
    ]:
        d[name] = nc.dram_tensor(name, shape, F32, kind="ExternalInput").ap()

    E = {"d": d}
    E["y"] = nc.dram_tensor("y", [C, T], F32, kind="ExternalOutput").ap()
    E["x1d"] = nc.dram_tensor("x1d", [C, T], F32).ap()
    E["agk_in"] = nc.dram_tensor("agk_in", [512 * 2048], F16).ap()
    E["agk_out"] = nc.dram_tensor("agk_out", [2 * 512 * 2048], F16).ap()
    E["agv_in"] = nc.dram_tensor("agv_in", [2048 * 520], F16).ap()
    E["agv_out"] = nc.dram_tensor("agv_out", [2 * 2048 * 520 + 2048], F16).ap()
    E["kv_in"] = nc.dram_tensor("kv_in", [64, H_A * HD], F32).ap()
    E["kv_out"] = nc.dram_tensor("kv_out", [64, H_A * HD], F32).ap()
    E["sums_d"] = nc.dram_tensor("sums_d", [8, T], F32).ap()
    E["tap_t"] = {}
    for t in taps:
        shp, dt_ = TAP_SHAPES[t]
        E["tap_t"][t] = nc.dram_tensor("tap_" + t, list(shp), dt_,
                                       kind="ExternalOutput").ap()

    E["A32"] = es.enter_context(nc.sbuf_tensor("af32", [128, NF32], F32))
    E["A16"] = es.enter_context(nc.sbuf_tensor("af16", [128, NF16], F16))
    E["psA"] = es.enter_context(nc.psum_tensor("psA", [128, 2048], F32))
    E["psB"] = es.enter_context(nc.psum_tensor("psB", [128, 1024], F32))
    E["psE"] = es.enter_context(nc.psum_tensor("psE", [128, 1024], F32))
    E["psC"] = E["psE"][:, 0:512]
    E["psD"] = E["psE"][:, 512:1024]
    sems = {s: es.enter_context(nc.semaphore(s)) for s in SEMS}

    ref_cnt = {}
    with nc.Block() as blk:
        def mk(role):
            def f(eng):
                cnt = {s: 0 for s in SEMS}
                bb = B(nc, role, sems, cnt)
                try:
                    _program(bb, E, set(taps), stop=stop)
                except Exception as ex:
                    if type(ex).__name__ != "_Stop":
                        raise
                if not ref_cnt:
                    ref_cnt.update(cnt)
                else:
                    assert ref_cnt == cnt, (role, ref_cnt, cnt)
            return f
        blk.tensor(mk("tensor"))
        blk.vector(mk("vector"))
        blk.scalar(mk("scalar"))
        blk.gpsimd(mk("gpsimd"))
        blk.sync(mk("sync"))
    es.close()
    return nc


def layout_inputs(inputs):
    f = lambda a: np.ascontiguousarray(np.asarray(a, dtype=np.float32))
    x = f(inputs["L0_attn_x"])
    qkv_w = f(inputs["L0_attn_qkv_w"])
    qkv_b = f(inputs["L0_attn_qkv_b"])
    p1_w, p1_b = f(inputs["L0_attn_p1_w"]), f(inputs["L0_attn_p1_b"])
    p2_w, p2_b = f(inputs["L0_attn_p2_w"]), f(inputs["L0_attn_p2_b"])
    kn_g, kn_b = f(inputs["L0_attn_kn_g"]), f(inputs["L0_attn_kn_b"])
    vn_g, vn_b = f(inputs["L0_attn_vn_g"]), f(inputs["L0_attn_vn_b"])
    temp = f(inputs["L0_attn_temperature"]).reshape(H_A)
    ab = f(inputs["L0_attn_attn_bias"])
    Wq, Wkv = f(inputs["L0_ffn_Wq"]), f(inputs["L0_ffn_Wkv"])
    Ws, Wd, Wo = f(inputs["L0_ffn_Ws"]), f(inputs["L0_ffn_Wd"]), f(inputs["L0_ffn_Wo"])

    blk = lambda w, mcs: np.ascontiguousarray(
        w.reshape(KCH, 128, mcs, 128).transpose(2, 0, 1, 3))
    shared = {
        "qkvB": blk(qkv_w.T, 24),
        "qkvb": np.ascontiguousarray(qkv_b.reshape(24, 128).T),
        "p1B": blk(p1_w.T, KCH),
        "p1b": np.ascontiguousarray(p1_b.reshape(KCH, 128).T),
        "p2B": blk(p2_w.T, KCH),
        "p2b": np.ascontiguousarray(p2_b.reshape(KCH, 128).T),
        "lngb": np.ascontiguousarray(np.stack(
            [np.tile(kn_g, 2), np.tile(kn_b, 2), np.tile(vn_g, 2),
             np.tile(vn_b, 2)], axis=1)),
        "tmpc": np.ascontiguousarray(np.repeat(temp, HD).reshape(KCH, 128).T),
        "abT": np.ascontiguousarray(ab.transpose(1, 0, 2).reshape(64, H_A * HD)),
        "wkvB": np.ascontiguousarray(
            Wkv.reshape(KCH, 128, 8, 640).transpose(2, 0, 1, 3)),
        "wqB": np.ascontiguousarray(
            Wq.reshape(KCH, 128, 4, 640).transpose(2, 0, 1, 3)),
        "wsdB": np.ascontiguousarray(
            np.concatenate([Ws, Wd], axis=1).reshape(KCH, 128, 80)),
        "wo2B": np.ascontiguousarray(
            np.tile(Wo.reshape(64, 5, 1024).transpose(1, 0, 2).reshape(320, 1024),
                    (8, 1)).reshape(20, 128, KCH, 128).transpose(0, 2, 1, 3)),
    }
    in_maps = []
    for c in range(8):
        bq, half = c // 2, c % 2
        m = dict(shared)
        m["xT"] = np.ascontiguousarray(x[bq, half * T:(half + 1) * T, :].T)
        in_maps.append(m)
    return in_maps


def kernel(**inputs):
    in_maps = layout_inputs(inputs)
    nc = build_nc()
    res = run_bass_kernel_spmd(nc, in_maps, core_ids=list(range(8)))
    out = np.empty((4, QFULL, C), dtype=np.float32)
    for c in range(8):
        bq, half = c // 2, c % 2
        out[bq, half * T:(half + 1) * T, :] = res.results[c]["y"].T
    return out
